# version 02
# baseline (speedup 1.0000x reference)
import numpy as np
from contextlib import ExitStack
import concourse.bass as bass
import concourse.mybir as mybir

F32 = mybir.dt.float32
BF16 = mybir.dt.bfloat16
U8 = mybir.dt.uint8
U32 = mybir.dt.uint32
I32 = mybir.dt.int32
AF = mybir.ActivationFunctionType
ALU = mybir.AluOpType
AX = mybir.AxisListType.X

NCORES = 8
T = 1024
D = 4096
EPS = 1e-5


class Step:
    __slots__ = ("eng", "fn", "deps", "seq", "sem", "count", "tag", "_n", "_amt")


class Sched:
    ENGINES = ("tensor", "vector", "scalar", "gpsimd", "sync")

    def __init__(self, nc):
        self.nc = nc
        self.es = ExitStack()
        self.steps = []
        self.sems = {}
        self.cnt = {e: 0 for e in self.ENGINES}
        self._nbuf = 0

    def sbuf(self, name, shape, dtype):
        return self.es.enter_context(self.nc.sbuf_tensor(name, shape, dtype))

    def psum(self, name, shape, dtype=F32):
        return self.es.enter_context(self.nc.psum_tensor(name, shape, dtype))

    def _sem(self, e):
        if e not in self.sems:
            self.sems[e] = self.es.enter_context(self.nc.semaphore("sem_" + e))
        return self.sems[e]

    def add(self, eng, fn, deps=(), n_instr=1, amt=None, tag=""):
        assert eng in self.ENGINES, eng
        if amt is None:
            amt = 16 if eng == "sync" else 1
        st = Step()
        st.eng = eng
        st.fn = fn
        st.deps = [d for d in deps if d is not None]
        st.seq = len(self.steps)
        st.sem = eng + "_dma" if (amt == 16 and eng != "sync") else eng
        if amt == 16:
            if not hasattr(self, "_last_dma"):
                self._last_dma = {}
            prevd = self._last_dma.get(st.sem)
            if prevd is not None and prevd not in st.deps:
                st.deps.append(prevd)
            self._last_dma[st.sem] = st
        for d in st.deps:
            assert d.seq < st.seq, f"dep cycle: {d.seq} !< {st.seq}"
        self.cnt.setdefault(st.sem, 0)
        self.cnt[st.sem] += n_instr * amt
        st.count = self.cnt[st.sem]
        st.tag = tag
        st._n = n_instr
        st._amt = amt
        self.steps.append(st)
        return st

    def dma(self, eng, fn, deps=(), n=1, tag=""):
        return self.add(eng, fn, deps, n_instr=n, amt=16, tag=tag)

    def barrier_step(self, eng, deps, tag="barrier"):
        return self.add(eng, lambda e: [e.engine_nop()] if hasattr(e, "engine_nop") else [e.nop()],
                        deps=deps, tag=tag)

    def emit(self):
        nc = self.nc
        with self.es, nc.Block() as block:
            for ename in self.ENGINES:
                mine = [s for s in self.steps if s.eng == ename]
                if not mine:
                    continue
                deco = getattr(block, ename)

                def section(eng, _mine=mine, _ename=ename):
                    done = {}
                    for st in _mine:
                        waits = {}
                        for d in st.deps:
                            if d.count <= done.get(d.sem, 0):
                                continue
                            waits[d.sem] = max(waits.get(d.sem, 0), d.count)
                        for s_name, c in sorted(waits.items()):
                            eng.wait_ge(self._sem(s_name), c)
                            done[s_name] = max(done.get(s_name, 0), c)
                        instrs = st.fn(eng)
                        assert len(instrs) == st._n, (st.tag, len(instrs), st._n)
                        for i in instrs:
                            i.then_inc(self._sem(st.sem), st._amt)

                deco(section)
            if getattr(self, "final_deps", None):
                fe = getattr(self, "final_engine", "gpsimd")
                deco = getattr(block, fe)

                def tail(eng):
                    waits = {}
                    for d in self.final_deps:
                        waits[d.sem] = max(waits.get(d.sem, 0), d.count)
                    for s_name, c in sorted(waits.items()):
                        eng.wait_ge(self._sem(s_name), c)

                deco(tail)


def cast_load(S, name, dram_ap, rows, cols, out_dtype=BF16, deps=(), tile_rows=128,
              eng_dma="sync", stage_cols=None):
    nc = S.nc
    nk = rows // tile_rows
    dst = S.sbuf(name, [tile_rows, nk * cols], out_dtype)
    stg_cols = stage_cols or cols
    nstg = 1 if nk == 1 else 2
    stg = [S.sbuf(f"{name}_stg{i}", [tile_rows, stg_cols], F32) for i in range(nstg)]
    ready = []
    prev_cast = [None, None]
    for k in range(nk):
        p = k % nstg
        war = prev_cast[p]
        ld = S.dma(eng_dma,
                   lambda e, k=k, p=p: [e.dma_start(out=stg[p][:, :cols],
                                                    in_=dram_ap[k * tile_rows:(k + 1) * tile_rows, :])],
                   deps=list(deps) + ([war] if war else []), tag=f"{name}_ld{k}")
        cst = S.add("vector",
                    lambda e, k=k, p=p: [e.tensor_copy(dst[:, k * cols:(k + 1) * cols], stg[p][:, :cols])],
                    deps=[ld], tag=f"{name}_cast{k}")
        prev_cast[p] = cst
        ready.append(cst)
    return dst, ready


def ln_stats_steps(S, x_tok_dram, t_tiles, d, stats, deps=(), tag="ln"):
    nc = S.nc
    nt = t_tiles
    stg = [S.sbuf(f"{tag}_xstg0", [128, d // 2], F32)]
    sq = S.sbuf(f"{tag}_sq", [128, d // 2], BF16)
    hcol = S.sbuf(f"{tag}_hcol", [128, 4], F32)
    mcol = S.sbuf(f"{tag}_m", [128, nt], F32)
    qcol = S.sbuf(f"{tag}_q", [128, nt], F32)
    rstd = S.sbuf(f"{tag}_rstd", [128, nt], F32)
    sstd = S.sbuf(f"{tag}_sstd", [128, nt], F32)
    negm = S.sbuf(f"{tag}_negm", [128, nt], F32)
    tmp = S.sbuf(f"{tag}_tmp", [128, 1], F32)
    var = S.sbuf(f"{tag}_var", [128, 1], F32)
    negm_row = S.sbuf(f"{tag}_negm_row", [1, nt * 128], BF16)
    sstd_row = S.sbuf(f"{tag}_sstd_row", [1, nt * 128], BF16)
    rowf = S.sbuf(f"{tag}_rowf", [1, 128], F32)
    stats.update(rstd_c=rstd, negm_row=negm_row, sstd_row=sstd_row, negm_c=negm, sstd_c=sstd)
    fin = []
    prev_use = [None, None]
    prev_sq = None
    prev_rowf = None
    prev_var = None
    prev_sstd = None
    for tt in range(nt):
        p = 0
        hsteps = []
        for hf in range(2):
            ld = S.dma("sync", lambda e, tt=tt, hf=hf: [e.dma_start(out=stg[0][:, :],
                       in_=x_tok_dram[tt * 128:(tt + 1) * 128, hf * (d // 2):(hf + 1) * (d // 2)])],
                       deps=list(deps) + ([prev_use[0]] if prev_use[0] else []), tag=f"{tag}_ld{tt}_{hf}")
            rs_h = S.add("vector", lambda e, hf=hf: [e.reduce_sum(hcol[:, hf:hf + 1], stg[0][:, :], axis=AX)],
                         deps=[ld] + ([prev_use[0]] if prev_use[0] else []), tag=f"{tag}_rsh{tt}_{hf}")
            sqs = S.add("scalar", lambda e, hf=hf: [e.activation(sq[:, :], stg[0][:, :], AF.Square)],
                        deps=[ld, rs_h] + ([prev_sq] if prev_sq else []), tag=f"{tag}_sqh{tt}_{hf}")
            rq_h = S.add("vector", lambda e, hf=hf: [e.reduce_sum(hcol[:, 2 + hf:3 + hf], sq[:, :], axis=AX)],
                         deps=[sqs], tag=f"{tag}_rqh{tt}_{hf}")
            prev_sq = rq_h
            prev_use[0] = rq_h
            hsteps.append(rq_h)
        rs = S.add("vector", lambda e, tt=tt: [e.tensor_tensor(mcol[:, tt:tt + 1], hcol[:, 0:1], hcol[:, 1:2], ALU.add)],
                   deps=hsteps, tag=f"{tag}_rs{tt}")
        rq = S.add("vector", lambda e, tt=tt: [e.tensor_tensor(qcol[:, tt:tt + 1], hcol[:, 2:3], hcol[:, 3:4], ALU.add)],
                   deps=hsteps, tag=f"{tag}_rq{tt}")
        sa = S.add("vector", lambda e, tt=tt: [e.tensor_scalar(negm[:, tt:tt + 1], mcol[:, tt:tt + 1], -1.0 / d, None, ALU.mult)],
                   deps=[rs], tag=f"{tag}_negm{tt}")
        sb = S.add("vector", lambda e, tt=tt: [e.tensor_scalar(tmp[:, :], negm[:, tt:tt + 1], negm[:, tt:tt + 1], None, ALU.mult)],
                   deps=[sa, prev_var], tag=f"{tag}_m2{tt}")
        st1 = S.add("vector", lambda e, tt=tt: [e.scalar_tensor_tensor(var[:, :], qcol[:, tt:tt + 1], 1.0 / d, tmp[:, :], ALU.mult, ALU.subtract)],
                    deps=[sb, rq, prev_sstd], tag=f"{tag}_var{tt}")
        st1b = S.add("vector", lambda e, tt=tt: [e.tensor_scalar(var[:, :], var[:, :], EPS, None, ALU.add)],
                     deps=[st1], tag=f"{tag}_vep{tt}")
        st2 = S.add("scalar", lambda e, tt=tt: [e.activation(sstd[:, tt:tt + 1], var[:, :], AF.Sqrt)],
                    deps=[st1b], tag=f"{tag}_sstd{tt}")
        st3 = S.add("vector", lambda e, tt=tt: [e.reciprocal(rstd[:, tt:tt + 1], sstd[:, tt:tt + 1])],
                    deps=[st2], tag=f"{tag}_rstd{tt}")
        prev_var = st1
        prev_sstd = st2
        d1 = S.dma("gpsimd", lambda e, tt=tt: [e.dma_start(out=rowf[:, :], in_=negm[:, tt:tt + 1])],
                   deps=[st1] + ([prev_rowf] if prev_rowf else []), tag=f"{tag}_rowm{tt}")
        c1 = S.add("vector", lambda e, tt=tt: [e.tensor_copy(negm_row[:, tt * 128:(tt + 1) * 128], rowf[:, :])],
                   deps=[d1], tag=f"{tag}_rowmc{tt}")
        d2 = S.dma("gpsimd", lambda e, tt=tt: [e.dma_start(out=rowf[:, :], in_=sstd[:, tt:tt + 1])],
                   deps=[st2, c1], tag=f"{tag}_rows{tt}")
        c2 = S.add("vector", lambda e, tt=tt: [e.tensor_copy(sstd_row[:, tt * 128:(tt + 1) * 128], rowf[:, :])],
                   deps=[d2], tag=f"{tag}_rowsc{tt}")
        prev_rowf = c2
        fin.append((st3, c1, c2))
    return fin


def small_cast(S, name, dram_ap, shape, deps=(), dtype=BF16):
    stg = S.sbuf(name + "_stg", shape, F32)
    dst = S.sbuf(name, shape, dtype)
    ld = S.dma("sync", lambda e: [e.dma_start(out=stg[:, :], in_=dram_ap)], deps=deps, tag=name + "_ld")
    cst = S.add("vector", lambda e: [e.tensor_copy(dst[:, :], stg[:, :])], deps=[ld], tag=name + "_cast")
    return dst, cst


def emit_l1ffn_core(S, io, pre=None):
    nc = S.nc
    nt = T // 128

    xT_bf, xT_rdy = cast_load(S, "xTs", io["xT"], D, T)
    WrT_bf, wr_rdy = cast_load(S, "WrTs", io["WrT"], D, 8)
    AT_bf, at_rdy = cast_load(S, "ATs", io["AT"], D, 128)
    Ball_bf, ball_rdy = cast_load(S, "Balls", io["Ball"], 128, D)
    br_bf, br_rdy = small_cast(S, "brs", io["br"], [1, 8])
    E8_bf, e8_rdy = small_cast(S, "E8s", io["E8"], [8, 128])
    ident = S.sbuf("idents", [128, 128], BF16)
    id_rdy = S.dma("sync", lambda e: [e.dma_start(out=ident[:, :], in_=io["ident"])], tag="id_ld")

    stats = {}
    ln_fin = ln_stats_steps(S, io["x_tok"], nt, D, stats, tag="ln4")
    rstd_c = stats["rstd_c"]; negm_row = stats["negm_row"]; sstd_row = stats["sstd_row"]

    ones_bf = S.sbuf("ones_bf", [128, 1], BF16)
    ones_rdy = S.add("vector", lambda e: [e.memset(ones_bf[:, :], 1.0)], tag="ones")

    ps_cs = S.psum("ps_cs", [1, 136], F32)
    cs_sb = S.sbuf("cs_sb", [1, 136], BF16)

    def mm_cs(e):
        last = None
        for kt in range(32):
            last = e.matmul(ps_cs[:, 0:8], ones_bf[:, :], WrT_bf[:, kt * 8:(kt + 1) * 8],
                            start=(kt == 0), stop=(kt == 31))
        for kt in range(32):
            last = e.matmul(ps_cs[:, 8:136], ones_bf[:, :], AT_bf[:, kt * 128:(kt + 1) * 128],
                            start=(kt == 0), stop=(kt == 31))
        return [last]
    cs_mm = S.add("tensor", mm_cs, deps=wr_rdy + at_rdy + [ones_rdy], tag="cs_mm")
    cs_ev = S.add("vector", lambda e: [e.tensor_copy(cs_sb[:, :], ps_cs[:, :])], deps=[cs_mm], tag="cs_ev")

    ps_log = S.psum("ps_log", [128, 8], F32)
    ps_low = S.psum("ps_low", [128, 128], F32)
    ps_str = S.psum("ps_str", [8, 128], BF16)
    ps_sexp = S.psum("ps_sexp", [128, 128], F32)
    ps_ltr = S.psum("ps_ltr", [128, 128], BF16)
    ps_out = [S.psum("ps_out0", [128, 512], F32), S.psum("ps_out1", [128, 512], F32)]

    logit_sb = S.sbuf("logit_sb", [128, 8], F32)
    mx = S.sbuf("mx", [128, 1], F32)
    esb = S.sbuf("esb", [128, 8], F32)
    ssum = S.sbuf("ssum", [128, 1], F32)
    rr = S.sbuf("rr", [128, 1], F32)
    scores_bf = S.sbuf("scores_bf", [128, 8], BF16)
    scoresT_bf = S.sbuf("scoresT_bf", [8, 128], BF16)
    low_sb = S.sbuf("low_sb", [128, 128], F32)
    lowp_bf = S.sbuf("lowp_bf", [128, 128], BF16)
    lowpT_bf = S.sbuf("lowpT_bf", [128, 128], BF16)
    so_sb = S.sbuf("so_sb", [128, D], F32)
    sq2 = S.sbuf("sq2", [128, D], F32)
    m2 = S.sbuf("m2", [128, 1], F32)
    q2 = S.sbuf("q2", [128, 1], F32)
    v2 = S.sbuf("v2", [128, 1], F32)
    sd2 = S.sbuf("sd2", [128, 1], F32)
    r2 = S.sbuf("r2", [128, 1], F32)
    xres = S.sbuf("xres", [128, D], F32)

    prev = {}
    out_steps = []
    for tt in range(nt):
        t0, t1 = tt * 128, (tt + 1) * 128
        fin_rstd, fin_negm, fin_sstd = ln_fin[tt]

        def mm_log(e, tt=tt, t0=t0):
            last = None
            for kt in range(32):
                e.matmul(ps_log[:, :], xT_bf[:, kt * T + t0:kt * T + t0 + 128],
                         WrT_bf[:, kt * 8:(kt + 1) * 8], start=(kt == 0), stop=False)
            e.matmul(ps_log[:, :], negm_row[:, t0:t0 + 128], cs_sb[:, 0:8], start=False, stop=False)
            last = e.matmul(ps_log[:, :], sstd_row[:, t0:t0 + 128], br_bf[:, :], start=False, stop=True)
            return [last]
        dlog = [cs_ev, br_rdy, fin_negm, fin_sstd] + xT_rdy + wr_rdy + ([prev["log_ev"]] if "log_ev" in prev else [])
        mmlog = S.add("tensor", mm_log, deps=dlog, tag=f"mmlog{tt}")

        def mm_low(e, tt=tt, t0=t0):
            for kt in range(32):
                e.matmul(ps_low[:, :], xT_bf[:, kt * T + t0:kt * T + t0 + 128],
                         AT_bf[:, kt * 128:(kt + 1) * 128], start=(kt == 0), stop=False)
            return [e.matmul(ps_low[:, :], negm_row[:, t0:t0 + 128], cs_sb[:, 8:136], start=False, stop=True)]
        mmlow = S.add("tensor", mm_low, deps=dlog + at_rdy + ([prev["low_ev"]] if "low_ev" in prev else []), tag=f"mmlow{tt}")

        log_ev = S.add("vector", lambda e, tt=tt: [e.tensor_scalar(logit_sb[:, :], ps_log[:, :], rstd_c[:, tt:tt + 1], None, ALU.mult)],
                       deps=[mmlog, fin_rstd] + ([prev["sm_done"]] if "sm_done" in prev else []), tag=f"logev{tt}")
        prev["log_ev"] = log_ev
        smx = S.add("vector", lambda e: [e.reduce_max(mx[:, :], logit_sb[:, :], axis=AX, negate=True)],
                    deps=[log_ev], tag=f"smx{tt}")
        sex = S.add("scalar", lambda e: [e.activation(esb[:, :], logit_sb[:, :], AF.Exp, bias=mx[:, :])],
                    deps=[smx], tag=f"sex{tt}")
        ssm = S.add("vector", lambda e: [e.reduce_sum(ssum[:, :], esb[:, :], axis=AX)], deps=[sex], tag=f"ssm{tt}")
        src = S.add("vector", lambda e: [e.reciprocal(rr[:, :], ssum[:, :])], deps=[ssm], tag=f"src{tt}")
        ssc = S.add("vector", lambda e: [e.tensor_scalar(scores_bf[:, :], esb[:, :], rr[:, :], None, ALU.mult)],
                    deps=[src], tag=f"ssc{tt}")
        prev["sm_done"] = ssc
        str_ = S.add("tensor", lambda e: [e.transpose(ps_str[:, :], scores_bf[:, :], ident[:, :])],
                     deps=[ssc, id_rdy] + ([prev["strev"]] if "strev" in prev else []), tag=f"str{tt}")
        strev = S.add("vector", lambda e: [e.tensor_copy(scoresT_bf[:, :], ps_str[:, :])], deps=[str_], tag=f"strev{tt}")
        prev["strev"] = strev
        sexp = S.add("tensor", lambda e: [e.matmul(ps_sexp[:, :], scoresT_bf[:, :], E8_bf[:, :], start=True, stop=True)],
                     deps=[strev, e8_rdy] + ([prev["lowp"]] if "lowp" in prev else []), tag=f"sexp{tt}")
        low_ev = S.add("vector", lambda e, tt=tt: [e.tensor_scalar(low_sb[:, :], ps_low[:, :], rstd_c[:, tt:tt + 1], None, ALU.mult)],
                       deps=[mmlow, fin_rstd] + ([prev["lowp"]] if "lowp" in prev else []), tag=f"lowev{tt}")
        prev["low_ev"] = low_ev
        lowp = S.add("vector", lambda e: [e.tensor_tensor(lowp_bf[:, :], low_sb[:, :], ps_sexp[:, :], ALU.mult)],
                     deps=[low_ev, sexp] + ([prev["ltr"]] if "ltr" in prev else []), tag=f"lowp{tt}")
        prev["lowp"] = lowp
        ltr = S.add("tensor", lambda e: [e.transpose(ps_ltr[:, :], lowp_bf[:, :], ident[:, :])],
                    deps=[lowp] + ([prev["ltrev"]] if "ltrev" in prev else []), tag=f"ltr{tt}")
        prev["ltr"] = ltr
        ltrev = S.add("vector", lambda e: [e.tensor_copy(lowpT_bf[:, :], ps_ltr[:, :])], deps=[ltr], tag=f"ltrev{tt}")
        prev["ltrev"] = ltrev

        so_evs = []
        for ds in range(8):
            pb = ds % 2
            mmout = S.add("tensor",
                          lambda e, ds=ds, pb=pb: [e.matmul(ps_out[pb][:, :], lowpT_bf[:, :], Ball_bf[:, ds * 512:(ds + 1) * 512], start=True, stop=True)],
                          deps=[ltrev] + ball_rdy + ([prev[f"so_ev{pb}"]] if f"so_ev{pb}" in prev else []), tag=f"mmout{tt}_{ds}")
            so_ev = S.add("vector", lambda e, ds=ds, pb=pb: [e.tensor_copy(so_sb[:, ds * 512:(ds + 1) * 512], ps_out[pb][:, :])],
                          deps=[mmout], tag=f"soev{tt}_{ds}")
            prev[f"so_ev{pb}"] = so_ev
            so_evs.append(so_ev)

        rs2 = S.add("vector", lambda e: [e.reduce_sum(m2[:, :], so_sb[:, :], axis=AX)], deps=so_evs, tag=f"rs2_{tt}")
        sq2s = S.add("scalar", lambda e: [e.activation(sq2[:, :], so_sb[:, :], AF.Square)],
                     deps=so_evs + ([prev["rq2"]] if "rq2" in prev else []), tag=f"sq2_{tt}")
        rq2 = S.add("vector", lambda e: [e.reduce_sum(q2[:, :], sq2[:, :], axis=AX)], deps=[sq2s], tag=f"rq2_{tt}")
        prev["rq2"] = rq2
        n2 = S.add("vector", lambda e: [e.tensor_scalar(m2[:, :], m2[:, :], 1.0 / D, None, ALU.mult)], deps=[rs2], tag=f"n2_{tt}")
        v2a = S.add("vector", lambda e: [e.tensor_scalar(v2[:, :], m2[:, :], m2[:, :], None, ALU.mult)], deps=[n2], tag=f"v2a_{tt}")
        v2b = S.add("vector", lambda e: [e.scalar_tensor_tensor(v2[:, :], q2[:, :], 1.0 / D, v2[:, :], ALU.mult, ALU.subtract)],
                    deps=[v2a, rq2], tag=f"v2b_{tt}")
        v2c = S.add("vector", lambda e: [e.tensor_scalar(v2[:, :], v2[:, :], EPS, None, ALU.add)], deps=[v2b], tag=f"v2c_{tt}")
        sd2s = S.add("scalar", lambda e: [e.activation(sd2[:, :], v2[:, :], AF.Sqrt)], deps=[v2c], tag=f"sd2_{tt}")
        r2s = S.add("vector", lambda e: [e.reciprocal(r2[:, :], sd2[:, :])], deps=[sd2s], tag=f"r2_{tt}")
        xld = S.dma("sync", lambda e, t0=t0: [e.dma_start(out=xres[:, :], in_=io["x_tok"][t0:t0 + 128, :])],
                    deps=([prev["oadd"]] if "oadd" in prev else []), tag=f"xld{tt}")
        oln = S.add("vector", lambda e: [e.tensor_scalar(so_sb[:, :], so_sb[:, :], m2[:, :], r2[:, :], ALU.subtract, ALU.mult)],
                    deps=[r2s] + so_evs, tag=f"oln{tt}")
        oadd = S.add("vector", lambda e: [e.tensor_tensor(xres[:, :], xres[:, :], so_sb[:, :], ALU.add)],
                     deps=[oln, xld] + ([prev["ost"]] if "ost" in prev else []), tag=f"oadd{tt}")
        prev["oadd"] = oadd
        ost = S.dma("sync", lambda e, t0=t0: [e.dma_start(out=io["xout"][t0:t0 + 128, :], in_=xres[:, :])],
                    deps=[oadd], tag=f"ost{tt}")
        prev["ost"] = ost
        out_steps.append(ost)
    return out_steps


def build_l1ffn_nc():
    nc = bass.Bass("TRN2", target_bir_lowering=False)
    io = {
        "xT": nc.dram_tensor("xT", [D, T], F32, kind="ExternalInput").ap(),
        "x_tok": nc.dram_tensor("x_tok", [T, D], F32, kind="ExternalInput").ap(),
        "WrT": nc.dram_tensor("WrT", [D, 8], F32, kind="ExternalInput").ap(),
        "br": nc.dram_tensor("br", [1, 8], F32, kind="ExternalInput").ap(),
        "AT": nc.dram_tensor("AT", [D, 128], F32, kind="ExternalInput").ap(),
        "Ball": nc.dram_tensor("Ball", [128, D], F32, kind="ExternalInput").ap(),
        "E8": nc.dram_tensor("E8", [8, 128], F32, kind="ExternalInput").ap(),
        "ident": nc.dram_tensor("ident", [128, 128], BF16, kind="ExternalInput").ap(),
        "xout": nc.dram_tensor("xout", [T, D], F32, kind="ExternalOutput").ap(),
    }
    S = Sched(nc)
    outs = emit_l1ffn_core(S, io)
    S.final_deps = outs
    S.final_engine = "gpsimd"
    S.emit()
    return nc


def stage_l1ffn_inputs(x_full, Wr, br, A, B, ml_bf16):
    E8 = np.zeros((8, 128), np.float32)
    for h in range(8):
        E8[h, 16 * h:16 * (h + 1)] = 16.0
    ident = np.eye(128).astype(ml_bf16)
    WrT = np.ascontiguousarray(Wr.T)
    AT = np.ascontiguousarray(A.reshape(128, D).T)
    Ball = np.ascontiguousarray(B.transpose(0, 2, 1).reshape(128, D))
    br_row = br.reshape(1, 8).astype(np.float32)
    in_maps = []
    for c in range(NCORES):
        xs = x_full[c * T:(c + 1) * T, :]
        in_maps.append({
            "xT": np.ascontiguousarray(xs.T),
            "x_tok": np.ascontiguousarray(xs),
            "WrT": WrT, "br": br_row, "AT": AT, "Ball": Ball,
            "E8": E8, "ident": ident,
        })
    return in_maps


def rows_to_cols(S, src_col_tensor, nt, name, deps_per_tile, tag):
    row = S.sbuf(name, [1, nt * 128], F32)
    steps = []
    for tt in range(nt):
        d = S.dma("gpsimd", lambda e, tt=tt: [e.dma_start(out=row[:, tt * 128:(tt + 1) * 128],
                                                          in_=src_col_tensor[:, tt:tt + 1])],
                  deps=deps_per_tile[tt], tag=f"{tag}{tt}")
        steps.append(d)
    return row, steps


def emit_l1attn_qkv(S, io):
    nt = T // 128
    DQ = 1024
    xT_bf, xT_rdy = cast_load(S, "xTs", io["xT"], D, T)
    bq_bf, bq_rdy = small_cast(S, "bqs", io["bqkv"], [1, 3 * DQ])
    stats = {}
    ln_fin = ln_stats_steps(S, io["x_tok"], nt, D, stats, tag="ln2")
    rstd_c = stats["rstd_c"]; negm_row = stats["negm_row"]; sstd_row = stats["sstd_row"]
    rstd_row, rr_steps = rows_to_cols(S, rstd_c, nt, "rstd_row2", [[f[0]] for f in ln_fin], "rr2_")
    rstd_bc = S.sbuf("rstd_bc", [128, T], F32)
    rbd = S.nc.dram_tensor("rstd_scr", [1, T], F32, kind="Internal").ap()
    rb0 = S.dma("sync", lambda e: [e.dma_start(out=rbd, in_=rstd_row[:, :])], deps=rr_steps, tag="rstd_st")
    rb = S.dma("sync", lambda e: [e.dma_start(out=rstd_bc[:, :], in_=rbd.partition_broadcast(128))],
               deps=[rb0], tag="rstd_bc")
    ones_bf = S.sbuf("ones_bf", [128, 1], BF16)
    ones_rdy = S.add("vector", lambda e: [e.memset(ones_bf[:, :], 1.0)], tag="ones")
    ln_all = [x for f in ln_fin for x in f]

    wblk = [S.sbuf(f"wblk{i}", [128, 32 * 128], BF16) for i in range(2)]
    wstg = [S.sbuf(f"wstg{i}", [128, 128], F32) for i in range(2)]
    ps_cs = S.psum("ps_cs", [1, 128], F32)
    cs_sb = S.sbuf("cs_sbq", [1, 128], BF16)
    ps_mm = [S.psum(f"ps_mmq{i}", [128, 512], F32) for i in range(2)]
    ob = [S.sbuf(f"oqk{i}", [128, 512], BF16) for i in range(2)]

    outs = []
    last_wuse = [None, None]
    last_stg = [None, None]
    prev_cs = None
    prev_ps = [None, None]
    prev_ob = [None, None]
    for cb in range(24):
        wb = cb % 2
        casts = []
        for kt in range(32):
            sp = kt % 2
            ld = S.dma("sync", lambda e, cb=cb, kt=kt, sp=sp: [e.dma_start(out=wstg[sp][:, :],
                        in_=io["Wqkv"][kt * 128:(kt + 1) * 128, cb * 128:(cb + 1) * 128])],
                       deps=[last_stg[sp]] + ([last_wuse[wb]] if kt == 0 else []), tag=f"wld{cb}_{kt}")
            cst = S.add("vector", lambda e, kt=kt, sp=sp, wb=wb: [e.tensor_copy(
                        wblk[wb][:, kt * 128:(kt + 1) * 128], wstg[sp][:, :])],
                        deps=[ld] + ([last_wuse[wb]] if kt == 0 else []), tag=f"wcst{cb}_{kt}")
            last_stg[sp] = cst
            casts.append(cst)
        csm = S.add("tensor", lambda e, wb=wb: [[e.matmul(ps_cs[:, :], ones_bf[:, :],
                    wblk[wb][:, kt * 128:(kt + 1) * 128], start=(kt == 0), stop=(kt == 31))
                    for kt in range(32)][-1]],
                    deps=casts + [ones_rdy] + ([prev_cs] if prev_cs else []), tag=f"csm{cb}")
        cse = S.add("vector", lambda e: [e.tensor_copy(cs_sb[:, :], ps_cs[:, :])], deps=[csm], tag=f"cse{cb}")
        prev_cs = cse
        uses = []
        if cb < 16:
            for th in range(2):
                pb = th
                def mmqk(e, cb=cb, th=th, pb=pb, wb=wb):
                    for kt in range(32):
                        e.matmul(ps_mm[pb][:, :], wblk[wb][:, kt * 128:(kt + 1) * 128],
                                 xT_bf[:, kt * T + th * 512:kt * T + (th + 1) * 512],
                                 start=(kt == 0), stop=False)
                    e.matmul(ps_mm[pb][:, :], cs_sb[:, :], negm_row[:, th * 512:(th + 1) * 512],
                             start=False, stop=False)
                    return [e.matmul(ps_mm[pb][:, :], bq_bf[:, cb * 128:(cb + 1) * 128],
                                     sstd_row[:, th * 512:(th + 1) * 512], start=False, stop=True)]
                mm = S.add("tensor", mmqk, deps=casts + xT_rdy + [cse, bq_rdy] + ln_all
                           + ([prev_ps[pb]] if prev_ps[pb] else []), tag=f"mmqk{cb}_{th}")
                ev = S.add("vector", lambda e, th=th, pb=pb: [e.tensor_tensor(ob[pb][:, :], ps_mm[pb][:, :],
                           rstd_bc[:, th * 512:(th + 1) * 512], ALU.mult)],
                           deps=[mm, rb] + ([prev_ob[pb]] if prev_ob[pb] else []), tag=f"evqk{cb}_{th}")
                prev_ps[pb] = ev
                dst = io["qT"] if cb < 8 else io["kT"]
                r0 = (cb % 8) * 128
                st = S.dma("sync", lambda e, dst=dst, r0=r0, th=th, pb=pb: [e.dma_start(
                            out=dst[r0:r0 + 128, th * 512:(th + 1) * 512], in_=ob[pb][:, :])],
                           deps=[ev], tag=f"stqk{cb}_{th}")
                prev_ob[pb] = st
                uses.append(mm)
                outs.append(st)
        else:
            dv0 = (cb - 16) * 128
            for tt in range(nt):
                pb = tt % 2
                def mmv(e, tt=tt, pb=pb, wb=wb, dv0=dv0):
                    for kt in range(32):
                        e.matmul(ps_mm[pb][:, 0:128], xT_bf[:, kt * T + tt * 128:kt * T + (tt + 1) * 128],
                                 wblk[wb][:, kt * 128:(kt + 1) * 128], start=(kt == 0), stop=False)
                    e.matmul(ps_mm[pb][:, 0:128], negm_row[:, tt * 128:(tt + 1) * 128], cs_sb[:, :],
                             start=False, stop=False)
                    return [e.matmul(ps_mm[pb][:, 0:128], sstd_row[:, tt * 128:(tt + 1) * 128],
                                     bq_bf[:, 2 * DQ + dv0:2 * DQ + dv0 + 128], start=False, stop=True)]
                mm = S.add("tensor", mmv, deps=casts + xT_rdy + [cse, bq_rdy] + ln_all
                           + ([prev_ps[pb]] if prev_ps[pb] else []), tag=f"mmv{cb}_{tt}")
                ev = S.add("vector", lambda e, tt=tt, pb=pb: [e.tensor_scalar(ob[pb][:, 0:128],
                           ps_mm[pb][:, 0:128], rstd_c[:, tt:tt + 1], None, ALU.mult)],
                           deps=[mm, ln_fin[tt][0]] + ([prev_ob[pb]] if prev_ob[pb] else []), tag=f"evv{cb}_{tt}")
                prev_ps[pb] = ev
                st = S.dma("sync", lambda e, tt=tt, pb=pb, dv0=dv0: [e.dma_start(
                            out=io["v"][tt * 128:(tt + 1) * 128, dv0:dv0 + 128], in_=ob[pb][:, 0:128])],
                           deps=[ev], tag=f"stv{cb}_{tt}")
                prev_ob[pb] = st
                uses.append(mm)
                outs.append(st)
        last_wuse[wb] = uses[-1]
    return outs


def build_l1attn_qkv_nc():
    nc = bass.Bass("TRN2", target_bir_lowering=False)
    DQ = 1024
    io = {
        "xT": nc.dram_tensor("xT", [D, T], F32, kind="ExternalInput").ap(),
        "x_tok": nc.dram_tensor("x_tok", [T, D], F32, kind="ExternalInput").ap(),
        "Wqkv": nc.dram_tensor("Wqkv", [D, 3 * DQ], F32, kind="ExternalInput").ap(),
        "bqkv": nc.dram_tensor("bqkv", [1, 3 * DQ], F32, kind="ExternalInput").ap(),
        "qT": nc.dram_tensor("qT", [DQ, T], BF16, kind="ExternalOutput").ap(),
        "kT": nc.dram_tensor("kT", [DQ, T], BF16, kind="ExternalOutput").ap(),
        "v": nc.dram_tensor("v", [T, DQ], BF16, kind="ExternalOutput").ap(),
    }
    S = Sched(nc)
    outs = emit_l1attn_qkv(S, io)
    S.final_deps = outs
    S.final_engine = "gpsimd"
    S.emit()
    return nc

def emit_l1attn_flash(S, io):
    nt = T // 128
    DQ = 1024
    S8 = 8192
    NKT = S8 // 128
    kT_sb = S.sbuf("kT_sb", [128, 8 * S8], BF16)
    qT_sb = S.sbuf("qT_sb", [128, 8 * 256], BF16)
    kld = []
    for dk in range(8):
        kld.append(S.dma("sync", lambda e, dk=dk: [e.dma_start(out=kT_sb[:, dk * S8:(dk + 1) * S8],
                    in_=io["kTf"][dk * 128:(dk + 1) * 128, :])], tag=f"kld{dk}"))
    bo_bf, bo_rdy = small_cast(S, "bos", io["bo"], [1, D])
    v_sb = [S.sbuf(f"vs{i}", [128, DQ], BF16) for i in range(2)]
    e_sb = [S.sbuf(f"es{i}", [128, 256], BF16) for i in range(2)]
    ps_sc = S.psum("ps_sc", [128, 256], F32)
    ps_se = S.psum("ps_se", [1, 256], F32)
    ps_oc = [S.psum(f"ps_oc{i}", [128, 512], F32) for i in range(4)]
    ps_z = [S.psum(f"ps_z{i}", [128, 512], F32) for i in range(2)]
    oc_bf = S.sbuf("oc_bf", [128, 8 * 256], BF16)
    se_f = S.sbuf("se_f", [1, 256], F32)
    se_bf = S.sbuf("se_bf", [1, 256], BF16)
    rec_f = S.sbuf("rec_f", [1, 256], F32)
    rec_c = S.sbuf("rec_c", [128, 2], F32)
    wo_stg = [S.sbuf(f"wo_stg{i}", [128, 512], F32) for i in range(2)]
    wo_bf = [S.sbuf(f"wo_bf{i}", [128, 512], BF16) for i in range(2)]
    z_sb = S.sbuf("z_sb", [128, 2 * D], BF16)
    sq3 = S.sbuf("sq3", [128, D], BF16)
    xres3 = S.sbuf("xres3", [128, D // 2], F32)
    m3 = S.sbuf("m3", [128, 1], F32); q3 = S.sbuf("q3", [128, 1], F32)
    v3 = S.sbuf("v3", [128, 1], F32); sd3 = S.sbuf("sd3", [128, 1], F32)
    r3 = S.sbuf("r3", [128, 1], F32)
    ones_bf = S.sbuf("ones_bff", [128, 1], BF16)
    ones_rdy = S.add("vector", lambda e: [e.memset(ones_bf[:, :], 1.0)], tag="ones")

    outs = []
    prev = {}
    for qt in range(4):
        q0 = qt * 256
        qlds = []
        for dk in range(8):
            qlds.append(S.dma("sync", lambda e, dk=dk, q0=q0: [e.dma_start(out=qT_sb[:, dk * 256:(dk + 1) * 256],
                        in_=io["qT"][dk * 128:(dk + 1) * 128, q0:q0 + 256])],
                        deps=[prev.get("q_use")], tag=f"qld{qt}_{dk}"))
        last_mmB = None
        for kt in range(NKT):
            vp = kt % 2
            ep = kt % 2
            vld = S.dma("sync", lambda e, kt=kt, vp=vp: [e.dma_start(out=v_sb[vp][:, :],
                        in_=io["vf"][kt * 128:(kt + 1) * 128, :])],
                        deps=[prev.get(("vuse", vp))], tag=f"vld{qt}_{kt}")
            def mmA(e, kt=kt, q0=q0):
                last = None
                for dk in range(8):
                    last = e.matmul(ps_sc[:, :], kT_sb[:, dk * S8 + kt * 128:dk * S8 + (kt + 1) * 128],
                                    qT_sb[:, dk * 256:(dk + 1) * 256], start=(dk == 0), stop=(dk == 7))
                return [last]
            mA = S.add("tensor", mmA,
                       deps=kld + qlds + [prev.get(("exp", 0)), prev.get(("exp", 1))], tag=f"mmA{qt}_{kt}")
            ex = S.add("scalar", lambda e, ep=ep: [e.activation(e_sb[ep][:, :], ps_sc[:, :], AF.Exp, scale=0.03125)],
                       deps=[mA, prev.get(("euse", ep))], tag=f"exp{qt}_{kt}")
            prev[("exp", ep)] = ex
            def mmB(e, kt=kt, ep=ep, vp=vp):
                e.matmul(ps_se[:, :], ones_bf[:, :], e_sb[ep][:, :], start=(kt == 0), stop=(kt == NKT - 1))
                last = None
                for dvt in range(8):
                    last = e.matmul(ps_oc[dvt // 2][:, (dvt % 2) * 256:(dvt % 2) * 256 + 256],
                                    v_sb[vp][:, dvt * 128:(dvt + 1) * 128], e_sb[ep][:, :],
                                    start=(kt == 0), stop=(kt == NKT - 1), skip_group_check=True)
                return [last]
            extra = (prev.get("oc_evs", []) + [prev.get("see")]) if kt == 0 else []
            mB = S.add("tensor", mmB, deps=[ex, vld, ones_rdy] + extra, tag=f"mmB{qt}_{kt}")
            prev[("vuse", vp)] = mB
            prev[("euse", ep)] = mB
            last_mmB = mB
        oc_evs = []
        for dvt in range(8):
            ev = S.add("vector", lambda e, dvt=dvt: [e.tensor_copy(oc_bf[:, dvt * 256:(dvt + 1) * 256],
                        ps_oc[dvt // 2][:, (dvt % 2) * 256:(dvt % 2) * 256 + 256])],
                       deps=[last_mmB] + ([prev.get("z_last")] if dvt == 0 else []), tag=f"ocev{qt}_{dvt}")
            oc_evs.append(ev)
        prev["oc_evs"] = oc_evs
        see = S.add("vector", lambda e: [e.tensor_copy(se_f[:, :], ps_se[:, :])],
                    deps=[last_mmB, prev.get("seb")], tag=f"see{qt}")
        prev["see"] = see
        seb = S.add("vector", lambda e: [e.tensor_copy(se_bf[:, :], se_f[:, :])], deps=[see], tag=f"seb{qt}")
        prev["seb"] = seb
        rec = S.add("vector", lambda e: [e.reciprocal(rec_f[:, :], se_f[:, :])],
                    deps=[see, prev.get("rc1")], tag=f"rec{qt}")
        rc = []
        for t2 in range(2):
            rc.append(S.dma("gpsimd", lambda e, t2=t2: [e.dma_start(out=rec_c[:, t2:t2 + 1],
                        in_=rec_f[:, t2 * 128:(t2 + 1) * 128])],
                        deps=[rec, prev.get("rc_use")], tag=f"rc{qt}_{t2}"))
        prev["rc1"] = rc[1]
        zevs = {0: [], 1: []}
        for ds in range(8):
            mms = {0: None, 1: None}
            for dvt in range(8):
                sp = dvt % 2
                wld = S.dma("sync", lambda e, ds=ds, dvt=dvt, sp=sp: [e.dma_start(out=wo_stg[sp][:, :],
                            in_=io["Wo"][dvt * 128:(dvt + 1) * 128, ds * 512:(ds + 1) * 512])],
                            deps=[prev.get(("wo_stg", sp))], tag=f"wold{qt}_{ds}_{dvt}")
                wcs = S.add("vector", lambda e, sp=sp: [e.tensor_copy(wo_bf[sp][:, :], wo_stg[sp][:, :])],
                            deps=[wld, prev.get(("wo_use", sp))], tag=f"wocs{qt}_{ds}_{dvt}")
                prev[("wo_stg", sp)] = wcs
                for tt2 in range(2):
                    mm = S.add("tensor", lambda e, dvt=dvt, tt2=tt2, sp=sp: [e.matmul(ps_z[tt2][:, :],
                                oc_bf[:, dvt * 256 + tt2 * 128:dvt * 256 + tt2 * 128 + 128], wo_bf[sp][:, :],
                                start=(dvt == 0), stop=False)],
                               deps=[wcs] + oc_evs + ([prev.get(("z_ev", tt2))] if dvt == 0 else [mms[tt2]]),
                               tag=f"mmz{qt}_{ds}_{dvt}_{tt2}")
                    mms[tt2] = mm
                prev[("wo_use", sp)] = mms[1]
            for tt2 in range(2):
                mmb = S.add("tensor", lambda e, tt2=tt2, ds=ds: [e.matmul(ps_z[tt2][:, :],
                            se_bf[:, tt2 * 128:(tt2 + 1) * 128], bo_bf[:, ds * 512:(ds + 1) * 512],
                            start=False, stop=True)],
                            deps=[mms[tt2], seb, bo_rdy], tag=f"mmzb{qt}_{ds}_{tt2}")
                zev = S.add("vector", lambda e, ds=ds, tt2=tt2: [e.tensor_scalar(
                            z_sb[:, tt2 * D + ds * 512:tt2 * D + (ds + 1) * 512],
                            ps_z[tt2][:, :], rec_c[:, tt2:tt2 + 1], None, ALU.mult)],
                            deps=[mmb] + rc + ([prev.get(("z_sb_use", tt2))] if ds == 0 else []),
                            tag=f"zev{qt}_{ds}_{tt2}")
                prev[("z_ev", tt2)] = zev
                zevs[tt2].append(zev)
        prev["rc_use"] = zevs[1][-1]
        prev["q_use"] = last_mmB
        for tt2 in range(2):
            tglob = qt * 2 + tt2
            zs = zevs[tt2]
            rs3 = S.add("vector", lambda e, tt2=tt2: [e.reduce_sum(m3[:, :], z_sb[:, tt2 * D:(tt2 + 1) * D], axis=AX)],
                        deps=zs + [prev.get("n3")], tag=f"rs3_{tglob}")
            sq3s = S.add("scalar", lambda e, tt2=tt2: [e.activation(sq3[:, :], z_sb[:, tt2 * D:(tt2 + 1) * D], AF.Square)],
                         deps=zs + [prev.get("rq3")], tag=f"sq3_{tglob}")
            rq3 = S.add("vector", lambda e: [e.reduce_sum(q3[:, :], sq3[:, :], axis=AX)], deps=[sq3s], tag=f"rq3_{tglob}")
            prev["rq3"] = rq3
            n3 = S.add("vector", lambda e: [e.tensor_scalar(m3[:, :], m3[:, :], 1.0 / D, None, ALU.mult)],
                       deps=[rs3], tag=f"n3_{tglob}")
            v3a = S.add("vector", lambda e: [e.tensor_scalar(v3[:, :], m3[:, :], m3[:, :], None, ALU.mult)],
                        deps=[n3, prev.get("v3c")], tag=f"v3a_{tglob}")
            v3b = S.add("vector", lambda e: [e.scalar_tensor_tensor(v3[:, :], q3[:, :], 1.0 / D, v3[:, :], ALU.mult, ALU.subtract)],
                        deps=[v3a, rq3], tag=f"v3b_{tglob}")
            v3c = S.add("vector", lambda e: [e.tensor_scalar(v3[:, :], v3[:, :], EPS, None, ALU.add)],
                        deps=[v3b, prev.get("sd3s")], tag=f"v3c_{tglob}")
            prev["v3c"] = v3c
            sd3s = S.add("scalar", lambda e: [e.activation(sd3[:, :], v3[:, :], AF.Sqrt)], deps=[v3c], tag=f"sd3_{tglob}")
            prev["sd3s"] = sd3s
            r3s = S.add("vector", lambda e: [e.reciprocal(r3[:, :], sd3[:, :])], deps=[sd3s, prev.get("oln3")], tag=f"r3_{tglob}")
            oln = S.add("vector", lambda e, tt2=tt2: [e.tensor_scalar(z_sb[:, tt2 * D:(tt2 + 1) * D],
                        z_sb[:, tt2 * D:(tt2 + 1) * D], m3[:, :], r3[:, :], ALU.subtract, ALU.mult)],
                        deps=[r3s, n3] + zs, tag=f"oln3_{tglob}")
            prev["oln3"] = oln
            prev[("z_sb_use", tt2)] = oln
            for hh in range(2):
                H = D // 2
                xld = S.dma("sync", lambda e, tglob=tglob, hh=hh, H=H: [e.dma_start(out=xres3[:, :],
                            in_=io["x_tok"][tglob * 128:(tglob + 1) * 128, hh * H:(hh + 1) * H])],
                            deps=[prev.get("oadd3")], tag=f"xld3_{tglob}_{hh}")
                oadd = S.add("vector", lambda e, tt2=tt2, hh=hh, H=H: [e.tensor_tensor(xres3[:, :], xres3[:, :],
                            z_sb[:, tt2 * D + hh * H:tt2 * D + (hh + 1) * H], ALU.add)],
                            deps=[oln, xld, prev.get("ost3")], tag=f"oadd3_{tglob}_{hh}")
                prev["oadd3"] = oadd
                ost = S.dma("sync", lambda e, tglob=tglob, hh=hh, H=H: [e.dma_start(
                            out=io["xout"][tglob * 128:(tglob + 1) * 128, hh * H:(hh + 1) * H], in_=xres3[:, :])],
                            deps=[oadd], tag=f"ost3_{tglob}_{hh}")
                prev["ost3"] = ost
                outs.append(ost)
            prev["n3"] = n3
    return outs


def build_l1attn_qkv_nc():
    nc = bass.Bass("TRN2", target_bir_lowering=False)
    DQ = 1024
    io = {
        "xT": nc.dram_tensor("xT", [D, T], F32, kind="ExternalInput").ap(),
        "x_tok": nc.dram_tensor("x_tok", [T, D], F32, kind="ExternalInput").ap(),
        "Wqkv": nc.dram_tensor("Wqkv", [D, 3 * DQ], F32, kind="ExternalInput").ap(),
        "bqkv": nc.dram_tensor("bqkv", [1, 3 * DQ], F32, kind="ExternalInput").ap(),
        "qT": nc.dram_tensor("qT", [DQ, T], BF16, kind="ExternalOutput").ap(),
        "kT": nc.dram_tensor("kT", [DQ, T], BF16, kind="ExternalOutput").ap(),
        "v": nc.dram_tensor("v", [T, DQ], BF16, kind="ExternalOutput").ap(),
    }
    S = Sched(nc)
    outs = emit_l1attn_qkv(S, io)
    S.final_deps = outs
    S.final_engine = "gpsimd"
    S.emit()
    return nc


def build_l1attn_flash_nc():
    nc = bass.Bass("TRN2", target_bir_lowering=False)
    DQ = 1024
    S8 = 8192
    io = {
        "qT": nc.dram_tensor("qT", [DQ, T], BF16, kind="ExternalInput").ap(),
        "kTf": nc.dram_tensor("kTf", [DQ, S8], BF16, kind="ExternalInput").ap(),
        "vf": nc.dram_tensor("vf", [S8, DQ], BF16, kind="ExternalInput").ap(),
        "Wo": nc.dram_tensor("Wo", [DQ, D], F32, kind="ExternalInput").ap(),
        "bo": nc.dram_tensor("bo", [1, D], F32, kind="ExternalInput").ap(),
        "x_tok": nc.dram_tensor("x_tok", [T, D], F32, kind="ExternalInput").ap(),
        "xout": nc.dram_tensor("xout", [T, D], F32, kind="ExternalOutput").ap(),
    }
    S = Sched(nc)
    outs = emit_l1attn_flash(S, io)
    S.final_deps = outs
    S.final_engine = "gpsimd"
    S.emit()
    return nc


INTER = 11008
JT_GU = 172
RT_D = 86


def emit_gptq_dequant(S, io, rows_g, cols, nG, name, qb_key, qz_key, sc_key, out_key):
    nc = S.nc
    zq_u8 = S.sbuf(name + "_zqu8", [nG, cols // 2], U8)
    zq_n8 = S.sbuf(name + "_zqn8", [nG, cols], U8)
    zq_f = S.sbuf(name + "_zqf", [nG, cols], F32)
    sc_sb = S.sbuf(name + "_sc", [nG, cols], F32)
    zqsc = S.sbuf(name + "_zqsc", [nG, cols], F32)
    scr = nc.dram_tensor(name + "_scr", [nG, cols], F32, kind="Internal").ap()
    l1 = S.dma("sync", lambda e: [e.dma_start(out=zq_u8[:, :], in_=io[qz_key])], tag=name + "zl")
    l2 = S.dma("sync", lambda e: [e.dma_start(out=sc_sb[:, :], in_=io[sc_key])], tag=name + "sl")
    u1 = S.add("vector", lambda e: [e.tensor_single_scalar(zq_n8[:, 0::2], zq_u8[:, :], 15, ALU.bitwise_and)],
               deps=[l1], tag=name + "u1")
    u2 = S.add("vector", lambda e: [e.tensor_single_scalar(zq_n8[:, 1::2], zq_u8[:, :], 4, ALU.logical_shift_right)],
               deps=[l1], tag=name + "u2")
    z1 = S.add("vector", lambda e: [e.tensor_scalar(zq_f[:, :], zq_n8[:, :], 1.0, None, ALU.add)],
               deps=[u1, u2], tag=name + "z1")
    z2 = S.add("vector", lambda e: [e.tensor_tensor(zqsc[:, :], zq_f[:, :], sc_sb[:, :], ALU.mult)],
               deps=[z1, l2], tag=name + "z2")
    zst = S.dma("sync", lambda e: [e.dma_start(out=scr, in_=zqsc[:, :])], deps=[z2], tag=name + "zst")

    qb = [S.sbuf(name + f"_qb{i}", [64, cols], U8) for i in range(2)]
    sc_bc = [S.sbuf(name + f"_scb{i}", [64, cols], F32) for i in range(2)]
    zq_bc = [S.sbuf(name + f"_zqb{i}", [64, cols], F32) for i in range(2)]
    lo = S.sbuf(name + "_lo", [64, cols], U8)
    hi = S.sbuf(name + "_hi", [64, cols], U8)
    tmpf = S.sbuf(name + "_tf", [64, cols], F32)
    wout = S.sbuf(name + "_w", [64, cols], BF16)
    outs = []
    prev = {}
    for g in range(nG):
        p = g % 2
        qld = S.dma("sync", lambda e, g=g, p=p: [e.dma_start(out=qb[p][:, :], in_=io[qb_key][64 * g:64 * (g + 1), :])],
                    deps=[prev.get(("qb", p))], tag=f"{name}qld{g}")
        sld = S.dma("sync", lambda e, g=g, p=p: [e.dma_start(out=sc_bc[p][:, :],
                    in_=io[sc_key][g:g + 1, :].partition_broadcast(64))],
                    deps=[prev.get(("scb", p))], tag=f"{name}sld{g}")
        zld = S.dma("sync", lambda e, g=g, p=p: [e.dma_start(out=zq_bc[p][:, :],
                    in_=scr[g:g + 1, :].partition_broadcast(64))],
                    deps=[zst, prev.get(("zqb", p))], tag=f"{name}zld{g}")
        for pl, src_op, shift in ((0, lo, None), (1, hi, None)):
            pass
        s_lo = S.add("vector", lambda e, p=p: [e.tensor_single_scalar(lo[:, :], qb[p][:, :], 15, ALU.bitwise_and)],
                     deps=[qld, prev.get("use_lo")], tag=f"{name}lo{g}")
        s_hi = S.add("vector", lambda e, p=p: [e.tensor_single_scalar(hi[:, :], qb[p][:, :], 4, ALU.logical_shift_right)],
                     deps=[qld, prev.get("use_hi")], tag=f"{name}hi{g}")
        for pl, nib, nibstep in ((0, lo, s_lo), (1, hi, s_hi)):
            t1 = S.add("vector", lambda e, nib=nib, p=p: [e.scalar_tensor_tensor(tmpf[:, :], nib[:, :], 1.0,
                        sc_bc[p][:, :], ALU.mult, ALU.mult)],
                       deps=[nibstep, sld, prev.get("t2")], tag=f"{name}t1_{g}_{pl}")
            t2 = S.add("vector", lambda e, p=p: [e.tensor_tensor(wout[:, :], tmpf[:, :], zq_bc[p][:, :], ALU.subtract)],
                       deps=[t1, zld, prev.get("wst")], tag=f"{name}t2_{g}_{pl}")
            prev["t2"] = t2
            wst = S.dma("sync", lambda e, g=g, pl=pl: [e.dma_start(
                        out=io[out_key][128 * g + 64 * pl:128 * g + 64 * (pl + 1), :], in_=wout[:, :])],
                        deps=[t2], tag=f"{name}wst{g}_{pl}")
            prev["wst"] = wst
            if pl == 0:
                prev["use_lo"] = t1
            else:
                prev["use_hi"] = t1
            outs.append(wst)
        prev[("qb", p)] = prev["use_hi"]
        prev[("scb", p)] = prev["t2"]
        prev[("zqb", p)] = prev["t2"]
    return outs


def build_gptq_dequant_nc():
    nc = bass.Bass("TRN2", target_bir_lowering=False)
    CG = 22016 // 8
    CD = 4096 // 8
    io = {
        "qb_gu": nc.dram_tensor("qb_gu", [2048, CG], U8, kind="ExternalInput").ap(),
        "qz_gu": nc.dram_tensor("qz_gu", [32, CG // 2], U8, kind="ExternalInput").ap(),
        "sc_gu": nc.dram_tensor("sc_gu", [32, CG], F32, kind="ExternalInput").ap(),
        "qb_d": nc.dram_tensor("qb_d", [5504, CD], U8, kind="ExternalInput").ap(),
        "qz_d": nc.dram_tensor("qz_d", [86, CD // 2], U8, kind="ExternalInput").ap(),
        "sc_d": nc.dram_tensor("sc_d", [86, CD], F32, kind="ExternalInput").ap(),
        "wgu": nc.dram_tensor("wgu", [D, CG], BF16, kind="ExternalOutput").ap(),
        "wd": nc.dram_tensor("wd", [INTER, CD], BF16, kind="ExternalOutput").ap(),
    }
    S = Sched(nc)
    o1 = emit_gptq_dequant(S, io, None, CG, 32, "gu", "qb_gu", "qz_gu", "sc_gu", "wgu")
    o2 = emit_gptq_dequant(S, io, None, CD, 86, "dw", "qb_d", "qz_d", "sc_d", "wd")
    S.final_deps = o1 + o2
    S.final_engine = "gpsimd"
    S.emit()
    return nc


def f32_row_bcast(S, col_tensor, nt, name, deps_per_tile):
    nc = S.nc
    row = S.sbuf(name + "_row", [1, nt * 128], F32)
    scr = nc.dram_tensor(name + "_scr", [1, nt * 128], F32, kind="Internal").ap()
    bc = S.sbuf(name + "_bc", [128, nt * 128], F32)
    steps = []
    for tt in range(nt):
        d = S.dma("gpsimd", lambda e, tt=tt: [e.dma_start(out=row[:, tt * 128:(tt + 1) * 128],
                  in_=col_tensor[:, tt:tt + 1])], deps=deps_per_tile[tt], tag=f"{name}r{tt}")
        steps.append(d)
    st = S.dma("sync", lambda e: [e.dma_start(out=scr, in_=row[:, :])], deps=steps, tag=name + "_st")
    ld = S.dma("sync", lambda e: [e.dma_start(out=bc[:, :], in_=scr.partition_broadcast(128))],
               deps=[st], tag=name + "_ld")
    return bc, ld


def emit_l0ffn_main(S, io):
    nt = T // 128
    xT_bf, xT_rdy = cast_load(S, "xTpb", io["xTp"], D, T + 1)
    stats = {}
    ln_fin = ln_stats_steps(S, io["x_tok"], nt, D, stats, tag="lnf")
    negm_bc, nmb = f32_row_bcast(S, stats["negm_c"], nt, "negmb", [[f[1]] for f in ln_fin])
    rstd_bc, rsb = f32_row_bcast(S, stats["rstd_c"], nt, "rstdb", [[f[0]] for f in ln_fin])
    h_scr = S.nc.dram_tensor("h_scr", [INTER, T], BF16, kind="Internal").ap()

    pool = [S.psum(f"pp{i}", [128, 512], F32) for i in range(8)]
    wslab = [S.sbuf(f"wslab{i}", [128, 32 * 128], BF16) for i in range(2)]
    cs_col = S.sbuf("cs_col", [128, 1], F32)
    tmp1 = S.sbuf("tmp1", [128, 512], F32)
    pre = [S.sbuf(f"pre{i}", [128, T], F32) for i in range(2)]
    hbf = S.sbuf("hbf", [128, T], BF16)
    prev = {}
    hsts = []
    TP = T + 1
    for b in range(2 * RT_D):
        wp = b % 2
        jt = (b // 2) + (0 if b % 2 == 0 else RT_D)
        is_gate = (b % 2 == 0)
        wld = S.dma("sync", lambda e, b=b, wp=wp: [e.dma_start(out=wslab[wp][:, :],
                    in_=io["wgu_s"][b])], deps=[prev.get(("w", wp))], tag=f"wld{b}")
        def mmup(e, wp=wp):
            for th in range(2):
                for kt in range(32):
                    e.matmul(pool[th][:, :], wslab[wp][:, kt * 128:(kt + 1) * 128],
                             xT_bf[:, kt * TP + th * 512:kt * TP + th * 512 + 512],
                             start=(kt == 0), stop=(kt == 31))
            last = None
            for kt in range(32):
                last = e.matmul(pool[4][:, 0:1], wslab[wp][:, kt * 128:(kt + 1) * 128],
                                xT_bf[:, kt * TP + 1024:kt * TP + 1025], start=(kt == 0), stop=(kt == 31))
            return [last]
        mm = S.add("tensor", mmup, deps=[wld] + xT_rdy +
                   [prev.get("evA0"), prev.get("evA1"), prev.get("cse")], tag=f"mmA{b}")
        prev[("w", wp)] = mm
        cse = S.add("vector", lambda e: [e.tensor_copy(cs_col[:, :], pool[4][:, 0:1])], deps=[mm], tag=f"cse{b}")
        prev["cse"] = cse
        dst = pre[0] if is_gate else pre[1]
        evs = []
        for th in range(2):
            tA = S.add("vector", lambda e, th=th: [e.scalar_tensor_tensor(tmp1[:, :], negm_bc[:, th * 512:(th + 1) * 512],
                        cs_col[:, :], pool[th][:, :], ALU.mult, ALU.add)],
                       deps=[mm, cse, nmb, prev.get("tB")], tag=f"tA{b}_{th}")
            tB = S.add("vector", lambda e, th=th, dst=dst: [e.tensor_tensor(dst[:, th * 512:(th + 1) * 512],
                        tmp1[:, :], rstd_bc[:, th * 512:(th + 1) * 512], ALU.mult)],
                       deps=[tA, rsb, prev.get("dst_use") if is_gate else None], tag=f"tB{b}_{th}")
            prev["tB"] = tB
            prev[f"evA{th}"] = tB
            evs.append(tB)
        if is_gate:
            sil = S.add("scalar", lambda e: [e.activation(pre[0][:, :], pre[0][:, :], AF.Silu)],
                        deps=evs, tag=f"sil{b}")
            prev["sil"] = sil
            prev["evA0"] = sil
            prev["evA1"] = sil
        else:
            hmu = S.add("vector", lambda e: [e.tensor_tensor(hbf[:, :], pre[0][:, :], pre[1][:, :], ALU.mult)],
                        deps=evs + [prev.get("sil"), prev.get("hst")], tag=f"hmu{b}")
            hst = S.dma("sync", lambda e, jt0=b // 2: [e.dma_start(out=h_scr[jt0 * 128:(jt0 + 1) * 128, :], in_=hbf[:, :])],
                        deps=[hmu], tag=f"hst{b}")
            prev["hst"] = hst
            prev["dst_use"] = hmu
            hsts.append(hst)

    hld_b = [S.sbuf(f"hldb{i}", [128, T], BF16) for i in range(2)]
    wd_b = [S.sbuf(f"wdb{i}", [128, 512], BF16) for i in range(2)]
    oev = S.sbuf("oev", [128, 512], F32)
    outs = []
    for dtg in range(8):
        last_mm = None
        mm_prev = None
        for rt in range(RT_D):
            p = rt % 2
            hl = S.dma("sync", lambda e, rt=rt, p=p: [
                    e.dma_start(out=hld_b[p][0:64, :], in_=h_scr[128 * rt:128 * (rt + 1):2, :]),
                    e.dma_start(out=hld_b[p][64:128, :], in_=h_scr[128 * rt + 1:128 * (rt + 1):2, :])],
                    n=2, deps=hsts + [prev.get(("hb", p))], tag=f"hl{dtg}_{rt}")
            wl = S.dma("sync", lambda e, dtg=dtg, rt=rt, p=p: [e.dma_start(out=wd_b[p][:, :],
                    in_=io["wd_s"][dtg, rt])],
                    deps=[prev.get(("wb", p))], tag=f"wl{dtg}_{rt}")
            def mmdn(e, rt=rt, p=p):
                last = None
                for dt4 in range(4):
                    for th in range(2):
                        last = e.matmul(pool[dt4 * 2 + th][:, :], wd_b[p][:, dt4 * 128:(dt4 + 1) * 128],
                                        hld_b[p][:, th * 512:(th + 1) * 512],
                                        start=(rt == 0), stop=(rt == RT_D - 1))
                return [last]
            dd = [hl, wl] + ([mm_prev] if mm_prev else [])
            if rt == 0:
                dd += [prev.get("evA0"), prev.get("evA1"), prev.get("cse"), prev.get("bev")]
            mm = S.add("tensor", mmdn, deps=dd, tag=f"mmB{dtg}_{rt}")
            mm_prev = mm
            prev[("hb", p)] = mm
            prev[("wb", p)] = mm
            last_mm = mm
        bevs = []
        for dt4 in range(4):
            for th in range(2):
                ev = S.add("vector", lambda e, dt4=dt4, th=th: [e.tensor_copy(oev[:, :], pool[dt4 * 2 + th][:, :])],
                           deps=[last_mm, prev.get("ost_f2")], tag=f"bev{dtg}_{dt4}_{th}")
                st = S.dma("sync", lambda e, dtg=dtg, dt4=dt4, th=th: [e.dma_start(
                        out=io["soT"][dtg * 512 + dt4 * 128:dtg * 512 + (dt4 + 1) * 128, th * 512:(th + 1) * 512],
                        in_=oev[:, :])], deps=[ev], tag=f"ost{dtg}_{dt4}_{th}")
                prev["ost_f2"] = st
                bevs.append(ev)
                outs.append(st)
        prev["bev"] = bevs[-1]
    return outs


def build_l0ffn_main_nc():
    nc = bass.Bass("TRN2", target_bir_lowering=False)
    io = {
        "xTp": nc.dram_tensor("xTp", [D, T + 1], F32, kind="ExternalInput").ap(),
        "x_tok": nc.dram_tensor("x_tok", [T, D], F32, kind="ExternalInput").ap(),
        "wgu_s": nc.dram_tensor("wgu_s", [2 * RT_D, 128, 32 * 128], BF16, kind="ExternalInput").ap(),
        "wd_s": nc.dram_tensor("wd_s", [8, RT_D, 128, 512], BF16, kind="ExternalInput").ap(),
        "soT": nc.dram_tensor("soT", [D, T], F32, kind="ExternalOutput").ap(),
    }
    S = Sched(nc)
    outs = emit_l0ffn_main(S, io)
    S.final_deps = outs
    S.final_engine = "gpsimd"
    S.emit()
    return nc


def emit_ln_residual(S, io, tag="lr"):
    nt = T // 128
    so = S.sbuf(tag + "_so", [128, D], F32)
    sq = S.sbuf(tag + "_sq", [128, D], BF16)
    xr = S.sbuf(tag + "_xr", [128, D], F32)
    m = S.sbuf(tag + "_m", [128, 1], F32); q = S.sbuf(tag + "_q", [128, 1], F32)
    v = S.sbuf(tag + "_v", [128, 1], F32); sd = S.sbuf(tag + "_sd", [128, 1], F32)
    r = S.sbuf(tag + "_r", [128, 1], F32)
    outs = []
    prev = {}
    for tt in range(nt):
        sld = S.dma("sync", lambda e, tt=tt: [e.dma_start(out=so[:, :], in_=io["so_tok"][tt * 128:(tt + 1) * 128, :])],
                    deps=[prev.get("oln")], tag=f"{tag}sld{tt}")
        xld = S.dma("sync", lambda e, tt=tt: [e.dma_start(out=xr[:, :], in_=io["x_tok"][tt * 128:(tt + 1) * 128, :])],
                    deps=[prev.get("oadd")], tag=f"{tag}xld{tt}")
        rs = S.add("vector", lambda e: [e.reduce_sum(m[:, :], so[:, :], axis=AX)],
                   deps=[sld, prev.get("n1")], tag=f"{tag}rs{tt}")
        sqs = S.add("scalar", lambda e: [e.activation(sq[:, :], so[:, :], AF.Square)],
                    deps=[sld, prev.get("rq")], tag=f"{tag}sq{tt}")
        rq = S.add("vector", lambda e: [e.reduce_sum(q[:, :], sq[:, :], axis=AX)], deps=[sqs], tag=f"{tag}rq{tt}")
        prev["rq"] = rq
        n1 = S.add("vector", lambda e: [e.tensor_scalar(m[:, :], m[:, :], 1.0 / D, None, ALU.mult)],
                   deps=[rs], tag=f"{tag}n1{tt}")
        prev["n1"] = n1
        va = S.add("vector", lambda e: [e.tensor_scalar(v[:, :], m[:, :], m[:, :], None, ALU.mult)],
                   deps=[n1, prev.get("vc")], tag=f"{tag}va{tt}")
        vb = S.add("vector", lambda e: [e.scalar_tensor_tensor(v[:, :], q[:, :], 1.0 / D, v[:, :], ALU.mult, ALU.subtract)],
                   deps=[va, rq], tag=f"{tag}vb{tt}")
        vc = S.add("vector", lambda e: [e.tensor_scalar(v[:, :], v[:, :], EPS, None, ALU.add)],
                   deps=[vb, prev.get("sds")], tag=f"{tag}vc{tt}")
        prev["vc"] = vc
        sds = S.add("scalar", lambda e: [e.activation(sd[:, :], v[:, :], AF.Sqrt)], deps=[vc], tag=f"{tag}sd{tt}")
        prev["sds"] = sds
        rr = S.add("vector", lambda e: [e.reciprocal(r[:, :], sd[:, :])], deps=[sds, prev.get("oln")], tag=f"{tag}r{tt}")
        oln = S.add("vector", lambda e: [e.tensor_scalar(so[:, :], so[:, :], m[:, :], r[:, :], ALU.subtract, ALU.mult)],
                    deps=[rr, n1, rs, rq], tag=f"{tag}oln{tt}")
        prev["oln"] = oln
        oadd = S.add("vector", lambda e: [e.tensor_tensor(xr[:, :], xr[:, :], so[:, :], ALU.add)],
                     deps=[oln, xld, prev.get("ost")], tag=f"{tag}oadd{tt}")
        prev["oadd"] = oadd
        ost = S.dma("sync", lambda e, tt=tt: [e.dma_start(out=io["xout"][tt * 128:(tt + 1) * 128, :], in_=xr[:, :])],
                    deps=[oadd], tag=f"{tag}ost{tt}")
        prev["ost"] = ost
        outs.append(ost)
    return outs


def build_ln_residual_nc():
    nc = bass.Bass("TRN2", target_bir_lowering=False)
    io = {
        "so_tok": nc.dram_tensor("so_tok", [T, D], F32, kind="ExternalInput").ap(),
        "x_tok": nc.dram_tensor("x_tok", [T, D], F32, kind="ExternalInput").ap(),
        "xout": nc.dram_tensor("xout", [T, D], F32, kind="ExternalOutput").ap(),
    }
    S = Sched(nc)
    outs = emit_ln_residual(S, io)
    S.final_deps = outs
    S.final_engine = "gpsimd"
    S.emit()
    return nc


def gu_perm():
    p = np.zeros(D, np.int64)
    for g in range(32):
        for s in range(64):
            p[128 * g + s] = 128 * g + 2 * s
            p[128 * g + 64 + s] = 128 * g + 2 * s + 1
    return p


def pack_bytes_rows(qw):
    R, C = qw.shape
    return np.ascontiguousarray(qw.view(np.uint8).reshape(R, C, 4).transpose(0, 2, 1).reshape(4 * R, C))


TE = 1152
NH = 128
HD = 64
SS = 128
NG = 8
GATE = 8192
CONV_DIM = 10240
PROJ = 18560
NCH = 8


def emit_mamba_m1(S, io):
    ntE = TE // 128
    TP = TE + 1
    xT_bf, xT_rdy = cast_load(S, "xTm", io["xT"], D, TP)
    stats = {}
    ln_fin = ln_stats_steps(S, io["x_tok"], ntE, D, stats, tag="lnm")
    negm_row = stats["negm_row"]
    rstd_bc, rsb = f32_row_bcast(S, stats["rstd_c"], ntE, "rstdm", [[f[0]] for f in ln_fin])

    pool = [S.psum(f"mp{i}", [128, 512], F32) for i in range(3)]
    aux = S.psum("mpaux", [128, 512], F32)
    ps_csr = aux[0:1, 0:128]
    wblk = [S.sbuf(f"mwb{i}", [128, D // 4], F32) for i in range(1)]
    wblk_bf = S.sbuf("mwbf", [128, D], BF16)
    cs_row = S.sbuf("mcs_row", [1, 128], BF16)
    pre = S.sbuf("mpre", [128, TE], F32)
    cacc = S.sbuf("mcacc", [128, T], F32)
    obuf = S.sbuf("mobuf", [128, T], BF16)
    convw = S.sbuf("convw", [128, 320], F32)
    convb = S.sbuf("convb", [128, 80], F32)
    dtb = S.sbuf("dtb", [128, 1], F32)
    dtv = S.sbuf("dtv", [128, T], F32)
    lndt = S.sbuf("lndt", [128, T], F32)
    dtt1 = cacc
    ones_f = S.sbuf("ones_f", [1, 128], F32)
    ones_bf = S.sbuf("ones_bfm", [128, 1], BF16)

    cwl = S.dma("sync", lambda e: [e.dma_start(out=convw[:, :], in_=io["conv_w"])], tag="cwl")
    cbl = S.dma("sync", lambda e: [e.dma_start(out=convb[:, :], in_=io["conv_b"])], tag="cbl")
    dtbl = S.dma("sync", lambda e: [e.dma_start(out=dtb[:, :], in_=io["dt_bias"])], tag="dtbl")
    of = S.add("vector", lambda e: [e.memset(ones_f[:, :], 1.0)], tag="onesf")
    ob1 = S.add("vector", lambda e: [e.memset(ones_bf[:, :], 1.0)], tag="onesb")

    ln_all = [x for f in ln_fin for x in f]
    prev = {}
    dt_done = [None]
    for cb in range(145):
        wp = 0
        wcs = None
        for qq in range(4):
            wldq = S.dma("sync", lambda e, cb=cb, qq=qq: [e.dma_start(out=wblk[0][:, :],
                         in_=io["wip_s"][cb, :, qq * 1024:(qq + 1) * 1024])],
                         deps=[wcs if wcs else prev.get(("wb", 0))], tag=f"mwld{cb}_{qq}")
            wcs = S.add("vector", lambda e, qq=qq: [e.tensor_copy(wblk_bf[:, qq * 1024:(qq + 1) * 1024], wblk[0][:, :])],
                        deps=[wldq] + ([prev.get("wbf_use")] if qq == 0 else []), tag=f"mwcs{cb}_{qq}")
        prev[("wb", 0)] = wcs
        csm = S.add("tensor", lambda e: [[e.matmul(ps_csr[:, :], ones_bf[:, :],
                    wblk_bf[:, kt * 128:(kt + 1) * 128], start=(kt == 0), stop=(kt == 31))
                    for kt in range(32)][-1]], deps=[wcs, ob1, prev.get("cse")], tag=f"mcsm{cb}")
        cse = S.add("vector", lambda e: [e.tensor_copy(cs_row[:, :], ps_csr[:, :])], deps=[csm], tag=f"mcse{cb}")
        prev["cse"] = cse

        def mmain(e):
            last = None
            for si, (t0, tw) in enumerate(((0, 512), (512, 512), (1024, 128))):
                for kt in range(32):
                    e.matmul(pool[si][:, 0:tw], wblk_bf[:, kt * 128:(kt + 1) * 128],
                             xT_bf[:, kt * TP + t0:kt * TP + t0 + tw], start=(kt == 0), stop=False)
                last = e.matmul(pool[si][:, 0:tw], cs_row[:, :], negm_row[:, t0:t0 + tw],
                                start=False, stop=True)
            return [last]
        mm = S.add("tensor", mmain, deps=[wcs, cse] + xT_rdy + ln_all + [prev.get("ev")], tag=f"mmm{cb}")
        prev["wbf_use"] = mm

        def evac(e):
            i1 = e.tensor_tensor(pre[:, 0:512], pool[0][:, :], rstd_bc[:, 0:512], ALU.mult)
            i2 = e.tensor_tensor(pre[:, 512:1024], pool[1][:, :], rstd_bc[:, 512:1024], ALU.mult)
            i3 = e.tensor_tensor(pre[:, 1024:1152], pool[2][:, 0:128], rstd_bc[:, 1024:1152], ALU.mult)
            return [i1, i2, i3]
        ev = S.add("vector", evac, n_instr=3, deps=[mm, rsb, prev.get("pre_done")], tag=f"mev{cb}")
        prev["ev"] = ev
        if cb < 64:
            sil = S.add("scalar", lambda e: [e.activation(obuf[:, :], pre[:, 128:1152], AF.Silu)],
                        deps=[ev, prev.get("obuf_use")], tag=f"msil{cb}")
            st = S.dma("sync", lambda e, cb=cb: [e.dma_start(out=io["gateT"][cb * 128:(cb + 1) * 128, :], in_=obuf[:, :])],
                       deps=[sil], tag=f"mgst{cb}")
            prev["obuf_use"] = st
            prev["pre_done"] = sil
        elif cb < 144:
            ct = cb - 64
            def conv(e, ct=ct):
                e.tensor_scalar(cacc[:, :], pre[:, 125:1149], convw[:, 4 * ct:4 * ct + 1], None, ALU.mult)
                return [e.tensor_scalar(cacc[:, :], cacc[:, :], convb[:, ct:ct + 1], None, ALU.add)]
            def conv_mid(e, ct=ct, k=0):
                return [e.scalar_tensor_tensor(cacc[:, :], pre[:, 125 + k:1149 + k],
                        convw[:, 4 * ct + k:4 * ct + k + 1], cacc[:, :], ALU.mult, ALU.add)]
            c0 = S.add("vector", lambda e, ct=ct: [e.tensor_scalar(cacc[:, :], pre[:, 125:1149],
                       convw[:, 4 * ct:4 * ct + 1], None, ALU.mult)],
                       deps=[ev, cwl, prev.get("sil_c")], tag=f"mc0{cb}")
            cks = c0
            for k in (1, 2, 3):
                cks = S.add("vector", lambda e, ct=ct, k=k: [e.scalar_tensor_tensor(cacc[:, :],
                            pre[:, 125 + k:1149 + k], convw[:, 4 * ct + k:4 * ct + k + 1],
                            cacc[:, :], ALU.mult, ALU.add)], deps=[cks], tag=f"mck{cb}_{k}")
            cb_add = S.add("vector", lambda e, ct=ct: [e.tensor_scalar(cacc[:, :], cacc[:, :],
                           convb[:, ct:ct + 1], None, ALU.add)], deps=[cks, cbl], tag=f"mcb{cb}")
            sil = S.add("scalar", lambda e: [e.activation(obuf[:, :], cacc[:, :], AF.Silu)],
                        deps=[cb_add, prev.get("obuf_use")], tag=f"msilc{cb}")
            prev["sil_c"] = sil
            st = S.dma("sync", lambda e, ct=ct: [e.dma_start(out=io["xbc_conv"][ct * 128:(ct + 1) * 128, :], in_=obuf[:, :])],
                       deps=[sil], tag=f"mxst{cb}")
            prev["obuf_use"] = st
            prev["pre_done"] = cb_add
        else:
            d0 = S.add("vector", lambda e: [e.tensor_scalar(dtt1[:, :], pre[:, 128:1152], dtb[:, :], None, ALU.add)],
                       deps=[ev, dtbl], tag="mdt0")
            d1 = S.add("scalar", lambda e: [e.activation(dtv[:, :], dtt1[:, :], AF.Abs)], deps=[d0], tag="mdt1")
            d2 = S.add("scalar", lambda e: [e.activation(dtv[:, :], dtv[:, :], AF.Exp, scale=-1.0)], deps=[d1], tag="mdt2")
            d3 = S.add("vector", lambda e: [e.tensor_scalar(dtv[:, :], dtv[:, :], 1.0, None, ALU.add)], deps=[d2], tag="mdt3")
            d4 = S.add("scalar", lambda e: [e.activation(dtv[:, :], dtv[:, :], AF.Ln)], deps=[d3], tag="mdt4")
            d5 = S.add("vector", lambda e: [e.tensor_scalar(dtt1[:, :], dtt1[:, :], 0.0, None, ALU.max)], deps=[d4, d0], tag="mdt5")
            d6 = S.add("vector", lambda e: [e.tensor_tensor(dtv[:, :], dtv[:, :], dtt1[:, :], ALU.add)], deps=[d5], tag="mdt6")
            d7 = S.add("vector", lambda e: [e.tensor_scalar(dtv[:, :], dtv[:, :], 0.001, 100.0, ALU.max, ALU.min)],
                       deps=[d6], tag="mdt7")
            d8 = S.add("scalar", lambda e: [e.activation(lndt[:, :], dtv[:, :], AF.Ln)], deps=[d7], tag="mdt8")
            dt_done[0] = d8
            prev["pre_done"] = d0

    A_col = S.sbuf("A_col_t", [128, 1], F32)
    D_rep = S.sbuf("D_rep_t", [128, 128], F32)
    triT = S.sbuf("triT_t", [128, 128], BF16)
    identb = S.sbuf("identb", [128, 128], BF16)
    identf = S.sbuf("identf_t", [128, 128], F32)
    al = S.dma("sync", lambda e: [e.dma_start(out=A_col[:, :], in_=io["A_col"])], tag="al")
    dl = S.dma("sync", lambda e: [e.dma_start(out=D_rep[:, :], in_=io["D_rep"])], tag="dl")
    tl = S.dma("sync", lambda e: [e.dma_start(out=triT[:, :], in_=io["triT"])], tag="tl")
    il = S.dma("sync", lambda e: [e.dma_start(out=identb[:, :], in_=io["ident"])], tag="il")
    ifl = S.dma("sync", lambda e: [e.dma_start(out=identf[:, :], in_=io["identf"])], tag="ifl")

    R = xT_bf[:, 0:16384].bitcast(F32)
    R_bf = xT_bf[:, 16384:24576]
    P_run = S.sbuf("P_run", [128, 1], F32)
    cdprefs = S.sbuf("cdprefs", [128, NCH], F32)
    r0 = S.add("vector", lambda e: [e.memset(R[:, :], 0.0)], deps=[dt_done[0]], tag="r0")
    p0 = S.add("vector", lambda e: [e.memset(P_run[:, :], 1.0)], tag="p0")
    rb0 = S.add("vector", lambda e: [e.memset(R_bf[:, :], 0.0)], deps=[dt_done[0]], tag="rb0")

    cs_hl = S.sbuf("cs_hl", [128, 128], F32)
    lncs_hl = S.sbuf("lncs_hl", [128, 128], F32)
    csrow = S.sbuf("csrow", [1, 512], F32)
    lncsrow = S.sbuf("lncsrow", [1, 512], F32)
    cslast = S.sbuf("cslast", [128, 1], F32)
    cd_col = S.sbuf("cd_col", [128, 1], F32)
    cd_row = S.sbuf("cd_row", [1, 128], F32)
    cd_bc = S.sbuf("cd_bc", [128, 128], F32)
    ds_hl = S.sbuf("ds_hl", [128, 128], F32)
    dtds_hl = S.sbuf("dtds_hl", [128, 128], F32)
    csT = S.sbuf("csT", [128, 128], F32)
    dtdsT = S.sbuf("dtdsT", [128, 128], F32)
    excsT = S.sbuf("excsT", [128, 128], F32)
    BT = S.sbuf("BT", [128, 8 * 128], BF16)
    CT = S.sbuf("CTm", [128, 8 * 128], BF16)
    Btok = S.sbuf("Btok", [128, 8 * 128], BF16)
    sgm = S.sbuf("sgm", [128, 8 * 128], BF16)
    sgm4 = S.sbuf("sgm4", [128, 512], BF16)
    xchan = [S.sbuf(f"xchan{i}", [128, 512], BF16) for i in range(2)]
    xtok = S.sbuf("xtok", [128, 8192], BF16)
    seg_sb = S.sbuf("seg_sb", [128, 512], BF16)
    expseg = S.sbuf("expseg", [128, 512], BF16)
    scoresM = S.sbuf("scoresM", [128, 512], BF16)
    B4 = S.sbuf("B4", [128, 512], BF16)
    tmp_y = S.sbuf("tmp_y", [128, 256], F32)
    tmp2_y = S.sbuf("tmp2_y", [128, 256], F32)
    ypre = [S.sbuf(f"ypre{i}", [128, 256], BF16) for i in range(2)]

    ps_seg = pool[0]
    ps_yyo = pool[1]
    ps_st = pool[2][:, 0:256]
    ps_trf = S.psum("ps_trf", [128, 128], F32)
    ps_small = aux[:, 256:512]
    ps_trb = S.psum("ps_trb", [128, 512], BF16)

    m1outs = []
    xbc_done = prev.get("obuf_use")
    gate_done = prev.get("obuf_use")
    for ch in range(NCH):
        t0 = ch * 128
        sc0 = S.add("vector", lambda e, t0=t0: [e.tensor_tensor_scan(cs_hl[:, :], dtv[:, t0:t0 + 128],
                    dtv[:, t0:t0 + 128], 0.0, ALU.add, ALU.bypass)],
                    deps=[dt_done[0], prev.get("cs_use")], tag=f"sc0_{ch}")
        sc1 = S.add("vector", lambda e: [e.tensor_scalar(cs_hl[:, :], cs_hl[:, :], A_col[:, :], None, ALU.mult)],
                    deps=[sc0, al], tag=f"sc1_{ch}")
        cl0 = S.add("vector", lambda e: [e.tensor_copy(cslast[:, :], cs_hl[:, 127:128])],
                    deps=[sc1, prev.get("cslast_use")], tag=f"cl0_{ch}")
        cd0 = S.add("scalar", lambda e: [e.activation(cd_col[:, :], cslast[:, :], AF.Exp)],
                    deps=[cl0, prev.get("cd_use")], tag=f"cd0_{ch}")
        cp0 = S.add("vector", lambda e, ch=ch: [e.tensor_copy(cdprefs[:, ch:ch + 1], P_run[:, :])],
                    deps=[p0, prev.get("pr_upd")], tag=f"cp0_{ch}")
        pr1 = S.add("vector", lambda e: [e.tensor_tensor(P_run[:, :], P_run[:, :], cd_col[:, :], ALU.mult)],
                    deps=[cp0, cd0], tag=f"pr1_{ch}")
        prev["pr_upd"] = pr1
        lc0 = S.add("vector", lambda e, t0=t0: [e.tensor_tensor(lncs_hl[:, :], lndt[:, t0:t0 + 128], cs_hl[:, :], ALU.subtract)],
                    deps=[sc1, prev.get("lncs_use")], tag=f"lc0_{ch}")

        fr3 = S.dma("gpsimd", lambda e: [e.dma_start(out=cd_row[:, :], in_=cd_col[:, :])],
                    deps=[cd0, prev.get("cdrow_use")], tag=f"fr3_{ch}")
        cdb = S.add("tensor", lambda e: [e.matmul(ps_small[:, 128:256], ones_f[:, :], cd_row[:, :],
                    start=True, stop=True, skip_group_check=True)],
                    deps=[fr3, of, prev.get("cdbc_use")], tag=f"cdb_{ch}")
        cdbc = S.add("vector", lambda e: [e.tensor_copy(cd_bc[:, :], ps_small[:, 128:256])], deps=[cdb], tag=f"cdbc_{ch}")
        ds0 = S.add("vector", lambda e: [e.tensor_scalar(ds_hl[:, :], cs_hl[:, :], cslast[:, :], None, ALU.subtract)],
                    deps=[sc1, cl0, prev.get("ds_use")], tag=f"ds0_{ch}")
        ds1 = S.add("scalar", lambda e: [e.activation(ds_hl[:, :], ds_hl[:, :], AF.Exp, scale=-1.0)],
                    deps=[ds0], tag=f"ds1_{ch}")
        dd0 = S.add("vector", lambda e, t0=t0: [e.tensor_tensor(dtds_hl[:, :], dtv[:, t0:t0 + 128], ds_hl[:, :], ALU.mult)],
                    deps=[ds1, prev.get("dtds_use")], tag=f"dd0_{ch}")
        tr1 = S.add("tensor", lambda e: [e.transpose(ps_trf[:, :], cs_hl[:, :], identf[:, :])],
                    deps=[sc1, ifl, prev.get("trf_ev")], tag=f"tr1_{ch}")
        te1 = S.add("vector", lambda e: [e.tensor_copy(csT[:, :], ps_trf[:, :])],
                    deps=[tr1, prev.get("csT_use")], tag=f"te1_{ch}")
        tr2 = S.add("tensor", lambda e: [e.transpose(ps_trf[:, :], dtds_hl[:, :], identf[:, :])],
                    deps=[dd0, te1], tag=f"tr2_{ch}")
        te2 = S.add("vector", lambda e: [e.tensor_copy(dtdsT[:, :], ps_trf[:, :])],
                    deps=[tr2, prev.get("dtdsT_use")], tag=f"te2_{ch}")
        prev["trf_ev"] = te2
        ex0 = S.add("scalar", lambda e: [e.activation(excsT[:, :], csT[:, :], AF.Exp)],
                    deps=[te1, prev.get("excs_use")], tag=f"ex0_{ch}")
        exst = S.dma("sync", lambda e, ch=ch: [e.dma_start(out=io["excs"][:, ch * 128:(ch + 1) * 128], in_=excsT[:, :])],
                     deps=[ex0], tag=f"exst_{ch}")
        gprep = {}
        for g in range(8):
            bld = S.dma("sync", lambda e, g=g, t0=t0: [e.dma_start(out=BT[:, g * 128:(g + 1) * 128],
                        in_=io["xbc_conv"][8192 + g * 128:8192 + (g + 1) * 128, t0:t0 + 128])],
                        deps=[xbc_done, prev.get(("BTu", g))], tag=f"bld{ch}_{g}")
            cld = S.dma("sync", lambda e, g=g, t0=t0: [e.dma_start(out=CT[:, g * 128:(g + 1) * 128],
                        in_=io["xbc_conv"][9216 + g * 128:9216 + (g + 1) * 128, t0:t0 + 128])],
                        deps=[xbc_done, prev.get(("CTu", g))], tag=f"cld{ch}_{g}")
            scm = S.add("tensor", lambda e, g=g: [e.matmul(ps_small[:, 0:128], BT[:, g * 128:(g + 1) * 128],
                        CT[:, g * 128:(g + 1) * 128], start=True, stop=True, skip_group_check=True)],
                        deps=[bld, cld, prev.get("scm_ev")], tag=f"scm{ch}_{g}")
            sce = S.add("vector", lambda e, g=g: [e.tensor_tensor(sgm[:, g * 128:(g + 1) * 128],
                        ps_small[:, 0:128], triT[:, :], ALU.mult)],
                        deps=[scm, tl, prev.get(("sgm_u", g))], tag=f"sce{ch}_{g}")
            prev["scm_ev"] = sce
            btr = S.add("tensor", lambda e, g=g: [e.transpose(ps_trb[:, 0:128], BT[:, g * 128:(g + 1) * 128], identb[:, :])],
                        deps=[bld, il, prev.get("trb_use")], tag=f"btr{ch}_{g}")
            bte = S.add("vector", lambda e, g=g: [e.tensor_copy(Btok[:, g * 128:(g + 1) * 128], ps_trb[:, 0:128])],
                        deps=[btr, prev.get(("Btok_u", g))], tag=f"bte{ch}_{g}")
            prev["trb_use"] = bte
            gprep[g] = (sce, bte, bld, cld)

        xt_evs = []
        for xq in range(16):
            p = xq % 2
            def xldf(e, xq=xq, p=p, t0=t0):
                return [e.dma_start(out=xchan[p][:, i * 128:(i + 1) * 128],
                        in_=io["xbc_conv"][(xq * 4 + i) * 128:(xq * 4 + i + 1) * 128, t0:t0 + 128])
                        for i in range(4)]
            xl = S.dma("sync", xldf, n=4, deps=[xbc_done, prev.get(("xch", p))], tag=f"xl{ch}_{xq}")
            def xtrf(e, p=p):
                return [e.transpose(ps_trb[:, 128 + i * 128:128 + (i + 1) * 128],
                        xchan[p][:, i * 128:(i + 1) * 128], identb[:, :]) for i in range(3)]
            xt1 = S.add("tensor", xtrf, n_instr=3, deps=[xl, il, prev.get("trb_use")], tag=f"xt1_{ch}_{xq}")
            xe1 = S.add("vector", lambda e, xq=xq: [e.tensor_copy(xtok[:, xq * 512:xq * 512 + 384], ps_trb[:, 128:512])],
                        deps=[xt1, prev.get("xtok_use")], tag=f"xe1_{ch}_{xq}")
            xt2 = S.add("tensor", lambda e, p=p: [e.transpose(ps_trb[:, 0:128], xchan[p][:, 384:512], identb[:, :])],
                        deps=[xl, il, prev.get("trb_use")], tag=f"xt2_{ch}_{xq}")
            xe2 = S.add("vector", lambda e, xq=xq: [e.tensor_copy(xtok[:, xq * 512 + 384:(xq + 1) * 512], ps_trb[:, 0:128])],
                        deps=[xt2, prev.get("xtok_use")], tag=f"xe2_{ch}_{xq}")
            prev["trb_use"] = xe2
            prev[("xch", p)] = xe2
            xt_evs += [xe1, xe2]

        rbf = S.add("vector", lambda e: [e.tensor_copy(R_bf[:, :], R[:, :])],
                    deps=[r0, rb0, prev.get("R_upd"), prev.get("Rbf_use")], tag=f"rbf_{ch}")

        ylast = []
        rlast = []
        for hb in range(32):
            g = hb // 4
            sce, bte, bld, cld = gprep[g]
            h0 = hb * 4
            if hb % 4 == 0:
                s4 = S.add("vector", lambda e, g=g: [e.tensor_copy(sgm4[:, :],
                           sgm[:, g * 128:(g + 1) * 128].unsqueeze(1).broadcast_to([128, 4, 128]))],
                           deps=[sce, prev.get("sgm4_use")], tag=f"s4_{ch}_{hb}")
                prev["sgm4_cur"] = s4
            s4 = prev["sgm4_cur"]
            fr1 = S.dma("gpsimd", lambda e, h0=h0: [e.dma_start(out=csrow[:, :], in_=cs_hl[h0:h0 + 4, :])],
                        deps=[sc1, prev.get("csrow_use")], tag=f"fr1_{ch}_{hb}")
            fr2 = S.dma("gpsimd", lambda e, h0=h0: [e.dma_start(out=lncsrow[:, :], in_=lncs_hl[h0:h0 + 4, :])],
                        deps=[lc0, prev.get("lncsrow_use")], tag=f"fr2_{ch}_{hb}")
            def segmm(e):
                last = None
                for hh in range(4):
                    e.matmul(ps_seg[:, hh * 128:(hh + 1) * 128], ones_f[:, :],
                             csrow[:, hh * 128:(hh + 1) * 128], start=True, stop=False, skip_group_check=True)
                    last = e.matmul(ps_seg[:, hh * 128:(hh + 1) * 128], lncsrow[:, hh * 128:(hh + 1) * 128],
                                    ones_f[:, :], start=False, stop=True, skip_group_check=True)
                return [last]
            sg1 = S.add("tensor", segmm, deps=[fr1, fr2, of, prev.get("seg_ev")], tag=f"sg1_{ch}_{hb}")
            prev["csrow_use"] = sg1
            prev["lncsrow_use"] = sg1
            sg2 = S.add("vector", lambda e: [e.tensor_scalar(seg_sb[:, :], ps_seg[:, :], 4.7, None, ALU.min)],
                        deps=[sg1, prev.get("seg_sb_use")], tag=f"sg2_{ch}_{hb}")
            prev["seg_ev"] = sg2
            sg3 = S.add("scalar", lambda e: [e.activation(expseg[:, :], seg_sb[:, :], AF.Exp)],
                        deps=[sg2, prev.get("expseg_use")], tag=f"sg3_{ch}_{hb}")
            prev["seg_sb_use"] = sg3
            sg4 = S.add("vector", lambda e: [e.tensor_tensor(scoresM[:, :], sgm4[:, :], expseg[:, :], ALU.mult)],
                        deps=[sg3, s4, prev.get("scoresM_use")], tag=f"sg4_{ch}_{hb}")
            prev["expseg_use"] = sg4
            prev["sgm4_use"] = sg4 if hb % 4 == 3 else prev.get("sgm4_use")
            b4s = S.add("vector", lambda e, h0=h0, g=g: [[e.tensor_scalar(B4[:, hh * 128:(hh + 1) * 128],
                        Btok[:, g * 128:(g + 1) * 128], dtdsT[:, h0 + hh:h0 + hh + 1], None, ALU.mult)
                        for hh in range(4)][-1]],
                        deps=[bte, te2, prev.get("B4_use")], tag=f"b4_{ch}_{hb}")
            def ymm(e, h0=h0, g=g):
                last = None
                for hh in range(4):
                    h = h0 + hh
                    e.matmul(ps_yyo[:, hh * 64:(hh + 1) * 64], scoresM[:, hh * 128:(hh + 1) * 128],
                             xtok[:, h * 64:(h + 1) * 64], start=True, stop=True, skip_group_check=True)
                    last = e.matmul(ps_yyo[:, 256 + hh * 64:256 + (hh + 1) * 64], CT[:, g * 128:(g + 1) * 128],
                                    R_bf[:, h * 64:(h + 1) * 64], start=True, stop=True, skip_group_check=True)
                return [last]
            ym = S.add("tensor", ymm, deps=[sg4, cld, rbf] + xt_evs + [prev.get("yev")], tag=f"ymm_{ch}_{hb}")
            def stmm(e, h0=h0):
                last = None
                for hh in range(4):
                    h = h0 + hh
                    last = e.matmul(ps_st[:, hh * 64:(hh + 1) * 64], B4[:, hh * 128:(hh + 1) * 128],
                                    xtok[:, h * 64:(h + 1) * 64], start=True, stop=True, skip_group_check=True)
                return [last]
            sm = S.add("tensor", stmm, deps=[b4s] + xt_evs + [prev.get("rup")], tag=f"stmm_{ch}_{hb}")
            ev1 = S.add("vector", lambda e, h0=h0: [[e.tensor_scalar(tmp_y[:, hh * 64:(hh + 1) * 64],
                        ps_yyo[:, 256 + hh * 64:256 + (hh + 1) * 64], excsT[:, h0 + hh:h0 + hh + 1], None, ALU.mult)
                        for hh in range(4)][-1]],
                        deps=[ym, ex0, prev.get("tmpy_use")], tag=f"ev1_{ch}_{hb}")
            ev2 = S.add("vector", lambda e, h0=h0: [[e.scalar_tensor_tensor(tmp2_y[:, hh * 64:(hh + 1) * 64],
                        xtok[:, (h0 + hh) * 64:(h0 + hh + 1) * 64], D_rep[:, h0 + hh:h0 + hh + 1],
                        ps_yyo[:, hh * 64:(hh + 1) * 64], ALU.mult, ALU.add)
                        for hh in range(4)][-1]],
                        deps=[ym, dl, prev.get("tmp2y_use")] + xt_evs, tag=f"ev2_{ch}_{hb}")
            prev["yev"] = ev2
            yb = hb % 2
            ev3 = S.add("vector", lambda e, yb=yb: [e.tensor_tensor(ypre[yb][:, :],
                        tmp_y[:, :], tmp2_y[:, :], ALU.add)],
                        deps=[ev1, ev2, prev.get(("ypre_use", yb))], tag=f"ev3_{ch}_{hb}")
            prev["tmpy_use"] = ev3
            prev["tmp2y_use"] = ev3
            yst1 = S.dma("sync", lambda e, t0=t0, hb=hb, yb=yb: [e.dma_start(
                        out=io["y_pre"][t0:t0 + 128, hb * 256:(hb + 1) * 256], in_=ypre[yb][:, :])],
                        deps=[ev3], tag=f"yst_{ch}_{hb}")
            prev[("ypre_use", yb)] = yst1
            ylast.append(yst1)
            rup = S.add("vector", lambda e, h0=h0: [[e.scalar_tensor_tensor(R[:, (h0 + hh) * 64:(h0 + hh + 1) * 64],
                        R[:, (h0 + hh) * 64:(h0 + hh + 1) * 64], cd_bc[:, h0 + hh:h0 + hh + 1],
                        ps_st[:, hh * 64:(hh + 1) * 64], ALU.mult, ALU.add) for hh in range(4)][-1]],
                        deps=[sm, cdbc, rbf, r0], tag=f"rup_{ch}_{hb}")
            prev["rup"] = rup
            rlast.append(rup)
        m1outs.append(ylast[-1])
        prev["R_upd"] = rlast[-1]
        prev["Rbf_use"] = ylast[-1]
        prev["cs_use"] = te2
        prev["cslast_use"] = ds0
        prev["cd_use"] = pr1
        prev["lncs_use"] = fr2
        prev["cdrow_use"] = cdb
        prev["cdbc_use"] = rlast[-1]
        prev["ds_use"] = dd0
        prev["dtds_use"] = te2
        prev["csT_use"] = ex0
        prev["dtdsT_use"] = rlast[-1]
        prev["excs_use"] = ylast[-1]
        for g in range(8):
            prev[("BTu", g)] = ylast[-1]
            prev[("CTu", g)] = ylast[-1]
            prev[("sgm_u", g)] = ylast[-1]
            prev[("Btok_u", g)] = ylast[-1]
        prev["xtok_use"] = ylast[-1]
        prev["scoresM_use"] = ylast[-1]
        prev["B4_use"] = ylast[-1]
        prev["seg_sb_use"] = sg3
    o1 = S.dma("sync", lambda e: [e.dma_start(out=io["R_agg"], in_=R[:, :])], deps=rlast, tag="o_R")
    o2 = S.dma("sync", lambda e: [e.dma_start(out=io["P_core"], in_=P_run[:, :])], deps=[prev["pr_upd"]], tag="o_P")
    o3 = S.dma("sync", lambda e: [e.dma_start(out=io["cdpref"], in_=cdprefs[:, :])], deps=[prev["pr_upd"]], tag="o_cd")
    return m1outs + [o1, o2, o3, exst]


def build_mamba_m1_nc():
    nc = bass.Bass("TRN2", target_bir_lowering=False)
    io = {
        "xT": nc.dram_tensor("xT", [D, TE + 1], F32, kind="ExternalInput").ap(),
        "x_tok": nc.dram_tensor("x_tok", [TE, D], F32, kind="ExternalInput").ap(),
        "wip_s": nc.dram_tensor("wip_s", [145, 128, D], F32, kind="ExternalInput").ap(),
        "conv_w": nc.dram_tensor("conv_w", [128, 320], F32, kind="ExternalInput").ap(),
        "conv_b": nc.dram_tensor("conv_b", [128, 80], F32, kind="ExternalInput").ap(),
        "dt_bias": nc.dram_tensor("dt_bias", [128, 1], F32, kind="ExternalInput").ap(),
        "A_col": nc.dram_tensor("A_col", [128, 1], F32, kind="ExternalInput").ap(),
        "D_rep": nc.dram_tensor("D_rep", [128, 128], F32, kind="ExternalInput").ap(),
        "triT": nc.dram_tensor("triT", [128, 128], BF16, kind="ExternalInput").ap(),
        "ident": nc.dram_tensor("ident", [128, 128], BF16, kind="ExternalInput").ap(),
        "identf": nc.dram_tensor("identf", [128, 128], F32, kind="ExternalInput").ap(),
        "gateT": nc.dram_tensor("gateT", [GATE, T], BF16, kind="ExternalOutput").ap(),
        "xbc_conv": nc.dram_tensor("xbc_conv", [CONV_DIM, T], BF16, kind="ExternalOutput").ap(),
        "y_pre": nc.dram_tensor("y_pre", [T, GATE], BF16, kind="ExternalOutput").ap(),
        "R_agg": nc.dram_tensor("R_agg", [128, 8192], F32, kind="ExternalOutput").ap(),
        "P_core": nc.dram_tensor("P_core", [128, 1], F32, kind="ExternalOutput").ap(),
        "cdpref": nc.dram_tensor("cdpref", [128, NCH], F32, kind="ExternalOutput").ap(),
        "excs": nc.dram_tensor("excs", [128, NCH * 128], F32, kind="ExternalOutput").ap(),
    }
    S = Sched(nc)
    outs = emit_mamba_m1(S, io)
    S.final_deps = outs
    S.final_engine = "gpsimd"
    S.emit()
    return nc


def emit_mamba_m2(S, io):
    nc = S.nc
    S0 = S.sbuf("S0", [128, 8192], F32)
    buf_a = S.sbuf("buf_a", [128, 8192], F32)
    buf_r = S.sbuf("buf_r", [128, 8192], F32)
    S0_bf = S.sbuf("S0_bf", [128, 8192], BF16)
    ypre_t = S.sbuf("ypre_t", [128, 8192], BF16)
    gate_t = S.sbuf("gate_t", [128, 8192], BF16)
    CTm2 = S.sbuf("CTm2", [128, 8 * 128], BF16)
    excs2 = S.sbuf("excs2", [128, NCH * 128], F32)
    nw_bc = S.sbuf("nw_bc", [128, 8192], F32)
    identb2 = S.sbuf("identb2", [128, 128], BF16)
    tmp_yc = S.sbuf("tmp_yc", [128, 256], F32)
    qg = S.sbuf("qg", [128, 1], F32)
    sqg = S.sbuf("sqg", [128, 1024], BF16)
    ytb = S.sbuf("ytb", [128, 512], BF16)
    rstdg = S.sbuf("rstdg", [128, 1], F32)
    ps_pool = [S.psum(f"m2p{i}", [128, 512], F32) for i in range(7)]
    ps_tr2 = S.psum("ps_tr2", [128, 1024], BF16)
    ynt_scr = nc.dram_tensor("ynt_scr", [GATE, T], BF16, kind="Internal").ap()

    il2 = S.dma("sync", lambda e: [e.dma_start(out=identb2[:, :], in_=io["ident"])], tag="il2")
    exl = S.dma("sync", lambda e: [e.dma_start(out=excs2[:, :], in_=io["excs"])], tag="exl")
    nl3 = S.dma("sync", lambda e: [e.dma_start(out=nw_bc[:, :], in_=io["norm_w"].partition_broadcast(128))],
                tag="nl3")

    z0 = S.add("vector", lambda e: [e.memset(S0[:, :], 0.0)], tag="z0")
    prev = {"s0": z0}
    for cp in range(7):
        pld = S.dma("sync", lambda e, cp=cp: [e.dma_start(out=buf_a[:, :],
                    in_=io["P_exp"][cp:cp + 1, :].partition_broadcast(128))],
                    deps=[prev.get("pa_use")], tag=f"pld{cp}")
        rld = S.dma("sync", lambda e, cp=cp: [e.dma_start(out=buf_r[:, :], in_=io["R_pre"][cp])],
                    deps=[prev.get("rb_use")], tag=f"rld{cp}")
        f1 = S.add("vector", lambda e: [e.tensor_tensor(S0[:, :], S0[:, :], buf_a[:, :], ALU.mult)],
                   deps=[pld, prev["s0"]], tag=f"f1_{cp}")
        f2 = S.add("vector", lambda e: [e.tensor_tensor(S0[:, :], S0[:, :], buf_r[:, :], ALU.add)],
                   deps=[rld, f1], tag=f"f2_{cp}")
        prev["s0"] = f2
        prev["pa_use"] = f1
        prev["rb_use"] = f2

    outs = []
    for ch in range(NCH):
        t0 = ch * 128
        cb = S.dma("sync", lambda e, ch=ch: [e.dma_start(out=buf_a[:, :],
                   in_=io["cdpref_exp"][ch:ch + 1, :].partition_broadcast(128))],
                   deps=[prev.get("pa_use")], tag=f"cpb{ch}")
        sj = S.add("vector", lambda e: [e.tensor_tensor(buf_r[:, :], S0[:, :], buf_a[:, :], ALU.mult)],
                   deps=[cb, prev["s0"], prev.get("rb_use")], tag=f"sj{ch}")
        prev["pa_use"] = sj
        prev["rb_use"] = sj
        sjb = S.add("vector", lambda e: [e.tensor_copy(S0_bf[:, :], buf_r[:, :])],
                    deps=[sj, prev.get("s0b_use")], tag=f"sjb{ch}")
        yld = S.dma("sync", lambda e, t0=t0: [e.dma_start(out=ypre_t[:, :], in_=io["y_pre"][t0:t0 + 128, :])],
                    deps=[prev.get("yp_use")], tag=f"yld{ch}")
        gld = S.dma("sync", lambda e, t0=t0: [e.dma_start(out=gate_t[:, :], in_=io["gate_tok"][t0:t0 + 128, :])],
                    deps=[prev.get("g_use")], tag=f"gld{ch}")
        clds = []
        for g in range(8):
            cl = S.dma("sync", lambda e, g=g, t0=t0: [e.dma_start(out=CTm2[:, g * 128:(g + 1) * 128],
                       in_=io["CT_in"][g * 128:(g + 1) * 128, t0:t0 + 128])],
                       deps=[prev.get("ct_use")] if g == 0 else [], tag=f"cl2_{ch}_{g}")
            clds.append(cl)
        yadds = []
        for hb in range(32):
            g = hb // 4
            h0 = hb * 4
            def ycmm(e, h0=h0, g=g):
                last = None
                for hh in range(4):
                    h = h0 + hh
                    last = e.matmul(ps_pool[0][:, hh * 64:(hh + 1) * 64], CTm2[:, g * 128:(g + 1) * 128],
                                    S0_bf[:, h * 64:(h + 1) * 64], start=True, stop=True, skip_group_check=True)
                return [last]
            ym = S.add("tensor", ycmm, deps=[sjb] + clds + [prev.get("yc_ev")], tag=f"ycm{ch}_{hb}")
            e1 = S.add("vector", lambda e, h0=h0, ch=ch: [[e.tensor_scalar(tmp_yc[:, hh * 64:(hh + 1) * 64],
                        ps_pool[0][:, hh * 64:(hh + 1) * 64], excs2[:, ch * 128 + h0 + hh:ch * 128 + h0 + hh + 1],
                        None, ALU.mult) for hh in range(4)][-1]],
                       deps=[ym, exl, prev.get("tyc_use")], tag=f"e1_{ch}_{hb}")
            prev["yc_ev"] = e1
            e2 = S.add("vector", lambda e, hb=hb: [e.tensor_tensor(ypre_t[:, hb * 256:(hb + 1) * 256],
                        ypre_t[:, hb * 256:(hb + 1) * 256], tmp_yc[:, :], ALU.add)],
                       deps=[e1, yld], tag=f"e2_{ch}_{hb}")
            prev["tyc_use"] = e2
            yadds.append(e2)
        yg = S.add("vector", lambda e: [e.tensor_tensor(ypre_t[:, :], ypre_t[:, :], gate_t[:, :], ALU.mult)],
                   deps=yadds + [gld], tag=f"yg{ch}")
        prev["g_use"] = yg
        gn_done = []
        for g in range(8):
            g0 = g * 1024
            q1 = S.add("scalar", lambda e, g0=g0: [e.activation(sqg[:, :], ypre_t[:, g0:g0 + 1024], AF.Square)],
                       deps=[yg, prev.get("sqg_use")], tag=f"q1_{ch}_{g}")
            q2 = S.add("vector", lambda e: [e.reduce_sum(qg[:, :], sqg[:, :], axis=AX)],
                       deps=[q1, prev.get("qg_use")], tag=f"q2_{ch}_{g}")
            prev["sqg_use"] = q2
            q3 = S.add("vector", lambda e: [e.tensor_scalar(qg[:, :], qg[:, :], 1.0 / 1024, EPS, ALU.mult, ALU.add)],
                       deps=[q2], tag=f"q3_{ch}_{g}")
            q4 = S.add("scalar", lambda e: [e.activation(rstdg[:, :], qg[:, :], AF.Sqrt)],
                       deps=[q3, prev.get("rstdg_use")], tag=f"q4_{ch}_{g}")
            prev["qg_use"] = q4
            q5 = S.add("vector", lambda e: [e.reciprocal(rstdg[:, :], rstdg[:, :])], deps=[q4], tag=f"q5_{ch}_{g}")
            q6 = S.add("vector", lambda e, g0=g0: [e.tensor_scalar(ypre_t[:, g0:g0 + 1024], ypre_t[:, g0:g0 + 1024],
                        rstdg[:, :], None, ALU.mult)], deps=[q5], tag=f"q6_{ch}_{g}")
            prev["rstdg_use"] = q6
            gn_done.append(q6)
        nw1 = S.add("vector", lambda e: [e.tensor_tensor(ypre_t[:, :], ypre_t[:, :], nw_bc[:, :], ALU.mult)],
                    deps=gn_done + [nl3], tag=f"nw1_{ch}")
        prev["yp_use"] = None
        tr_done = []
        for yq in range(16):
            def ytr(e, yq=yq):
                return [e.transpose(ps_tr2[:, i * 128:(i + 1) * 128],
                        ypre_t[:, (yq * 4 + i) * 128:(yq * 4 + i + 1) * 128], identb2[:, :]) for i in range(4)]
            t1 = S.add("tensor", ytr, n_instr=4, deps=[nw1, il2, prev.get("tr2_ev")], tag=f"ytr{ch}_{yq}")
            t2 = S.add("vector", lambda e: [e.tensor_copy(ytb[:, :], ps_tr2[:, 0:512])],
                       deps=[t1, prev.get("ytb_use")], tag=f"yte{ch}_{yq}")
            prev["tr2_ev"] = t2
            t3 = S.dma("sync", lambda e, yq=yq, t0=t0: [e.dma_start(
                        out=ynt_scr[yq * 512:(yq + 1) * 512, t0:t0 + 128].rearrange("(q p) t -> p q t", p=128),
                        in_=ytb[:, :].rearrange("p (q t) -> p q t", q=4))],
                       deps=[t2], tag=f"yts{ch}_{yq}")
            prev["ytb_use"] = t3
            tr_done.append(t3)
        prev["ct_use"] = yadds[-1]
        prev["yp_use"] = nw1
        outs += tr_done

    ynt_b = [S.sbuf(f"yntb{i}", [128, T], BF16) for i in range(2)]
    wo_stg = [S.sbuf(f"mwo{i}", [128, 512], F32) for i in range(2)]
    wo_bf = [S.sbuf(f"mwobf{i}", [128, 512], BF16) for i in range(2)]
    oev2 = S.sbuf("oev2", [128, 512], F32)
    final = []
    for dtg in range(8):
        last_mm = None
        mm_prev = None
        for ct in range(64):
            p = ct % 2
            yl = S.dma("sync", lambda e, ct=ct, p=p: [e.dma_start(out=ynt_b[p][:, :],
                        in_=ynt_scr[ct * 128:(ct + 1) * 128, :])],
                        deps=outs + [prev.get(("ynb", p))], tag=f"oyl{dtg}_{ct}")
            wl = S.dma("sync", lambda e, dtg=dtg, ct=ct, p=p: [e.dma_start(out=wo_stg[p][:, :],
                        in_=io["wout_s"][dtg, ct])], deps=[prev.get(("mwo", p))], tag=f"owl{dtg}_{ct}")
            wc = S.add("vector", lambda e, p=p: [e.tensor_copy(wo_bf[p][:, :], wo_stg[p][:, :])],
                       deps=[wl, prev.get(("mwobf", p))], tag=f"owc{dtg}_{ct}")
            prev[("mwo", p)] = wc
            def omm(e, ct=ct, p=p):
                last = None
                for dt4 in range(4):
                    for th in range(2):
                        ps = ps_pool[dt4] if th == 0 else (ps_pool[4 + dt4] if dt4 < 3 else ps_tr2.bitcast(F32))
                        last = e.matmul(ps[:, :], wo_bf[p][:, dt4 * 128:(dt4 + 1) * 128],
                                        ynt_b[p][:, th * 512:(th + 1) * 512],
                                        start=(ct == 0), stop=(ct == 63))
                return [last]
            dd = [yl, wc] + ([mm_prev] if mm_prev else [])
            if ct == 0:
                dd += [prev.get("oev2_done"), prev.get("yc_ev")]
            mm = S.add("tensor", omm, deps=dd, tag=f"omm{dtg}_{ct}")
            mm_prev = mm
            prev[("ynb", p)] = mm
            prev[("mwobf", p)] = mm
            last_mm = mm
        for dt4 in range(4):
            for th in range(2):
                psx = ps_pool[dt4] if th == 0 else (ps_pool[4 + dt4] if dt4 < 3 else ps_tr2.bitcast(F32))
                ev = S.add("vector", lambda e, psx=psx: [e.tensor_copy(oev2[:, :], psx[:, :])],
                           deps=[last_mm, prev.get("ost_m2")], tag=f"oev{dtg}_{dt4}_{th}")
                st = S.dma("sync", lambda e, dtg=dtg, dt4=dt4, th=th: [e.dma_start(
                        out=io["soT"][dtg * 512 + dt4 * 128:dtg * 512 + (dt4 + 1) * 128, th * 512:(th + 1) * 512],
                        in_=oev2[:, :])], deps=[ev], tag=f"osm{dtg}_{dt4}_{th}")
                prev["ost_m2"] = st
                prev["oev2_done"] = ev
                final.append(st)
    return final


def build_mamba_m2_nc():
    nc = bass.Bass("TRN2", target_bir_lowering=False)
    io = {
        "y_pre": nc.dram_tensor("y_pre", [T, GATE], BF16, kind="ExternalInput").ap(),
        "gate_tok": nc.dram_tensor("gate_tok", [T, GATE], BF16, kind="ExternalInput").ap(),
        "R_pre": nc.dram_tensor("R_pre", [7, 128, 8192], F32, kind="ExternalInput").ap(),
        "P_exp": nc.dram_tensor("P_exp", [7, 8192], F32, kind="ExternalInput").ap(),
        "cdpref_exp": nc.dram_tensor("cdpref_exp", [NCH, 8192], F32, kind="ExternalInput").ap(),
        "CT_in": nc.dram_tensor("CT_in", [1024, T], BF16, kind="ExternalInput").ap(),
        "excs": nc.dram_tensor("excs", [128, NCH * 128], F32, kind="ExternalInput").ap(),
        "norm_w": nc.dram_tensor("norm_w", [1, GATE], F32, kind="ExternalInput").ap(),
        "wout_s": nc.dram_tensor("wout_s", [8, 64, 128, 512], F32, kind="ExternalInput").ap(),
        "ident": nc.dram_tensor("ident", [128, 128], BF16, kind="ExternalInput").ap(),
        "soT": nc.dram_tensor("soT", [D, T], F32, kind="ExternalOutput").ap(),
    }
    S = Sched(nc)
    outs = emit_mamba_m2(S, io)
    S.final_deps = outs
    S.final_engine = "gpsimd"
    S.emit()
    return nc


from concourse.bass_utils import run_bass_kernel_spmd
import ml_dtypes

_NC_CACHE = {}

def _get_nc(name, fn):
    if name not in _NC_CACHE:
        _NC_CACHE[name] = fn()
    return _NC_CACHE[name]


def _run(name, fn, in_maps):
    nc = _get_nc(name, fn)
    res = run_bass_kernel_spmd(nc, in_maps, core_ids=list(range(NCORES)))
    return [dict(res.results[c]) for c in range(NCORES)]


def _stage_m1(x0, ins):
    Wt = np.ascontiguousarray(ins["L0_attn_in_proj_w"].T)
    blocks = []
    for cb in range(145):
        blk = Wt[:, cb * 128:(cb + 1) * 128]
        blocks.append(blk.reshape(32, 128, 128).transpose(1, 0, 2).reshape(128, 4096))
    wip_s = np.ascontiguousarray(np.stack(blocks, 0))
    conv_w = np.ascontiguousarray(ins["L0_attn_conv_w"][:, 0, :].reshape(80, 128, 4).transpose(1, 0, 2).reshape(128, 320)).astype(np.float32)
    conv_b = np.ascontiguousarray(ins["L0_attn_conv_b"].reshape(80, 128).T).astype(np.float32)
    dtb = ins["L0_attn_dt_bias"].reshape(128, 1).astype(np.float32)
    A_col = (-np.arange(1, 129, dtype=np.float32)).reshape(128, 1)
    D_rep = np.ascontiguousarray(np.broadcast_to(ins["L0_attn_D"][None, :], (128, 128))).astype(np.float32)
    triT = (np.arange(128)[None, :] >= np.arange(128)[:, None]).astype(ml_dtypes.bfloat16)
    ident = np.eye(128).astype(ml_dtypes.bfloat16)
    identf = np.eye(128).astype(np.float32)
    in_maps = []
    for c in range(NCORES):
        lo = c * T - 128
        if lo < 0:
            xe = np.concatenate([np.zeros((128, D), np.float32), x0[:c * T + T]], axis=0)
        else:
            xe = x0[lo:c * T + T]
        xT = np.concatenate([np.ascontiguousarray(xe.T), np.ones((D, 1), np.float32)], axis=1)
        in_maps.append({
            "xT": np.ascontiguousarray(xT), "x_tok": np.ascontiguousarray(xe),
            "wip_s": wip_s, "conv_w": conv_w, "conv_b": conv_b, "dt_bias": dtb,
            "A_col": A_col, "D_rep": D_rep, "triT": triT, "ident": ident, "identf": identf,
        })
    return in_maps


def _stage_m2(res, ins):
    Wt = np.ascontiguousarray(ins["L0_attn_out_proj_w"].T)
    wout_s = np.ascontiguousarray(Wt.reshape(64, 128, 8, 512).transpose(2, 0, 1, 3))
    nw = ins["L0_attn_norm_w"].reshape(1, GATE).astype(np.float32)
    ident = np.eye(128).astype(ml_dtypes.bfloat16)
    R_list = [np.asarray(res[c]["R_agg"]) for c in range(NCORES)]
    P_list = [np.repeat(np.asarray(res[c]["P_core"])[:, 0], 64) for c in range(NCORES)]
    in_maps = []
    for c in range(NCORES):
        R_pre = np.stack([R_list[cp] if cp < c else np.zeros((128, 8192), np.float32) for cp in range(7)])
        P_exp = np.stack([P_list[cp] if cp < c else np.ones(8192, np.float32) for cp in range(7)])
        cdp = np.asarray(res[c]["cdpref"])
        cdpref_exp = np.stack([np.repeat(cdp[:, ch], 64) for ch in range(NCH)])
        in_maps.append({
            "y_pre": np.asarray(res[c]["y_pre"]),
            "gate_tok": np.ascontiguousarray(np.asarray(res[c]["gateT"]).T),
            "R_pre": np.ascontiguousarray(R_pre), "P_exp": np.ascontiguousarray(P_exp),
            "cdpref_exp": np.ascontiguousarray(cdpref_exp),
            "CT_in": np.ascontiguousarray(np.asarray(res[c]["xbc_conv"])[9216:10240]),
            "excs": np.asarray(res[c]["excs"]), "norm_w": nw,
            "wout_s": wout_s, "ident": ident,
        })
    return in_maps


def _ln_res(soT_or_so, x_full, transpose_so):
    in_maps = []
    for c in range(NCORES):
        so = soT_or_so[c]
        if transpose_so:
            so = np.ascontiguousarray(np.asarray(so).T)
        in_maps.append({"so_tok": so, "x_tok": np.ascontiguousarray(x_full[c * T:(c + 1) * T])})
    res = _run("lnres", build_ln_residual_nc, in_maps)
    return np.concatenate([np.asarray(r["xout"]) for r in res], axis=0)


def _l0ffn(x1, ins):
    CG, CD = 22016 // 8, 4096 // 8
    qb_gu = pack_bytes_rows(ins["L0_ffn_qweight_gu"])
    qz_gu = np.ascontiguousarray(ins["L0_ffn_qzeros_gu"].view(np.uint8))
    sc_gu = ins["L0_ffn_scales_gu"]
    qb_d = pack_bytes_rows(ins["L0_ffn_qweight_down"])
    qz_d = np.ascontiguousarray(ins["L0_ffn_qzeros_down"].view(np.uint8))
    sc_d = ins["L0_ffn_scales_down"]
    in1 = []
    for c in range(NCORES):
        in1.append({
            "qb_gu": np.ascontiguousarray(qb_gu[:, c * CG:(c + 1) * CG]),
            "qz_gu": np.ascontiguousarray(qz_gu[:, c * (CG // 2):(c + 1) * (CG // 2)]),
            "sc_gu": np.ascontiguousarray(sc_gu[:, c * CG:(c + 1) * CG]),
            "qb_d": np.ascontiguousarray(qb_d[:, c * CD:(c + 1) * CD]),
            "qz_d": np.ascontiguousarray(qz_d[:, c * (CD // 2):(c + 1) * (CD // 2)]),
            "sc_d": np.ascontiguousarray(sc_d[:, c * CD:(c + 1) * CD]),
        })
    r1 = _run("gptq", build_gptq_dequant_nc, in1)
    wgu_full = np.concatenate([np.asarray(r["wgu"]) for r in r1], axis=1)
    wd_full = np.concatenate([np.asarray(r["wd"]) for r in r1], axis=1)
    perm = gu_perm()
    blocks = []
    for g in range(RT_D):
        for jt in (g, g + RT_D):
            blk = wgu_full[:, jt * 128:(jt + 1) * 128]
            blocks.append(blk.reshape(32, 128, 128).transpose(1, 0, 2).reshape(128, 4096))
    wgu_s = np.ascontiguousarray(np.stack(blocks, 0))
    wd_s = np.ascontiguousarray(wd_full.reshape(RT_D, 128, 8, 512).transpose(2, 0, 1, 3))
    in2 = []
    for c in range(NCORES):
        xs = x1[c * T:(c + 1) * T, :]
        xTp = np.concatenate([np.ascontiguousarray(xs.T)[perm], np.ones((D, 1), np.float32)], axis=1)
        in2.append({"xTp": np.ascontiguousarray(xTp), "x_tok": np.ascontiguousarray(xs),
                    "wgu_s": wgu_s, "wd_s": wd_s})
    r2 = _run("l0ffn", build_l0ffn_main_nc, in2)
    return _ln_res([r["soT"] for r in r2], x1, True)


def _l1attn(x2, ins):
    Wqkv = ins["L1_attn_Wqkv"].astype(np.float32)
    bqkv = ins["L1_attn_bqkv"].reshape(1, -1).astype(np.float32)
    Wo = ins["L1_attn_Wo"].astype(np.float32)
    bo = ins["L1_attn_bo"].reshape(1, -1).astype(np.float32)
    in1 = []
    for c in range(NCORES):
        xs = x2[c * T:(c + 1) * T, :]
        in1.append({"xT": np.ascontiguousarray(xs.T), "x_tok": np.ascontiguousarray(xs),
                    "Wqkv": Wqkv, "bqkv": bqkv})
    r1 = _run("l1a_qkv", build_l1attn_qkv_nc, in1)
    kTf = np.concatenate([np.asarray(r["kT"]) for r in r1], axis=1)
    vf = np.concatenate([np.asarray(r["v"]) for r in r1], axis=0)
    in2 = []
    for c in range(NCORES):
        in2.append({"qT": np.asarray(r1[c]["qT"]), "kTf": kTf, "vf": vf,
                    "Wo": Wo, "bo": bo,
                    "x_tok": np.ascontiguousarray(x2[c * T:(c + 1) * T, :])})
    r2 = _run("l1a_flash", build_l1attn_flash_nc, in2)
    return np.concatenate([np.asarray(r["xout"]) for r in r2], axis=0)


def _l1ffn(x3, ins):
    in_maps = stage_l1ffn_inputs(x3, ins["L1_ffn_Wr"].astype(np.float32),
                                 ins["L1_ffn_br"].astype(np.float32),
                                 ins["L1_ffn_A"].astype(np.float32),
                                 ins["L1_ffn_B"].astype(np.float32), ml_dtypes.bfloat16)
    res = _run("l1ffn", build_l1ffn_nc, in_maps)
    return np.concatenate([np.asarray(r["xout"]) for r in res], axis=0)


def kernel(**inputs):
    ins = inputs
    assert np.array_equal(np.asarray(ins["L0_ffn_g_idx_gu"]).ravel(), np.arange(4096) // 128), "nonstandard g_idx_gu"
    assert np.array_equal(np.asarray(ins["L0_ffn_g_idx_down"]).ravel(), np.arange(INTER) // 128), "nonstandard g_idx_down"
    x0 = np.ascontiguousarray(np.asarray(ins["L0_attn_hidden_states"], dtype=np.float32).reshape(8192, D))
    r1 = _run("m1", build_mamba_m1_nc, _stage_m1(x0, ins))
    r2 = _run("m2", build_mamba_m2_nc, _stage_m2(r1, ins))
    x1 = _ln_res([r["soT"] for r in r2], x0, True)
    x2 = _l0ffn(x1, ins)
    x3 = _l1attn(x2, ins)
    x4 = _l1ffn(x3, ins)
    return x4.reshape(1, 8192, D)


# version 04
# speedup vs baseline: 1.0438x; 1.0438x over previous
import numpy as np
from contextlib import ExitStack
import concourse.bass as bass
import concourse.mybir as mybir

F32 = mybir.dt.float32
BF16 = mybir.dt.bfloat16
U8 = mybir.dt.uint8
U32 = mybir.dt.uint32
I32 = mybir.dt.int32
AF = mybir.ActivationFunctionType
ALU = mybir.AluOpType
AX = mybir.AxisListType.X

NCORES = 8
T = 1024
D = 4096
EPS = 1e-5


class Step:
    __slots__ = ("eng", "fn", "deps", "seq", "sem", "count", "tag", "_n", "_amt")


class Sched:
    ENGINES = ("tensor", "vector", "scalar", "gpsimd", "sync")

    def __init__(self, nc):
        self.nc = nc
        self.es = ExitStack()
        self.steps = []
        self.sems = {}
        self.cnt = {e: 0 for e in self.ENGINES}
        self._nbuf = 0

    def sbuf(self, name, shape, dtype):
        if getattr(self, "arena", None) is None:
            return self.es.enter_context(self.nc.sbuf_tensor(name, shape, dtype))
        nbytes = shape[1] * {F32: 4, BF16: 2, U8: 1, U32: 4, I32: 4}[dtype]
        nbytes = (nbytes + 63) & ~63
        off = self._arena_off
        assert off + nbytes <= self._arena_cap, f"arena overflow at {name}: {off}+{nbytes}"
        self._arena_off = off + nbytes
        ap = self.arena[0:shape[0], off:off + nbytes]
        if dtype != U8:
            ap = ap.bitcast(dtype)
        return ap[0:shape[0], 0:shape[1]]

    def psum(self, name, shape, dtype=F32):
        if getattr(self, "arena", None) is None:
            return self.es.enter_context(self.nc.psum_tensor(name, shape, dtype))
        nbytes = shape[1] * {F32: 4, BF16: 2}[dtype]
        nbanks = (nbytes + 2047) // 2048
        b = self._psum_off
        assert b + nbanks <= 8, f"psum arena overflow at {name}"
        self._psum_off = b + nbanks
        assert nbanks == 1, f"psum multi-bank not supported in arena: {name}"
        ap = self._psum_banks[b][:, :]
        if dtype != F32:
            ap = ap.bitcast(dtype)
        return ap[0:shape[0], 0:shape[1]]

    def setup_arena(self, sbuf_bytes=196608):
        t = self.es.enter_context(self.nc.sbuf_tensor("ARENA", [128, sbuf_bytes], U8))
        self.arena = t[:, :]
        self._arena_cap = sbuf_bytes
        self._arena_off = 0
        self._psum_off = 0
        self._psum_banks = [self.es.enter_context(self.nc.psum_tensor(f"PSB{i}", [128, 512], F32)) for i in range(8)]
        self._scopes = []

    def scope_push(self):
        self._scopes.append((self._arena_off, self._psum_off if hasattr(self, "_psum_off") else 0))

    def scope_pop(self):
        self._arena_off, self._psum_off = self._scopes.pop()

    def barrier(self, tag="barrier"):
        deps = []
        for semname, c in self.cnt.items():
            if c > 0:
                d = Step()
                d.sem = semname
                d.count = c
                d.seq = len(self.steps) - 1
                deps.append(d)
        st = self.add("gpsimd", lambda e: [e.engine_nop()], deps=deps, tag=tag)
        self.floor = st
        return st

    def _sem(self, e):
        if e not in self.sems:
            self.sems[e] = self.es.enter_context(self.nc.semaphore("sem_" + e))
        return self.sems[e]

    def add(self, eng, fn, deps=(), n_instr=1, amt=None, tag=""):
        assert eng in self.ENGINES, eng
        if amt is None:
            amt = 16 if eng == "sync" else 1
        st = Step()
        st.eng = eng
        st.fn = fn
        st.deps = [d for d in deps if d is not None]
        fl = getattr(self, "floor", None)
        if fl is not None and fl is not st:
            st.deps.append(fl)
        st.seq = len(self.steps)
        st.sem = eng + "_dma" if (amt == 16 and eng != "sync") else eng
        if amt == 16:
            if not hasattr(self, "_last_dma"):
                self._last_dma = {}
            prevd = self._last_dma.get(st.sem)
            if prevd is not None and prevd not in st.deps:
                st.deps.append(prevd)
            self._last_dma[st.sem] = st
        for d in st.deps:
            assert d.seq < st.seq, f"dep cycle: {d.seq} !< {st.seq}"
        self.cnt.setdefault(st.sem, 0)
        self.cnt[st.sem] += n_instr * amt
        st.count = self.cnt[st.sem]
        st.tag = tag
        st._n = n_instr
        st._amt = amt
        self.steps.append(st)
        return st

    def dma(self, eng, fn, deps=(), n=1, tag=""):
        return self.add(eng, fn, deps, n_instr=n, amt=16, tag=tag)

    def barrier_step(self, eng, deps, tag="barrier"):
        return self.add(eng, lambda e: [e.engine_nop()] if hasattr(e, "engine_nop") else [e.nop()],
                        deps=deps, tag=tag)

    def emit(self):
        nc = self.nc
        with self.es, nc.Block() as block:
            for ename in self.ENGINES:
                mine = [s for s in self.steps if s.eng == ename]
                if not mine:
                    continue
                deco = getattr(block, ename)

                def section(eng, _mine=mine, _ename=ename):
                    done = {}
                    for st in _mine:
                        waits = {}
                        for d in st.deps:
                            if d.count <= done.get(d.sem, 0):
                                continue
                            waits[d.sem] = max(waits.get(d.sem, 0), d.count)
                        for s_name, c in sorted(waits.items()):
                            eng.wait_ge(self._sem(s_name), c)
                            done[s_name] = max(done.get(s_name, 0), c)
                        instrs = st.fn(eng)
                        assert len(instrs) == st._n, (st.tag, len(instrs), st._n)
                        for i in instrs:
                            i.then_inc(self._sem(st.sem), st._amt)

                deco(section)
            if getattr(self, "final_deps", None):
                fe = getattr(self, "final_engine", "gpsimd")
                deco = getattr(block, fe)

                def tail(eng):
                    waits = {}
                    for d in self.final_deps:
                        waits[d.sem] = max(waits.get(d.sem, 0), d.count)
                    for s_name, c in sorted(waits.items()):
                        eng.wait_ge(self._sem(s_name), c)

                deco(tail)


def cast_load(S, name, dram_ap, rows, cols, out_dtype=BF16, deps=(), tile_rows=128,
              eng_dma="sync", stage_cols=None):
    nc = S.nc
    nk = rows // tile_rows
    dst = S.sbuf(name, [tile_rows, nk * cols], out_dtype)
    stg_cols = stage_cols or cols
    nstg = 1 if nk == 1 else 2
    stg = [S.sbuf(f"{name}_stg{i}", [tile_rows, stg_cols], F32) for i in range(nstg)]
    ready = []
    prev_cast = [None, None]
    for k in range(nk):
        p = k % nstg
        war = prev_cast[p]
        ld = S.dma(eng_dma,
                   lambda e, k=k, p=p: [e.dma_start(out=stg[p][:, :cols],
                                                    in_=dram_ap[k * tile_rows:(k + 1) * tile_rows, :])],
                   deps=list(deps) + ([war] if war else []), tag=f"{name}_ld{k}")
        cst = S.add("vector",
                    lambda e, k=k, p=p: [e.tensor_copy(dst[:, k * cols:(k + 1) * cols], stg[p][:, :cols])],
                    deps=[ld], tag=f"{name}_cast{k}")
        prev_cast[p] = cst
        ready.append(cst)
    return dst, ready


def ln_stats_steps(S, x_tok_dram, t_tiles, d, stats, deps=(), tag="ln"):
    nc = S.nc
    nt = t_tiles
    stg = [S.sbuf(f"{tag}_xstg0", [128, d // 2], F32)]
    sq = S.sbuf(f"{tag}_sq", [128, d // 2], BF16)
    hcol = S.sbuf(f"{tag}_hcol", [128, 4], F32)
    mcol = S.sbuf(f"{tag}_m", [128, nt], F32)
    qcol = S.sbuf(f"{tag}_q", [128, nt], F32)
    rstd = S.sbuf(f"{tag}_rstd", [128, nt], F32)
    sstd = S.sbuf(f"{tag}_sstd", [128, nt], F32)
    negm = S.sbuf(f"{tag}_negm", [128, nt], F32)
    tmp = S.sbuf(f"{tag}_tmp", [128, 1], F32)
    var = S.sbuf(f"{tag}_var", [128, 1], F32)
    negm_row = S.sbuf(f"{tag}_negm_row", [1, nt * 128], BF16)
    sstd_row = S.sbuf(f"{tag}_sstd_row", [1, nt * 128], BF16)
    rowf = S.sbuf(f"{tag}_rowf", [1, 128], F32)
    stats.update(rstd_c=rstd, negm_row=negm_row, sstd_row=sstd_row, negm_c=negm, sstd_c=sstd)
    fin = []
    prev_use = [None, None]
    prev_sq = None
    prev_rowf = None
    prev_var = None
    prev_sstd = None
    for tt in range(nt):
        p = 0
        hsteps = []
        for hf in range(2):
            ld = S.dma("gpsimd", lambda e, tt=tt, hf=hf: [e.dma_start(out=stg[0][:, :],
                       in_=x_tok_dram[tt * 128:(tt + 1) * 128, hf * (d // 2):(hf + 1) * (d // 2)])],
                       deps=list(deps) + ([prev_use[0]] if prev_use[0] else []), tag=f"{tag}_ld{tt}_{hf}")
            rs_h = S.add("vector", lambda e, hf=hf: [e.reduce_sum(hcol[:, hf:hf + 1], stg[0][:, :], axis=AX)],
                         deps=[ld] + ([prev_use[0]] if prev_use[0] else []), tag=f"{tag}_rsh{tt}_{hf}")
            sqs = S.add("scalar", lambda e, hf=hf: [e.activation(sq[:, :], stg[0][:, :], AF.Square)],
                        deps=[ld, rs_h] + ([prev_sq] if prev_sq else []), tag=f"{tag}_sqh{tt}_{hf}")
            rq_h = S.add("vector", lambda e, hf=hf: [e.reduce_sum(hcol[:, 2 + hf:3 + hf], sq[:, :], axis=AX)],
                         deps=[sqs], tag=f"{tag}_rqh{tt}_{hf}")
            prev_sq = rq_h
            prev_use[0] = rq_h
            hsteps.append(rq_h)
        rs = S.add("vector", lambda e, tt=tt: [e.tensor_tensor(mcol[:, tt:tt + 1], hcol[:, 0:1], hcol[:, 1:2], ALU.add)],
                   deps=hsteps, tag=f"{tag}_rs{tt}")
        rq = S.add("vector", lambda e, tt=tt: [e.tensor_tensor(qcol[:, tt:tt + 1], hcol[:, 2:3], hcol[:, 3:4], ALU.add)],
                   deps=hsteps, tag=f"{tag}_rq{tt}")
        sa = S.add("vector", lambda e, tt=tt: [e.tensor_scalar(negm[:, tt:tt + 1], mcol[:, tt:tt + 1], -1.0 / d, None, ALU.mult)],
                   deps=[rs], tag=f"{tag}_negm{tt}")
        sb = S.add("vector", lambda e, tt=tt: [e.tensor_scalar(tmp[:, :], negm[:, tt:tt + 1], negm[:, tt:tt + 1], None, ALU.mult)],
                   deps=[sa, prev_var], tag=f"{tag}_m2{tt}")
        st1 = S.add("vector", lambda e, tt=tt: [e.scalar_tensor_tensor(var[:, :], qcol[:, tt:tt + 1], 1.0 / d, tmp[:, :], ALU.mult, ALU.subtract)],
                    deps=[sb, rq, prev_sstd], tag=f"{tag}_var{tt}")
        st1b = S.add("vector", lambda e, tt=tt: [e.tensor_scalar(var[:, :], var[:, :], EPS, None, ALU.add)],
                     deps=[st1], tag=f"{tag}_vep{tt}")
        st2 = S.add("scalar", lambda e, tt=tt: [e.activation(sstd[:, tt:tt + 1], var[:, :], AF.Sqrt)],
                    deps=[st1b], tag=f"{tag}_sstd{tt}")
        st3 = S.add("vector", lambda e, tt=tt: [e.reciprocal(rstd[:, tt:tt + 1], sstd[:, tt:tt + 1])],
                    deps=[st2], tag=f"{tag}_rstd{tt}")
        prev_var = st1
        prev_sstd = st2
        d1 = S.dma("gpsimd", lambda e, tt=tt: [e.dma_start(out=rowf[:, :], in_=negm[:, tt:tt + 1])],
                   deps=[st1] + ([prev_rowf] if prev_rowf else []), tag=f"{tag}_rowm{tt}")
        c1 = S.add("vector", lambda e, tt=tt: [e.tensor_copy(negm_row[:, tt * 128:(tt + 1) * 128], rowf[:, :])],
                   deps=[d1], tag=f"{tag}_rowmc{tt}")
        d2 = S.dma("gpsimd", lambda e, tt=tt: [e.dma_start(out=rowf[:, :], in_=sstd[:, tt:tt + 1])],
                   deps=[st2, c1], tag=f"{tag}_rows{tt}")
        c2 = S.add("vector", lambda e, tt=tt: [e.tensor_copy(sstd_row[:, tt * 128:(tt + 1) * 128], rowf[:, :])],
                   deps=[d2], tag=f"{tag}_rowsc{tt}")
        prev_rowf = c2
        fin.append((st3, c1, c2))
    return fin


def small_cast(S, name, dram_ap, shape, deps=(), dtype=BF16):
    nh = 2 if shape[1] >= 2048 else 1
    stg = S.sbuf(name + "_stg", [shape[0], shape[1] // nh], F32)
    dst = S.sbuf(name, shape, dtype)
    cst = None
    for hf in range(nh):
        w = shape[1] // nh
        ld = S.dma("sync", lambda e, hf=hf, w=w: [e.dma_start(out=stg[:, :], in_=dram_ap[:, hf * w:(hf + 1) * w])],
                   deps=list(deps) + ([cst] if cst else []), tag=f"{name}_ld{hf}")
        cst = S.add("vector", lambda e, hf=hf, w=w: [e.tensor_copy(dst[:, hf * w:(hf + 1) * w], stg[:, :])],
                    deps=[ld], tag=f"{name}_cast{hf}")
    return dst, cst


def emit_l1ffn_core(S, io, pre=None):
    nc = S.nc
    nt = T // 128

    xT_bf, xT_rdy = io["xT_pre"] if "xT_pre" in io else cast_load(S, "xTs", io["xT"], D, T)
    WrT_bf, wr_rdy = cast_load(S, "WrTs", io["WrT"], D, 8)
    AT_bf, at_rdy = cast_load(S, "ATs", io["AT"], D, 128)
    Ball_bf, ball_rdy = cast_load(S, "Balls", io["Ball"], 128, D)
    br_bf, br_rdy = small_cast(S, "brs", io["br"], [1, 8])
    E8_bf, e8_rdy = small_cast(S, "E8s", io["E8"], [8, 128])
    ident = S.sbuf("idents", [128, 128], BF16)
    id_rdy = S.dma("sync", lambda e: [e.dma_start(out=ident[:, :], in_=io["ident"])], tag="id_ld")

    stats = {}
    ln_fin = ln_stats_steps(S, io["x_tok"], nt, D, stats, tag="ln4")
    rstd_c = stats["rstd_c"]; negm_row = stats["negm_row"]; sstd_row = stats["sstd_row"]

    ones_bf = S.sbuf("ones_bf", [128, 1], BF16)
    ones_rdy = S.add("vector", lambda e: [e.memset(ones_bf[:, :], 1.0)], tag="ones")

    ps_cs = S.psum("ps_cs", [1, 136], F32)
    cs_sb = S.sbuf("cs_sb", [1, 136], BF16)

    def mm_cs(e):
        last = None
        for kt in range(32):
            last = e.matmul(ps_cs[:, 0:8], ones_bf[:, :], WrT_bf[:, kt * 8:(kt + 1) * 8],
                            start=(kt == 0), stop=(kt == 31))
        for kt in range(32):
            last = e.matmul(ps_cs[:, 8:136], ones_bf[:, :], AT_bf[:, kt * 128:(kt + 1) * 128],
                            start=(kt == 0), stop=(kt == 31))
        return [last]
    cs_mm = S.add("tensor", mm_cs, deps=wr_rdy + at_rdy + [ones_rdy], tag="cs_mm")
    cs_ev = S.add("vector", lambda e: [e.tensor_copy(cs_sb[:, :], ps_cs[:, :])], deps=[cs_mm], tag="cs_ev")

    ps_log = S.psum("ps_log", [128, 8], F32)
    ps_low = S.psum("ps_low", [128, 128], F32)
    ps_str = S.psum("ps_str", [8, 128], BF16)
    ps_sexp = S.psum("ps_sexp", [128, 128], F32)
    ps_ltr = S.psum("ps_ltr", [128, 128], BF16)
    ps_out = [S.psum("ps_out0", [128, 512], F32), S.psum("ps_out1", [128, 512], F32)]

    logit_sb = S.sbuf("logit_sb", [128, 8], F32)
    mx = S.sbuf("mx", [128, 1], F32)
    esb = S.sbuf("esb", [128, 8], F32)
    ssum = S.sbuf("ssum", [128, 1], F32)
    rr = S.sbuf("rr", [128, 1], F32)
    scores_bf = S.sbuf("scores_bf", [128, 8], BF16)
    scoresT_bf = S.sbuf("scoresT_bf", [8, 128], BF16)
    low_sb = S.sbuf("low_sb", [128, 128], F32)
    lowp_bf = S.sbuf("lowp_bf", [128, 128], BF16)
    lowpT_bf = S.sbuf("lowpT_bf", [128, 128], BF16)
    so_sb = S.sbuf("so_sb", [128, D], F32)
    sq2 = S.sbuf("sq2", [128, D], F32)
    m2 = S.sbuf("m2", [128, 1], F32)
    q2 = S.sbuf("q2", [128, 1], F32)
    v2 = S.sbuf("v2", [128, 1], F32)
    sd2 = S.sbuf("sd2", [128, 1], F32)
    r2 = S.sbuf("r2", [128, 1], F32)
    xres = S.sbuf("xres", [128, D], F32)

    prev = {}
    out_steps = []
    for tt in range(nt):
        t0, t1 = tt * 128, (tt + 1) * 128
        fin_rstd, fin_negm, fin_sstd = ln_fin[tt]

        def mm_log(e, tt=tt, t0=t0):
            last = None
            for kt in range(32):
                e.matmul(ps_log[:, :], xT_bf[:, kt * T + t0:kt * T + t0 + 128],
                         WrT_bf[:, kt * 8:(kt + 1) * 8], start=(kt == 0), stop=False)
            e.matmul(ps_log[:, :], negm_row[:, t0:t0 + 128], cs_sb[:, 0:8], start=False, stop=False)
            last = e.matmul(ps_log[:, :], sstd_row[:, t0:t0 + 128], br_bf[:, :], start=False, stop=True)
            return [last]
        dlog = [cs_ev, br_rdy, fin_negm, fin_sstd] + xT_rdy + wr_rdy + ([prev["log_ev"]] if "log_ev" in prev else [])
        mmlog = S.add("tensor", mm_log, deps=dlog, tag=f"mmlog{tt}")

        def mm_low(e, tt=tt, t0=t0):
            for kt in range(32):
                e.matmul(ps_low[:, :], xT_bf[:, kt * T + t0:kt * T + t0 + 128],
                         AT_bf[:, kt * 128:(kt + 1) * 128], start=(kt == 0), stop=False)
            return [e.matmul(ps_low[:, :], negm_row[:, t0:t0 + 128], cs_sb[:, 8:136], start=False, stop=True)]
        mmlow = S.add("tensor", mm_low, deps=dlog + at_rdy + ([prev["low_ev"]] if "low_ev" in prev else []), tag=f"mmlow{tt}")

        log_ev = S.add("vector", lambda e, tt=tt: [e.tensor_scalar(logit_sb[:, :], ps_log[:, :], rstd_c[:, tt:tt + 1], None, ALU.mult)],
                       deps=[mmlog, fin_rstd] + ([prev["sm_done"]] if "sm_done" in prev else []), tag=f"logev{tt}")
        prev["log_ev"] = log_ev
        smx = S.add("vector", lambda e: [e.reduce_max(mx[:, :], logit_sb[:, :], axis=AX, negate=True)],
                    deps=[log_ev], tag=f"smx{tt}")
        sex = S.add("scalar", lambda e: [e.activation(esb[:, :], logit_sb[:, :], AF.Exp, bias=mx[:, :])],
                    deps=[smx], tag=f"sex{tt}")
        ssm = S.add("vector", lambda e: [e.reduce_sum(ssum[:, :], esb[:, :], axis=AX)], deps=[sex], tag=f"ssm{tt}")
        src = S.add("vector", lambda e: [e.reciprocal(rr[:, :], ssum[:, :])], deps=[ssm], tag=f"src{tt}")
        ssc = S.add("vector", lambda e: [e.tensor_scalar(scores_bf[:, :], esb[:, :], rr[:, :], None, ALU.mult)],
                    deps=[src], tag=f"ssc{tt}")
        prev["sm_done"] = ssc
        str_ = S.add("tensor", lambda e: [e.transpose(ps_str[:, :], scores_bf[:, :], ident[:, :])],
                     deps=[ssc, id_rdy] + ([prev["strev"]] if "strev" in prev else []), tag=f"str{tt}")
        strev = S.add("vector", lambda e: [e.tensor_copy(scoresT_bf[:, :], ps_str[:, :])], deps=[str_], tag=f"strev{tt}")
        prev["strev"] = strev
        sexp = S.add("tensor", lambda e: [e.matmul(ps_sexp[:, :], scoresT_bf[:, :], E8_bf[:, :], start=True, stop=True)],
                     deps=[strev, e8_rdy] + ([prev["lowp"]] if "lowp" in prev else []), tag=f"sexp{tt}")
        low_ev = S.add("vector", lambda e, tt=tt: [e.tensor_scalar(low_sb[:, :], ps_low[:, :], rstd_c[:, tt:tt + 1], None, ALU.mult)],
                       deps=[mmlow, fin_rstd] + ([prev["lowp"]] if "lowp" in prev else []), tag=f"lowev{tt}")
        prev["low_ev"] = low_ev
        lowp = S.add("vector", lambda e: [e.tensor_tensor(lowp_bf[:, :], low_sb[:, :], ps_sexp[:, :], ALU.mult)],
                     deps=[low_ev, sexp] + ([prev["ltr"]] if "ltr" in prev else []), tag=f"lowp{tt}")
        prev["lowp"] = lowp
        ltr = S.add("tensor", lambda e: [e.transpose(ps_ltr[:, :], lowp_bf[:, :], ident[:, :])],
                    deps=[lowp] + ([prev["ltrev"]] if "ltrev" in prev else []), tag=f"ltr{tt}")
        prev["ltr"] = ltr
        ltrev = S.add("vector", lambda e: [e.tensor_copy(lowpT_bf[:, :], ps_ltr[:, :])], deps=[ltr], tag=f"ltrev{tt}")
        prev["ltrev"] = ltrev

        so_evs = []
        for ds in range(8):
            pb = ds % 2
            mmout = S.add("tensor",
                          lambda e, ds=ds, pb=pb: [e.matmul(ps_out[pb][:, :], lowpT_bf[:, :], Ball_bf[:, ds * 512:(ds + 1) * 512], start=True, stop=True)],
                          deps=[ltrev] + ball_rdy + ([prev[f"so_ev{pb}"]] if f"so_ev{pb}" in prev else []), tag=f"mmout{tt}_{ds}")
            so_ev = S.add("vector", lambda e, ds=ds, pb=pb: [e.tensor_copy(so_sb[:, ds * 512:(ds + 1) * 512], ps_out[pb][:, :])],
                          deps=[mmout], tag=f"soev{tt}_{ds}")
            prev[f"so_ev{pb}"] = so_ev
            so_evs.append(so_ev)

        rs2 = S.add("vector", lambda e: [e.reduce_sum(m2[:, :], so_sb[:, :], axis=AX)], deps=so_evs, tag=f"rs2_{tt}")
        sq2s = S.add("scalar", lambda e: [e.activation(sq2[:, :], so_sb[:, :], AF.Square)],
                     deps=so_evs + ([prev["rq2"]] if "rq2" in prev else []), tag=f"sq2_{tt}")
        rq2 = S.add("vector", lambda e: [e.reduce_sum(q2[:, :], sq2[:, :], axis=AX)], deps=[sq2s], tag=f"rq2_{tt}")
        prev["rq2"] = rq2
        n2 = S.add("vector", lambda e: [e.tensor_scalar(m2[:, :], m2[:, :], 1.0 / D, None, ALU.mult)], deps=[rs2], tag=f"n2_{tt}")
        v2a = S.add("vector", lambda e: [e.tensor_scalar(v2[:, :], m2[:, :], m2[:, :], None, ALU.mult)], deps=[n2], tag=f"v2a_{tt}")
        v2b = S.add("vector", lambda e: [e.scalar_tensor_tensor(v2[:, :], q2[:, :], 1.0 / D, v2[:, :], ALU.mult, ALU.subtract)],
                    deps=[v2a, rq2], tag=f"v2b_{tt}")
        v2c = S.add("vector", lambda e: [e.tensor_scalar(v2[:, :], v2[:, :], EPS, None, ALU.add)], deps=[v2b], tag=f"v2c_{tt}")
        sd2s = S.add("scalar", lambda e: [e.activation(sd2[:, :], v2[:, :], AF.Sqrt)], deps=[v2c], tag=f"sd2_{tt}")
        r2s = S.add("vector", lambda e: [e.reciprocal(r2[:, :], sd2[:, :])], deps=[sd2s], tag=f"r2_{tt}")
        xld = S.dma("sync", lambda e, t0=t0: [e.dma_start(out=xres[:, :], in_=io["x_tok"][t0:t0 + 128, :])],
                    deps=([prev["oadd"]] if "oadd" in prev else []), tag=f"xld{tt}")
        oln = S.add("vector", lambda e: [e.tensor_scalar(so_sb[:, :], so_sb[:, :], m2[:, :], r2[:, :], ALU.subtract, ALU.mult)],
                    deps=[r2s] + so_evs, tag=f"oln{tt}")
        oadd = S.add("vector", lambda e: [e.tensor_tensor(xres[:, :], xres[:, :], so_sb[:, :], ALU.add)],
                     deps=[oln, xld] + ([prev["ost"]] if "ost" in prev else []), tag=f"oadd{tt}")
        prev["oadd"] = oadd
        ost = S.dma("sync", lambda e, t0=t0: [e.dma_start(out=io["xout"][t0:t0 + 128, :], in_=xres[:, :])],
                    deps=[oadd], tag=f"ost{tt}")
        prev["ost"] = ost
        out_steps.append(ost)
    return out_steps


def build_l1ffn_nc():
    nc = bass.Bass("TRN2", target_bir_lowering=False)
    io = {
        "xT": nc.dram_tensor("xT", [D, T], F32, kind="ExternalInput").ap(),
        "x_tok": nc.dram_tensor("x_tok", [T, D], F32, kind="ExternalInput").ap(),
        "WrT": nc.dram_tensor("WrT", [D, 8], F32, kind="ExternalInput").ap(),
        "br": nc.dram_tensor("br", [1, 8], F32, kind="ExternalInput").ap(),
        "AT": nc.dram_tensor("AT", [D, 128], F32, kind="ExternalInput").ap(),
        "Ball": nc.dram_tensor("Ball", [128, D], F32, kind="ExternalInput").ap(),
        "E8": nc.dram_tensor("E8", [8, 128], F32, kind="ExternalInput").ap(),
        "ident": nc.dram_tensor("ident", [128, 128], BF16, kind="ExternalInput").ap(),
        "xout": nc.dram_tensor("xout", [T, D], F32, kind="ExternalOutput").ap(),
    }
    S = Sched(nc)
    outs = emit_l1ffn_core(S, io)
    S.final_deps = outs
    S.final_engine = "gpsimd"
    S.emit()
    return nc


def stage_l1ffn_inputs(x_full, Wr, br, A, B, ml_bf16):
    E8 = np.zeros((8, 128), np.float32)
    for h in range(8):
        E8[h, 16 * h:16 * (h + 1)] = 16.0
    ident = np.eye(128).astype(ml_bf16)
    WrT = np.ascontiguousarray(Wr.T)
    AT = np.ascontiguousarray(A.reshape(128, D).T)
    Ball = np.ascontiguousarray(B.transpose(0, 2, 1).reshape(128, D))
    br_row = br.reshape(1, 8).astype(np.float32)
    in_maps = []
    for c in range(NCORES):
        xs = x_full[c * T:(c + 1) * T, :]
        in_maps.append({
            "xT": np.ascontiguousarray(xs.T),
            "x_tok": np.ascontiguousarray(xs),
            "WrT": WrT, "br": br_row, "AT": AT, "Ball": Ball,
            "E8": E8, "ident": ident,
        })
    return in_maps


def rows_to_cols(S, src_col_tensor, nt, name, deps_per_tile, tag):
    row = S.sbuf(name, [1, nt * 128], F32)
    steps = []
    for tt in range(nt):
        d = S.dma("gpsimd", lambda e, tt=tt: [e.dma_start(out=row[:, tt * 128:(tt + 1) * 128],
                                                          in_=src_col_tensor[:, tt:tt + 1])],
                  deps=deps_per_tile[tt], tag=f"{tag}{tt}")
        steps.append(d)
    return row, steps


def emit_l1attn_qkv(S, io):
    nt = T // 128
    DQ = 1024
    xT_bf, xT_rdy = io["xT_pre"] if "xT_pre" in io else cast_load(S, "xTs", io["xT"], D, T)
    bq_bf, bq_rdy = small_cast(S, "bqs", io["bqkv"], [1, 3 * DQ])
    stats = {}
    ln_fin = ln_stats_steps(S, io["x_tok"], nt, D, stats, tag="ln2")
    rstd_c = stats["rstd_c"]; negm_row = stats["negm_row"]; sstd_row = stats["sstd_row"]
    rstd_row, rr_steps = rows_to_cols(S, rstd_c, nt, "rstd_row2", [[f[0]] for f in ln_fin], "rr2_")
    rstd_bc = S.sbuf("rstd_bc", [128, T], F32)
    rbd = S.nc.dram_tensor("rstd_scr", [1, T], F32, kind="Internal").ap()
    rb0 = S.dma("sync", lambda e: [e.dma_start(out=rbd, in_=rstd_row[:, :])], deps=rr_steps, tag="rstd_st")
    rb = S.dma("sync", lambda e: [e.dma_start(out=rstd_bc[:, :], in_=rbd.partition_broadcast(128))],
               deps=[rb0], tag="rstd_bc")
    ones_bf = S.sbuf("ones_bf", [128, 1], BF16)
    ones_rdy = S.add("vector", lambda e: [e.memset(ones_bf[:, :], 1.0)], tag="ones")
    ln_all = [x for f in ln_fin for x in f]

    wblk = [S.sbuf(f"wblk{i}", [128, 32 * 128], BF16) for i in range(2)]
    wstg = [S.sbuf(f"wstg{i}", [128, 128], F32) for i in range(2)]
    ps_cs = S.psum("ps_cs", [1, 128], F32)
    cs_sb = S.sbuf("cs_sbq", [1, 128], BF16)
    ps_mm = [S.psum(f"ps_mmq{i}", [128, 512], F32) for i in range(2)]
    ob = [S.sbuf(f"oqk{i}", [128, 512], BF16) for i in range(2)]

    outs = []
    last_wuse = [None, None]
    last_stg = [None, None]
    prev_cs = None
    prev_ps = [None, None]
    prev_ob = [None, None]
    for cb in range(24):
        wb = cb % 2
        casts = []
        for kt in range(32):
            sp = kt % 2
            ld = S.dma(("sync", "scalar")[kt % 2], lambda e, cb=cb, kt=kt, sp=sp: [e.dma_start(out=wstg[sp][:, :],
                        in_=io["Wqkv"][kt * 128:(kt + 1) * 128, cb * 128:(cb + 1) * 128])],
                       deps=[last_stg[sp]] + ([last_wuse[wb]] if kt == 0 else []), tag=f"wld{cb}_{kt}")
            cst = S.add("vector", lambda e, kt=kt, sp=sp, wb=wb: [e.tensor_copy(
                        wblk[wb][:, kt * 128:(kt + 1) * 128], wstg[sp][:, :])],
                        deps=[ld] + ([last_wuse[wb]] if kt == 0 else []), tag=f"wcst{cb}_{kt}")
            last_stg[sp] = cst
            casts.append(cst)
        csm = S.add("tensor", lambda e, wb=wb: [[e.matmul(ps_cs[:, :], ones_bf[:, :],
                    wblk[wb][:, kt * 128:(kt + 1) * 128], start=(kt == 0), stop=(kt == 31))
                    for kt in range(32)][-1]],
                    deps=casts + [ones_rdy] + ([prev_cs] if prev_cs else []), tag=f"csm{cb}")
        cse = S.add("vector", lambda e: [e.tensor_copy(cs_sb[:, :], ps_cs[:, :])], deps=[csm], tag=f"cse{cb}")
        prev_cs = cse
        uses = []
        if cb < 16:
            for th in range(2):
                pb = th
                def mmqk(e, cb=cb, th=th, pb=pb, wb=wb):
                    for kt in range(32):
                        e.matmul(ps_mm[pb][:, :], wblk[wb][:, kt * 128:(kt + 1) * 128],
                                 xT_bf[:, kt * T + th * 512:kt * T + (th + 1) * 512],
                                 start=(kt == 0), stop=False)
                    e.matmul(ps_mm[pb][:, :], cs_sb[:, :], negm_row[:, th * 512:(th + 1) * 512],
                             start=False, stop=False)
                    return [e.matmul(ps_mm[pb][:, :], bq_bf[:, cb * 128:(cb + 1) * 128],
                                     sstd_row[:, th * 512:(th + 1) * 512], start=False, stop=True)]
                mm = S.add("tensor", mmqk, deps=casts + xT_rdy + [cse, bq_rdy] + ln_all
                           + ([prev_ps[pb]] if prev_ps[pb] else []), tag=f"mmqk{cb}_{th}")
                ev = S.add("vector", lambda e, th=th, pb=pb: [e.tensor_tensor(ob[pb][:, :], ps_mm[pb][:, :],
                           rstd_bc[:, th * 512:(th + 1) * 512], ALU.mult)],
                           deps=[mm, rb] + ([prev_ob[pb]] if prev_ob[pb] else []), tag=f"evqk{cb}_{th}")
                prev_ps[pb] = ev
                dst = io["qT"] if cb < 8 else io["kT"]
                r0 = (cb % 8) * 128
                st = S.dma("gpsimd", lambda e, dst=dst, r0=r0, th=th, pb=pb: [e.dma_start(
                            out=dst[r0:r0 + 128, th * 512:(th + 1) * 512], in_=ob[pb][:, :])],
                           deps=[ev], tag=f"stqk{cb}_{th}")
                prev_ob[pb] = st
                uses.append(mm)
                outs.append(st)
        else:
            dv0 = (cb - 16) * 128
            for tt in range(nt):
                pb = tt % 2
                def mmv(e, tt=tt, pb=pb, wb=wb, dv0=dv0):
                    for kt in range(32):
                        e.matmul(ps_mm[pb][:, 0:128], xT_bf[:, kt * T + tt * 128:kt * T + (tt + 1) * 128],
                                 wblk[wb][:, kt * 128:(kt + 1) * 128], start=(kt == 0), stop=False)
                    e.matmul(ps_mm[pb][:, 0:128], negm_row[:, tt * 128:(tt + 1) * 128], cs_sb[:, :],
                             start=False, stop=False)
                    return [e.matmul(ps_mm[pb][:, 0:128], sstd_row[:, tt * 128:(tt + 1) * 128],
                                     bq_bf[:, 2 * DQ + dv0:2 * DQ + dv0 + 128], start=False, stop=True)]
                mm = S.add("tensor", mmv, deps=casts + xT_rdy + [cse, bq_rdy] + ln_all
                           + ([prev_ps[pb]] if prev_ps[pb] else []), tag=f"mmv{cb}_{tt}")
                ev = S.add("vector", lambda e, tt=tt, pb=pb: [e.tensor_scalar(ob[pb][:, 0:128],
                           ps_mm[pb][:, 0:128], rstd_c[:, tt:tt + 1], None, ALU.mult)],
                           deps=[mm, ln_fin[tt][0]] + ([prev_ob[pb]] if prev_ob[pb] else []), tag=f"evv{cb}_{tt}")
                prev_ps[pb] = ev
                st = S.dma("gpsimd", lambda e, tt=tt, pb=pb, dv0=dv0: [e.dma_start(
                            out=io["v"][tt * 128:(tt + 1) * 128, dv0:dv0 + 128], in_=ob[pb][:, 0:128])],
                           deps=[ev], tag=f"stv{cb}_{tt}")
                prev_ob[pb] = st
                uses.append(mm)
                outs.append(st)
        last_wuse[wb] = uses[-1]
    return outs


def build_l1attn_qkv_nc():
    nc = bass.Bass("TRN2", target_bir_lowering=False)
    DQ = 1024
    io = {
        "xT": nc.dram_tensor("xT", [D, T], F32, kind="ExternalInput").ap(),
        "x_tok": nc.dram_tensor("x_tok", [T, D], F32, kind="ExternalInput").ap(),
        "Wqkv": nc.dram_tensor("Wqkv", [D, 3 * DQ], F32, kind="ExternalInput").ap(),
        "bqkv": nc.dram_tensor("bqkv", [1, 3 * DQ], F32, kind="ExternalInput").ap(),
        "qT": nc.dram_tensor("qT", [DQ, T], BF16, kind="ExternalOutput").ap(),
        "kT": nc.dram_tensor("kT", [DQ, T], BF16, kind="ExternalOutput").ap(),
        "v": nc.dram_tensor("v", [T, DQ], BF16, kind="ExternalOutput").ap(),
    }
    S = Sched(nc)
    outs = emit_l1attn_qkv(S, io)
    S.final_deps = outs
    S.final_engine = "gpsimd"
    S.emit()
    return nc

def emit_l1attn_flash(S, io):
    nt = T // 128
    DQ = 1024
    S8 = 8192
    NKT = S8 // 128
    kT_sb = S.sbuf("kT_sb", [128, 8 * S8], BF16)
    qT_sb = S.sbuf("qT_sb", [128, 8 * 256], BF16)
    kld = []
    if "kv_ag" in io:
        for dk in range(8):
            kld.append(S.dma("sync", lambda e, dk=dk: [e.dma_start(
                        out=kT_sb[:, dk * S8 + r * 1024:dk * S8 + (r + 1) * 1024],
                        in_=io["kv_ag"][2048 * r + dk * 128:2048 * r + (dk + 1) * 128, :])
                        for r in range(8)],
                        n=8, tag=f"kld{dk}"))
    else:
        for dk in range(8):
            kld.append(S.dma("sync", lambda e, dk=dk: [e.dma_start(out=kT_sb[:, dk * S8:(dk + 1) * S8],
                        in_=io["kTf"][dk * 128:(dk + 1) * 128, :])], tag=f"kld{dk}"))
    bo_bf, bo_rdy = small_cast(S, "bos", io["bo"], [1, D])
    v_sb = [S.sbuf(f"vs{i}", [128, DQ], BF16) for i in range(2)]
    e_sb = [S.sbuf(f"es{i}", [128, 256], BF16) for i in range(2)]
    ps_sc = S.psum("ps_sc", [128, 256], F32)
    ps_se = S.psum("ps_se", [1, 256], F32)
    ps_oc = [S.psum(f"ps_oc{i}", [128, 512], F32) for i in range(4)]
    ps_z = [S.psum(f"ps_z{i}", [128, 512], F32) for i in range(2)]
    oc_bf = S.sbuf("oc_bf", [128, 8 * 256], BF16)
    se_f = S.sbuf("se_f", [1, 256], F32)
    se_bf = S.sbuf("se_bf", [1, 256], BF16)
    rec_f = S.sbuf("rec_f", [1, 256], F32)
    rec_c = S.sbuf("rec_c", [128, 2], F32)
    wo_stg = [S.sbuf(f"wo_stg{i}", [128, 512], F32) for i in range(2)]
    wo_bf = [S.sbuf(f"wo_bf{i}", [128, 512], BF16) for i in range(2)]
    z_sb = S.sbuf("z_sb", [128, 2 * D], BF16)
    sq3 = S.sbuf("sq3", [128, D], BF16)
    xres3 = S.sbuf("xres3", [128, D // 2], F32)
    m3 = S.sbuf("m3", [128, 1], F32); q3 = S.sbuf("q3", [128, 1], F32)
    v3 = S.sbuf("v3", [128, 1], F32); sd3 = S.sbuf("sd3", [128, 1], F32)
    r3 = S.sbuf("r3", [128, 1], F32)
    ones_bf = S.sbuf("ones_bff", [128, 1], BF16)
    ones_rdy = S.add("vector", lambda e: [e.memset(ones_bf[:, :], 1.0)], tag="ones")

    outs = []
    prev = {}
    for qt in range(4):
        q0 = qt * 256
        qlds = []
        for dk in range(8):
            qlds.append(S.dma("sync", lambda e, dk=dk, q0=q0: [e.dma_start(out=qT_sb[:, dk * 256:(dk + 1) * 256],
                        in_=io["qT"][dk * 128:(dk + 1) * 128, q0:q0 + 256])],
                        deps=[prev.get("q_use")], tag=f"qld{qt}_{dk}"))
        last_mmB = None
        for kt in range(NKT):
            vp = kt % 2
            ep = kt % 2
            if "kv_ag" in io:
                vsrc = io["kv_ag"][2048 * (kt // 8) + 1024 + (kt % 8) * 128:2048 * (kt // 8) + 1024 + (kt % 8) * 128 + 128, :]
            else:
                vsrc = io["vf"][kt * 128:(kt + 1) * 128, :]
            vld = S.dma("scalar", lambda e, vsrc=vsrc, vp=vp: [e.dma_start(out=v_sb[vp][:, :], in_=vsrc)],
                        deps=[prev.get(("vuse", vp))], tag=f"vld{qt}_{kt}")
            def mmA(e, kt=kt, q0=q0):
                last = None
                for dk in range(8):
                    last = e.matmul(ps_sc[:, :], kT_sb[:, dk * S8 + kt * 128:dk * S8 + (kt + 1) * 128],
                                    qT_sb[:, dk * 256:(dk + 1) * 256], start=(dk == 0), stop=(dk == 7))
                return [last]
            mA = S.add("tensor", mmA,
                       deps=kld + qlds + [prev.get(("exp", 0)), prev.get(("exp", 1))], tag=f"mmA{qt}_{kt}")
            ex = S.add("scalar", lambda e, ep=ep: [e.activation(e_sb[ep][:, :], ps_sc[:, :], AF.Exp, scale=0.03125)],
                       deps=[mA, prev.get(("euse", ep))], tag=f"exp{qt}_{kt}")
            prev[("exp", ep)] = ex
            def mmB(e, kt=kt, ep=ep, vp=vp):
                e.matmul(ps_se[:, :], ones_bf[:, :], e_sb[ep][:, :], start=(kt == 0), stop=(kt == NKT - 1))
                last = None
                for dvt in range(8):
                    last = e.matmul(ps_oc[dvt // 2][:, (dvt % 2) * 256:(dvt % 2) * 256 + 256],
                                    v_sb[vp][:, dvt * 128:(dvt + 1) * 128], e_sb[ep][:, :],
                                    start=(kt == 0), stop=(kt == NKT - 1), skip_group_check=True)
                return [last]
            extra = (prev.get("oc_evs", []) + [prev.get("see")]) if kt == 0 else []
            mB = S.add("tensor", mmB, deps=[ex, vld, ones_rdy] + extra, tag=f"mmB{qt}_{kt}")
            prev[("vuse", vp)] = mB
            prev[("euse", ep)] = mB
            last_mmB = mB
        oc_evs = []
        for dvt in range(8):
            ev = S.add("vector", lambda e, dvt=dvt: [e.tensor_copy(oc_bf[:, dvt * 256:(dvt + 1) * 256],
                        ps_oc[dvt // 2][:, (dvt % 2) * 256:(dvt % 2) * 256 + 256])],
                       deps=[last_mmB] + ([prev.get("z_last")] if dvt == 0 else []), tag=f"ocev{qt}_{dvt}")
            oc_evs.append(ev)
        prev["oc_evs"] = oc_evs
        see = S.add("vector", lambda e: [e.tensor_copy(se_f[:, :], ps_se[:, :])],
                    deps=[last_mmB, prev.get("seb")], tag=f"see{qt}")
        prev["see"] = see
        seb = S.add("vector", lambda e: [e.tensor_copy(se_bf[:, :], se_f[:, :])], deps=[see], tag=f"seb{qt}")
        prev["seb"] = seb
        rec = S.add("vector", lambda e: [e.reciprocal(rec_f[:, :], se_f[:, :])],
                    deps=[see, prev.get("rc1")], tag=f"rec{qt}")
        rc = []
        for t2 in range(2):
            rc.append(S.dma("gpsimd", lambda e, t2=t2: [e.dma_start(out=rec_c[:, t2:t2 + 1],
                        in_=rec_f[:, t2 * 128:(t2 + 1) * 128])],
                        deps=[rec, prev.get("rc_use")], tag=f"rc{qt}_{t2}"))
        prev["rc1"] = rc[1]
        zevs = {0: [], 1: []}
        for ds in range(8):
            mms = {0: None, 1: None}
            for dvt in range(8):
                sp = dvt % 2
                wld = S.dma("sync", lambda e, ds=ds, dvt=dvt, sp=sp: [e.dma_start(out=wo_stg[sp][:, :],
                            in_=io["Wo"][dvt * 128:(dvt + 1) * 128, ds * 512:(ds + 1) * 512])],
                            deps=[prev.get(("wo_stg", sp))], tag=f"wold{qt}_{ds}_{dvt}")
                wcs = S.add("vector", lambda e, sp=sp: [e.tensor_copy(wo_bf[sp][:, :], wo_stg[sp][:, :])],
                            deps=[wld, prev.get(("wo_use", sp))], tag=f"wocs{qt}_{ds}_{dvt}")
                prev[("wo_stg", sp)] = wcs
                for tt2 in range(2):
                    mm = S.add("tensor", lambda e, dvt=dvt, tt2=tt2, sp=sp: [e.matmul(ps_z[tt2][:, :],
                                oc_bf[:, dvt * 256 + tt2 * 128:dvt * 256 + tt2 * 128 + 128], wo_bf[sp][:, :],
                                start=(dvt == 0), stop=False)],
                               deps=[wcs] + oc_evs + ([prev.get(("z_ev", tt2))] if dvt == 0 else [mms[tt2]]),
                               tag=f"mmz{qt}_{ds}_{dvt}_{tt2}")
                    mms[tt2] = mm
                prev[("wo_use", sp)] = mms[1]
            for tt2 in range(2):
                mmb = S.add("tensor", lambda e, tt2=tt2, ds=ds: [e.matmul(ps_z[tt2][:, :],
                            se_bf[:, tt2 * 128:(tt2 + 1) * 128], bo_bf[:, ds * 512:(ds + 1) * 512],
                            start=False, stop=True)],
                            deps=[mms[tt2], seb, bo_rdy], tag=f"mmzb{qt}_{ds}_{tt2}")
                zev = S.add("vector", lambda e, ds=ds, tt2=tt2: [e.tensor_scalar(
                            z_sb[:, tt2 * D + ds * 512:tt2 * D + (ds + 1) * 512],
                            ps_z[tt2][:, :], rec_c[:, tt2:tt2 + 1], None, ALU.mult)],
                            deps=[mmb] + rc + ([prev.get(("z_sb_use", tt2))] if ds == 0 else []),
                            tag=f"zev{qt}_{ds}_{tt2}")
                prev[("z_ev", tt2)] = zev
                zevs[tt2].append(zev)
        prev["rc_use"] = zevs[1][-1]
        prev["q_use"] = last_mmB
        for tt2 in range(2):
            tglob = qt * 2 + tt2
            zs = zevs[tt2]
            rs3 = S.add("vector", lambda e, tt2=tt2: [e.reduce_sum(m3[:, :], z_sb[:, tt2 * D:(tt2 + 1) * D], axis=AX)],
                        deps=zs + [prev.get("n3")], tag=f"rs3_{tglob}")
            sq3s = S.add("scalar", lambda e, tt2=tt2: [e.activation(sq3[:, :], z_sb[:, tt2 * D:(tt2 + 1) * D], AF.Square)],
                         deps=zs + [prev.get("rq3")], tag=f"sq3_{tglob}")
            rq3 = S.add("vector", lambda e: [e.reduce_sum(q3[:, :], sq3[:, :], axis=AX)], deps=[sq3s], tag=f"rq3_{tglob}")
            prev["rq3"] = rq3
            n3 = S.add("vector", lambda e: [e.tensor_scalar(m3[:, :], m3[:, :], 1.0 / D, None, ALU.mult)],
                       deps=[rs3], tag=f"n3_{tglob}")
            v3a = S.add("vector", lambda e: [e.tensor_scalar(v3[:, :], m3[:, :], m3[:, :], None, ALU.mult)],
                        deps=[n3, prev.get("v3c")], tag=f"v3a_{tglob}")
            v3b = S.add("vector", lambda e: [e.scalar_tensor_tensor(v3[:, :], q3[:, :], 1.0 / D, v3[:, :], ALU.mult, ALU.subtract)],
                        deps=[v3a, rq3], tag=f"v3b_{tglob}")
            v3c = S.add("vector", lambda e: [e.tensor_scalar(v3[:, :], v3[:, :], EPS, None, ALU.add)],
                        deps=[v3b, prev.get("sd3s")], tag=f"v3c_{tglob}")
            prev["v3c"] = v3c
            sd3s = S.add("scalar", lambda e: [e.activation(sd3[:, :], v3[:, :], AF.Sqrt)], deps=[v3c], tag=f"sd3_{tglob}")
            prev["sd3s"] = sd3s
            r3s = S.add("vector", lambda e: [e.reciprocal(r3[:, :], sd3[:, :])], deps=[sd3s, prev.get("oln3")], tag=f"r3_{tglob}")
            oln = S.add("vector", lambda e, tt2=tt2: [e.tensor_scalar(z_sb[:, tt2 * D:(tt2 + 1) * D],
                        z_sb[:, tt2 * D:(tt2 + 1) * D], m3[:, :], r3[:, :], ALU.subtract, ALU.mult)],
                        deps=[r3s, n3] + zs, tag=f"oln3_{tglob}")
            prev["oln3"] = oln
            prev[("z_sb_use", tt2)] = oln
            for hh in range(2):
                H = D // 2
                xld = S.dma("gpsimd", lambda e, tglob=tglob, hh=hh, H=H: [e.dma_start(out=xres3[:, :],
                            in_=io["x_tok"][tglob * 128:(tglob + 1) * 128, hh * H:(hh + 1) * H])],
                            deps=[prev.get("oadd3")], tag=f"xld3_{tglob}_{hh}")
                oadd = S.add("vector", lambda e, tt2=tt2, hh=hh, H=H: [e.tensor_tensor(xres3[:, :], xres3[:, :],
                            z_sb[:, tt2 * D + hh * H:tt2 * D + (hh + 1) * H], ALU.add)],
                            deps=[oln, xld, prev.get("ost3")], tag=f"oadd3_{tglob}_{hh}")
                prev["oadd3"] = oadd
                ost = S.dma("gpsimd", lambda e, tglob=tglob, hh=hh, H=H: [e.dma_start(
                            out=io["xout"][tglob * 128:(tglob + 1) * 128, hh * H:(hh + 1) * H], in_=xres3[:, :])],
                            deps=[oadd], tag=f"ost3_{tglob}_{hh}")
                prev["ost3"] = ost
                outs.append(ost)
            prev["n3"] = n3
    return outs


def build_l1attn_qkv_nc():
    nc = bass.Bass("TRN2", target_bir_lowering=False)
    DQ = 1024
    io = {
        "xT": nc.dram_tensor("xT", [D, T], F32, kind="ExternalInput").ap(),
        "x_tok": nc.dram_tensor("x_tok", [T, D], F32, kind="ExternalInput").ap(),
        "Wqkv": nc.dram_tensor("Wqkv", [D, 3 * DQ], F32, kind="ExternalInput").ap(),
        "bqkv": nc.dram_tensor("bqkv", [1, 3 * DQ], F32, kind="ExternalInput").ap(),
        "qT": nc.dram_tensor("qT", [DQ, T], BF16, kind="ExternalOutput").ap(),
        "kT": nc.dram_tensor("kT", [DQ, T], BF16, kind="ExternalOutput").ap(),
        "v": nc.dram_tensor("v", [T, DQ], BF16, kind="ExternalOutput").ap(),
    }
    S = Sched(nc)
    outs = emit_l1attn_qkv(S, io)
    S.final_deps = outs
    S.final_engine = "gpsimd"
    S.emit()
    return nc


def build_l1attn_flash_nc():
    nc = bass.Bass("TRN2", target_bir_lowering=False)
    DQ = 1024
    S8 = 8192
    io = {
        "qT": nc.dram_tensor("qT", [DQ, T], BF16, kind="ExternalInput").ap(),
        "kTf": nc.dram_tensor("kTf", [DQ, S8], BF16, kind="ExternalInput").ap(),
        "vf": nc.dram_tensor("vf", [S8, DQ], BF16, kind="ExternalInput").ap(),
        "Wo": nc.dram_tensor("Wo", [DQ, D], F32, kind="ExternalInput").ap(),
        "bo": nc.dram_tensor("bo", [1, D], F32, kind="ExternalInput").ap(),
        "x_tok": nc.dram_tensor("x_tok", [T, D], F32, kind="ExternalInput").ap(),
        "xout": nc.dram_tensor("xout", [T, D], F32, kind="ExternalOutput").ap(),
    }
    S = Sched(nc)
    outs = emit_l1attn_flash(S, io)
    S.final_deps = outs
    S.final_engine = "gpsimd"
    S.emit()
    return nc


INTER = 11008
JT_GU = 172
RT_D = 86


def emit_gptq_dequant(S, io, rows_g, cols, nG, name, qb_key, qz_key, sc_key, out_key, col_chunk=None):
    nc = S.nc
    CC = col_chunk or cols
    ncc = cols // CC
    zq_u8 = S.sbuf(name + "_zqu8", [nG, CC // 2], U8)
    zq_n8 = S.sbuf(name + "_zqn8", [nG, CC], U8)
    zq_f = S.sbuf(name + "_zqf", [nG, CC], F32)
    sc_sb = S.sbuf(name + "_sc", [nG, CC], F32)
    zqsc = S.sbuf(name + "_zqsc", [nG, CC], F32)
    scr = nc.dram_tensor(name + "_scr", [nG, cols], F32, kind="Internal").ap()
    zsteps = []
    prevz = {}
    for cc in range(ncc):
        c0 = cc * CC
        l1 = S.dma("sync", lambda e, cc=cc: [e.dma_start(out=zq_u8[:, :], in_=io[qz_key][:, cc * CC // 2:(cc + 1) * CC // 2])],
                   deps=[prevz.get("u")], tag=f"{name}zl{cc}")
        l2 = S.dma("sync", lambda e, c0=c0: [e.dma_start(out=sc_sb[:, :], in_=io[sc_key][:, c0:c0 + CC])],
                   deps=[prevz.get("z2")], tag=f"{name}sl{cc}")
        u1 = S.add("vector", lambda e: [e.tensor_single_scalar(zq_n8[:, 0::2], zq_u8[:, :], 15, ALU.bitwise_and)],
                   deps=[l1, prevz.get("z1")], tag=f"{name}u1_{cc}")
        u2 = S.add("vector", lambda e: [e.tensor_single_scalar(zq_n8[:, 1::2], zq_u8[:, :], 4, ALU.logical_shift_right)],
                   deps=[l1, prevz.get("z1")], tag=f"{name}u2_{cc}")
        prevz["u"] = u2
        z1 = S.add("vector", lambda e: [e.tensor_scalar(zq_f[:, :], zq_n8[:, :], 1.0, None, ALU.add)],
                   deps=[u1, u2, prevz.get("z2")], tag=f"{name}z1_{cc}")
        prevz["z1"] = z1
        z2 = S.add("vector", lambda e: [e.tensor_tensor(zqsc[:, :], zq_f[:, :], sc_sb[:, :], ALU.mult)],
                   deps=[z1, l2, prevz.get("zst")], tag=f"{name}z2_{cc}")
        prevz["z2"] = z2
        zst = S.dma("sync", lambda e, c0=c0: [e.dma_start(out=scr[:, c0:c0 + CC], in_=zqsc[:, :])],
                    deps=[z2], tag=f"{name}zst{cc}")
        prevz["zst"] = zst
        zsteps.append(zst)

    qb = [S.sbuf(name + f"_qb{i}", [64, CC], U8) for i in range(2)]
    sc_bc = [S.sbuf(name + f"_scb{i}", [64, CC], F32) for i in range(2)]
    zq_bc = [S.sbuf(name + f"_zqb{i}", [64, CC], F32) for i in range(2)]
    lo = S.sbuf(name + "_lo", [64, CC], U8)
    hi = S.sbuf(name + "_hi", [64, CC], U8)
    tmpf = S.sbuf(name + "_tf", [64, CC], F32)
    wout = S.sbuf(name + "_w", [64, CC], BF16)
    outs = []
    prev = {}
    for g in range(nG):
        for cc in range(ncc):
            c0 = cc * CC
            p = (g * ncc + cc) % 2
            qld = S.dma("sync", lambda e, g=g, c0=c0, p=p: [e.dma_start(out=qb[p][:, :],
                        in_=io[qb_key][64 * g:64 * (g + 1), c0:c0 + CC])],
                        deps=[prev.get(("qb", p))], tag=f"{name}qld{g}_{cc}")
            sld = S.dma("scalar", lambda e, g=g, c0=c0, p=p: [e.dma_start(out=sc_bc[p][:, :],
                        in_=io[sc_key][g:g + 1, c0:c0 + CC].partition_broadcast(64))],
                        deps=[prev.get(("scb", p))], tag=f"{name}sld{g}_{cc}")
            zld = S.dma("scalar", lambda e, g=g, c0=c0, p=p: [e.dma_start(out=zq_bc[p][:, :],
                        in_=scr[g:g + 1, c0:c0 + CC].partition_broadcast(64))],
                        deps=zsteps + [prev.get(("zqb", p))], tag=f"{name}zld{g}_{cc}")
            s_lo = S.add("vector", lambda e, p=p: [e.tensor_single_scalar(lo[:, :], qb[p][:, :], 15, ALU.bitwise_and)],
                         deps=[qld, prev.get("use_lo")], tag=f"{name}lo{g}_{cc}")
            s_hi = S.add("vector", lambda e, p=p: [e.tensor_single_scalar(hi[:, :], qb[p][:, :], 4, ALU.logical_shift_right)],
                         deps=[qld, prev.get("use_hi")], tag=f"{name}hi{g}_{cc}")
            for pl, nib, nibstep in ((0, lo, s_lo), (1, hi, s_hi)):
                t1 = S.add("vector", lambda e, nib=nib, p=p: [e.scalar_tensor_tensor(tmpf[:, :], nib[:, :], 1.0,
                            sc_bc[p][:, :], ALU.mult, ALU.mult)],
                           deps=[nibstep, sld, prev.get("t2")], tag=f"{name}t1_{g}_{cc}_{pl}")
                t2 = S.add("vector", lambda e, p=p: [e.tensor_tensor(wout[:, :], tmpf[:, :], zq_bc[p][:, :], ALU.subtract)],
                           deps=[t1, zld, prev.get("wst")], tag=f"{name}t2_{g}_{cc}_{pl}")
                prev["t2"] = t2
                wst = S.dma("gpsimd", lambda e, g=g, pl=pl, c0=c0: [e.dma_start(
                            out=io[out_key][128 * g + 64 * pl:128 * g + 64 * (pl + 1), c0:c0 + CC], in_=wout[:, :])],
                            deps=[t2], tag=f"{name}wst{g}_{cc}_{pl}")
                prev["wst"] = wst
                if pl == 0:
                    prev["use_lo"] = t1
                else:
                    prev["use_hi"] = t1
                outs.append(wst)
            prev[("qb", p)] = prev["use_hi"]
            prev[("scb", p)] = prev["t2"]
            prev[("zqb", p)] = prev["t2"]
    return outs


def build_gptq_dequant_nc():
    nc = bass.Bass("TRN2", target_bir_lowering=False)
    CG = 22016 // 8
    CD = 4096 // 8
    io = {
        "qb_gu": nc.dram_tensor("qb_gu", [2048, CG], U8, kind="ExternalInput").ap(),
        "qz_gu": nc.dram_tensor("qz_gu", [32, CG // 2], U8, kind="ExternalInput").ap(),
        "sc_gu": nc.dram_tensor("sc_gu", [32, CG], F32, kind="ExternalInput").ap(),
        "qb_d": nc.dram_tensor("qb_d", [5504, CD], U8, kind="ExternalInput").ap(),
        "qz_d": nc.dram_tensor("qz_d", [86, CD // 2], U8, kind="ExternalInput").ap(),
        "sc_d": nc.dram_tensor("sc_d", [86, CD], F32, kind="ExternalInput").ap(),
        "wgu": nc.dram_tensor("wgu", [D, CG], BF16, kind="ExternalOutput").ap(),
        "wd": nc.dram_tensor("wd", [INTER, CD], BF16, kind="ExternalOutput").ap(),
    }
    S = Sched(nc)
    o1 = emit_gptq_dequant(S, io, None, CG, 32, "gu", "qb_gu", "qz_gu", "sc_gu", "wgu")
    o2 = emit_gptq_dequant(S, io, None, CD, 86, "dw", "qb_d", "qz_d", "sc_d", "wd")
    S.final_deps = o1 + o2
    S.final_engine = "gpsimd"
    S.emit()
    return nc


def f32_row_bcast(S, col_tensor, nt, name, deps_per_tile):
    nc = S.nc
    row = S.sbuf(name + "_row", [1, nt * 128], F32)
    scr = nc.dram_tensor(name + "_scr", [1, nt * 128], F32, kind="Internal").ap()
    bc = S.sbuf(name + "_bc", [128, nt * 128], F32)
    steps = []
    for tt in range(nt):
        d = S.dma("gpsimd", lambda e, tt=tt: [e.dma_start(out=row[:, tt * 128:(tt + 1) * 128],
                  in_=col_tensor[:, tt:tt + 1])], deps=deps_per_tile[tt], tag=f"{name}r{tt}")
        steps.append(d)
    st = S.dma("sync", lambda e: [e.dma_start(out=scr, in_=row[:, :])], deps=steps, tag=name + "_st")
    ld = S.dma("sync", lambda e: [e.dma_start(out=bc[:, :], in_=scr.partition_broadcast(128))],
               deps=[st], tag=name + "_ld")
    return bc, ld


def emit_l0ffn_main(S, io):
    nt = T // 128
    xT_bf, xT_rdy = io["xT_pre"] if "xT_pre" in io else cast_load(S, "xTpb", io["xTp"], D, T + 1)
    stats = {}
    ln_fin = ln_stats_steps(S, io["x_tok"], nt, D, stats, tag="lnf")
    negm_bc, nmb = f32_row_bcast(S, stats["negm_c"], nt, "negmb", [[f[1]] for f in ln_fin])
    rstd_bc, rsb = f32_row_bcast(S, stats["rstd_c"], nt, "rstdb", [[f[0]] for f in ln_fin])
    h_scr = S.nc.dram_tensor("h_scr", [INTER, T], BF16, kind="Internal").ap()

    pool = [S.psum(f"pp{i}", [128, 512], F32) for i in range(8)]
    wslab = [S.sbuf(f"wslab{i}", [128, 32 * 128], BF16) for i in range(2)]
    cs_col = S.sbuf("cs_col", [128, 1], F32)
    tmp1 = S.sbuf("tmp1", [128, 512], F32)
    pre = [S.sbuf(f"pre{i}", [128, T], F32) for i in range(2)]
    hbf = S.sbuf("hbf", [128, T], BF16)
    prev = {}
    hsts = []
    TP = T + 1
    fusedw = "wgu_ag" in io
    for b in range(2 * RT_D):
        wp = b % 2
        jt = (b // 2) + (0 if b % 2 == 0 else RT_D)
        is_gate = (b % 2 == 0)
        if fusedw:
            wld = S.dma("sync", lambda e, jt=jt, wp=wp: [e.dma_start(
                        out=wslab[wp][:, kt * 128:(kt + 1) * 128],
                        in_=io["wgu_ag"][kt * 128:(kt + 1) * 128, jt * 128:(jt + 1) * 128])
                        for kt in range(32)],
                        n=32, deps=[prev.get(("w", wp))], tag=f"wld{b}")
        else:
            wld = S.dma("sync", lambda e, b=b, wp=wp: [e.dma_start(out=wslab[wp][:, :],
                        in_=io["wgu_s"][b])], deps=[prev.get(("w", wp))], tag=f"wld{b}")
        def mmup(e, wp=wp):
            for th in range(2):
                for kt in range(32):
                    e.matmul(pool[th][:, :], wslab[wp][:, kt * 128:(kt + 1) * 128],
                             xT_bf[:, kt * TP + th * 512:kt * TP + th * 512 + 512],
                             start=(kt == 0), stop=(kt == 31))
            last = None
            for kt in range(32):
                last = e.matmul(pool[4][:, 0:1], wslab[wp][:, kt * 128:(kt + 1) * 128],
                                xT_bf[:, kt * TP + 1024:kt * TP + 1025], start=(kt == 0), stop=(kt == 31))
            return [last]
        mm = S.add("tensor", mmup, deps=[wld] + xT_rdy +
                   [prev.get("evA0"), prev.get("evA1"), prev.get("cse")], tag=f"mmA{b}")
        prev[("w", wp)] = mm
        cse = S.add("vector", lambda e: [e.tensor_copy(cs_col[:, :], pool[4][:, 0:1])], deps=[mm], tag=f"cse{b}")
        prev["cse"] = cse
        dst = pre[0] if is_gate else pre[1]
        evs = []
        for th in range(2):
            tA = S.add("vector", lambda e, th=th: [e.scalar_tensor_tensor(tmp1[:, :], negm_bc[:, th * 512:(th + 1) * 512],
                        cs_col[:, :], pool[th][:, :], ALU.mult, ALU.add)],
                       deps=[mm, cse, nmb, prev.get("tB")], tag=f"tA{b}_{th}")
            tB = S.add("vector", lambda e, th=th, dst=dst: [e.tensor_tensor(dst[:, th * 512:(th + 1) * 512],
                        tmp1[:, :], rstd_bc[:, th * 512:(th + 1) * 512], ALU.mult)],
                       deps=[tA, rsb, prev.get("dst_use") if is_gate else None], tag=f"tB{b}_{th}")
            prev["tB"] = tB
            prev[f"evA{th}"] = tB
            evs.append(tB)
        if is_gate:
            sil = S.add("scalar", lambda e: [e.activation(pre[0][:, :], pre[0][:, :], AF.Silu)],
                        deps=evs, tag=f"sil{b}")
            prev["sil"] = sil
            prev["evA0"] = sil
            prev["evA1"] = sil
        else:
            hmu = S.add("vector", lambda e: [e.tensor_tensor(hbf[:, :], pre[0][:, :], pre[1][:, :], ALU.mult)],
                        deps=evs + [prev.get("sil"), prev.get("hst")], tag=f"hmu{b}")
            hst = S.dma("gpsimd", lambda e, jt0=b // 2: [e.dma_start(out=h_scr[jt0 * 128:(jt0 + 1) * 128, :], in_=hbf[:, :])],
                        deps=[hmu], tag=f"hst{b}")
            prev["hst"] = hst
            prev["dst_use"] = hmu
            hsts.append(hst)

    hld_b = [S.sbuf(f"hldb{i}", [128, T], BF16) for i in range(2)]
    wd_b = [S.sbuf(f"wdb{i}", [128, 512], BF16) for i in range(2)]
    oev = S.sbuf("oev", [128, 512], F32)
    outs = []
    for dtg in range(8):
        last_mm = None
        mm_prev = None
        for rt in range(RT_D):
            p = rt % 2
            hl = S.dma("scalar", lambda e, rt=rt, p=p: [
                    e.dma_start(out=hld_b[p][0:64, :], in_=h_scr[128 * rt:128 * (rt + 1):2, :]),
                    e.dma_start(out=hld_b[p][64:128, :], in_=h_scr[128 * rt + 1:128 * (rt + 1):2, :])],
                    n=2, deps=hsts + [prev.get(("hb", p))], tag=f"hl{dtg}_{rt}")
            if fusedw:
                roff = (rt // 11) * 1408 + (rt % 11) * 128
                wl = S.dma("sync", lambda e, dtg=dtg, roff=roff, p=p: [e.dma_start(out=wd_b[p][:, :],
                        in_=io["wd_ag"][roff:roff + 128, dtg * 512:(dtg + 1) * 512])],
                        deps=[prev.get(("wb", p))], tag=f"wl{dtg}_{rt}")
            else:
                wl = S.dma("sync", lambda e, dtg=dtg, rt=rt, p=p: [e.dma_start(out=wd_b[p][:, :],
                        in_=io["wd_s"][dtg, rt])],
                        deps=[prev.get(("wb", p))], tag=f"wl{dtg}_{rt}")
            def mmdn(e, rt=rt, p=p):
                last = None
                for dt4 in range(4):
                    for th in range(2):
                        last = e.matmul(pool[dt4 * 2 + th][:, :], wd_b[p][:, dt4 * 128:(dt4 + 1) * 128],
                                        hld_b[p][:, th * 512:(th + 1) * 512],
                                        start=(rt == 0), stop=(rt == RT_D - 1))
                return [last]
            dd = [hl, wl] + ([mm_prev] if mm_prev else [])
            if rt == 0:
                dd += [prev.get("evA0"), prev.get("evA1"), prev.get("cse"), prev.get("bev")]
            mm = S.add("tensor", mmdn, deps=dd, tag=f"mmB{dtg}_{rt}")
            mm_prev = mm
            prev[("hb", p)] = mm
            prev[("wb", p)] = mm
            last_mm = mm
        bevs = []
        for dt4 in range(4):
            for th in range(2):
                ev = S.add("vector", lambda e, dt4=dt4, th=th: [e.tensor_copy(oev[:, :], pool[dt4 * 2 + th][:, :])],
                           deps=[last_mm, prev.get("ost_f2")], tag=f"bev{dtg}_{dt4}_{th}")
                st = S.dma("gpsimd", lambda e, dtg=dtg, dt4=dt4, th=th: [e.dma_start(
                        out=io["soT"][dtg * 512 + dt4 * 128:dtg * 512 + (dt4 + 1) * 128, th * 512:(th + 1) * 512],
                        in_=oev[:, :])], deps=[ev], tag=f"ost{dtg}_{dt4}_{th}")
                prev["ost_f2"] = st
                bevs.append(ev)
                outs.append(st)
        prev["bev"] = bevs[-1]
    return outs


def build_l0ffn_main_nc():
    nc = bass.Bass("TRN2", target_bir_lowering=False)
    io = {
        "xTp": nc.dram_tensor("xTp", [D, T + 1], F32, kind="ExternalInput").ap(),
        "x_tok": nc.dram_tensor("x_tok", [T, D], F32, kind="ExternalInput").ap(),
        "wgu_s": nc.dram_tensor("wgu_s", [2 * RT_D, 128, 32 * 128], BF16, kind="ExternalInput").ap(),
        "wd_s": nc.dram_tensor("wd_s", [8, RT_D, 128, 512], BF16, kind="ExternalInput").ap(),
        "soT": nc.dram_tensor("soT", [D, T], F32, kind="ExternalOutput").ap(),
    }
    S = Sched(nc)
    outs = emit_l0ffn_main(S, io)
    S.final_deps = outs
    S.final_engine = "gpsimd"
    S.emit()
    return nc


def emit_ln_residual(S, io, tag="lr"):
    nt = T // 128
    so = S.sbuf(tag + "_so", [128, D], F32)
    sq = S.sbuf(tag + "_sq", [128, D], BF16)
    xr = S.sbuf(tag + "_xr", [128, D], F32)
    m = S.sbuf(tag + "_m", [128, 1], F32); q = S.sbuf(tag + "_q", [128, 1], F32)
    v = S.sbuf(tag + "_v", [128, 1], F32); sd = S.sbuf(tag + "_sd", [128, 1], F32)
    r = S.sbuf(tag + "_r", [128, 1], F32)
    outs = []
    prev = {}
    for tt in range(nt):
        sld = S.dma("sync", lambda e, tt=tt: [e.dma_start(out=so[:, :], in_=io["so_tok"][tt * 128:(tt + 1) * 128, :])],
                    deps=[prev.get("oln")], tag=f"{tag}sld{tt}")
        xld = S.dma("gpsimd", lambda e, tt=tt: [e.dma_start(out=xr[:, :], in_=io["x_tok"][tt * 128:(tt + 1) * 128, :])],
                    deps=[prev.get("oadd")], tag=f"{tag}xld{tt}")
        rs = S.add("vector", lambda e: [e.reduce_sum(m[:, :], so[:, :], axis=AX)],
                   deps=[sld, prev.get("n1")], tag=f"{tag}rs{tt}")
        sqs = S.add("scalar", lambda e: [e.activation(sq[:, :], so[:, :], AF.Square)],
                    deps=[sld, prev.get("rq")], tag=f"{tag}sq{tt}")
        rq = S.add("vector", lambda e: [e.reduce_sum(q[:, :], sq[:, :], axis=AX)], deps=[sqs], tag=f"{tag}rq{tt}")
        prev["rq"] = rq
        n1 = S.add("vector", lambda e: [e.tensor_scalar(m[:, :], m[:, :], 1.0 / D, None, ALU.mult)],
                   deps=[rs], tag=f"{tag}n1{tt}")
        prev["n1"] = n1
        va = S.add("vector", lambda e: [e.tensor_scalar(v[:, :], m[:, :], m[:, :], None, ALU.mult)],
                   deps=[n1, prev.get("vc")], tag=f"{tag}va{tt}")
        vb = S.add("vector", lambda e: [e.scalar_tensor_tensor(v[:, :], q[:, :], 1.0 / D, v[:, :], ALU.mult, ALU.subtract)],
                   deps=[va, rq], tag=f"{tag}vb{tt}")
        vc = S.add("vector", lambda e: [e.tensor_scalar(v[:, :], v[:, :], EPS, None, ALU.add)],
                   deps=[vb, prev.get("sds")], tag=f"{tag}vc{tt}")
        prev["vc"] = vc
        sds = S.add("scalar", lambda e: [e.activation(sd[:, :], v[:, :], AF.Sqrt)], deps=[vc], tag=f"{tag}sd{tt}")
        prev["sds"] = sds
        rr = S.add("vector", lambda e: [e.reciprocal(r[:, :], sd[:, :])], deps=[sds, prev.get("oln")], tag=f"{tag}r{tt}")
        oln = S.add("vector", lambda e: [e.tensor_scalar(so[:, :], so[:, :], m[:, :], r[:, :], ALU.subtract, ALU.mult)],
                    deps=[rr, n1, rs, rq], tag=f"{tag}oln{tt}")
        prev["oln"] = oln
        oadd = S.add("vector", lambda e: [e.tensor_tensor(xr[:, :], xr[:, :], so[:, :], ALU.add)],
                     deps=[oln, xld, prev.get("ost")], tag=f"{tag}oadd{tt}")
        prev["oadd"] = oadd
        ost = S.dma("sync", lambda e, tt=tt: [e.dma_start(out=io["xout"][tt * 128:(tt + 1) * 128, :], in_=xr[:, :])],
                    deps=[oadd], tag=f"{tag}ost{tt}")
        prev["ost"] = ost
        outs.append(ost)
    return outs


def build_ln_residual_nc():
    nc = bass.Bass("TRN2", target_bir_lowering=False)
    io = {
        "so_tok": nc.dram_tensor("so_tok", [T, D], F32, kind="ExternalInput").ap(),
        "x_tok": nc.dram_tensor("x_tok", [T, D], F32, kind="ExternalInput").ap(),
        "xout": nc.dram_tensor("xout", [T, D], F32, kind="ExternalOutput").ap(),
    }
    S = Sched(nc)
    outs = emit_ln_residual(S, io)
    S.final_deps = outs
    S.final_engine = "gpsimd"
    S.emit()
    return nc


def gu_perm():
    p = np.zeros(D, np.int64)
    for g in range(32):
        for s in range(64):
            p[128 * g + s] = 128 * g + 2 * s
            p[128 * g + 64 + s] = 128 * g + 2 * s + 1
    return p


def pack_bytes_rows(qw):
    R, C = qw.shape
    return np.ascontiguousarray(qw.view(np.uint8).reshape(R, C, 4).transpose(0, 2, 1).reshape(4 * R, C))


TE = 1152
NH = 128
HD = 64
SS = 128
NG = 8
GATE = 8192
CONV_DIM = 10240
PROJ = 18560
NCH = 8


def emit_mamba_m1(S, io):
    ntE = TE // 128
    TP = TE + 1
    xT_bf, xT_rdy = cast_load(S, "xTm", io["xT"], D, TP)
    stats = {}
    ln_fin = ln_stats_steps(S, io["x_tok"], ntE, D, stats, tag="lnm")
    negm_row = stats["negm_row"]
    rstd_bc, rsb = f32_row_bcast(S, stats["rstd_c"], ntE, "rstdm", [[f[0]] for f in ln_fin])

    pool = [S.psum(f"mp{i}", [128, 512], F32) for i in range(3)]
    aux = S.psum("mpaux", [128, 512], F32)
    ps_csr = aux[0:1, 0:128]
    wblk = [S.sbuf(f"mwb{i}", [128, D // 4], F32) for i in range(3)]
    wblk_bf = S.sbuf("mwbf", [128, D], BF16)
    cs_row = S.sbuf("mcs_row", [1, 128], BF16)
    pre = S.sbuf("mpre", [128, TE], F32)
    cacc = S.sbuf("mcacc", [128, T], F32)
    obuf = S.sbuf("mobuf", [128, T], BF16)
    convw = S.sbuf("convw", [128, 320], F32)
    convb = S.sbuf("convb", [128, 80], F32)
    dtb = S.sbuf("dtb", [128, 1], F32)
    dtv = S.sbuf("dtv", [128, T], F32)
    lndt = S.sbuf("lndt", [128, T], F32)
    dtt1 = cacc
    ones_f = S.sbuf("ones_f", [1, 128], F32)
    ones_bf = S.sbuf("ones_bfm", [128, 1], BF16)

    cwl = S.dma("sync", lambda e: [e.dma_start(out=convw[:, :], in_=io["conv_w"])], tag="cwl")
    cbl = S.dma("sync", lambda e: [e.dma_start(out=convb[:, :], in_=io["conv_b"])], tag="cbl")
    dtbl = S.dma("sync", lambda e: [e.dma_start(out=dtb[:, :], in_=io["dt_bias"])], tag="dtbl")
    of = S.add("vector", lambda e: [e.memset(ones_f[:, :], 1.0)], tag="onesf")
    ob1 = S.add("vector", lambda e: [e.memset(ones_bf[:, :], 1.0)], tag="onesb")

    ln_all = [x for f in ln_fin for x in f]
    prev = {}
    dt_done = [None]
    for cb in range(145):
        wp = 0
        wcs = None
        wcss = []
        for qq in range(4):
            wb3 = (cb * 4 + qq) % 3
            qname = ("sync", "scalar", "gpsimd")[qq % 3]
            wldq = S.dma(qname, lambda e, cb=cb, qq=qq, wb3=wb3: [e.dma_start(out=wblk[wb3][:, :],
                         in_=io["wip_s"][cb, :, qq * 1024:(qq + 1) * 1024])],
                         deps=[prev.get(("wb", wb3))], tag=f"mwld{cb}_{qq}")
            wcs = S.add("vector", lambda e, qq=qq, wb3=wb3: [e.tensor_copy(wblk_bf[:, qq * 1024:(qq + 1) * 1024], wblk[wb3][:, :])],
                        deps=[wldq] + ([prev.get("wbf_use")] if qq == 0 else []), tag=f"mwcs{cb}_{qq}")
            prev[("wb", wb3)] = wcs
            wcss.append(wcs)
        wcs = wcss[-1]
        csm = S.add("tensor", lambda e: [[e.matmul(ps_csr[:, :], ones_bf[:, :],
                    wblk_bf[:, kt * 128:(kt + 1) * 128], start=(kt == 0), stop=(kt == 31))
                    for kt in range(32)][-1]], deps=[wcs, ob1, prev.get("cse")], tag=f"mcsm{cb}")
        cse = S.add("vector", lambda e: [e.tensor_copy(cs_row[:, :], ps_csr[:, :])], deps=[csm], tag=f"mcse{cb}")
        prev["cse"] = cse

        def mmain(e):
            last = None
            for si, (t0, tw) in enumerate(((0, 512), (512, 512), (1024, 128))):
                for kt in range(32):
                    e.matmul(pool[si][:, 0:tw], wblk_bf[:, kt * 128:(kt + 1) * 128],
                             xT_bf[:, kt * TP + t0:kt * TP + t0 + tw], start=(kt == 0), stop=False)
                last = e.matmul(pool[si][:, 0:tw], cs_row[:, :], negm_row[:, t0:t0 + tw],
                                start=False, stop=True)
            return [last]
        mm = S.add("tensor", mmain, deps=[wcs, cse] + xT_rdy + ln_all + [prev.get("ev")], tag=f"mmm{cb}")
        prev["wbf_use"] = mm

        def evac(e):
            i1 = e.tensor_tensor(pre[:, 0:512], pool[0][:, :], rstd_bc[:, 0:512], ALU.mult)
            i2 = e.tensor_tensor(pre[:, 512:1024], pool[1][:, :], rstd_bc[:, 512:1024], ALU.mult)
            i3 = e.tensor_tensor(pre[:, 1024:1152], pool[2][:, 0:128], rstd_bc[:, 1024:1152], ALU.mult)
            return [i1, i2, i3]
        ev = S.add("vector", evac, n_instr=3, deps=[mm, rsb, prev.get("pre_done")], tag=f"mev{cb}")
        prev["ev"] = ev
        if cb < 64:
            sil = S.add("scalar", lambda e: [e.activation(obuf[:, :], pre[:, 128:1152], AF.Silu)],
                        deps=[ev, prev.get("obuf_use")], tag=f"msil{cb}")
            st = S.dma("gpsimd", lambda e, cb=cb: [e.dma_start(out=io["gateT"][cb * 128:(cb + 1) * 128, :], in_=obuf[:, :])],
                       deps=[sil], tag=f"mgst{cb}")
            prev["obuf_use"] = st
            prev["pre_done"] = sil
        elif cb < 144:
            ct = cb - 64
            def conv(e, ct=ct):
                e.tensor_scalar(cacc[:, :], pre[:, 125:1149], convw[:, 4 * ct:4 * ct + 1], None, ALU.mult)
                return [e.tensor_scalar(cacc[:, :], cacc[:, :], convb[:, ct:ct + 1], None, ALU.add)]
            def conv_mid(e, ct=ct, k=0):
                return [e.scalar_tensor_tensor(cacc[:, :], pre[:, 125 + k:1149 + k],
                        convw[:, 4 * ct + k:4 * ct + k + 1], cacc[:, :], ALU.mult, ALU.add)]
            c0 = S.add("vector", lambda e, ct=ct: [e.tensor_scalar(cacc[:, :], pre[:, 125:1149],
                       convw[:, 4 * ct:4 * ct + 1], None, ALU.mult)],
                       deps=[ev, cwl, prev.get("sil_c")], tag=f"mc0{cb}")
            cks = c0
            for k in (1, 2, 3):
                cks = S.add("vector", lambda e, ct=ct, k=k: [e.scalar_tensor_tensor(cacc[:, :],
                            pre[:, 125 + k:1149 + k], convw[:, 4 * ct + k:4 * ct + k + 1],
                            cacc[:, :], ALU.mult, ALU.add)], deps=[cks], tag=f"mck{cb}_{k}")
            cb_add = S.add("vector", lambda e, ct=ct: [e.tensor_scalar(cacc[:, :], cacc[:, :],
                           convb[:, ct:ct + 1], None, ALU.add)], deps=[cks, cbl], tag=f"mcb{cb}")
            sil = S.add("scalar", lambda e: [e.activation(obuf[:, :], cacc[:, :], AF.Silu)],
                        deps=[cb_add, prev.get("obuf_use")], tag=f"msilc{cb}")
            prev["sil_c"] = sil
            st = S.dma("gpsimd", lambda e, ct=ct: [e.dma_start(out=io["xbc_conv"][ct * 128:(ct + 1) * 128, :], in_=obuf[:, :])],
                       deps=[sil], tag=f"mxst{cb}")
            prev["obuf_use"] = st
            prev["pre_done"] = cb_add
        else:
            d0 = S.add("vector", lambda e: [e.tensor_scalar(dtt1[:, :], pre[:, 128:1152], dtb[:, :], None, ALU.add)],
                       deps=[ev, dtbl], tag="mdt0")
            d1 = S.add("scalar", lambda e: [e.activation(dtv[:, :], dtt1[:, :], AF.Abs)], deps=[d0], tag="mdt1")
            d2 = S.add("scalar", lambda e: [e.activation(dtv[:, :], dtv[:, :], AF.Exp, scale=-1.0)], deps=[d1], tag="mdt2")
            d3 = S.add("vector", lambda e: [e.tensor_scalar(dtv[:, :], dtv[:, :], 1.0, None, ALU.add)], deps=[d2], tag="mdt3")
            d4 = S.add("scalar", lambda e: [e.activation(dtv[:, :], dtv[:, :], AF.Ln)], deps=[d3], tag="mdt4")
            d5 = S.add("vector", lambda e: [e.tensor_scalar(dtt1[:, :], dtt1[:, :], 0.0, None, ALU.max)], deps=[d4, d0], tag="mdt5")
            d6 = S.add("vector", lambda e: [e.tensor_tensor(dtv[:, :], dtv[:, :], dtt1[:, :], ALU.add)], deps=[d5], tag="mdt6")
            d7 = S.add("vector", lambda e: [e.tensor_scalar(dtv[:, :], dtv[:, :], 0.001, 100.0, ALU.max, ALU.min)],
                       deps=[d6], tag="mdt7")
            d8 = S.add("scalar", lambda e: [e.activation(lndt[:, :], dtv[:, :], AF.Ln)], deps=[d7], tag="mdt8")
            dt_done[0] = d8
            prev["pre_done"] = d0

    A_col = S.sbuf("A_col_t", [128, 1], F32)
    D_rep = S.sbuf("D_rep_t", [128, 128], F32)
    triT = S.sbuf("triT_t", [128, 128], BF16)
    identb = S.sbuf("identb", [128, 128], BF16)
    identf = S.sbuf("identf_t", [128, 128], F32)
    al = S.dma("sync", lambda e: [e.dma_start(out=A_col[:, :], in_=io["A_col"])], tag="al")
    dl = S.dma("sync", lambda e: [e.dma_start(out=D_rep[:, :], in_=io["D_rep"])], tag="dl")
    tl = S.dma("sync", lambda e: [e.dma_start(out=triT[:, :], in_=io["triT"])], tag="tl")
    il = S.dma("sync", lambda e: [e.dma_start(out=identb[:, :], in_=io["ident"])], tag="il")
    ifl = S.dma("sync", lambda e: [e.dma_start(out=identf[:, :], in_=io["identf"])], tag="ifl")

    R = xT_bf[:, 0:16384].bitcast(F32)
    R_bf = xT_bf[:, 16384:24576]
    P_run = S.sbuf("P_run", [128, 1], F32)
    cdprefs = S.sbuf("cdprefs", [128, NCH], F32)
    r0 = S.add("vector", lambda e: [e.memset(R[:, :], 0.0)], deps=[dt_done[0]], tag="r0")
    p0 = S.add("vector", lambda e: [e.memset(P_run[:, :], 1.0)], tag="p0")
    rb0 = S.add("vector", lambda e: [e.memset(R_bf[:, :], 0.0)], deps=[dt_done[0]], tag="rb0")

    cs_hl = S.sbuf("cs_hl", [128, 128], F32)
    lncs_hl = S.sbuf("lncs_hl", [128, 128], F32)
    csrow = S.sbuf("csrow", [1, 512], F32)
    lncsrow = S.sbuf("lncsrow", [1, 512], F32)
    cslast = S.sbuf("cslast", [128, 1], F32)
    cd_col = S.sbuf("cd_col", [128, 1], F32)
    cd_row = S.sbuf("cd_row", [1, 128], F32)
    cd_bc = S.sbuf("cd_bc", [128, 128], F32)
    ds_hl = S.sbuf("ds_hl", [128, 128], F32)
    dtds_hl = S.sbuf("dtds_hl", [128, 128], F32)
    csT = S.sbuf("csT", [128, 128], F32)
    dtdsT = S.sbuf("dtdsT", [128, 128], F32)
    excsT = S.sbuf("excsT", [128, 128], F32)
    BT = S.sbuf("BT", [128, 8 * 128], BF16)
    CT = S.sbuf("CTm", [128, 8 * 128], BF16)
    Btok = S.sbuf("Btok", [128, 8 * 128], BF16)
    sgm = S.sbuf("sgm", [128, 8 * 128], BF16)
    sgm4 = S.sbuf("sgm4", [128, 512], BF16)
    xchan = [S.sbuf(f"xchan{i}", [128, 512], BF16) for i in range(2)]
    xtok = S.sbuf("xtok", [128, 8192], BF16)
    seg_sb = S.sbuf("seg_sb", [128, 512], BF16)
    expseg = S.sbuf("expseg", [128, 512], BF16)
    scoresM = S.sbuf("scoresM", [128, 512], BF16)
    B4 = S.sbuf("B4", [128, 512], BF16)
    tmp_y = S.sbuf("tmp_y", [128, 256], F32)
    tmp2_y = S.sbuf("tmp2_y", [128, 256], F32)
    ypre = [S.sbuf(f"ypre{i}", [128, 256], BF16) for i in range(2)]

    ps_seg = pool[0]
    ps_yyo = pool[1]
    ps_st = pool[2][:, 0:256]
    ps_trf = S.psum("ps_trf", [128, 128], F32)
    ps_small = aux[:, 256:512]
    ps_trb = S.psum("ps_trb", [128, 512], BF16)

    m1outs = []
    xbc_done = prev.get("obuf_use")
    gate_done = prev.get("obuf_use")
    for ch in range(NCH):
        t0 = ch * 128
        sc0 = S.add("vector", lambda e, t0=t0: [e.tensor_tensor_scan(cs_hl[:, :], dtv[:, t0:t0 + 128],
                    dtv[:, t0:t0 + 128], 0.0, ALU.add, ALU.bypass)],
                    deps=[dt_done[0], prev.get("cs_use")], tag=f"sc0_{ch}")
        sc1 = S.add("vector", lambda e: [e.tensor_scalar(cs_hl[:, :], cs_hl[:, :], A_col[:, :], None, ALU.mult)],
                    deps=[sc0, al], tag=f"sc1_{ch}")
        cl0 = S.add("vector", lambda e: [e.tensor_copy(cslast[:, :], cs_hl[:, 127:128])],
                    deps=[sc1, prev.get("cslast_use")], tag=f"cl0_{ch}")
        cd0 = S.add("scalar", lambda e: [e.activation(cd_col[:, :], cslast[:, :], AF.Exp)],
                    deps=[cl0, prev.get("cd_use")], tag=f"cd0_{ch}")
        cp0 = S.add("vector", lambda e, ch=ch: [e.tensor_copy(cdprefs[:, ch:ch + 1], P_run[:, :])],
                    deps=[p0, prev.get("pr_upd")], tag=f"cp0_{ch}")
        pr1 = S.add("vector", lambda e: [e.tensor_tensor(P_run[:, :], P_run[:, :], cd_col[:, :], ALU.mult)],
                    deps=[cp0, cd0], tag=f"pr1_{ch}")
        prev["pr_upd"] = pr1
        lc0 = S.add("vector", lambda e, t0=t0: [e.tensor_tensor(lncs_hl[:, :], lndt[:, t0:t0 + 128], cs_hl[:, :], ALU.subtract)],
                    deps=[sc1, prev.get("lncs_use")], tag=f"lc0_{ch}")

        fr3 = S.dma("gpsimd", lambda e: [e.dma_start(out=cd_row[:, :], in_=cd_col[:, :])],
                    deps=[cd0, prev.get("cdrow_use")], tag=f"fr3_{ch}")
        cdb = S.add("tensor", lambda e: [e.matmul(ps_small[:, 128:256], ones_f[:, :], cd_row[:, :],
                    start=True, stop=True, skip_group_check=True)],
                    deps=[fr3, of, prev.get("cdbc_use")], tag=f"cdb_{ch}")
        cdbc = S.add("vector", lambda e: [e.tensor_copy(cd_bc[:, :], ps_small[:, 128:256])], deps=[cdb], tag=f"cdbc_{ch}")
        ds0 = S.add("vector", lambda e: [e.tensor_scalar(ds_hl[:, :], cs_hl[:, :], cslast[:, :], None, ALU.subtract)],
                    deps=[sc1, cl0, prev.get("ds_use")], tag=f"ds0_{ch}")
        ds1 = S.add("scalar", lambda e: [e.activation(ds_hl[:, :], ds_hl[:, :], AF.Exp, scale=-1.0)],
                    deps=[ds0], tag=f"ds1_{ch}")
        dd0 = S.add("vector", lambda e, t0=t0: [e.tensor_tensor(dtds_hl[:, :], dtv[:, t0:t0 + 128], ds_hl[:, :], ALU.mult)],
                    deps=[ds1, prev.get("dtds_use")], tag=f"dd0_{ch}")
        tr1 = S.add("tensor", lambda e: [e.transpose(ps_trf[:, :], cs_hl[:, :], identf[:, :])],
                    deps=[sc1, ifl, prev.get("trf_ev")], tag=f"tr1_{ch}")
        te1 = S.add("vector", lambda e: [e.tensor_copy(csT[:, :], ps_trf[:, :])],
                    deps=[tr1, prev.get("csT_use")], tag=f"te1_{ch}")
        tr2 = S.add("tensor", lambda e: [e.transpose(ps_trf[:, :], dtds_hl[:, :], identf[:, :])],
                    deps=[dd0, te1], tag=f"tr2_{ch}")
        te2 = S.add("vector", lambda e: [e.tensor_copy(dtdsT[:, :], ps_trf[:, :])],
                    deps=[tr2, prev.get("dtdsT_use")], tag=f"te2_{ch}")
        prev["trf_ev"] = te2
        ex0 = S.add("scalar", lambda e: [e.activation(excsT[:, :], csT[:, :], AF.Exp)],
                    deps=[te1, prev.get("excs_use")], tag=f"ex0_{ch}")
        exst = S.dma("gpsimd", lambda e, ch=ch: [e.dma_start(out=io["excs"][:, ch * 128:(ch + 1) * 128], in_=excsT[:, :])],
                     deps=[ex0], tag=f"exst_{ch}")
        gprep = {}
        for g in range(8):
            bld = S.dma("scalar", lambda e, g=g, t0=t0: [e.dma_start(out=BT[:, g * 128:(g + 1) * 128],
                        in_=io["xbc_conv"][8192 + g * 128:8192 + (g + 1) * 128, t0:t0 + 128])],
                        deps=[xbc_done, prev.get(("BTu", g))], tag=f"bld{ch}_{g}")
            cld = S.dma("scalar", lambda e, g=g, t0=t0: [e.dma_start(out=CT[:, g * 128:(g + 1) * 128],
                        in_=io["xbc_conv"][9216 + g * 128:9216 + (g + 1) * 128, t0:t0 + 128])],
                        deps=[xbc_done, prev.get(("CTu", g))], tag=f"cld{ch}_{g}")
            scm = S.add("tensor", lambda e, g=g: [e.matmul(ps_small[:, 0:128], BT[:, g * 128:(g + 1) * 128],
                        CT[:, g * 128:(g + 1) * 128], start=True, stop=True, skip_group_check=True)],
                        deps=[bld, cld, prev.get("scm_ev")], tag=f"scm{ch}_{g}")
            sce = S.add("vector", lambda e, g=g: [e.tensor_tensor(sgm[:, g * 128:(g + 1) * 128],
                        ps_small[:, 0:128], triT[:, :], ALU.mult)],
                        deps=[scm, tl, prev.get(("sgm_u", g))], tag=f"sce{ch}_{g}")
            prev["scm_ev"] = sce
            btr = S.add("tensor", lambda e, g=g: [e.transpose(ps_trb[:, 0:128], BT[:, g * 128:(g + 1) * 128], identb[:, :])],
                        deps=[bld, il, prev.get("trb_use")], tag=f"btr{ch}_{g}")
            bte = S.add("vector", lambda e, g=g: [e.tensor_copy(Btok[:, g * 128:(g + 1) * 128], ps_trb[:, 0:128])],
                        deps=[btr, prev.get(("Btok_u", g))], tag=f"bte{ch}_{g}")
            prev["trb_use"] = bte
            gprep[g] = (sce, bte, bld, cld)

        xt_evs = []
        for xq in range(16):
            p = xq % 2
            def xldf(e, xq=xq, p=p, t0=t0):
                return [e.dma_start(out=xchan[p][:, i * 128:(i + 1) * 128],
                        in_=io["xbc_conv"][(xq * 4 + i) * 128:(xq * 4 + i + 1) * 128, t0:t0 + 128])
                        for i in range(4)]
            xl = S.dma("scalar", xldf, n=4, deps=[xbc_done, prev.get(("xch", p))], tag=f"xl{ch}_{xq}")
            def xtrf(e, p=p):
                return [e.transpose(ps_trb[:, 128 + i * 128:128 + (i + 1) * 128],
                        xchan[p][:, i * 128:(i + 1) * 128], identb[:, :]) for i in range(3)]
            xt1 = S.add("tensor", xtrf, n_instr=3, deps=[xl, il, prev.get("trb_use")], tag=f"xt1_{ch}_{xq}")
            xe1 = S.add("vector", lambda e, xq=xq: [e.tensor_copy(xtok[:, xq * 512:xq * 512 + 384], ps_trb[:, 128:512])],
                        deps=[xt1, prev.get("xtok_use")], tag=f"xe1_{ch}_{xq}")
            xt2 = S.add("tensor", lambda e, p=p: [e.transpose(ps_trb[:, 0:128], xchan[p][:, 384:512], identb[:, :])],
                        deps=[xl, il, prev.get("trb_use")], tag=f"xt2_{ch}_{xq}")
            xe2 = S.add("vector", lambda e, xq=xq: [e.tensor_copy(xtok[:, xq * 512 + 384:(xq + 1) * 512], ps_trb[:, 0:128])],
                        deps=[xt2, prev.get("xtok_use")], tag=f"xe2_{ch}_{xq}")
            prev["trb_use"] = xe2
            prev[("xch", p)] = xe2
            xt_evs += [xe1, xe2]

        rbf = S.add("vector", lambda e: [e.tensor_copy(R_bf[:, :], R[:, :])],
                    deps=[r0, rb0, prev.get("R_upd"), prev.get("Rbf_use")], tag=f"rbf_{ch}")

        ylast = []
        rlast = []
        for hb in range(32):
            g = hb // 4
            sce, bte, bld, cld = gprep[g]
            h0 = hb * 4
            if hb % 4 == 0:
                s4 = S.add("vector", lambda e, g=g: [e.tensor_copy(sgm4[:, :],
                           sgm[:, g * 128:(g + 1) * 128].unsqueeze(1).broadcast_to([128, 4, 128]))],
                           deps=[sce, prev.get("sgm4_use")], tag=f"s4_{ch}_{hb}")
                prev["sgm4_cur"] = s4
            s4 = prev["sgm4_cur"]
            fr1 = S.dma("gpsimd", lambda e, h0=h0: [e.dma_start(out=csrow[:, :], in_=cs_hl[h0:h0 + 4, :])],
                        deps=[sc1, prev.get("csrow_use")], tag=f"fr1_{ch}_{hb}")
            fr2 = S.dma("gpsimd", lambda e, h0=h0: [e.dma_start(out=lncsrow[:, :], in_=lncs_hl[h0:h0 + 4, :])],
                        deps=[lc0, prev.get("lncsrow_use")], tag=f"fr2_{ch}_{hb}")
            def segmm(e):
                last = None
                for hh in range(4):
                    e.matmul(ps_seg[:, hh * 128:(hh + 1) * 128], ones_f[:, :],
                             csrow[:, hh * 128:(hh + 1) * 128], start=True, stop=False, skip_group_check=True)
                    last = e.matmul(ps_seg[:, hh * 128:(hh + 1) * 128], lncsrow[:, hh * 128:(hh + 1) * 128],
                                    ones_f[:, :], start=False, stop=True, skip_group_check=True)
                return [last]
            sg1 = S.add("tensor", segmm, deps=[fr1, fr2, of, prev.get("seg_ev")], tag=f"sg1_{ch}_{hb}")
            prev["csrow_use"] = sg1
            prev["lncsrow_use"] = sg1
            sg2 = S.add("vector", lambda e: [e.tensor_scalar(seg_sb[:, :], ps_seg[:, :], 4.7, None, ALU.min)],
                        deps=[sg1, prev.get("seg_sb_use")], tag=f"sg2_{ch}_{hb}")
            prev["seg_ev"] = sg2
            sg3 = S.add("scalar", lambda e: [e.activation(expseg[:, :], seg_sb[:, :], AF.Exp)],
                        deps=[sg2, prev.get("expseg_use")], tag=f"sg3_{ch}_{hb}")
            prev["seg_sb_use"] = sg3
            sg4 = S.add("vector", lambda e: [e.tensor_tensor(scoresM[:, :], sgm4[:, :], expseg[:, :], ALU.mult)],
                        deps=[sg3, s4, prev.get("scoresM_use")], tag=f"sg4_{ch}_{hb}")
            prev["expseg_use"] = sg4
            prev["sgm4_use"] = sg4 if hb % 4 == 3 else prev.get("sgm4_use")
            b4s = S.add("vector", lambda e, h0=h0, g=g: [[e.tensor_scalar(B4[:, hh * 128:(hh + 1) * 128],
                        Btok[:, g * 128:(g + 1) * 128], dtdsT[:, h0 + hh:h0 + hh + 1], None, ALU.mult)
                        for hh in range(4)][-1]],
                        deps=[bte, te2, prev.get("B4_use")], tag=f"b4_{ch}_{hb}")
            def ymm(e, h0=h0, g=g):
                last = None
                for hh in range(4):
                    h = h0 + hh
                    e.matmul(ps_yyo[:, hh * 64:(hh + 1) * 64], scoresM[:, hh * 128:(hh + 1) * 128],
                             xtok[:, h * 64:(h + 1) * 64], start=True, stop=True, skip_group_check=True)
                    last = e.matmul(ps_yyo[:, 256 + hh * 64:256 + (hh + 1) * 64], CT[:, g * 128:(g + 1) * 128],
                                    R_bf[:, h * 64:(h + 1) * 64], start=True, stop=True, skip_group_check=True)
                return [last]
            ym = S.add("tensor", ymm, deps=[sg4, cld, rbf] + xt_evs + [prev.get("yev")], tag=f"ymm_{ch}_{hb}")
            def stmm(e, h0=h0):
                last = None
                for hh in range(4):
                    h = h0 + hh
                    last = e.matmul(ps_st[:, hh * 64:(hh + 1) * 64], B4[:, hh * 128:(hh + 1) * 128],
                                    xtok[:, h * 64:(h + 1) * 64], start=True, stop=True, skip_group_check=True)
                return [last]
            sm = S.add("tensor", stmm, deps=[b4s] + xt_evs + [prev.get("rup")], tag=f"stmm_{ch}_{hb}")
            ev1 = S.add("vector", lambda e, h0=h0: [[e.tensor_scalar(tmp_y[:, hh * 64:(hh + 1) * 64],
                        ps_yyo[:, 256 + hh * 64:256 + (hh + 1) * 64], excsT[:, h0 + hh:h0 + hh + 1], None, ALU.mult)
                        for hh in range(4)][-1]],
                        deps=[ym, ex0, prev.get("tmpy_use")], tag=f"ev1_{ch}_{hb}")
            ev2 = S.add("vector", lambda e, h0=h0: [[e.scalar_tensor_tensor(tmp2_y[:, hh * 64:(hh + 1) * 64],
                        xtok[:, (h0 + hh) * 64:(h0 + hh + 1) * 64], D_rep[:, h0 + hh:h0 + hh + 1],
                        ps_yyo[:, hh * 64:(hh + 1) * 64], ALU.mult, ALU.add)
                        for hh in range(4)][-1]],
                        deps=[ym, dl, prev.get("tmp2y_use")] + xt_evs, tag=f"ev2_{ch}_{hb}")
            prev["yev"] = ev2
            yb = hb % 2
            ev3 = S.add("vector", lambda e, yb=yb: [e.tensor_tensor(ypre[yb][:, :],
                        tmp_y[:, :], tmp2_y[:, :], ALU.add)],
                        deps=[ev1, ev2, prev.get(("ypre_use", yb))], tag=f"ev3_{ch}_{hb}")
            prev["tmpy_use"] = ev3
            prev["tmp2y_use"] = ev3
            yst1 = S.dma("gpsimd", lambda e, t0=t0, hb=hb, yb=yb: [e.dma_start(
                        out=io["y_pre"][t0:t0 + 128, hb * 256:(hb + 1) * 256], in_=ypre[yb][:, :])],
                        deps=[ev3], tag=f"yst_{ch}_{hb}")
            prev[("ypre_use", yb)] = yst1
            ylast.append(yst1)
            rup = S.add("vector", lambda e, h0=h0: [[e.scalar_tensor_tensor(R[:, (h0 + hh) * 64:(h0 + hh + 1) * 64],
                        R[:, (h0 + hh) * 64:(h0 + hh + 1) * 64], cd_bc[:, h0 + hh:h0 + hh + 1],
                        ps_st[:, hh * 64:(hh + 1) * 64], ALU.mult, ALU.add) for hh in range(4)][-1]],
                        deps=[sm, cdbc, rbf, r0], tag=f"rup_{ch}_{hb}")
            prev["rup"] = rup
            rlast.append(rup)
        m1outs.append(ylast[-1])
        prev["R_upd"] = rlast[-1]
        prev["Rbf_use"] = ylast[-1]
        prev["cs_use"] = te2
        prev["cslast_use"] = ds0
        prev["cd_use"] = pr1
        prev["lncs_use"] = fr2
        prev["cdrow_use"] = cdb
        prev["cdbc_use"] = rlast[-1]
        prev["ds_use"] = dd0
        prev["dtds_use"] = te2
        prev["csT_use"] = ex0
        prev["dtdsT_use"] = rlast[-1]
        prev["excs_use"] = ylast[-1]
        for g in range(8):
            prev[("BTu", g)] = ylast[-1]
            prev[("CTu", g)] = ylast[-1]
            prev[("sgm_u", g)] = ylast[-1]
            prev[("Btok_u", g)] = ylast[-1]
        prev["xtok_use"] = ylast[-1]
        prev["scoresM_use"] = ylast[-1]
        prev["B4_use"] = ylast[-1]
        prev["seg_sb_use"] = sg3
    o1 = S.dma("sync", lambda e: [e.dma_start(out=io["R_agg"], in_=R[:, :])], deps=rlast, tag="o_R")
    o2 = S.dma("sync", lambda e: [e.dma_start(out=io["P_core"], in_=P_run[:, :])], deps=[prev["pr_upd"]], tag="o_P")
    o3 = S.dma("sync", lambda e: [e.dma_start(out=io["cdpref"], in_=cdprefs[:, :])], deps=[prev["pr_upd"]], tag="o_cd")
    return m1outs + [o1, o2, o3, exst]


def build_mamba_m1_nc():
    nc = bass.Bass("TRN2", target_bir_lowering=False)
    io = {
        "xT": nc.dram_tensor("xT", [D, TE + 1], F32, kind="ExternalInput").ap(),
        "x_tok": nc.dram_tensor("x_tok", [TE, D], F32, kind="ExternalInput").ap(),
        "wip_s": nc.dram_tensor("wip_s", [145, 128, D], F32, kind="ExternalInput").ap(),
        "conv_w": nc.dram_tensor("conv_w", [128, 320], F32, kind="ExternalInput").ap(),
        "conv_b": nc.dram_tensor("conv_b", [128, 80], F32, kind="ExternalInput").ap(),
        "dt_bias": nc.dram_tensor("dt_bias", [128, 1], F32, kind="ExternalInput").ap(),
        "A_col": nc.dram_tensor("A_col", [128, 1], F32, kind="ExternalInput").ap(),
        "D_rep": nc.dram_tensor("D_rep", [128, 128], F32, kind="ExternalInput").ap(),
        "triT": nc.dram_tensor("triT", [128, 128], BF16, kind="ExternalInput").ap(),
        "ident": nc.dram_tensor("ident", [128, 128], BF16, kind="ExternalInput").ap(),
        "identf": nc.dram_tensor("identf", [128, 128], F32, kind="ExternalInput").ap(),
        "gateT": nc.dram_tensor("gateT", [GATE, T], BF16, kind="ExternalOutput").ap(),
        "xbc_conv": nc.dram_tensor("xbc_conv", [CONV_DIM, T], BF16, kind="ExternalOutput").ap(),
        "y_pre": nc.dram_tensor("y_pre", [T, GATE], BF16, kind="ExternalOutput").ap(),
        "R_agg": nc.dram_tensor("R_agg", [128, 8192], F32, kind="ExternalOutput").ap(),
        "P_core": nc.dram_tensor("P_core", [128, 1], F32, kind="ExternalOutput").ap(),
        "cdpref": nc.dram_tensor("cdpref", [128, NCH], F32, kind="ExternalOutput").ap(),
        "excs": nc.dram_tensor("excs", [128, NCH * 128], F32, kind="ExternalOutput").ap(),
    }
    S = Sched(nc)
    outs = emit_mamba_m1(S, io)
    S.final_deps = outs
    S.final_engine = "gpsimd"
    S.emit()
    return nc


def emit_mamba_m2(S, io):
    nc = S.nc
    S0 = S.sbuf("S0", [128, 8192], F32)
    buf_a = S.sbuf("buf_a", [128, 8192], F32)
    buf_r = S.sbuf("buf_r", [128, 8192], F32)
    S0_bf = S.sbuf("S0_bf", [128, 8192], BF16)
    ypre_t = S.sbuf("ypre_t", [128, 8192], BF16)
    gate_t = S.sbuf("gate_t", [128, 8192], BF16)
    CTm2 = S.sbuf("CTm2", [128, 8 * 128], BF16)
    excs2 = S.sbuf("excs2", [128, NCH * 128], F32)
    nw_bc = S.sbuf("nw_bc", [128, 8192], F32)
    identb2 = S.sbuf("identb2", [128, 128], BF16)
    tmp_yc = S.sbuf("tmp_yc", [128, 256], F32)
    qg = S.sbuf("qg", [128, 1], F32)
    sqg = S.sbuf("sqg", [128, 1024], BF16)
    ytb = S.sbuf("ytb", [128, 512], BF16)
    rstdg = S.sbuf("rstdg", [128, 1], F32)
    ps_pool = [S.psum(f"m2p{i}", [128, 512], F32) for i in range(7)]
    ps_tr2 = S.psum("ps_tr2", [128, 1024], BF16)
    ynt_scr = nc.dram_tensor("ynt_scr", [GATE, T], BF16, kind="Internal").ap()

    il2 = S.dma("sync", lambda e: [e.dma_start(out=identb2[:, :], in_=io["ident"])], tag="il2")
    exl = S.dma("sync", lambda e: [e.dma_start(out=excs2[:, :], in_=io["excs"])], tag="exl")
    nl3 = S.dma("sync", lambda e: [e.dma_start(out=nw_bc[:, :], in_=io["norm_w"].partition_broadcast(128))],
                tag="nl3")

    fused = "R_ag" in io
    if fused:
        cmask = S.sbuf("cmask_sb", [128, 7], F32)
        cml = S.dma("sync", lambda e: [e.dma_start(out=cmask[:, :], in_=io["cmask"])], tag="cml")
    z0 = S.add("vector", lambda e: [e.memset(S0[:, :], 0.0)], tag="z0")
    prev = {"s0": z0}
    for cp in range(7):
        if fused:
            pld0 = S.dma("sync", lambda e, cp=cp: [e.dma_start(out=buf_a[:, :],
                        in_=io["P_ag"][cp:cp + 1, :].partition_broadcast(128))],
                        deps=[prev.get("pa_use")], tag=f"pld{cp}")
            pm1 = S.add("vector", lambda e: [e.tensor_scalar(buf_a[:, :], buf_a[:, :], 1.0, None, ALU.subtract)],
                        deps=[pld0], tag=f"pm1_{cp}")
            pld = S.add("vector", lambda e, cp=cp: [e.tensor_scalar(buf_a[:, :], buf_a[:, :], cmask[:, cp:cp + 1], 1.0, ALU.mult, ALU.add)],
                        deps=[pm1, cml], tag=f"pm2_{cp}")
            rld0 = S.dma("sync", lambda e, cp=cp: [e.dma_start(out=buf_r[:, :],
                        in_=io["R_ag"][cp * 128:(cp + 1) * 128, :])],
                        deps=[prev.get("rb_use")], tag=f"rld{cp}")
            rld = S.add("vector", lambda e, cp=cp: [e.tensor_scalar(buf_r[:, :], buf_r[:, :], cmask[:, cp:cp + 1], None, ALU.mult)],
                        deps=[rld0, cml], tag=f"rm_{cp}")
        else:
            pld = S.dma("sync", lambda e, cp=cp: [e.dma_start(out=buf_a[:, :],
                        in_=io["P_exp"][cp:cp + 1, :].partition_broadcast(128))],
                        deps=[prev.get("pa_use")], tag=f"pld{cp}")
            rld = S.dma("sync", lambda e, cp=cp: [e.dma_start(out=buf_r[:, :], in_=io["R_pre"][cp])],
                        deps=[prev.get("rb_use")], tag=f"rld{cp}")
        f1 = S.add("vector", lambda e: [e.tensor_tensor(S0[:, :], S0[:, :], buf_a[:, :], ALU.mult)],
                   deps=[pld, prev["s0"]], tag=f"f1_{cp}")
        f2 = S.add("vector", lambda e: [e.tensor_tensor(S0[:, :], S0[:, :], buf_r[:, :], ALU.add)],
                   deps=[rld, f1], tag=f"f2_{cp}")
        prev["s0"] = f2
        prev["pa_use"] = f1
        prev["rb_use"] = f2

    outs = []
    for ch in range(NCH):
        t0 = ch * 128
        cb = S.dma("sync", lambda e, ch=ch: [e.dma_start(out=buf_a[:, :],
                   in_=io["cdpref_exp"][ch:ch + 1, :].partition_broadcast(128))],
                   deps=[prev.get("pa_use")], tag=f"cpb{ch}")
        sj = S.add("vector", lambda e: [e.tensor_tensor(buf_r[:, :], S0[:, :], buf_a[:, :], ALU.mult)],
                   deps=[cb, prev["s0"], prev.get("rb_use")], tag=f"sj{ch}")
        prev["pa_use"] = sj
        prev["rb_use"] = sj
        sjb = S.add("vector", lambda e: [e.tensor_copy(S0_bf[:, :], buf_r[:, :])],
                    deps=[sj, prev.get("s0b_use")], tag=f"sjb{ch}")
        yld = S.dma("scalar", lambda e, t0=t0: [e.dma_start(out=ypre_t[:, :], in_=io["y_pre"][t0:t0 + 128, :])],
                    deps=[prev.get("yp_use")], tag=f"yld{ch}")
        if fused:
            gevs = []
            for q8 in range(16):
                gl4 = S.dma("scalar", lambda e, q8=q8, t0=t0: [e.dma_start(out=ytb[:, i * 128:(i + 1) * 128],
                            in_=io["gateT"][(q8 * 4 + i) * 128:(q8 * 4 + i + 1) * 128, t0:t0 + 128])
                            for i in range(4)],
                            n=4, deps=[prev.get("gtr"), prev.get("ytb_use")], tag=f"gl{ch}_{q8}")
                gtr = S.add("tensor", lambda e: [e.transpose(ps_tr2[:, i * 128:(i + 1) * 128],
                            ytb[:, i * 128:(i + 1) * 128], identb2[:, :]) for i in range(4)],
                            n_instr=4, deps=[gl4, il2, prev.get("gev"), prev.get("tr2_ev")], tag=f"gtr{ch}_{q8}")
                prev["gtr"] = gtr
                gev = S.add("vector", lambda e, q8=q8: [e.tensor_copy(gate_t[:, q8 * 512:(q8 + 1) * 512],
                            ps_tr2[:, 0:512])], deps=[gtr, prev.get("g_use")], tag=f"gev{ch}_{q8}")
                prev["gev"] = gev
                gevs.append(gev)
            gld = gevs[-1]
        else:
            gld = S.dma("sync", lambda e, t0=t0: [e.dma_start(out=gate_t[:, :], in_=io["gate_tok"][t0:t0 + 128, :])],
                        deps=[prev.get("g_use")], tag=f"gld{ch}")
        clds = []
        for g in range(8):
            cl = S.dma("scalar", lambda e, g=g, t0=t0: [e.dma_start(out=CTm2[:, g * 128:(g + 1) * 128],
                       in_=io["CT_in"][g * 128:(g + 1) * 128, t0:t0 + 128])],
                       deps=[prev.get("ct_use")] if g == 0 else [], tag=f"cl2_{ch}_{g}")
            clds.append(cl)
        yadds = []
        for hb in range(32):
            g = hb // 4
            h0 = hb * 4
            def ycmm(e, h0=h0, g=g):
                last = None
                for hh in range(4):
                    h = h0 + hh
                    last = e.matmul(ps_pool[0][:, hh * 64:(hh + 1) * 64], CTm2[:, g * 128:(g + 1) * 128],
                                    S0_bf[:, h * 64:(h + 1) * 64], start=True, stop=True, skip_group_check=True)
                return [last]
            ym = S.add("tensor", ycmm, deps=[sjb] + clds + [prev.get("yc_ev")], tag=f"ycm{ch}_{hb}")
            e1 = S.add("vector", lambda e, h0=h0, ch=ch: [[e.tensor_scalar(tmp_yc[:, hh * 64:(hh + 1) * 64],
                        ps_pool[0][:, hh * 64:(hh + 1) * 64], excs2[:, ch * 128 + h0 + hh:ch * 128 + h0 + hh + 1],
                        None, ALU.mult) for hh in range(4)][-1]],
                       deps=[ym, exl, prev.get("tyc_use")], tag=f"e1_{ch}_{hb}")
            prev["yc_ev"] = e1
            e2 = S.add("vector", lambda e, hb=hb: [e.tensor_tensor(ypre_t[:, hb * 256:(hb + 1) * 256],
                        ypre_t[:, hb * 256:(hb + 1) * 256], tmp_yc[:, :], ALU.add)],
                       deps=[e1, yld], tag=f"e2_{ch}_{hb}")
            prev["tyc_use"] = e2
            yadds.append(e2)
        yg = S.add("vector", lambda e: [e.tensor_tensor(ypre_t[:, :], ypre_t[:, :], gate_t[:, :], ALU.mult)],
                   deps=yadds + [gld], tag=f"yg{ch}")
        prev["g_use"] = yg
        gn_done = []
        for g in range(8):
            g0 = g * 1024
            q1 = S.add("scalar", lambda e, g0=g0: [e.activation(sqg[:, :], ypre_t[:, g0:g0 + 1024], AF.Square)],
                       deps=[yg, prev.get("sqg_use")], tag=f"q1_{ch}_{g}")
            q2 = S.add("vector", lambda e: [e.reduce_sum(qg[:, :], sqg[:, :], axis=AX)],
                       deps=[q1, prev.get("qg_use")], tag=f"q2_{ch}_{g}")
            prev["sqg_use"] = q2
            q3 = S.add("vector", lambda e: [e.tensor_scalar(qg[:, :], qg[:, :], 1.0 / 1024, EPS, ALU.mult, ALU.add)],
                       deps=[q2], tag=f"q3_{ch}_{g}")
            q4 = S.add("scalar", lambda e: [e.activation(rstdg[:, :], qg[:, :], AF.Sqrt)],
                       deps=[q3, prev.get("rstdg_use")], tag=f"q4_{ch}_{g}")
            prev["qg_use"] = q4
            q5 = S.add("vector", lambda e: [e.reciprocal(rstdg[:, :], rstdg[:, :])], deps=[q4], tag=f"q5_{ch}_{g}")
            q6 = S.add("vector", lambda e, g0=g0: [e.tensor_scalar(ypre_t[:, g0:g0 + 1024], ypre_t[:, g0:g0 + 1024],
                        rstdg[:, :], None, ALU.mult)], deps=[q5], tag=f"q6_{ch}_{g}")
            prev["rstdg_use"] = q6
            gn_done.append(q6)
        nw1 = S.add("vector", lambda e: [e.tensor_tensor(ypre_t[:, :], ypre_t[:, :], nw_bc[:, :], ALU.mult)],
                    deps=gn_done + [nl3], tag=f"nw1_{ch}")
        prev["yp_use"] = None
        tr_done = []
        last_t1 = None
        for yq in range(16):
            def ytr(e, yq=yq):
                return [e.transpose(ps_tr2[:, i * 128:(i + 1) * 128],
                        ypre_t[:, (yq * 4 + i) * 128:(yq * 4 + i + 1) * 128], identb2[:, :]) for i in range(4)]
            t1 = S.add("tensor", ytr, n_instr=4, deps=[nw1, il2, prev.get("tr2_ev")], tag=f"ytr{ch}_{yq}")
            last_t1 = t1
            t2 = S.add("vector", lambda e: [e.tensor_copy(ytb[:, :], ps_tr2[:, 0:512])],
                       deps=[t1, prev.get("ytb_use")], tag=f"yte{ch}_{yq}")
            prev["tr2_ev"] = t2
            t3 = S.dma("gpsimd", lambda e, yq=yq, t0=t0: [e.dma_start(
                        out=ynt_scr[yq * 512:(yq + 1) * 512, t0:t0 + 128].rearrange("(q p) t -> p q t", p=128),
                        in_=ytb[:, :].rearrange("p (q t) -> p q t", q=4))],
                       deps=[t2], tag=f"yts{ch}_{yq}")
            prev["ytb_use"] = t3
            tr_done.append(t3)
        prev["ct_use"] = yadds[-1]
        prev["yp_use"] = last_t1
        outs += tr_done

    ynt_b = [S.sbuf(f"yntb{i}", [128, T], BF16) for i in range(2)]
    wo_stg = [S.sbuf(f"mwo{i}", [128, 512], F32) for i in range(2)]
    wo_bf = [S.sbuf(f"mwobf{i}", [128, 512], BF16) for i in range(2)]
    oev2 = S.sbuf("oev2", [128, 512], F32)
    final = []
    for dtg in range(8):
        last_mm = None
        mm_prev = None
        for ct in range(64):
            p = ct % 2
            yl = S.dma("scalar", lambda e, ct=ct, p=p: [e.dma_start(out=ynt_b[p][:, :],
                        in_=ynt_scr[ct * 128:(ct + 1) * 128, :])],
                        deps=outs + [prev.get(("ynb", p))], tag=f"oyl{dtg}_{ct}")
            wl = S.dma("sync", lambda e, dtg=dtg, ct=ct, p=p: [e.dma_start(out=wo_stg[p][:, :],
                        in_=io["wout_s"][dtg, ct])], deps=[prev.get(("mwo", p))], tag=f"owl{dtg}_{ct}")
            wc = S.add("vector", lambda e, p=p: [e.tensor_copy(wo_bf[p][:, :], wo_stg[p][:, :])],
                       deps=[wl, prev.get(("mwobf", p))], tag=f"owc{dtg}_{ct}")
            prev[("mwo", p)] = wc
            def omm(e, ct=ct, p=p):
                last = None
                for dt4 in range(4):
                    for th in range(2):
                        ps = ps_pool[dt4] if th == 0 else (ps_pool[4 + dt4] if dt4 < 3 else ps_tr2.bitcast(F32))
                        last = e.matmul(ps[:, :], wo_bf[p][:, dt4 * 128:(dt4 + 1) * 128],
                                        ynt_b[p][:, th * 512:(th + 1) * 512],
                                        start=(ct == 0), stop=(ct == 63))
                return [last]
            dd = [yl, wc] + ([mm_prev] if mm_prev else [])
            if ct == 0:
                dd += [prev.get("oev2_done"), prev.get("yc_ev")]
            mm = S.add("tensor", omm, deps=dd, tag=f"omm{dtg}_{ct}")
            mm_prev = mm
            prev[("ynb", p)] = mm
            prev[("mwobf", p)] = mm
            last_mm = mm
        for dt4 in range(4):
            for th in range(2):
                psx = ps_pool[dt4] if th == 0 else (ps_pool[4 + dt4] if dt4 < 3 else ps_tr2.bitcast(F32))
                ev = S.add("vector", lambda e, psx=psx: [e.tensor_copy(oev2[:, :], psx[:, :])],
                           deps=[last_mm, prev.get("ost_m2")], tag=f"oev{dtg}_{dt4}_{th}")
                st = S.dma("gpsimd", lambda e, dtg=dtg, dt4=dt4, th=th: [e.dma_start(
                        out=io["soT"][dtg * 512 + dt4 * 128:dtg * 512 + (dt4 + 1) * 128, th * 512:(th + 1) * 512],
                        in_=oev2[:, :])], deps=[ev], tag=f"osm{dtg}_{dt4}_{th}")
                prev["ost_m2"] = st
                prev["oev2_done"] = ev
                final.append(st)
    return final


def build_mamba_m2_nc():
    nc = bass.Bass("TRN2", target_bir_lowering=False)
    io = {
        "y_pre": nc.dram_tensor("y_pre", [T, GATE], BF16, kind="ExternalInput").ap(),
        "gate_tok": nc.dram_tensor("gate_tok", [T, GATE], BF16, kind="ExternalInput").ap(),
        "R_pre": nc.dram_tensor("R_pre", [7, 128, 8192], F32, kind="ExternalInput").ap(),
        "P_exp": nc.dram_tensor("P_exp", [7, 8192], F32, kind="ExternalInput").ap(),
        "cdpref_exp": nc.dram_tensor("cdpref_exp", [NCH, 8192], F32, kind="ExternalInput").ap(),
        "CT_in": nc.dram_tensor("CT_in", [1024, T], BF16, kind="ExternalInput").ap(),
        "excs": nc.dram_tensor("excs", [128, NCH * 128], F32, kind="ExternalInput").ap(),
        "norm_w": nc.dram_tensor("norm_w", [1, GATE], F32, kind="ExternalInput").ap(),
        "wout_s": nc.dram_tensor("wout_s", [8, 64, 128, 512], F32, kind="ExternalInput").ap(),
        "ident": nc.dram_tensor("ident", [128, 128], BF16, kind="ExternalInput").ap(),
        "soT": nc.dram_tensor("soT", [D, T], F32, kind="ExternalOutput").ap(),
    }
    S = Sched(nc)
    outs = emit_mamba_m2(S, io)
    S.final_deps = outs
    S.final_engine = "gpsimd"
    S.emit()
    return nc



def emit_xT_build(S, x_dram, identf, ifl, name, perm=False, ones_col=False, deps=()):
    TP = T + (1 if ones_col else 0)
    xT_bf = S.sbuf(name, [128, 32 * TP], BF16)
    S.scope_push()
    xtile = S.sbuf(name + "_xt", [128, D], F32)
    ps_t = S.psum(name + "_pst", [128, 512], F32)
    ready = []
    prev = {}
    one_done = None
    if ones_col:
        one_done = S.add("vector", lambda e: [e.memset(xT_bf[:, 1024::TP], 1.0)],
                         deps=list(deps), tag=name + "_ones")
        ready.append(one_done)
    for tt in range(T // 128):
        ld = S.dma("scalar", lambda e, tt=tt: [e.dma_start(out=xtile[:, :], in_=x_dram[tt * 128:(tt + 1) * 128, :])],
                   deps=list(deps) + [prev.get("used")], tag=f"{name}_ld{tt}")
        if not perm:
            for q in range(8):
                tr = S.add("tensor", lambda e, q=q: [e.transpose(ps_t[:, i * 128:(i + 1) * 128],
                            xtile[:, (q * 4 + i) * 128:(q * 4 + i + 1) * 128], identf[:, :]) for i in range(4)],
                           n_instr=4, deps=[ld, ifl, prev.get("ev")], tag=f"{name}_tr{tt}_{q}")
                ev = S.add("vector", lambda e, q=q, tt=tt: [[e.tensor_copy(
                            xT_bf[:, (q * 4 + i) * TP + tt * 128:(q * 4 + i) * TP + (tt + 1) * 128],
                            ps_t[:, i * 128:(i + 1) * 128]) for i in range(4)][-1]],
                           deps=[tr], tag=f"{name}_ev{tt}_{q}")
                prev["ev"] = ev
                ready.append(ev)
            prev["used"] = ready[-1]
        else:
            for q in range(8):
                def trf(e, q=q):
                    ii = []
                    for i in range(4):
                        g = q * 4 + i
                        ii.append(e.transpose(ps_t[0:64, i * 128:(i + 1) * 128],
                                  xtile[:, 128 * g:128 * (g + 1):2], identf[:, :]))
                    return ii
                tr1 = S.add("tensor", trf, n_instr=4, deps=[ld, ifl, prev.get("ev")], tag=f"{name}_tre{tt}_{q}")
                ev1 = S.add("vector", lambda e, q=q, tt=tt: [[e.tensor_copy(
                            xT_bf[0:64, (q * 4 + i) * TP + tt * 128:(q * 4 + i) * TP + (tt + 1) * 128],
                            ps_t[0:64, i * 128:(i + 1) * 128]) for i in range(4)][-1]],
                            deps=[tr1], tag=f"{name}_eve{tt}_{q}")
                def trg(e, q=q):
                    ii = []
                    for i in range(4):
                        g = q * 4 + i
                        ii.append(e.transpose(ps_t[0:64, i * 128:(i + 1) * 128],
                                  xtile[:, 128 * g + 1:128 * (g + 1):2], identf[:, :]))
                    return ii
                tr2 = S.add("tensor", trg, n_instr=4, deps=[ld, ifl, ev1], tag=f"{name}_tro{tt}_{q}")
                ev2 = S.add("vector", lambda e, q=q, tt=tt: [[e.tensor_copy(
                            xT_bf[64:128, (q * 4 + i) * TP + tt * 128:(q * 4 + i) * TP + (tt + 1) * 128],
                            ps_t[0:64, i * 128:(i + 1) * 128]) for i in range(4)][-1]],
                            deps=[tr2], tag=f"{name}_evo{tt}_{q}")
                prev["ev"] = ev2
                ready += [ev1, ev2]
            prev["used"] = ready[-1]
    S.scope_pop()
    S.barrier(name + "_bar")
    return xT_bf, ready


def emit_ln_residual_T(S, io, identf, ifl, tag="lrT"):
    nt = T // 128
    so = S.sbuf(tag + "_so", [128, D], F32)
    sld4 = [S.sbuf(tag + f"_l{i}", [128, 512], F32) for i in range(1)]
    ps_t = S.psum(tag + "_pst", [128, 512], F32)
    sq = S.sbuf(tag + "_sq", [128, D], BF16)
    xr = S.sbuf(tag + "_xr", [128, D], F32)
    m = S.sbuf(tag + "_m", [128, 1], F32); q = S.sbuf(tag + "_q", [128, 1], F32)
    v = S.sbuf(tag + "_v", [128, 1], F32); sd = S.sbuf(tag + "_sd", [128, 1], F32)
    r = S.sbuf(tag + "_r", [128, 1], F32)
    outs = []
    prev = {}
    for tt in range(nt):
        so_evs = []
        for q8 in range(8):
            ld = S.dma("scalar", lambda e, q8=q8, tt=tt: [e.dma_start(out=sld4[0][:, i * 128:(i + 1) * 128],
                        in_=io["soT"][(q8 * 4 + i) * 128:(q8 * 4 + i + 1) * 128, tt * 128:(tt + 1) * 128])
                        for i in range(4)],
                       n=4, deps=[prev.get("tr")], tag=f"{tag}sld{tt}_{q8}")
            tr = S.add("tensor", lambda e: [e.transpose(ps_t[:, i * 128:(i + 1) * 128],
                        sld4[0][:, i * 128:(i + 1) * 128], identf[:, :]) for i in range(4)],
                       n_instr=4, deps=[ld, ifl, prev.get("ev")], tag=f"{tag}tr{tt}_{q8}")
            prev["tr"] = tr
            ev = S.add("vector", lambda e, q8=q8: [e.tensor_copy(so[:, q8 * 512:(q8 + 1) * 512], ps_t[:, :])],
                       deps=[tr, prev.get("oln"), prev.get("oadd")], tag=f"{tag}ev{tt}_{q8}")
            prev["ev"] = ev
            so_evs.append(ev)
        xld = S.dma("gpsimd", lambda e, tt=tt: [e.dma_start(out=xr[:, :], in_=io["x_tok"][tt * 128:(tt + 1) * 128, :])],
                    deps=[prev.get("ost")], tag=f"{tag}xld{tt}")
        rs = S.add("vector", lambda e: [e.reduce_sum(m[:, :], so[:, :], axis=AX)],
                   deps=so_evs + [prev.get("n1")], tag=f"{tag}rs{tt}")
        sqs = S.add("scalar", lambda e: [e.activation(sq[:, :], so[:, :], AF.Square)],
                    deps=so_evs + [prev.get("rq")], tag=f"{tag}sq{tt}")
        rq = S.add("vector", lambda e: [e.reduce_sum(q[:, :], sq[:, :], axis=AX)], deps=[sqs], tag=f"{tag}rq{tt}")
        prev["rq"] = rq
        n1 = S.add("vector", lambda e: [e.tensor_scalar(m[:, :], m[:, :], 1.0 / D, None, ALU.mult)],
                   deps=[rs], tag=f"{tag}n1{tt}")
        prev["n1"] = n1
        va = S.add("vector", lambda e: [e.tensor_scalar(v[:, :], m[:, :], m[:, :], None, ALU.mult)],
                   deps=[n1, prev.get("vc")], tag=f"{tag}va{tt}")
        vb = S.add("vector", lambda e: [e.scalar_tensor_tensor(v[:, :], q[:, :], 1.0 / D, v[:, :], ALU.mult, ALU.subtract)],
                   deps=[va, rq], tag=f"{tag}vb{tt}")
        vc = S.add("vector", lambda e: [e.tensor_scalar(v[:, :], v[:, :], EPS, None, ALU.add)],
                   deps=[vb, prev.get("sds")], tag=f"{tag}vc{tt}")
        prev["vc"] = vc
        sds = S.add("scalar", lambda e: [e.activation(sd[:, :], v[:, :], AF.Sqrt)], deps=[vc], tag=f"{tag}sd{tt}")
        prev["sds"] = sds
        rr = S.add("vector", lambda e: [e.reciprocal(r[:, :], sd[:, :])], deps=[sds, prev.get("oln")], tag=f"{tag}r{tt}")
        oln = S.add("vector", lambda e: [e.tensor_scalar(so[:, :], so[:, :], m[:, :], r[:, :], ALU.subtract, ALU.mult)],
                    deps=[rr, n1, rs, rq], tag=f"{tag}oln{tt}")
        prev["oln"] = oln
        oadd = S.add("vector", lambda e: [e.tensor_tensor(xr[:, :], xr[:, :], so[:, :], ALU.add)],
                     deps=[oln, xld, prev.get("ost")], tag=f"{tag}oadd{tt}")
        prev["oadd"] = oadd
        ost = S.dma("gpsimd", lambda e, tt=tt: [e.dma_start(out=io["xout"][tt * 128:(tt + 1) * 128, :], in_=xr[:, :])],
                    deps=[oadd], tag=f"{tag}ost{tt}")
        prev["ost"] = ost
        outs.append(ost)
    return outs


def build_fused_nc(upto="all", sim1=False):
    nc = bass.Bass("TRN2", target_bir_lowering=False)
    S = Sched(nc)
    S.setup_arena(204800)
    A = lambda name, shape, dt, kind="ExternalInput": nc.dram_tensor(name, shape, dt, kind=kind).ap()
    RG = [list(range(8))]

    def AG(in_ap, out_ap, rows, deps, tag):
        if sim1:
            return S.dma("gpsimd", lambda e: [e.dma_start(out=out_ap[r * rows:(r + 1) * rows, :], in_=in_ap)
                                              for r in range(8)], n=8, deps=deps, tag=tag)
        return S.add("gpsimd", lambda e: [e.collective_compute("AllGather", ALU.bypass, RG,
                     ins=[in_ap.opt()], outs=[out_ap.opt()])], deps=deps, tag=tag)

    xT0 = A("xT", [D, TE + 1], F32)
    x0_tok = A("x_tok", [TE, D], F32)
    wip_s = A("wip_s", [145, 128, D], F32)
    conv_w = A("conv_w", [128, 320], F32)
    conv_b = A("conv_b", [128, 80], F32)
    dt_bias = A("dt_bias", [128, 1], F32)
    A_col = A("A_col", [128, 1], F32)
    D_rep = A("D_rep", [128, 128], F32)
    triT = A("triT", [128, 128], BF16)
    ident = A("ident", [128, 128], BF16)
    identf = A("identf", [128, 128], F32)
    qb_gu = A("qb_gu", [256, 22016], U8)
    qz_gu = A("qz_gu", [4, 11008], U8)
    sc_gu = A("sc_gu", [4, 22016], F32)
    qb_d = A("qb_d", [704, 4096], U8)
    qz_d = A("qz_d", [11, 2048], U8)
    sc_d = A("sc_d", [11, 4096], F32)
    norm_w = A("norm_w", [1, GATE], F32)
    wout_s = A("wout_s", [8, 64, 128, 512], F32)
    Wqkv = A("Wqkv", [D, 3072], F32)
    bqkv = A("bqkv", [1, 3072], F32)
    Wo = A("Wo", [1024, D], F32)
    bo = A("bo", [1, D], F32)
    WrT = A("WrT", [D, 8], F32)
    br = A("br", [1, 8], F32)
    AT = A("AT", [D, 128], F32)
    Ball = A("Ball", [128, D], F32)
    E8 = A("E8", [8, 128], F32)
    cmask = A("cmask", [128, 7], F32)

    I = lambda name, shape, dt: nc.dram_tensor(name, shape, dt, kind="Internal").ap()
    wgu_part = I("wgu_part", [512, 22016], BF16)
    wgu_ag = I("wgu_ag", [4096, 22016], BF16)
    wd_part = I("wd_part", [1408, 4096], BF16)
    wd_ag = I("wd_ag", [11264, 4096], BF16)
    gateT_d = I("gateT", [GATE, T], BF16)
    xbc_d = I("xbc_conv", [CONV_DIM, T], BF16)
    ypre_d = I("y_pre", [T, GATE], BF16)
    excs_d = I("excs", [128, NCH * 128], F32)
    R_d = I("R_d", [128, 8192], F32)
    R_ag = I("R_ag", [1024, 8192], F32)
    cdpref_d = I("cdpref_d", [128, NCH], F32)
    cdpref_exp = I("cdpref_exp", [NCH, 8192], F32)
    P_d = I("P_d", [128, 1], F32)
    P_exp_d = I("P_exp_d", [1, 8192], F32)
    P_ag = I("P_ag", [8, 8192], F32)
    soT1 = I("soT1", [D, T], F32)
    x1_d = A("x1_d", [T, D], F32, kind="ExternalOutput") if upto in ("mamba", "l0ffn") else I("x1_d", [T, D], F32)
    soT2 = A("soT2", [D, T], F32, kind="ExternalOutput") if upto == "l0ffn" else I("soT2", [D, T], F32)
    x2_d = A("x2_d", [T, D], F32, kind="ExternalOutput") if upto == "l0ffn" else I("x2_d", [T, D], F32)
    qT_d = I("qT_d", [1024, T], BF16)
    kv_part = I("kv_part", [2048, 1024], BF16)
    kv_ag = I("kv_ag", [16384, 1024], BF16)
    x3_d = A("x3_d", [T, D], F32, kind="ExternalOutput") if upto == "l1attn" else I("x3_d", [T, D], F32)
    xout = A("xout", [T, D], F32, kind="ExternalOutput")

    identf_sb = S.sbuf("identf_root", [128, 128], F32)
    ifl = S.dma("sync", lambda e: [e.dma_start(out=identf_sb[:, :], in_=identf)], tag="ifl_root")

    io_dq = {"qb_gu": qb_gu, "qz_gu": qz_gu, "sc_gu": sc_gu, "wgu_part": wgu_part,
             "qb_d": qb_d, "qz_d": qz_d, "sc_d": sc_d, "wd_part": wd_part}
    S.scope_push()
    o_gu = emit_gptq_dequant(S, io_dq, None, 22016, 4, "gu", "qb_gu", "qz_gu", "sc_gu", "wgu_part", col_chunk=2752)
    S.scope_pop()
    S.barrier("b_dq")
    S.scope_push()
    o_d = emit_gptq_dequant(S, io_dq, None, 4096, 11, "dw", "qb_d", "qz_d", "sc_d", "wd_part", col_chunk=2048)
    S.scope_pop()
    ag_wgu = AG(wgu_part, wgu_ag, 512, o_gu, "ag_wgu")
    ag_wd = AG(wd_part, wd_ag, 1408, o_d, "ag_wd")

    S.barrier("b_m1")
    S.scope_push()
    io1 = {"xT": xT0, "x_tok": x0_tok, "wip_s": wip_s, "conv_w": conv_w, "conv_b": conv_b,
           "dt_bias": dt_bias, "A_col": A_col, "D_rep": D_rep, "triT": triT, "ident": ident,
           "identf": identf, "gateT": gateT_d, "xbc_conv": xbc_d, "y_pre": ypre_d,
           "R_agg": R_d, "P_core": P_d, "cdpref": cdpref_d, "excs": excs_d}
    m1o = emit_mamba_m1(S, io1)
    S.scope_pop()
    S.barrier("b_pexp")
    S.scope_push()
    pcol = S.sbuf("pcol", [128, 1], F32)
    prow = S.sbuf("prow", [1, 128], F32)
    perow = S.sbuf("perow", [1, 8192], F32)
    l1p = S.dma("sync", lambda e: [e.dma_start(out=pcol[:, :], in_=P_d)], deps=m1o, tag="pl1")
    l2p = S.dma("gpsimd", lambda e: [e.dma_start(out=prow[:, :], in_=pcol[:, :])], deps=[l1p], tag="pl2")
    l3p = S.add("vector", lambda e: [e.tensor_copy(perow[:, :].rearrange("a (h r) -> a h r", r=64),
                prow[:, :].unsqueeze(2).broadcast_to([1, 128, 64]))], deps=[l2p], tag="pl3")
    pexp = S.dma("sync", lambda e: [e.dma_start(out=P_exp_d, in_=perow[:, :])], deps=[l3p], tag="pexp")
    cdexps = []
    prev_cd = None
    for ch in range(NCH):
        c1 = S.dma("sync", lambda e, ch=ch: [e.dma_start(out=pcol[:, :], in_=cdpref_d[:, ch:ch + 1], allow_slow_non_contiguous=True)],
                   deps=[m1o[-1], l3p if ch == 0 else prev_cd], tag=f"cd1_{ch}")
        c2 = S.dma("gpsimd", lambda e: [e.dma_start(out=prow[:, :], in_=pcol[:, :])], deps=[c1], tag=f"cd2_{ch}")
        c3 = S.add("vector", lambda e: [e.tensor_copy(perow[:, :].rearrange("a (h r) -> a h r", r=64),
                    prow[:, :].unsqueeze(2).broadcast_to([1, 128, 64]))], deps=[c2], tag=f"cd3_{ch}")
        c4 = S.dma("sync", lambda e, ch=ch: [e.dma_start(out=cdpref_exp[ch:ch + 1, :], in_=perow[:, :])],
                   deps=[c3], tag=f"cd4_{ch}")
        prev_cd = c4
        cdexps.append(c4)
    ag_R = AG(R_d, R_ag, 128, m1o, "ag_R")
    ag_P = AG(P_exp_d, P_ag, 1, [pexp], "ag_P")
    S.scope_pop()

    S.barrier("b_m2")
    S.scope_push()
    io2 = {"y_pre": ypre_d, "gateT": gateT_d, "R_ag": R_ag, "P_ag": P_ag, "cmask": cmask,
           "cdpref_exp": cdpref_exp, "CT_in": xbc_d[9216:10240, :], "excs": excs_d,
           "norm_w": norm_w, "wout_s": wout_s, "ident": ident, "soT": soT1}
    m2o = emit_mamba_m2(S, io2)
    for st in [ag_R, ag_P]:
        pass
    S.scope_pop()

    S.barrier("b_lr1")
    S.scope_push()
    lr1 = emit_ln_residual_T(S, {"soT": soT1, "x_tok": x0_tok[128:TE, :], "xout": x1_d},
                             identf_sb, ifl, tag="lr1")
    S.scope_pop()
    if upto == "mamba":
        S.final_deps = lr1
        S.final_engine = "gpsimd"
        S.emit()
        return nc

    S.barrier("b_f2")
    S.scope_push()
    xTp_bf, xTp_rdy = emit_xT_build(S, x1_d, identf_sb, ifl, "xTp_f", perm=True, ones_col=True, deps=lr1)
    io_f2 = {"xT_pre": (xTp_bf, xTp_rdy), "x_tok": x1_d, "wgu_ag": wgu_ag, "wd_ag": wd_ag, "soT": soT2}
    f2o = emit_l0ffn_main(S, io_f2)
    S.scope_pop()
    S.barrier("b_lr2")
    S.scope_push()
    lr2 = emit_ln_residual_T(S, {"soT": soT2, "x_tok": x1_d, "xout": x2_d}, identf_sb, ifl, tag="lr2")
    S.scope_pop()
    if upto == "l0ffn":
        S.final_deps = lr2
        S.final_engine = "gpsimd"
        S.emit()
        return nc

    S.barrier("b_qkv")
    S.scope_push()
    xT2_bf, xT2_rdy = emit_xT_build(S, x2_d, identf_sb, ifl, "xT2_f", deps=lr2)
    io_qkv = {"xT_pre": (xT2_bf, xT2_rdy), "x_tok": x2_d, "Wqkv": Wqkv, "bqkv": bqkv,
              "qT": qT_d, "kT": kv_part[0:1024, :], "v": kv_part[1024:2048, :]}
    qo = emit_l1attn_qkv(S, io_qkv)
    S.scope_pop()
    ag_kv = AG(kv_part, kv_ag, 2048, qo, "ag_kv")
    S.barrier("b_flash")
    S.scope_push()
    io_fl = {"qT": qT_d, "kv_ag": kv_ag, "Wo": Wo, "bo": bo, "x_tok": x2_d, "xout": x3_d}
    flo = emit_l1attn_flash(S, io_fl)
    S.scope_pop()
    if upto == "l1attn":
        S.final_deps = flo
        S.final_engine = "gpsimd"
        S.emit()
        return nc

    S.barrier("b_l1f")
    S.scope_push()
    xT3_bf, xT3_rdy = emit_xT_build(S, x3_d, identf_sb, ifl, "xT3_f", deps=flo)
    io_lf = {"xT_pre": (xT3_bf, xT3_rdy), "x_tok": x3_d, "WrT": WrT, "br": br, "AT": AT,
             "Ball": Ball, "E8": E8, "ident": ident, "xout": xout}
    lfo = emit_l1ffn_core(S, io_lf)
    S.scope_pop()
    S.final_deps = lfo
    S.final_engine = "gpsimd"
    S.emit()
    return nc


from concourse.bass_utils import run_bass_kernel_spmd
import ml_dtypes

_NC_CACHE = {}


def _stage_fused(ins):
    x0 = np.ascontiguousarray(np.asarray(ins["L0_attn_hidden_states"], dtype=np.float32).reshape(8192, D))
    Wt = np.ascontiguousarray(ins["L0_attn_in_proj_w"].T)
    blocks = []
    for cb in range(145):
        blk = Wt[:, cb * 128:(cb + 1) * 128]
        blocks.append(blk.reshape(32, 128, 128).transpose(1, 0, 2).reshape(128, 4096))
    wip_s = np.ascontiguousarray(np.stack(blocks, 0))
    conv_w = np.ascontiguousarray(ins["L0_attn_conv_w"][:, 0, :].reshape(80, 128, 4).transpose(1, 0, 2).reshape(128, 320)).astype(np.float32)
    conv_b = np.ascontiguousarray(ins["L0_attn_conv_b"].reshape(80, 128).T).astype(np.float32)
    dtb = ins["L0_attn_dt_bias"].reshape(128, 1).astype(np.float32)
    A_col = (-np.arange(1, 129, dtype=np.float32)).reshape(128, 1)
    D_rep = np.ascontiguousarray(np.broadcast_to(ins["L0_attn_D"][None, :], (128, 128))).astype(np.float32)
    triT = (np.arange(128)[None, :] >= np.arange(128)[:, None]).astype(ml_dtypes.bfloat16)
    ident = np.eye(128).astype(ml_dtypes.bfloat16)
    identf = np.eye(128).astype(np.float32)
    WtO = np.ascontiguousarray(ins["L0_attn_out_proj_w"].T)
    wout_s = np.ascontiguousarray(WtO.reshape(64, 128, 8, 512).transpose(2, 0, 1, 3))
    nw = ins["L0_attn_norm_w"].reshape(1, GATE).astype(np.float32)
    qb_gu = pack_bytes_rows(ins["L0_ffn_qweight_gu"])
    qz_gu_u8 = np.ascontiguousarray(ins["L0_ffn_qzeros_gu"].view(np.uint8))
    sc_gu = ins["L0_ffn_scales_gu"].astype(np.float32)
    qb_d = np.concatenate([pack_bytes_rows(ins["L0_ffn_qweight_down"]), np.zeros((128, 4096), np.uint8)], axis=0)
    qz_d = np.concatenate([np.ascontiguousarray(ins["L0_ffn_qzeros_down"].view(np.uint8)), np.zeros((2, 2048), np.uint8)], axis=0)
    sc_d = np.concatenate([ins["L0_ffn_scales_down"].astype(np.float32), np.zeros((2, 4096), np.float32)], axis=0)
    Wqkv = ins["L1_attn_Wqkv"].astype(np.float32)
    bqkv = ins["L1_attn_bqkv"].reshape(1, -1).astype(np.float32)
    Wo = ins["L1_attn_Wo"].astype(np.float32)
    bo = ins["L1_attn_bo"].reshape(1, -1).astype(np.float32)
    E8m = np.zeros((8, 128), np.float32)
    for h in range(8):
        E8m[h, 16 * h:16 * (h + 1)] = 16.0
    WrT = np.ascontiguousarray(ins["L1_ffn_Wr"].astype(np.float32).T)
    AT = np.ascontiguousarray(ins["L1_ffn_A"].astype(np.float32).reshape(128, D).T)
    BallM = np.ascontiguousarray(ins["L1_ffn_B"].astype(np.float32).transpose(0, 2, 1).reshape(128, D))
    brr = ins["L1_ffn_br"].reshape(1, 8).astype(np.float32)
    in_maps = []
    for c in range(NCORES):
        lo = c * T - 128
        if lo < 0:
            xe = np.concatenate([np.zeros((128, D), np.float32), x0[:c * T + T]], axis=0)
        else:
            xe = x0[lo:c * T + T]
        xT = np.concatenate([np.ascontiguousarray(xe.T), np.ones((D, 1), np.float32)], axis=1)
        cm = np.zeros((128, 7), np.float32)
        cm[:, :c] = 1.0
        in_maps.append({
            "xT": np.ascontiguousarray(xT), "x_tok": np.ascontiguousarray(xe),
            "wip_s": wip_s, "conv_w": conv_w, "conv_b": conv_b, "dt_bias": dtb,
            "A_col": A_col, "D_rep": D_rep, "triT": triT, "ident": ident, "identf": identf,
            "qb_gu": np.ascontiguousarray(qb_gu[c * 256:(c + 1) * 256, :]),
            "qz_gu": np.ascontiguousarray(qz_gu_u8[c * 4:(c + 1) * 4, :]),
            "sc_gu": np.ascontiguousarray(sc_gu[c * 4:(c + 1) * 4, :]),
            "qb_d": np.ascontiguousarray(qb_d[c * 704:(c + 1) * 704, :]),
            "qz_d": np.ascontiguousarray(qz_d[c * 11:(c + 1) * 11, :]),
            "sc_d": np.ascontiguousarray(sc_d[c * 11:(c + 1) * 11, :]),
            "norm_w": nw, "wout_s": wout_s,
            "Wqkv": Wqkv, "bqkv": bqkv, "Wo": Wo, "bo": bo,
            "WrT": WrT, "br": brr, "AT": AT, "Ball": BallM, "E8": E8m,
            "cmask": cm,
        })
    return in_maps


def kernel(**inputs):
    ins = inputs
    assert np.array_equal(np.asarray(ins["L0_ffn_g_idx_gu"]).ravel(), np.arange(4096) // 128), "nonstandard g_idx_gu"
    assert np.array_equal(np.asarray(ins["L0_ffn_g_idx_down"]).ravel(), np.arange(INTER) // 128), "nonstandard g_idx_down"
    in_maps = _stage_fused(ins)
    if "fused" not in _NC_CACHE:
        _NC_CACHE["fused"] = build_fused_nc(upto="all")
    res = run_bass_kernel_spmd(_NC_CACHE["fused"], in_maps, core_ids=list(range(NCORES)))
    x4 = np.concatenate([np.asarray(res.results[c]["xout"]) for c in range(NCORES)], axis=0)
    return x4.reshape(1, 8192, D)


# version 08
# speedup vs baseline: 1.0644x; 1.0197x over previous
import numpy as np
from contextlib import ExitStack
import concourse.bass as bass
import concourse.mybir as mybir

F32 = mybir.dt.float32
BF16 = mybir.dt.bfloat16
U8 = mybir.dt.uint8
U32 = mybir.dt.uint32
I32 = mybir.dt.int32
AF = mybir.ActivationFunctionType
ALU = mybir.AluOpType
AX = mybir.AxisListType.X

NCORES = 8
T = 1024
D = 4096
EPS = 1e-5


class Step:
    __slots__ = ("eng", "fn", "deps", "seq", "sem", "count", "tag", "_n", "_amt")


class Sched:
    ENGINES = ("tensor", "vector", "scalar", "gpsimd", "sync")

    def __init__(self, nc):
        self.nc = nc
        self.es = ExitStack()
        self.steps = []
        self.sems = {}
        self.cnt = {e: 0 for e in self.ENGINES}
        self._nbuf = 0

    def sbuf(self, name, shape, dtype):
        if getattr(self, "arena", None) is None:
            return self.es.enter_context(self.nc.sbuf_tensor(name, shape, dtype))
        nbytes = shape[1] * {F32: 4, BF16: 2, U8: 1, U32: 4, I32: 4}[dtype]
        nbytes = (nbytes + 63) & ~63
        off = self._arena_off
        assert off + nbytes <= self._arena_cap, f"arena overflow at {name}: {off}+{nbytes}"
        self._arena_off = off + nbytes
        ap = self.arena[0:shape[0], off:off + nbytes]
        if dtype != U8:
            ap = ap.bitcast(dtype)
        return ap[0:shape[0], 0:shape[1]]

    def psum(self, name, shape, dtype=F32):
        if getattr(self, "arena", None) is None:
            return self.es.enter_context(self.nc.psum_tensor(name, shape, dtype))
        nbytes = shape[1] * {F32: 4, BF16: 2}[dtype]
        nbanks = (nbytes + 2047) // 2048
        b = self._psum_off
        assert b + nbanks <= 8, f"psum arena overflow at {name}"
        self._psum_off = b + nbanks
        assert nbanks == 1, f"psum multi-bank not supported in arena: {name}"
        ap = self._psum_banks[b][:, :]
        if dtype != F32:
            ap = ap.bitcast(dtype)
        return ap[0:shape[0], 0:shape[1]]

    def setup_arena(self, sbuf_bytes=196608):
        t = self.es.enter_context(self.nc.sbuf_tensor("ARENA", [128, sbuf_bytes], U8))
        self.arena = t[:, :]
        self._arena_cap = sbuf_bytes
        self._arena_off = 0
        self._psum_off = 0
        self._psum_banks = [self.es.enter_context(self.nc.psum_tensor(f"PSB{i}", [128, 512], F32)) for i in range(8)]
        self._scopes = []

    def scope_push(self):
        self._scopes.append((self._arena_off, self._psum_off if hasattr(self, "_psum_off") else 0))

    def arena_report(self, tag):
        print(f"[arena] {tag}: sbuf_off={self._arena_off} psum={self._psum_off}")

    def scope_pop(self):
        self._arena_off, self._psum_off = self._scopes.pop()

    def barrier(self, tag="barrier"):
        deps = []
        for semname, c in self.cnt.items():
            if c > 0:
                d = Step()
                d.sem = semname
                d.count = c
                d.seq = len(self.steps) - 1
                deps.append(d)
        st = self.add("gpsimd", lambda e: [e.engine_nop()], deps=deps, tag=tag)
        self.floor = st
        return st

    def _sem(self, e):
        if e not in self.sems:
            self.sems[e] = self.es.enter_context(self.nc.semaphore("sem_" + e))
        return self.sems[e]

    def add(self, eng, fn, deps=(), n_instr=1, amt=None, tag=""):
        assert eng in self.ENGINES, eng
        if amt is None:
            amt = 16 if eng == "sync" else 1
        st = Step()
        st.eng = eng
        st.fn = fn
        st.deps = [d for d in deps if d is not None]
        fl = getattr(self, "floor", None)
        if fl is not None and fl is not st:
            st.deps.append(fl)
        st.seq = len(self.steps)
        st.sem = eng + "_dma" if (amt == 16 and eng != "sync") else eng
        if amt == 16:
            if not hasattr(self, "_last_dma"):
                self._last_dma = {}
            prevd = self._last_dma.get(st.sem)
            if prevd is not None and prevd not in st.deps:
                st.deps.append(prevd)
            self._last_dma[st.sem] = st
        for d in st.deps:
            assert d.seq < st.seq, f"dep cycle: {d.seq} !< {st.seq}"
        self.cnt.setdefault(st.sem, 0)
        self.cnt[st.sem] += n_instr * amt
        st.count = self.cnt[st.sem]
        st.tag = tag
        st._n = n_instr
        st._amt = amt
        self.steps.append(st)
        return st

    def dma(self, eng, fn, deps=(), n=1, tag=""):
        return self.add(eng, fn, deps, n_instr=n, amt=16, tag=tag)

    def barrier_step(self, eng, deps, tag="barrier"):
        return self.add(eng, lambda e: [e.engine_nop()] if hasattr(e, "engine_nop") else [e.nop()],
                        deps=deps, tag=tag)

    def emit(self):
        nc = self.nc
        with self.es, nc.Block() as block:
            for ename in self.ENGINES:
                mine = [s for s in self.steps if s.eng == ename]
                if not mine:
                    continue
                deco = getattr(block, ename)

                def section(eng, _mine=mine, _ename=ename):
                    done = {}
                    for st in _mine:
                        waits = {}
                        for d in st.deps:
                            if d.count <= done.get(d.sem, 0):
                                continue
                            waits[d.sem] = max(waits.get(d.sem, 0), d.count)
                        for s_name, c in sorted(waits.items()):
                            eng.wait_ge(self._sem(s_name), c)
                            done[s_name] = max(done.get(s_name, 0), c)
                        instrs = st.fn(eng)
                        assert len(instrs) == st._n, (st.tag, len(instrs), st._n)
                        for i in instrs:
                            i.then_inc(self._sem(st.sem), st._amt)

                deco(section)
            if getattr(self, "final_deps", None):
                fe = getattr(self, "final_engine", "gpsimd")
                deco = getattr(block, fe)

                def tail(eng):
                    waits = {}
                    for d in self.final_deps:
                        waits[d.sem] = max(waits.get(d.sem, 0), d.count)
                    for s_name, c in sorted(waits.items()):
                        eng.wait_ge(self._sem(s_name), c)

                deco(tail)


def cast_load(S, name, dram_ap, rows, cols, out_dtype=BF16, deps=(), tile_rows=128,
              eng_dma="sync", stage_cols=None):
    nc = S.nc
    nk = rows // tile_rows
    dst = S.sbuf(name, [tile_rows, nk * cols], out_dtype)
    stg_cols = stage_cols or cols
    nstg = 1 if nk == 1 else 2
    stg = [S.sbuf(f"{name}_stg{i}", [tile_rows, stg_cols], F32) for i in range(nstg)]
    ready = []
    prev_cast = [None, None]
    for k in range(nk):
        p = k % nstg
        war = prev_cast[p]
        ld = S.dma(eng_dma,
                   lambda e, k=k, p=p: [e.dma_start(out=stg[p][:, :cols],
                                                    in_=dram_ap[k * tile_rows:(k + 1) * tile_rows, :])],
                   deps=list(deps) + ([war] if war else []), tag=f"{name}_ld{k}")
        cst = S.add("vector",
                    lambda e, k=k, p=p: [e.tensor_copy(dst[:, k * cols:(k + 1) * cols], stg[p][:, :cols])],
                    deps=[ld], tag=f"{name}_cast{k}")
        prev_cast[p] = cst
        ready.append(cst)
    return dst, ready


def ln_stats_steps(S, x_tok_dram, t_tiles, d, stats, deps=(), tag="ln"):
    nc = S.nc
    nt = t_tiles
    stg = [S.sbuf(f"{tag}_xstg0", [128, d // 2], F32)]
    sq = S.sbuf(f"{tag}_sq", [128, d // 2], BF16)
    hcol = S.sbuf(f"{tag}_hcol", [128, 4], F32)
    mcol = S.sbuf(f"{tag}_m", [128, nt], F32)
    qcol = S.sbuf(f"{tag}_q", [128, nt], F32)
    rstd = S.sbuf(f"{tag}_rstd", [128, nt], F32)
    sstd = S.sbuf(f"{tag}_sstd", [128, nt], F32)
    negm = S.sbuf(f"{tag}_negm", [128, nt], F32)
    tmp = S.sbuf(f"{tag}_tmp", [128, 1], F32)
    var = S.sbuf(f"{tag}_var", [128, 1], F32)
    negm_row = S.sbuf(f"{tag}_negm_row", [1, nt * 128], BF16)
    sstd_row = S.sbuf(f"{tag}_sstd_row", [1, nt * 128], BF16)
    rowf = S.sbuf(f"{tag}_rowf", [1, 128], F32)
    stats.update(rstd_c=rstd, negm_row=negm_row, sstd_row=sstd_row, negm_c=negm, sstd_c=sstd)
    fin = []
    prev_use = [None, None]
    prev_sq = None
    prev_rowf = None
    prev_var = None
    prev_sstd = None
    for tt in range(nt):
        p = 0
        hsteps = []
        for hf in range(2):
            ld = S.dma("gpsimd", lambda e, tt=tt, hf=hf: [e.dma_start(out=stg[0][:, :],
                       in_=x_tok_dram[tt * 128:(tt + 1) * 128, hf * (d // 2):(hf + 1) * (d // 2)])],
                       deps=list(deps) + ([prev_use[0]] if prev_use[0] else []), tag=f"{tag}_ld{tt}_{hf}")
            rs_h = S.add("vector", lambda e, hf=hf: [e.reduce_sum(hcol[:, hf:hf + 1], stg[0][:, :], axis=AX)],
                         deps=[ld] + ([prev_use[0]] if prev_use[0] else []), tag=f"{tag}_rsh{tt}_{hf}")
            sqs = S.add("scalar", lambda e, hf=hf: [e.activation(sq[:, :], stg[0][:, :], AF.Square)],
                        deps=[ld, rs_h] + ([prev_sq] if prev_sq else []), tag=f"{tag}_sqh{tt}_{hf}")
            rq_h = S.add("vector", lambda e, hf=hf: [e.reduce_sum(hcol[:, 2 + hf:3 + hf], sq[:, :], axis=AX)],
                         deps=[sqs], tag=f"{tag}_rqh{tt}_{hf}")
            prev_sq = rq_h
            prev_use[0] = rq_h
            hsteps.append(rq_h)
        rs = S.add("vector", lambda e, tt=tt: [e.tensor_tensor(mcol[:, tt:tt + 1], hcol[:, 0:1], hcol[:, 1:2], ALU.add)],
                   deps=hsteps, tag=f"{tag}_rs{tt}")
        rq = S.add("vector", lambda e, tt=tt: [e.tensor_tensor(qcol[:, tt:tt + 1], hcol[:, 2:3], hcol[:, 3:4], ALU.add)],
                   deps=hsteps, tag=f"{tag}_rq{tt}")
        sa = S.add("vector", lambda e, tt=tt: [e.tensor_scalar(negm[:, tt:tt + 1], mcol[:, tt:tt + 1], -1.0 / d, None, ALU.mult)],
                   deps=[rs], tag=f"{tag}_negm{tt}")
        sb = S.add("vector", lambda e, tt=tt: [e.tensor_scalar(tmp[:, :], negm[:, tt:tt + 1], negm[:, tt:tt + 1], None, ALU.mult)],
                   deps=[sa, prev_var], tag=f"{tag}_m2{tt}")
        st1 = S.add("vector", lambda e, tt=tt: [e.scalar_tensor_tensor(var[:, :], qcol[:, tt:tt + 1], 1.0 / d, tmp[:, :], ALU.mult, ALU.subtract)],
                    deps=[sb, rq, prev_sstd], tag=f"{tag}_var{tt}")
        st1b = S.add("vector", lambda e, tt=tt: [e.tensor_scalar(var[:, :], var[:, :], EPS, None, ALU.add)],
                     deps=[st1], tag=f"{tag}_vep{tt}")
        st2 = S.add("scalar", lambda e, tt=tt: [e.activation(sstd[:, tt:tt + 1], var[:, :], AF.Sqrt)],
                    deps=[st1b], tag=f"{tag}_sstd{tt}")
        st3 = S.add("vector", lambda e, tt=tt: [e.reciprocal(rstd[:, tt:tt + 1], sstd[:, tt:tt + 1])],
                    deps=[st2], tag=f"{tag}_rstd{tt}")
        prev_var = st1
        prev_sstd = st2
        d1 = S.dma("gpsimd", lambda e, tt=tt: [e.dma_start(out=rowf[:, :], in_=negm[:, tt:tt + 1])],
                   deps=[st1] + ([prev_rowf] if prev_rowf else []), tag=f"{tag}_rowm{tt}")
        c1 = S.add("vector", lambda e, tt=tt: [e.tensor_copy(negm_row[:, tt * 128:(tt + 1) * 128], rowf[:, :])],
                   deps=[d1], tag=f"{tag}_rowmc{tt}")
        d2 = S.dma("gpsimd", lambda e, tt=tt: [e.dma_start(out=rowf[:, :], in_=sstd[:, tt:tt + 1])],
                   deps=[st2, c1], tag=f"{tag}_rows{tt}")
        c2 = S.add("vector", lambda e, tt=tt: [e.tensor_copy(sstd_row[:, tt * 128:(tt + 1) * 128], rowf[:, :])],
                   deps=[d2], tag=f"{tag}_rowsc{tt}")
        prev_rowf = c2
        fin.append((st3, c1, c2))
    return fin


def small_cast(S, name, dram_ap, shape, deps=(), dtype=BF16):
    nh = 2 if shape[1] >= 2048 else 1
    stg = S.sbuf(name + "_stg", [shape[0], shape[1] // nh], F32)
    dst = S.sbuf(name, shape, dtype)
    cst = None
    for hf in range(nh):
        w = shape[1] // nh
        ld = S.dma("sync", lambda e, hf=hf, w=w: [e.dma_start(out=stg[:, :], in_=dram_ap[:, hf * w:(hf + 1) * w])],
                   deps=list(deps) + ([cst] if cst else []), tag=f"{name}_ld{hf}")
        cst = S.add("vector", lambda e, hf=hf, w=w: [e.tensor_copy(dst[:, hf * w:(hf + 1) * w], stg[:, :])],
                    deps=[ld], tag=f"{name}_cast{hf}")
    return dst, cst


def emit_l1ffn_core(S, io, pre=None):
    nc = S.nc
    nt = T // 128

    xT_bf, xT_rdy = io["xT_pre"] if "xT_pre" in io else cast_load(S, "xTs", io["xT"], D, T)
    WrT_bf, wr_rdy = cast_load(S, "WrTs", io["WrT"], D, 8)
    AT_bf, at_rdy = cast_load(S, "ATs", io["AT"], D, 128)
    Ball_bf, ball_rdy = cast_load(S, "Balls", io["Ball"], 128, D)
    br_bf, br_rdy = small_cast(S, "brs", io["br"], [1, 8])
    E8_bf, e8_rdy = small_cast(S, "E8s", io["E8"], [8, 128])
    ident = S.sbuf("idents", [128, 128], BF16)
    id_rdy = S.dma("sync", lambda e: [e.dma_start(out=ident[:, :], in_=io["ident"])], tag="id_ld")

    stats = {}
    ln_fin = ln_stats_steps(S, io["x_tok"], nt, D, stats, tag="ln4")
    rstd_c = stats["rstd_c"]; negm_row = stats["negm_row"]; sstd_row = stats["sstd_row"]

    ones_bf = S.sbuf("ones_bf", [128, 1], BF16)
    ones_rdy = S.add("vector", lambda e: [e.memset(ones_bf[:, :], 1.0)], tag="ones")

    ps_cs = S.psum("ps_cs", [1, 136], F32)
    cs_sb = S.sbuf("cs_sb", [1, 136], BF16)

    def mm_cs(e):
        last = None
        for kt in range(32):
            last = e.matmul(ps_cs[:, 0:8], ones_bf[:, :], WrT_bf[:, kt * 8:(kt + 1) * 8],
                            start=(kt == 0), stop=(kt == 31))
        for kt in range(32):
            last = e.matmul(ps_cs[:, 8:136], ones_bf[:, :], AT_bf[:, kt * 128:(kt + 1) * 128],
                            start=(kt == 0), stop=(kt == 31))
        return [last]
    cs_mm = S.add("tensor", mm_cs, deps=wr_rdy + at_rdy + [ones_rdy], tag="cs_mm")
    cs_ev = S.add("vector", lambda e: [e.tensor_copy(cs_sb[:, :], ps_cs[:, :])], deps=[cs_mm], tag="cs_ev")

    ps_log = S.psum("ps_log", [128, 8], F32)
    ps_low = S.psum("ps_low", [128, 128], F32)
    ps_str = S.psum("ps_str", [8, 128], BF16)
    ps_sexp = S.psum("ps_sexp", [128, 128], F32)
    ps_ltr = S.psum("ps_ltr", [128, 128], BF16)
    ps_out = [S.psum("ps_out0", [128, 512], F32), S.psum("ps_out1", [128, 512], F32)]

    logit_sb = S.sbuf("logit_sb", [128, 8], F32)
    mx = S.sbuf("mx", [128, 1], F32)
    esb = S.sbuf("esb", [128, 8], F32)
    ssum = S.sbuf("ssum", [128, 1], F32)
    rr = S.sbuf("rr", [128, 1], F32)
    scores_bf = S.sbuf("scores_bf", [128, 8], BF16)
    scoresT_bf = S.sbuf("scoresT_bf", [8, 128], BF16)
    low_sb = S.sbuf("low_sb", [128, 128], F32)
    lowp_bf = S.sbuf("lowp_bf", [128, 128], BF16)
    lowpT_bf = S.sbuf("lowpT_bf", [128, 128], BF16)
    so_sb = S.sbuf("so_sb", [128, D], F32)
    sq2 = S.sbuf("sq2", [128, D], F32)
    m2 = S.sbuf("m2", [128, 1], F32)
    q2 = S.sbuf("q2", [128, 1], F32)
    v2 = S.sbuf("v2", [128, 1], F32)
    sd2 = S.sbuf("sd2", [128, 1], F32)
    r2 = S.sbuf("r2", [128, 1], F32)
    xres = S.sbuf("xres", [128, D], F32)

    prev = {}
    out_steps = []
    for tt in range(nt):
        t0, t1 = tt * 128, (tt + 1) * 128
        fin_rstd, fin_negm, fin_sstd = ln_fin[tt]

        def mm_log(e, tt=tt, t0=t0):
            last = None
            for kt in range(32):
                e.matmul(ps_log[:, :], xT_bf[:, kt * T + t0:kt * T + t0 + 128],
                         WrT_bf[:, kt * 8:(kt + 1) * 8], start=(kt == 0), stop=False)
            e.matmul(ps_log[:, :], negm_row[:, t0:t0 + 128], cs_sb[:, 0:8], start=False, stop=False)
            last = e.matmul(ps_log[:, :], sstd_row[:, t0:t0 + 128], br_bf[:, :], start=False, stop=True)
            return [last]
        dlog = [cs_ev, br_rdy, fin_negm, fin_sstd] + xT_rdy + wr_rdy + ([prev["log_ev"]] if "log_ev" in prev else [])
        mmlog = S.add("tensor", mm_log, deps=dlog, tag=f"mmlog{tt}")

        def mm_low(e, tt=tt, t0=t0):
            for kt in range(32):
                e.matmul(ps_low[:, :], xT_bf[:, kt * T + t0:kt * T + t0 + 128],
                         AT_bf[:, kt * 128:(kt + 1) * 128], start=(kt == 0), stop=False)
            return [e.matmul(ps_low[:, :], negm_row[:, t0:t0 + 128], cs_sb[:, 8:136], start=False, stop=True)]
        mmlow = S.add("tensor", mm_low, deps=dlog + at_rdy + ([prev["low_ev"]] if "low_ev" in prev else []), tag=f"mmlow{tt}")

        log_ev = S.add("vector", lambda e, tt=tt: [e.tensor_scalar(logit_sb[:, :], ps_log[:, :], rstd_c[:, tt:tt + 1], None, ALU.mult)],
                       deps=[mmlog, fin_rstd] + ([prev["sm_done"]] if "sm_done" in prev else []), tag=f"logev{tt}")
        prev["log_ev"] = log_ev
        smx = S.add("vector", lambda e: [e.reduce_max(mx[:, :], logit_sb[:, :], axis=AX, negate=True)],
                    deps=[log_ev], tag=f"smx{tt}")
        sex = S.add("scalar", lambda e: [e.activation(esb[:, :], logit_sb[:, :], AF.Exp, bias=mx[:, :])],
                    deps=[smx], tag=f"sex{tt}")
        ssm = S.add("vector", lambda e: [e.reduce_sum(ssum[:, :], esb[:, :], axis=AX)], deps=[sex], tag=f"ssm{tt}")
        src = S.add("vector", lambda e: [e.reciprocal(rr[:, :], ssum[:, :])], deps=[ssm], tag=f"src{tt}")
        ssc = S.add("vector", lambda e: [e.tensor_scalar(scores_bf[:, :], esb[:, :], rr[:, :], None, ALU.mult)],
                    deps=[src], tag=f"ssc{tt}")
        prev["sm_done"] = ssc
        str_ = S.add("tensor", lambda e: [e.transpose(ps_str[:, :], scores_bf[:, :], ident[:, :])],
                     deps=[ssc, id_rdy] + ([prev["strev"]] if "strev" in prev else []), tag=f"str{tt}")
        strev = S.add("vector", lambda e: [e.tensor_copy(scoresT_bf[:, :], ps_str[:, :])], deps=[str_], tag=f"strev{tt}")
        prev["strev"] = strev
        sexp = S.add("tensor", lambda e: [e.matmul(ps_sexp[:, :], scoresT_bf[:, :], E8_bf[:, :], start=True, stop=True)],
                     deps=[strev, e8_rdy] + ([prev["lowp"]] if "lowp" in prev else []), tag=f"sexp{tt}")
        low_ev = S.add("vector", lambda e, tt=tt: [e.tensor_scalar(low_sb[:, :], ps_low[:, :], rstd_c[:, tt:tt + 1], None, ALU.mult)],
                       deps=[mmlow, fin_rstd] + ([prev["lowp"]] if "lowp" in prev else []), tag=f"lowev{tt}")
        prev["low_ev"] = low_ev
        lowp = S.add("vector", lambda e: [e.tensor_tensor(lowp_bf[:, :], low_sb[:, :], ps_sexp[:, :], ALU.mult)],
                     deps=[low_ev, sexp] + ([prev["ltr"]] if "ltr" in prev else []), tag=f"lowp{tt}")
        prev["lowp"] = lowp
        ltr = S.add("tensor", lambda e: [e.transpose(ps_ltr[:, :], lowp_bf[:, :], ident[:, :])],
                    deps=[lowp] + ([prev["ltrev"]] if "ltrev" in prev else []), tag=f"ltr{tt}")
        prev["ltr"] = ltr
        ltrev = S.add("vector", lambda e: [e.tensor_copy(lowpT_bf[:, :], ps_ltr[:, :])], deps=[ltr], tag=f"ltrev{tt}")
        prev["ltrev"] = ltrev

        so_evs = []
        for ds in range(8):
            pb = ds % 2
            mmout = S.add("tensor",
                          lambda e, ds=ds, pb=pb: [e.matmul(ps_out[pb][:, :], lowpT_bf[:, :], Ball_bf[:, ds * 512:(ds + 1) * 512], start=True, stop=True)],
                          deps=[ltrev] + ball_rdy + ([prev[f"so_ev{pb}"]] if f"so_ev{pb}" in prev else []), tag=f"mmout{tt}_{ds}")
            so_ev = S.add("vector", lambda e, ds=ds, pb=pb: [e.tensor_copy(so_sb[:, ds * 512:(ds + 1) * 512], ps_out[pb][:, :])],
                          deps=[mmout], tag=f"soev{tt}_{ds}")
            prev[f"so_ev{pb}"] = so_ev
            so_evs.append(so_ev)

        rs2 = S.add("vector", lambda e: [e.reduce_sum(m2[:, :], so_sb[:, :], axis=AX)], deps=so_evs, tag=f"rs2_{tt}")
        sq2s = S.add("scalar", lambda e: [e.activation(sq2[:, :], so_sb[:, :], AF.Square)],
                     deps=so_evs + ([prev["rq2"]] if "rq2" in prev else []), tag=f"sq2_{tt}")
        rq2 = S.add("vector", lambda e: [e.reduce_sum(q2[:, :], sq2[:, :], axis=AX)], deps=[sq2s], tag=f"rq2_{tt}")
        prev["rq2"] = rq2
        n2 = S.add("vector", lambda e: [e.tensor_scalar(m2[:, :], m2[:, :], 1.0 / D, None, ALU.mult)], deps=[rs2], tag=f"n2_{tt}")
        v2a = S.add("vector", lambda e: [e.tensor_scalar(v2[:, :], m2[:, :], m2[:, :], None, ALU.mult)], deps=[n2], tag=f"v2a_{tt}")
        v2b = S.add("vector", lambda e: [e.scalar_tensor_tensor(v2[:, :], q2[:, :], 1.0 / D, v2[:, :], ALU.mult, ALU.subtract)],
                    deps=[v2a, rq2], tag=f"v2b_{tt}")
        v2c = S.add("vector", lambda e: [e.tensor_scalar(v2[:, :], v2[:, :], EPS, None, ALU.add)], deps=[v2b], tag=f"v2c_{tt}")
        sd2s = S.add("scalar", lambda e: [e.activation(sd2[:, :], v2[:, :], AF.Sqrt)], deps=[v2c], tag=f"sd2_{tt}")
        r2s = S.add("vector", lambda e: [e.reciprocal(r2[:, :], sd2[:, :])], deps=[sd2s], tag=f"r2_{tt}")
        xld = S.dma("sync", lambda e, t0=t0: [e.dma_start(out=xres[:, :], in_=io["x_tok"][t0:t0 + 128, :])],
                    deps=([prev["oadd"]] if "oadd" in prev else []), tag=f"xld{tt}")
        oln = S.add("vector", lambda e: [e.tensor_scalar(so_sb[:, :], so_sb[:, :], m2[:, :], r2[:, :], ALU.subtract, ALU.mult)],
                    deps=[r2s] + so_evs, tag=f"oln{tt}")
        oadd = S.add("vector", lambda e: [e.tensor_tensor(xres[:, :], xres[:, :], so_sb[:, :], ALU.add)],
                     deps=[oln, xld] + ([prev["ost"]] if "ost" in prev else []), tag=f"oadd{tt}")
        prev["oadd"] = oadd
        ost = S.dma("sync", lambda e, t0=t0: [e.dma_start(out=io["xout"][t0:t0 + 128, :], in_=xres[:, :])],
                    deps=[oadd], tag=f"ost{tt}")
        prev["ost"] = ost
        out_steps.append(ost)
    return out_steps


def build_l1ffn_nc():
    nc = bass.Bass("TRN2", target_bir_lowering=False)
    io = {
        "xT": nc.dram_tensor("xT", [D, T], F32, kind="ExternalInput").ap(),
        "x_tok": nc.dram_tensor("x_tok", [T, D], F32, kind="ExternalInput").ap(),
        "WrT": nc.dram_tensor("WrT", [D, 8], F32, kind="ExternalInput").ap(),
        "br": nc.dram_tensor("br", [1, 8], F32, kind="ExternalInput").ap(),
        "AT": nc.dram_tensor("AT", [D, 128], F32, kind="ExternalInput").ap(),
        "Ball": nc.dram_tensor("Ball", [128, D], F32, kind="ExternalInput").ap(),
        "E8": nc.dram_tensor("E8", [8, 128], F32, kind="ExternalInput").ap(),
        "ident": nc.dram_tensor("ident", [128, 128], BF16, kind="ExternalInput").ap(),
        "xout": nc.dram_tensor("xout", [T, D], F32, kind="ExternalOutput").ap(),
    }
    S = Sched(nc)
    outs = emit_l1ffn_core(S, io)
    S.final_deps = outs
    S.final_engine = "gpsimd"
    S.emit()
    return nc


def stage_l1ffn_inputs(x_full, Wr, br, A, B, ml_bf16):
    E8 = np.zeros((8, 128), np.float32)
    for h in range(8):
        E8[h, 16 * h:16 * (h + 1)] = 16.0
    ident = np.eye(128).astype(ml_bf16)
    WrT = np.ascontiguousarray(Wr.T)
    AT = np.ascontiguousarray(A.reshape(128, D).T)
    Ball = np.ascontiguousarray(B.transpose(0, 2, 1).reshape(128, D))
    br_row = br.reshape(1, 8).astype(np.float32)
    in_maps = []
    for c in range(NCORES):
        xs = x_full[c * T:(c + 1) * T, :]
        in_maps.append({
            "xT": np.ascontiguousarray(xs.T),
            "x_tok": np.ascontiguousarray(xs),
            "WrT": WrT, "br": br_row, "AT": AT, "Ball": Ball,
            "E8": E8, "ident": ident,
        })
    return in_maps


def rows_to_cols(S, src_col_tensor, nt, name, deps_per_tile, tag):
    row = S.sbuf(name, [1, nt * 128], F32)
    steps = []
    for tt in range(nt):
        d = S.dma("gpsimd", lambda e, tt=tt: [e.dma_start(out=row[:, tt * 128:(tt + 1) * 128],
                                                          in_=src_col_tensor[:, tt:tt + 1])],
                  deps=deps_per_tile[tt], tag=f"{tag}{tt}")
        steps.append(d)
    return row, steps


def emit_l1attn_qkv(S, io):
    nt = T // 128
    DQ = 1024
    xT_bf, xT_rdy = io["xT_pre"] if "xT_pre" in io else cast_load(S, "xTs", io["xT"], D, T)
    bq_bf, bq_rdy = small_cast(S, "bqs", io["bqkv"], [1, 3 * DQ])
    stats = {}
    ln_fin = ln_stats_steps(S, io["x_tok"], nt, D, stats, tag="ln2")
    rstd_c = stats["rstd_c"]; negm_row = stats["negm_row"]; sstd_row = stats["sstd_row"]
    rstd_row, rr_steps = rows_to_cols(S, rstd_c, nt, "rstd_row2", [[f[0]] for f in ln_fin], "rr2_")
    rstd_bc = S.sbuf("rstd_bc", [128, T], F32)
    rbd = S.nc.dram_tensor("rstd_scr", [1, T], F32, kind="Internal").ap()
    rb0 = S.dma("sync", lambda e: [e.dma_start(out=rbd, in_=rstd_row[:, :])], deps=rr_steps, tag="rstd_st")
    rb = S.dma("sync", lambda e: [e.dma_start(out=rstd_bc[:, :], in_=rbd.partition_broadcast(128))],
               deps=[rb0], tag="rstd_bc")
    ones_bf = S.sbuf("ones_bf", [128, 1], BF16)
    ones_rdy = S.add("vector", lambda e: [e.memset(ones_bf[:, :], 1.0)], tag="ones")
    ln_all = [x for f in ln_fin for x in f]

    wblk = [S.sbuf(f"wblk{i}", [128, 32 * 128], BF16) for i in range(2)]
    wstg = [S.sbuf(f"wstg{i}", [128, 128], F32) for i in range(2)]
    ps_cs = S.psum("ps_cs", [1, 128], F32)
    cs_sb = S.sbuf("cs_sbq", [1, 128], BF16)
    ps_mm = [S.psum(f"ps_mmq{i}", [128, 512], F32) for i in range(2)]
    ob = [S.sbuf(f"oqk{i}", [128, 512], BF16) for i in range(2)]

    outs = []
    last_wuse = [None, None]
    last_stg = [None, None]
    prev_cs = None
    prev_ps = [None, None]
    prev_ob = [None, None]
    for cb in range(24):
        wb = cb % 2
        casts = []
        for kt in range(32):
            sp = kt % 2
            ld = S.dma(("sync", "scalar")[kt % 2], lambda e, cb=cb, kt=kt, sp=sp: [e.dma_start(out=wstg[sp][:, :],
                        in_=io["Wqkv"][kt * 128:(kt + 1) * 128, cb * 128:(cb + 1) * 128])],
                       deps=[last_stg[sp]] + ([last_wuse[wb]] if kt == 0 else []), tag=f"wld{cb}_{kt}")
            cst = S.add("vector", lambda e, kt=kt, sp=sp, wb=wb: [e.tensor_copy(
                        wblk[wb][:, kt * 128:(kt + 1) * 128], wstg[sp][:, :])],
                        deps=[ld] + ([last_wuse[wb]] if kt == 0 else []), tag=f"wcst{cb}_{kt}")
            last_stg[sp] = cst
            casts.append(cst)
        csm = S.add("tensor", lambda e, wb=wb: [[e.matmul(ps_cs[:, :], ones_bf[:, :],
                    wblk[wb][:, kt * 128:(kt + 1) * 128], start=(kt == 0), stop=(kt == 31))
                    for kt in range(32)][-1]],
                    deps=casts + [ones_rdy] + ([prev_cs] if prev_cs else []), tag=f"csm{cb}")
        cse = S.add("vector", lambda e: [e.tensor_copy(cs_sb[:, :], ps_cs[:, :])], deps=[csm], tag=f"cse{cb}")
        prev_cs = cse
        uses = []
        if cb < 16:
            for th in range(2):
                pb = th
                def mmqk(e, cb=cb, th=th, pb=pb, wb=wb):
                    for kt in range(32):
                        e.matmul(ps_mm[pb][:, :], wblk[wb][:, kt * 128:(kt + 1) * 128],
                                 xT_bf[:, kt * T + th * 512:kt * T + (th + 1) * 512],
                                 start=(kt == 0), stop=False)
                    e.matmul(ps_mm[pb][:, :], cs_sb[:, :], negm_row[:, th * 512:(th + 1) * 512],
                             start=False, stop=False)
                    return [e.matmul(ps_mm[pb][:, :], bq_bf[:, cb * 128:(cb + 1) * 128],
                                     sstd_row[:, th * 512:(th + 1) * 512], start=False, stop=True)]
                mm = S.add("tensor", mmqk, deps=casts + xT_rdy + [cse, bq_rdy] + ln_all
                           + ([prev_ps[pb]] if prev_ps[pb] else []), tag=f"mmqk{cb}_{th}")
                ev = S.add("vector", lambda e, th=th, pb=pb: [e.tensor_tensor(ob[pb][:, :], ps_mm[pb][:, :],
                           rstd_bc[:, th * 512:(th + 1) * 512], ALU.mult)],
                           deps=[mm, rb] + ([prev_ob[pb]] if prev_ob[pb] else []), tag=f"evqk{cb}_{th}")
                prev_ps[pb] = ev
                dst = io["qT"] if cb < 8 else io["kT"]
                r0 = (cb % 8) * 128
                st = S.dma("gpsimd", lambda e, dst=dst, r0=r0, th=th, pb=pb: [e.dma_start(
                            out=dst[r0:r0 + 128, th * 512:(th + 1) * 512], in_=ob[pb][:, :])],
                           deps=[ev], tag=f"stqk{cb}_{th}")
                prev_ob[pb] = st
                uses.append(mm)
                outs.append(st)
        else:
            dv0 = (cb - 16) * 128
            for tt in range(nt):
                pb = tt % 2
                def mmv(e, tt=tt, pb=pb, wb=wb, dv0=dv0):
                    for kt in range(32):
                        e.matmul(ps_mm[pb][:, 0:128], xT_bf[:, kt * T + tt * 128:kt * T + (tt + 1) * 128],
                                 wblk[wb][:, kt * 128:(kt + 1) * 128], start=(kt == 0), stop=False)
                    e.matmul(ps_mm[pb][:, 0:128], negm_row[:, tt * 128:(tt + 1) * 128], cs_sb[:, :],
                             start=False, stop=False)
                    return [e.matmul(ps_mm[pb][:, 0:128], sstd_row[:, tt * 128:(tt + 1) * 128],
                                     bq_bf[:, 2 * DQ + dv0:2 * DQ + dv0 + 128], start=False, stop=True)]
                mm = S.add("tensor", mmv, deps=casts + xT_rdy + [cse, bq_rdy] + ln_all
                           + ([prev_ps[pb]] if prev_ps[pb] else []), tag=f"mmv{cb}_{tt}")
                ev = S.add("vector", lambda e, tt=tt, pb=pb: [e.tensor_scalar(ob[pb][:, 0:128],
                           ps_mm[pb][:, 0:128], rstd_c[:, tt:tt + 1], None, ALU.mult)],
                           deps=[mm, ln_fin[tt][0]] + ([prev_ob[pb]] if prev_ob[pb] else []), tag=f"evv{cb}_{tt}")
                prev_ps[pb] = ev
                st = S.dma("gpsimd", lambda e, tt=tt, pb=pb, dv0=dv0: [e.dma_start(
                            out=io["v"][tt * 128:(tt + 1) * 128, dv0:dv0 + 128], in_=ob[pb][:, 0:128])],
                           deps=[ev], tag=f"stv{cb}_{tt}")
                prev_ob[pb] = st
                uses.append(mm)
                outs.append(st)
        last_wuse[wb] = uses[-1]
    return outs


def build_l1attn_qkv_nc():
    nc = bass.Bass("TRN2", target_bir_lowering=False)
    DQ = 1024
    io = {
        "xT": nc.dram_tensor("xT", [D, T], F32, kind="ExternalInput").ap(),
        "x_tok": nc.dram_tensor("x_tok", [T, D], F32, kind="ExternalInput").ap(),
        "Wqkv": nc.dram_tensor("Wqkv", [D, 3 * DQ], F32, kind="ExternalInput").ap(),
        "bqkv": nc.dram_tensor("bqkv", [1, 3 * DQ], F32, kind="ExternalInput").ap(),
        "qT": nc.dram_tensor("qT", [DQ, T], BF16, kind="ExternalOutput").ap(),
        "kT": nc.dram_tensor("kT", [DQ, T], BF16, kind="ExternalOutput").ap(),
        "v": nc.dram_tensor("v", [T, DQ], BF16, kind="ExternalOutput").ap(),
    }
    S = Sched(nc)
    outs = emit_l1attn_qkv(S, io)
    S.final_deps = outs
    S.final_engine = "gpsimd"
    S.emit()
    return nc

def emit_l1attn_flash(S, io):
    nt = T // 128
    DQ = 1024
    S8 = 8192
    NKT = S8 // 128
    kT_sb = S.sbuf("kT_sb", [128, 8 * S8], BF16)
    qT_sb = S.sbuf("qT_sb", [128, 8 * 256], BF16)
    kld = []
    if "kv_ag" in io:
        for dk in range(8):
            kld.append(S.dma("sync", lambda e, dk=dk: [e.dma_start(
                        out=kT_sb[:, dk * S8 + r * 1024:dk * S8 + (r + 1) * 1024],
                        in_=io["kv_ag"][2048 * r + dk * 128:2048 * r + (dk + 1) * 128, :])
                        for r in range(8)],
                        n=8, tag=f"kld{dk}"))
    else:
        for dk in range(8):
            kld.append(S.dma("sync", lambda e, dk=dk: [e.dma_start(out=kT_sb[:, dk * S8:(dk + 1) * S8],
                        in_=io["kTf"][dk * 128:(dk + 1) * 128, :])], tag=f"kld{dk}"))
    bo_bf, bo_rdy = small_cast(S, "bos", io["bo"], [1, D])
    v_sb = [S.sbuf(f"vs{i}", [128, DQ], BF16) for i in range(2)]
    e_sb = [S.sbuf(f"es{i}", [128, 256], BF16) for i in range(2)]
    ps_sc = S.psum("ps_sc", [128, 256], F32)
    ps_se = S.psum("ps_se", [1, 256], F32)
    ps_oc = [S.psum(f"ps_oc{i}", [128, 512], F32) for i in range(4)]
    ps_z = [S.psum(f"ps_z{i}", [128, 512], F32) for i in range(2)]
    oc_bf = S.sbuf("oc_bf", [128, 8 * 256], BF16)
    se_f = S.sbuf("se_f", [1, 256], F32)
    se_bf = S.sbuf("se_bf", [1, 256], BF16)
    rec_f = S.sbuf("rec_f", [1, 256], F32)
    rec_c = S.sbuf("rec_c", [128, 2], F32)
    wo_stg = [S.sbuf(f"wo_stg{i}", [128, 512], F32) for i in range(2)]
    wo_bf = [S.sbuf(f"wo_bf{i}", [128, 512], BF16) for i in range(2)]
    z_sb = S.sbuf("z_sb", [128, 2 * D], BF16)
    sq3 = S.sbuf("sq3", [128, D], BF16)
    xres3 = S.sbuf("xres3", [128, D // 2], F32)
    m3 = S.sbuf("m3", [128, 1], F32); q3 = S.sbuf("q3", [128, 1], F32)
    v3 = S.sbuf("v3", [128, 1], F32); sd3 = S.sbuf("sd3", [128, 1], F32)
    r3 = S.sbuf("r3", [128, 1], F32)
    ones_bf = S.sbuf("ones_bff", [128, 1], BF16)
    ones_rdy = S.add("vector", lambda e: [e.memset(ones_bf[:, :], 1.0)], tag="ones")

    outs = []
    prev = {}
    for qt in range(4):
        q0 = qt * 256
        qlds = []
        for dk in range(8):
            qlds.append(S.dma("sync", lambda e, dk=dk, q0=q0: [e.dma_start(out=qT_sb[:, dk * 256:(dk + 1) * 256],
                        in_=io["qT"][dk * 128:(dk + 1) * 128, q0:q0 + 256])],
                        deps=[prev.get("q_use")], tag=f"qld{qt}_{dk}"))
        last_mmB = None
        for kt in range(NKT):
            vp = kt % 2
            ep = kt % 2
            if "kv_ag" in io:
                vsrc = io["kv_ag"][2048 * (kt // 8) + 1024 + (kt % 8) * 128:2048 * (kt // 8) + 1024 + (kt % 8) * 128 + 128, :]
            else:
                vsrc = io["vf"][kt * 128:(kt + 1) * 128, :]
            vld = S.dma("scalar", lambda e, vsrc=vsrc, vp=vp: [e.dma_start(out=v_sb[vp][:, :], in_=vsrc)],
                        deps=[prev.get(("vuse", vp))], tag=f"vld{qt}_{kt}")
            scp = (ps_sc, ps_z[0][:, 0:256])[kt % 2]
            def mmA(e, kt=kt, q0=q0, scp=scp):
                last = None
                for dk in range(8):
                    last = e.matmul(scp[:, :], kT_sb[:, dk * S8 + kt * 128:dk * S8 + (kt + 1) * 128],
                                    qT_sb[:, dk * 256:(dk + 1) * 256], start=(dk == 0), stop=(dk == 7))
                return [last]
            mA = S.add("tensor", mmA,
                       deps=kld + qlds + [prev.get(("exp", kt % 2))] +
                       ([prev.get(("z_ev", 0)), prev.get(("z_ev", 1))] if kt <= 1 else []), tag=f"mmA{qt}_{kt}")
            ex = S.add("scalar", lambda e, ep=ep, scp=scp: [e.activation(e_sb[ep][:, :], scp[:, :], AF.Exp, scale=0.03125)],
                       deps=[mA, prev.get(("euse", ep))], tag=f"exp{qt}_{kt}")
            prev[("exp", ep)] = ex
            def mmB(e, kt=kt, ep=ep, vp=vp):
                e.matmul(ps_se[:, :], ones_bf[:, :], e_sb[ep][:, :], start=(kt == 0), stop=(kt == NKT - 1))
                last = None
                for dvt in range(8):
                    last = e.matmul(ps_oc[dvt // 2][:, (dvt % 2) * 256:(dvt % 2) * 256 + 256],
                                    v_sb[vp][:, dvt * 128:(dvt + 1) * 128], e_sb[ep][:, :],
                                    start=(kt == 0), stop=(kt == NKT - 1), skip_group_check=True)
                return [last]
            extra = (prev.get("oc_evs", []) + [prev.get("see")]) if kt == 0 else []
            mB = S.add("tensor", mmB, deps=[ex, vld, ones_rdy] + extra, tag=f"mmB{qt}_{kt}")
            prev[("vuse", vp)] = mB
            prev[("euse", ep)] = mB
            last_mmB = mB
        oc_evs = []
        for dvt in range(8):
            ev = S.add("vector", lambda e, dvt=dvt: [e.tensor_copy(oc_bf[:, dvt * 256:(dvt + 1) * 256],
                        ps_oc[dvt // 2][:, (dvt % 2) * 256:(dvt % 2) * 256 + 256])],
                       deps=[last_mmB] + ([prev.get("z_last")] if dvt == 0 else []), tag=f"ocev{qt}_{dvt}")
            oc_evs.append(ev)
        prev["oc_evs"] = oc_evs
        see = S.add("vector", lambda e: [e.tensor_copy(se_f[:, :], ps_se[:, :])],
                    deps=[last_mmB, prev.get("seb")], tag=f"see{qt}")
        prev["see"] = see
        seb = S.add("vector", lambda e: [e.tensor_copy(se_bf[:, :], se_f[:, :])], deps=[see], tag=f"seb{qt}")
        prev["seb"] = seb
        rec = S.add("vector", lambda e: [e.reciprocal(rec_f[:, :], se_f[:, :])],
                    deps=[see, prev.get("rc1")], tag=f"rec{qt}")
        rc = []
        for t2 in range(2):
            rc.append(S.dma("gpsimd", lambda e, t2=t2: [e.dma_start(out=rec_c[:, t2:t2 + 1],
                        in_=rec_f[:, t2 * 128:(t2 + 1) * 128])],
                        deps=[rec, prev.get("rc_use")], tag=f"rc{qt}_{t2}"))
        prev["rc1"] = rc[1]
        zevs = {0: [], 1: []}
        for ds in range(8):
            mms = {0: None, 1: None}
            for dvt in range(8):
                sp = dvt % 2
                wld = S.dma("sync", lambda e, ds=ds, dvt=dvt, sp=sp: [e.dma_start(out=wo_stg[sp][:, :],
                            in_=io["Wo"][dvt * 128:(dvt + 1) * 128, ds * 512:(ds + 1) * 512])],
                            deps=[prev.get(("wo_stg", sp))], tag=f"wold{qt}_{ds}_{dvt}")
                wcs = S.add("vector", lambda e, sp=sp: [e.tensor_copy(wo_bf[sp][:, :], wo_stg[sp][:, :])],
                            deps=[wld, prev.get(("wo_use", sp))], tag=f"wocs{qt}_{ds}_{dvt}")
                prev[("wo_stg", sp)] = wcs
                for tt2 in range(2):
                    mm = S.add("tensor", lambda e, dvt=dvt, tt2=tt2, sp=sp: [e.matmul(ps_z[tt2][:, :],
                                oc_bf[:, dvt * 256 + tt2 * 128:dvt * 256 + tt2 * 128 + 128], wo_bf[sp][:, :],
                                start=(dvt == 0), stop=False)],
                               deps=[wcs] + oc_evs + ([prev.get(("z_ev", tt2)), prev.get(("exp", 1))] if dvt == 0 else [mms[tt2]]),
                               tag=f"mmz{qt}_{ds}_{dvt}_{tt2}")
                    mms[tt2] = mm
                prev[("wo_use", sp)] = mms[1]
            for tt2 in range(2):
                mmb = S.add("tensor", lambda e, tt2=tt2, ds=ds: [e.matmul(ps_z[tt2][:, :],
                            se_bf[:, tt2 * 128:(tt2 + 1) * 128], bo_bf[:, ds * 512:(ds + 1) * 512],
                            start=False, stop=True)],
                            deps=[mms[tt2], seb, bo_rdy], tag=f"mmzb{qt}_{ds}_{tt2}")
                zev = S.add("vector", lambda e, ds=ds, tt2=tt2: [e.tensor_scalar(
                            z_sb[:, tt2 * D + ds * 512:tt2 * D + (ds + 1) * 512],
                            ps_z[tt2][:, :], rec_c[:, tt2:tt2 + 1], None, ALU.mult)],
                            deps=[mmb] + rc + ([prev.get(("z_sb_use", tt2))] if ds == 0 else []),
                            tag=f"zev{qt}_{ds}_{tt2}")
                prev[("z_ev", tt2)] = zev
                zevs[tt2].append(zev)
        prev["rc_use"] = zevs[1][-1]
        prev["q_use"] = last_mmB
        for tt2 in range(2):
            tglob = qt * 2 + tt2
            zs = zevs[tt2]
            rs3 = S.add("vector", lambda e, tt2=tt2: [e.reduce_sum(m3[:, :], z_sb[:, tt2 * D:(tt2 + 1) * D], axis=AX)],
                        deps=zs + [prev.get("n3")], tag=f"rs3_{tglob}")
            sq3s = S.add("scalar", lambda e, tt2=tt2: [e.activation(sq3[:, :], z_sb[:, tt2 * D:(tt2 + 1) * D], AF.Square)],
                         deps=zs + [prev.get("rq3")], tag=f"sq3_{tglob}")
            rq3 = S.add("vector", lambda e: [e.reduce_sum(q3[:, :], sq3[:, :], axis=AX)], deps=[sq3s], tag=f"rq3_{tglob}")
            prev["rq3"] = rq3
            n3 = S.add("vector", lambda e: [e.tensor_scalar(m3[:, :], m3[:, :], 1.0 / D, None, ALU.mult)],
                       deps=[rs3], tag=f"n3_{tglob}")
            v3a = S.add("vector", lambda e: [e.tensor_scalar(v3[:, :], m3[:, :], m3[:, :], None, ALU.mult)],
                        deps=[n3, prev.get("v3c")], tag=f"v3a_{tglob}")
            v3b = S.add("vector", lambda e: [e.scalar_tensor_tensor(v3[:, :], q3[:, :], 1.0 / D, v3[:, :], ALU.mult, ALU.subtract)],
                        deps=[v3a, rq3], tag=f"v3b_{tglob}")
            v3c = S.add("vector", lambda e: [e.tensor_scalar(v3[:, :], v3[:, :], EPS, None, ALU.add)],
                        deps=[v3b, prev.get("sd3s")], tag=f"v3c_{tglob}")
            prev["v3c"] = v3c
            sd3s = S.add("scalar", lambda e: [e.activation(sd3[:, :], v3[:, :], AF.Sqrt)], deps=[v3c], tag=f"sd3_{tglob}")
            prev["sd3s"] = sd3s
            r3s = S.add("vector", lambda e: [e.reciprocal(r3[:, :], sd3[:, :])], deps=[sd3s, prev.get("oln3")], tag=f"r3_{tglob}")
            oln = S.add("vector", lambda e, tt2=tt2: [e.tensor_scalar(z_sb[:, tt2 * D:(tt2 + 1) * D],
                        z_sb[:, tt2 * D:(tt2 + 1) * D], m3[:, :], r3[:, :], ALU.subtract, ALU.mult)],
                        deps=[r3s, n3] + zs, tag=f"oln3_{tglob}")
            prev["oln3"] = oln
            prev[("z_sb_use", tt2)] = oln
            for hh in range(2):
                H = D // 2
                xld = S.dma("gpsimd", lambda e, tglob=tglob, hh=hh, H=H: [e.dma_start(out=xres3[:, :],
                            in_=io["x_tok"][tglob * 128:(tglob + 1) * 128, hh * H:(hh + 1) * H])],
                            deps=[prev.get("oadd3")], tag=f"xld3_{tglob}_{hh}")
                oadd = S.add("vector", lambda e, tt2=tt2, hh=hh, H=H: [e.tensor_tensor(xres3[:, :], xres3[:, :],
                            z_sb[:, tt2 * D + hh * H:tt2 * D + (hh + 1) * H], ALU.add)],
                            deps=[oln, xld, prev.get("ost3")], tag=f"oadd3_{tglob}_{hh}")
                prev["oadd3"] = oadd
                ost = S.dma("gpsimd", lambda e, tglob=tglob, hh=hh, H=H: [e.dma_start(
                            out=io["xout"][tglob * 128:(tglob + 1) * 128, hh * H:(hh + 1) * H], in_=xres3[:, :])],
                            deps=[oadd], tag=f"ost3_{tglob}_{hh}")
                prev["ost3"] = ost
                outs.append(ost)
            prev["n3"] = n3
    return outs


def build_l1attn_qkv_nc():
    nc = bass.Bass("TRN2", target_bir_lowering=False)
    DQ = 1024
    io = {
        "xT": nc.dram_tensor("xT", [D, T], F32, kind="ExternalInput").ap(),
        "x_tok": nc.dram_tensor("x_tok", [T, D], F32, kind="ExternalInput").ap(),
        "Wqkv": nc.dram_tensor("Wqkv", [D, 3 * DQ], F32, kind="ExternalInput").ap(),
        "bqkv": nc.dram_tensor("bqkv", [1, 3 * DQ], F32, kind="ExternalInput").ap(),
        "qT": nc.dram_tensor("qT", [DQ, T], BF16, kind="ExternalOutput").ap(),
        "kT": nc.dram_tensor("kT", [DQ, T], BF16, kind="ExternalOutput").ap(),
        "v": nc.dram_tensor("v", [T, DQ], BF16, kind="ExternalOutput").ap(),
    }
    S = Sched(nc)
    outs = emit_l1attn_qkv(S, io)
    S.final_deps = outs
    S.final_engine = "gpsimd"
    S.emit()
    return nc


def build_l1attn_flash_nc():
    nc = bass.Bass("TRN2", target_bir_lowering=False)
    DQ = 1024
    S8 = 8192
    io = {
        "qT": nc.dram_tensor("qT", [DQ, T], BF16, kind="ExternalInput").ap(),
        "kTf": nc.dram_tensor("kTf", [DQ, S8], BF16, kind="ExternalInput").ap(),
        "vf": nc.dram_tensor("vf", [S8, DQ], BF16, kind="ExternalInput").ap(),
        "Wo": nc.dram_tensor("Wo", [DQ, D], F32, kind="ExternalInput").ap(),
        "bo": nc.dram_tensor("bo", [1, D], F32, kind="ExternalInput").ap(),
        "x_tok": nc.dram_tensor("x_tok", [T, D], F32, kind="ExternalInput").ap(),
        "xout": nc.dram_tensor("xout", [T, D], F32, kind="ExternalOutput").ap(),
    }
    S = Sched(nc)
    outs = emit_l1attn_flash(S, io)
    S.final_deps = outs
    S.final_engine = "gpsimd"
    S.emit()
    return nc


INTER = 11008
JT_GU = 172
RT_D = 86


def emit_gptq_dequant(S, io, rows_g, cols, nG, name, qb_key, qz_key, sc_key, out_key, col_chunk=None, blocked=False):
    nc = S.nc
    CC = col_chunk or cols
    ncc = cols // CC
    zq_u8 = S.sbuf(name + "_zqu8", [nG, CC // 2], U8)
    zq_n8 = S.sbuf(name + "_zqn8", [nG, CC], U8)
    zq_f = S.sbuf(name + "_zqf", [nG, CC], F32)
    sc_sb = S.sbuf(name + "_sc", [nG, CC], F32)
    zqsc = S.sbuf(name + "_zqsc", [nG, CC], F32)
    scr = nc.dram_tensor(name + "_scr", [nG, cols], F32, kind="Internal").ap()
    zsteps = []
    prevz = {}
    for cc in range(ncc):
        c0 = cc * CC
        l1 = S.dma("sync", lambda e, cc=cc: [e.dma_start(out=zq_u8[:, :], in_=io[qz_key][:, cc * CC // 2:(cc + 1) * CC // 2])],
                   deps=[prevz.get("u")], tag=f"{name}zl{cc}")
        l2 = S.dma("sync", lambda e, c0=c0: [e.dma_start(out=sc_sb[:, :], in_=io[sc_key][:, c0:c0 + CC])],
                   deps=[prevz.get("z2")], tag=f"{name}sl{cc}")
        u1 = S.add("vector", lambda e: [e.tensor_single_scalar(zq_n8[:, 0::2], zq_u8[:, :], 15, ALU.bitwise_and)],
                   deps=[l1, prevz.get("z1")], tag=f"{name}u1_{cc}")
        u2 = S.add("vector", lambda e: [e.tensor_single_scalar(zq_n8[:, 1::2], zq_u8[:, :], 4, ALU.logical_shift_right)],
                   deps=[l1, prevz.get("z1")], tag=f"{name}u2_{cc}")
        prevz["u"] = u2
        z1 = S.add("vector", lambda e: [e.tensor_scalar(zq_f[:, :], zq_n8[:, :], 1.0, None, ALU.add)],
                   deps=[u1, u2, prevz.get("z2")], tag=f"{name}z1_{cc}")
        prevz["z1"] = z1
        z2 = S.add("vector", lambda e: [e.tensor_tensor(zqsc[:, :], zq_f[:, :], sc_sb[:, :], ALU.mult)],
                   deps=[z1, l2, prevz.get("zst")], tag=f"{name}z2_{cc}")
        prevz["z2"] = z2
        zst = S.dma("sync", lambda e, c0=c0: [e.dma_start(out=scr[:, c0:c0 + CC], in_=zqsc[:, :])],
                    deps=[z2], tag=f"{name}zst{cc}")
        prevz["zst"] = zst
        zsteps.append(zst)

    nbc = 1 if blocked else 2
    qb = [S.sbuf(name + f"_qb{i}", [64, CC], U8) for i in range(2)]
    sc_bc = [S.sbuf(name + f"_scb{i}", [64, CC], F32) for i in range(nbc)]
    zq_bc = [S.sbuf(name + f"_zqb{i}", [64, CC], F32) for i in range(nbc)]
    lo = S.sbuf(name + "_lo", [64, CC], U8)
    hi = S.sbuf(name + "_hi", [64, CC], U8)
    ntw = 1 if blocked else 2
    tmpf = [S.sbuf(name + f"_tf{i}", [64, CC], F32) for i in range(ntw)]
    wout = [S.sbuf(name + f"_w{i}", [64, CC], BF16) for i in range(ntw)]
    outs = []
    prev = {}
    for g in range(nG):
        for cc in range(ncc):
            c0 = cc * CC
            p = (g * ncc + cc) % 2 if not blocked else 0
            pq = (g * ncc + cc) % 2
            qld = S.dma("sync", lambda e, g=g, c0=c0, pq=pq: [e.dma_start(out=qb[pq][:, :],
                        in_=io[qb_key][64 * g:64 * (g + 1), c0:c0 + CC])],
                        deps=[prev.get(("qb", pq))], tag=f"{name}qld{g}_{cc}")
            sld = S.dma("scalar", lambda e, g=g, c0=c0, p=p: [e.dma_start(out=sc_bc[p][:, :],
                        in_=io[sc_key][g:g + 1, c0:c0 + CC].partition_broadcast(64))],
                        deps=[prev.get(("scb", p))], tag=f"{name}sld{g}_{cc}")
            zld = S.dma("scalar", lambda e, g=g, c0=c0, p=p: [e.dma_start(out=zq_bc[p][:, :],
                        in_=scr[g:g + 1, c0:c0 + CC].partition_broadcast(64))],
                        deps=zsteps + [prev.get(("zqb", p))], tag=f"{name}zld{g}_{cc}")
            s_lo = S.add("vector", lambda e, pq=pq: [e.tensor_single_scalar(lo[:, :], qb[pq][:, :], 15, ALU.bitwise_and)],
                         deps=[qld, prev.get("use_lo")], tag=f"{name}lo{g}_{cc}")
            s_hi = S.add("vector", lambda e, pq=pq: [e.tensor_single_scalar(hi[:, :], qb[pq][:, :], 4, ALU.logical_shift_right)],
                         deps=[qld, prev.get("use_hi")], tag=f"{name}hi{g}_{cc}")
            for pl, nib, nibstep in ((0, lo, s_lo), (1, hi, s_hi)):
                tw = pl % ntw
                t1 = S.add("vector", lambda e, nib=nib, p=p, tw=tw: [e.scalar_tensor_tensor(tmpf[tw][:, :], nib[:, :], 1.0,
                            sc_bc[p][:, :], ALU.mult, ALU.mult)],
                           deps=[nibstep, sld, prev.get(("t2", tw))], tag=f"{name}t1_{g}_{cc}_{pl}")
                t2 = S.add("vector", lambda e, p=p, tw=tw: [e.tensor_tensor(wout[tw][:, :], tmpf[tw][:, :], zq_bc[p][:, :], ALU.subtract)],
                           deps=[t1, zld, prev.get(("wst", tw))], tag=f"{name}t2_{g}_{cc}_{pl}")
                prev[("t2", tw)] = t2
                prev["t2"] = t2
                if blocked:
                    njt = CC // 128
                    jt0 = (c0 // 128)
                    chunks = [(0, 11), (11, 11), (22, 11), (33, 10)] if njt == 43 else [(0, njt)]
                    def wstf(e, g=g, pl=pl, jt0=jt0, chunks=chunks, tw=tw):
                        out3 = io[out_key]
                        ii = []
                        for (j0, jn) in chunks:
                            ii.append(e.dma_start(
                                out=out3[jt0 + j0:jt0 + j0 + jn, 64 * pl:64 * (pl + 1),
                                         g * 128:(g + 1) * 128].transpose([1, 0, 2]),
                                in_=wout[tw][:, :].rearrange("p (j c) -> p j c", c=128)[:, j0:j0 + jn, :]))
                        return ii
                    wst = S.dma("gpsimd", wstf, n=len(chunks), deps=[t2], tag=f"{name}wst{g}_{cc}_{pl}")
                else:
                    wst = S.dma("gpsimd", lambda e, g=g, pl=pl, c0=c0, tw=tw: [e.dma_start(
                                out=io[out_key][128 * g + 64 * pl:128 * g + 64 * (pl + 1), c0:c0 + CC], in_=wout[tw][:, :])],
                                deps=[t2], tag=f"{name}wst{g}_{cc}_{pl}")
                prev[("wst", tw)] = wst
                prev["wst"] = wst
                if pl == 0:
                    prev["use_lo"] = t1
                else:
                    prev["use_hi"] = t1
                outs.append(wst)
            prev[("qb", p)] = prev["use_hi"]
            prev[("scb", p)] = prev["t2"]
            prev[("zqb", p)] = prev["t2"]
    return outs


def build_gptq_dequant_nc():
    nc = bass.Bass("TRN2", target_bir_lowering=False)
    CG = 22016 // 8
    CD = 4096 // 8
    io = {
        "qb_gu": nc.dram_tensor("qb_gu", [2048, CG], U8, kind="ExternalInput").ap(),
        "qz_gu": nc.dram_tensor("qz_gu", [32, CG // 2], U8, kind="ExternalInput").ap(),
        "sc_gu": nc.dram_tensor("sc_gu", [32, CG], F32, kind="ExternalInput").ap(),
        "qb_d": nc.dram_tensor("qb_d", [5504, CD], U8, kind="ExternalInput").ap(),
        "qz_d": nc.dram_tensor("qz_d", [86, CD // 2], U8, kind="ExternalInput").ap(),
        "sc_d": nc.dram_tensor("sc_d", [86, CD], F32, kind="ExternalInput").ap(),
        "wgu": nc.dram_tensor("wgu", [D, CG], BF16, kind="ExternalOutput").ap(),
        "wd": nc.dram_tensor("wd", [INTER, CD], BF16, kind="ExternalOutput").ap(),
    }
    S = Sched(nc)
    o1 = emit_gptq_dequant(S, io, None, CG, 32, "gu", "qb_gu", "qz_gu", "sc_gu", "wgu")
    o2 = emit_gptq_dequant(S, io, None, CD, 86, "dw", "qb_d", "qz_d", "sc_d", "wd")
    S.final_deps = o1 + o2
    S.final_engine = "gpsimd"
    S.emit()
    return nc


def f32_row_bcast(S, col_tensor, nt, name, deps_per_tile):
    nc = S.nc
    row = S.sbuf(name + "_row", [1, nt * 128], F32)
    scr = nc.dram_tensor(name + "_scr", [1, nt * 128], F32, kind="Internal").ap()
    bc = S.sbuf(name + "_bc", [128, nt * 128], F32)
    steps = []
    for tt in range(nt):
        d = S.dma("gpsimd", lambda e, tt=tt: [e.dma_start(out=row[:, tt * 128:(tt + 1) * 128],
                  in_=col_tensor[:, tt:tt + 1])], deps=deps_per_tile[tt], tag=f"{name}r{tt}")
        steps.append(d)
    st = S.dma("sync", lambda e: [e.dma_start(out=scr, in_=row[:, :])], deps=steps, tag=name + "_st")
    ld = S.dma("sync", lambda e: [e.dma_start(out=bc[:, :], in_=scr.partition_broadcast(128))],
               deps=[st], tag=name + "_ld")
    return bc, ld


def emit_l0ffn_main(S, io):
    nt = T // 128
    xT_bf, xT_rdy = io["xT_pre"] if "xT_pre" in io else cast_load(S, "xTpb", io["xTp"], D, T + 1)
    stats = {}
    ln_fin = ln_stats_steps(S, io["x_tok"], nt, D, stats, tag="lnf")
    negm_bc, nmb = f32_row_bcast(S, stats["negm_c"], nt, "negmb", [[f[1]] for f in ln_fin])
    rstd_bc, rsb = f32_row_bcast(S, stats["rstd_c"], nt, "rstdb", [[f[0]] for f in ln_fin])
    h_scr = S.nc.dram_tensor("h_scr", [INTER, T], BF16, kind="Internal").ap()

    pool = [S.psum(f"pp{i}", [128, 512], F32) for i in range(8)]
    wslab = [S.sbuf(f"wslab{i}", [128, 32 * 128], BF16) for i in range(4)]
    cs_col = S.sbuf("cs_col", [128, 2], F32)
    tmp1 = S.sbuf("tmp1", [128, 512], F32)
    pre = [S.sbuf(f"pre{i}", [128, T], F32) for i in range(2)]
    hbf = S.sbuf("hbf", [128, T], BF16)
    prev = {}
    hsts = []
    TP = T + 1
    fusedw = "wgu_ag" in io
    for b in range(2 * RT_D):
        wp = b % 4
        jt = (b // 2) + (0 if b % 2 == 0 else RT_D)
        is_gate = (b % 2 == 0)
        if fusedw:
            wld = S.dma("sync", lambda e, jt=jt, wp=wp: [e.dma_start(
                        out=wslab[wp][:, r * 512:(r + 1) * 512],
                        in_=io["wgu_ag"][r * 172 + jt])
                        for r in range(8)],
                        n=8, deps=[prev.get(("w", wp))], tag=f"wld{b}")
        else:
            wld = S.dma("sync", lambda e, b=b, wp=wp: [e.dma_start(out=wslab[wp][:, :],
                        in_=io["wgu_s"][b])], deps=[prev.get(("w", wp))], tag=f"wld{b}")
        fpar = b % 2
        fp0, fp1, fpc = (pool[0], pool[1], pool[4][:, 0:1]) if fpar == 0 else (pool[2], pool[3], pool[4][:, 4:5])
        cs_c = cs_col[:, fpar:fpar + 1]

        def mmup(e, wp=wp, fp0=fp0, fp1=fp1, fpc=fpc):
            for th, fp in ((0, fp0), (1, fp1)):
                for kt in range(32):
                    e.matmul(fp[:, :], wslab[wp][:, kt * 128:(kt + 1) * 128],
                             xT_bf[:, kt * TP + th * 512:kt * TP + th * 512 + 512],
                             start=(kt == 0), stop=(kt == 31), skip_group_check=True)
            last = None
            for kt in range(32):
                last = e.matmul(fpc, wslab[wp][:, kt * 128:(kt + 1) * 128],
                                xT_bf[:, kt * TP + 1024:kt * TP + 1025], start=(kt == 0), stop=(kt == 31),
                                skip_group_check=True)
            return [last]
        mm = S.add("tensor", mmup, deps=[wld] + xT_rdy +
                   [prev.get(("evA0", fpar)), prev.get(("evA1", fpar)), prev.get(("cse", fpar))], tag=f"mmA{b}")
        prev[("w", wp)] = mm
        cse = S.add("vector", lambda e, cs_c=cs_c, fpc=fpc: [e.tensor_copy(cs_c, fpc)], deps=[mm], tag=f"cse{b}")
        prev[("cse", fpar)] = cse
        dst = pre[0] if is_gate else pre[1]
        evs = []
        for th in range(2):
            fpx = (fp0, fp1)[th]
            tA = S.add("vector", lambda e, th=th, fpx=fpx, cs_c=cs_c: [e.scalar_tensor_tensor(tmp1[:, :],
                        negm_bc[:, th * 512:(th + 1) * 512],
                        cs_c, fpx[:, :], ALU.mult, ALU.add)],
                       deps=[mm, cse, nmb, prev.get("tB")], tag=f"tA{b}_{th}")
            tB = S.add("vector", lambda e, th=th, dst=dst: [e.tensor_tensor(dst[:, th * 512:(th + 1) * 512],
                        tmp1[:, :], rstd_bc[:, th * 512:(th + 1) * 512], ALU.mult)],
                       deps=[tA, rsb, prev.get("dst_use") if is_gate else None], tag=f"tB{b}_{th}")
            prev["tB"] = tB
            prev[(f"evA{th}", fpar)] = tB
            evs.append(tB)
        if is_gate:
            sil = S.add("scalar", lambda e: [e.activation(pre[0][:, :], pre[0][:, :], AF.Silu)],
                        deps=evs, tag=f"sil{b}")
            prev["sil"] = sil
            prev["evA0"] = sil
            prev["evA1"] = sil
        else:
            hmu = S.add("vector", lambda e: [e.tensor_tensor(hbf[:, :], pre[0][:, :], pre[1][:, :], ALU.mult)],
                        deps=evs + [prev.get("sil"), prev.get("hst")], tag=f"hmu{b}")
            hst = S.dma("gpsimd", lambda e, jt0=b // 2: [e.dma_start(out=h_scr[jt0 * 128:(jt0 + 1) * 128, :], in_=hbf[:, :])],
                        deps=[hmu], tag=f"hst{b}")
            prev["hst"] = hst
            prev["dst_use"] = hmu
            hsts.append(hst)

    hld_b = [S.sbuf(f"hldb{i}", [128, T], BF16) for i in range(4)]
    wd_b = [S.sbuf(f"wdb{i}", [128, 512], BF16) for i in range(4)]
    oev = S.sbuf("oev", [128, 512], F32)
    outs = []
    for dtg in range(8):
        last_mm = None
        mm_prev = None
        for rt in range(RT_D):
            p = rt % 4
            hl = S.dma(("scalar", "gpsimd")[rt % 2], lambda e, rt=rt, p=p: [
                    e.dma_start(out=hld_b[p][0:64, :], in_=h_scr[128 * rt:128 * (rt + 1):2, :]),
                    e.dma_start(out=hld_b[p][64:128, :], in_=h_scr[128 * rt + 1:128 * (rt + 1):2, :])],
                    n=2, deps=hsts + [prev.get(("hb", p))], tag=f"hl{dtg}_{rt}")
            if fusedw:
                roff = (rt // 11) * 1408 + (rt % 11) * 128
                wl = S.dma("sync", lambda e, dtg=dtg, roff=roff, p=p: [e.dma_start(out=wd_b[p][:, :],
                        in_=io["wd_ag"][roff:roff + 128, dtg * 512:(dtg + 1) * 512])],
                        deps=[prev.get(("wb", p))], tag=f"wl{dtg}_{rt}")
            else:
                wl = S.dma("sync", lambda e, dtg=dtg, rt=rt, p=p: [e.dma_start(out=wd_b[p][:, :],
                        in_=io["wd_s"][dtg, rt])],
                        deps=[prev.get(("wb", p))], tag=f"wl{dtg}_{rt}")
            def mmdn(e, rt=rt, p=p):
                last = None
                for dt4 in range(4):
                    for th in range(2):
                        last = e.matmul(pool[dt4 * 2 + th][:, :], wd_b[p][:, dt4 * 128:(dt4 + 1) * 128],
                                        hld_b[p][:, th * 512:(th + 1) * 512],
                                        start=(rt == 0), stop=(rt == RT_D - 1))
                return [last]
            dd = [hl, wl] + ([mm_prev] if mm_prev else [])
            if rt == 0:
                dd += [prev.get(("evA0", 0)), prev.get(("evA1", 0)), prev.get(("evA0", 1)),
                       prev.get(("evA1", 1)), prev.get(("cse", 0)), prev.get(("cse", 1)), prev.get("bev")]
            mm = S.add("tensor", mmdn, deps=dd, tag=f"mmB{dtg}_{rt}")
            mm_prev = mm
            prev[("hb", p)] = mm
            prev[("wb", p)] = mm
            last_mm = mm
        bevs = []
        for dt4 in range(4):
            for th in range(2):
                ev = S.add("vector", lambda e, dt4=dt4, th=th: [e.tensor_copy(oev[:, :], pool[dt4 * 2 + th][:, :])],
                           deps=[last_mm, prev.get("ost_f2")], tag=f"bev{dtg}_{dt4}_{th}")
                st = S.dma("gpsimd", lambda e, dtg=dtg, dt4=dt4, th=th: [e.dma_start(
                        out=io["soT"][dtg * 512 + dt4 * 128:dtg * 512 + (dt4 + 1) * 128, th * 512:(th + 1) * 512],
                        in_=oev[:, :])], deps=[ev], tag=f"ost{dtg}_{dt4}_{th}")
                prev["ost_f2"] = st
                bevs.append(ev)
                outs.append(st)
        prev["bev"] = bevs[-1]
    return outs


def build_l0ffn_main_nc():
    nc = bass.Bass("TRN2", target_bir_lowering=False)
    io = {
        "xTp": nc.dram_tensor("xTp", [D, T + 1], F32, kind="ExternalInput").ap(),
        "x_tok": nc.dram_tensor("x_tok", [T, D], F32, kind="ExternalInput").ap(),
        "wgu_s": nc.dram_tensor("wgu_s", [2 * RT_D, 128, 32 * 128], BF16, kind="ExternalInput").ap(),
        "wd_s": nc.dram_tensor("wd_s", [8, RT_D, 128, 512], BF16, kind="ExternalInput").ap(),
        "soT": nc.dram_tensor("soT", [D, T], F32, kind="ExternalOutput").ap(),
    }
    S = Sched(nc)
    outs = emit_l0ffn_main(S, io)
    S.final_deps = outs
    S.final_engine = "gpsimd"
    S.emit()
    return nc


def emit_ln_residual(S, io, tag="lr"):
    nt = T // 128
    so = S.sbuf(tag + "_so", [128, D], F32)
    sq = S.sbuf(tag + "_sq", [128, D], BF16)
    xr = S.sbuf(tag + "_xr", [128, D], F32)
    m = S.sbuf(tag + "_m", [128, 1], F32); q = S.sbuf(tag + "_q", [128, 1], F32)
    v = S.sbuf(tag + "_v", [128, 1], F32); sd = S.sbuf(tag + "_sd", [128, 1], F32)
    r = S.sbuf(tag + "_r", [128, 1], F32)
    outs = []
    prev = {}
    for tt in range(nt):
        sld = S.dma("sync", lambda e, tt=tt: [e.dma_start(out=so[:, :], in_=io["so_tok"][tt * 128:(tt + 1) * 128, :])],
                    deps=[prev.get("oln")], tag=f"{tag}sld{tt}")
        xld = S.dma("gpsimd", lambda e, tt=tt: [e.dma_start(out=xr[:, :], in_=io["x_tok"][tt * 128:(tt + 1) * 128, :])],
                    deps=[prev.get("oadd")], tag=f"{tag}xld{tt}")
        rs = S.add("vector", lambda e: [e.reduce_sum(m[:, :], so[:, :], axis=AX)],
                   deps=[sld, prev.get("n1")], tag=f"{tag}rs{tt}")
        sqs = S.add("scalar", lambda e: [e.activation(sq[:, :], so[:, :], AF.Square)],
                    deps=[sld, prev.get("rq")], tag=f"{tag}sq{tt}")
        rq = S.add("vector", lambda e: [e.reduce_sum(q[:, :], sq[:, :], axis=AX)], deps=[sqs], tag=f"{tag}rq{tt}")
        prev["rq"] = rq
        n1 = S.add("vector", lambda e: [e.tensor_scalar(m[:, :], m[:, :], 1.0 / D, None, ALU.mult)],
                   deps=[rs], tag=f"{tag}n1{tt}")
        prev["n1"] = n1
        va = S.add("vector", lambda e: [e.tensor_scalar(v[:, :], m[:, :], m[:, :], None, ALU.mult)],
                   deps=[n1, prev.get("vc")], tag=f"{tag}va{tt}")
        vb = S.add("vector", lambda e: [e.scalar_tensor_tensor(v[:, :], q[:, :], 1.0 / D, v[:, :], ALU.mult, ALU.subtract)],
                   deps=[va, rq], tag=f"{tag}vb{tt}")
        vc = S.add("vector", lambda e: [e.tensor_scalar(v[:, :], v[:, :], EPS, None, ALU.add)],
                   deps=[vb, prev.get("sds")], tag=f"{tag}vc{tt}")
        prev["vc"] = vc
        sds = S.add("scalar", lambda e: [e.activation(sd[:, :], v[:, :], AF.Sqrt)], deps=[vc], tag=f"{tag}sd{tt}")
        prev["sds"] = sds
        rr = S.add("vector", lambda e: [e.reciprocal(r[:, :], sd[:, :])], deps=[sds, prev.get("oln")], tag=f"{tag}r{tt}")
        oln = S.add("vector", lambda e: [e.tensor_scalar(so[:, :], so[:, :], m[:, :], r[:, :], ALU.subtract, ALU.mult)],
                    deps=[rr, n1, rs, rq], tag=f"{tag}oln{tt}")
        prev["oln"] = oln
        oadd = S.add("vector", lambda e: [e.tensor_tensor(xr[:, :], xr[:, :], so[:, :], ALU.add)],
                     deps=[oln, xld, prev.get("ost")], tag=f"{tag}oadd{tt}")
        prev["oadd"] = oadd
        ost = S.dma("sync", lambda e, tt=tt: [e.dma_start(out=io["xout"][tt * 128:(tt + 1) * 128, :], in_=xr[:, :])],
                    deps=[oadd], tag=f"{tag}ost{tt}")
        prev["ost"] = ost
        outs.append(ost)
    return outs


def build_ln_residual_nc():
    nc = bass.Bass("TRN2", target_bir_lowering=False)
    io = {
        "so_tok": nc.dram_tensor("so_tok", [T, D], F32, kind="ExternalInput").ap(),
        "x_tok": nc.dram_tensor("x_tok", [T, D], F32, kind="ExternalInput").ap(),
        "xout": nc.dram_tensor("xout", [T, D], F32, kind="ExternalOutput").ap(),
    }
    S = Sched(nc)
    outs = emit_ln_residual(S, io)
    S.final_deps = outs
    S.final_engine = "gpsimd"
    S.emit()
    return nc


def gu_perm():
    p = np.zeros(D, np.int64)
    for g in range(32):
        for s in range(64):
            p[128 * g + s] = 128 * g + 2 * s
            p[128 * g + 64 + s] = 128 * g + 2 * s + 1
    return p


def pack_bytes_rows(qw):
    R, C = qw.shape
    return np.ascontiguousarray(qw.view(np.uint8).reshape(R, C, 4).transpose(0, 2, 1).reshape(4 * R, C))


TE = 1152
NH = 128
HD = 64
SS = 128
NG = 8
GATE = 8192
CONV_DIM = 10240
PROJ = 18560
NCH = 8


def emit_mamba_m1(S, io):
    ntE = TE // 128
    TP = TE + 1
    xT_bf, xT_rdy = cast_load(S, "xTm", io["xT"], D, TP)
    stats = {}
    ln_fin = ln_stats_steps(S, io["x_tok"], ntE, D, stats, tag="lnm")
    negm_row = stats["negm_row"]
    rstd_bc, rsb = f32_row_bcast(S, stats["rstd_c"], ntE, "rstdm", [[f[0]] for f in ln_fin])
    negm_bc, nmb = f32_row_bcast(S, stats["negm_c"], ntE, "negmm", [[f[1]] for f in ln_fin])

    pool = [S.psum(f"mp{i}", [128, 512], F32) for i in range(3)]
    aux = S.psum("mpaux", [128, 512], F32)
    trfb = S.psum("mtrfb", [128, 512], F32)
    trbb = S.psum("mtrbb", [128, 1024], BF16)
    ps_csr = trfb[0:1, 384:512]
    pools1 = [(pool[0], pool[1], pool[2][:, 0:128]),
              (aux, trbb.bitcast(F32), trfb[:, 128:256])]
    cscols1 = [pool[2][:, 128:129], trfb[:, 256:257]]
    wblk = [S.sbuf(f"mwb{i}", [128, D // 4], F32) for i in range(6)]
    wblk_bf = S.sbuf("mwbf", [128, D], BF16)
    cs_row = S.sbuf("mcs_row", [1, 128], BF16)
    pre = S.sbuf("mpre", [128, TE], F32)
    cacc = S.sbuf("mcacc", [128, T], F32)
    obuf = S.sbuf("mobuf", [128, T], BF16)
    convw = S.sbuf("convw", [128, 320], F32)
    convb = S.sbuf("convb", [128, 80], F32)
    dtb = S.sbuf("dtb", [128, 1], F32)
    dtv = S.sbuf("dtv", [128, T], F32)
    lndt = S.sbuf("lndt", [128, T], F32)
    dtt1 = cacc
    ones_f = S.sbuf("ones_f", [1, 128], F32)
    ones_bf = S.sbuf("ones_bfm", [128, 1], BF16)

    cwl = S.dma("sync", lambda e: [e.dma_start(out=convw[:, :], in_=io["conv_w"])], tag="cwl")
    cbl = S.dma("sync", lambda e: [e.dma_start(out=convb[:, :], in_=io["conv_b"])], tag="cbl")
    dtbl = S.dma("sync", lambda e: [e.dma_start(out=dtb[:, :], in_=io["dt_bias"])], tag="dtbl")
    of = S.add("vector", lambda e: [e.memset(ones_f[:, :], 1.0)], tag="onesf")
    ob1 = S.add("vector", lambda e: [e.memset(ones_bf[:, :], 1.0)], tag="onesb")

    ln_all = [x for f in ln_fin for x in f]
    prev = {}
    dt_done = [None]
    for cb in range(145):
        wp = 0
        wcs = None
        wcss = []
        for qq in range(4):
            wb3 = (cb * 4 + qq) % 6
            qname = ("sync", "scalar", "gpsimd")[qq % 3]
            wldq = S.dma(qname, lambda e, cb=cb, qq=qq, wb3=wb3: [e.dma_start(out=wblk[wb3][:, :],
                         in_=io["wip_s"][cb, :, qq * 1024:(qq + 1) * 1024])],
                         deps=[prev.get(("wb", wb3))], tag=f"mwld{cb}_{qq}")
            wcs = S.add("vector", lambda e, qq=qq, wb3=wb3: [e.tensor_copy(wblk_bf[:, qq * 1024:(qq + 1) * 1024], wblk[wb3][:, :])],
                        deps=[wldq] + ([prev.get("wbf_use")] if qq == 0 else []), tag=f"mwcs{cb}_{qq}")
            prev[("wb", wb3)] = wcs
            wcss.append(wcs)
        wcs = wcss[-1]
        par = cb % 2
        pset = pools1[par]
        cscol = cscols1[par]

        def mmain(e, pset=pset, cscol=cscol):
            last = None
            for kt in range(32):
                e.matmul(cscol, wblk_bf[:, kt * 128:(kt + 1) * 128], ones_bf[:, :],
                         start=(kt == 0), stop=(kt == 31), skip_group_check=True)
            for si, (t0, tw) in enumerate(((0, 512), (512, 512), (1024, 128))):
                for kt in range(32):
                    last = e.matmul(pset[si][:, 0:tw], wblk_bf[:, kt * 128:(kt + 1) * 128],
                                    xT_bf[:, kt * TP + t0:kt * TP + t0 + tw],
                                    start=(kt == 0), stop=(kt == 31), skip_group_check=True)
            return [last]
        mm = S.add("tensor", mmain, deps=[wcs, ob1] + xT_rdy + [prev.get(("ev", par))], tag=f"mmm{cb}")
        prev["wbf_use"] = mm

        ev1 = S.add("vector", lambda e, pset=pset, cscol=cscol: [[e.scalar_tensor_tensor(
                    pre[:, t0:t0 + tw], negm_bc[:, t0:t0 + tw], cscol, pset[si][:, 0:tw],
                    ALU.mult, ALU.add) for si, (t0, tw) in enumerate(((0, 512), (512, 512), (1024, 128)))][-1]],
                    deps=[mm, nmb, prev.get("pre_done")], tag=f"mev1_{cb}")
        ev = S.add("vector", lambda e: [[e.tensor_tensor(pre[:, t0:t0 + tw], pre[:, t0:t0 + tw],
                   rstd_bc[:, t0:t0 + tw], ALU.mult) for (t0, tw) in ((0, 512), (512, 512), (1024, 128))][-1]],
                   deps=[ev1, rsb], tag=f"mev{cb}")
        prev[("ev", par)] = ev
        if cb < 64:
            sil = S.add("scalar", lambda e: [e.activation(obuf[:, :], pre[:, 128:1152], AF.Silu)],
                        deps=[ev, prev.get("obuf_use")], tag=f"msil{cb}")
            st = S.dma("gpsimd", lambda e, cb=cb: [e.dma_start(out=io["gateT"][cb * 128:(cb + 1) * 128, :], in_=obuf[:, :])],
                       deps=[sil], tag=f"mgst{cb}")
            prev["obuf_use"] = st
            prev["pre_done"] = sil
        elif cb < 144:
            ct = cb - 64
            def conv(e, ct=ct):
                e.tensor_scalar(cacc[:, :], pre[:, 125:1149], convw[:, 4 * ct:4 * ct + 1], None, ALU.mult)
                return [e.tensor_scalar(cacc[:, :], cacc[:, :], convb[:, ct:ct + 1], None, ALU.add)]
            def conv_mid(e, ct=ct, k=0):
                return [e.scalar_tensor_tensor(cacc[:, :], pre[:, 125 + k:1149 + k],
                        convw[:, 4 * ct + k:4 * ct + k + 1], cacc[:, :], ALU.mult, ALU.add)]
            c0 = S.add("vector", lambda e, ct=ct: [e.tensor_scalar(cacc[:, :], pre[:, 125:1149],
                       convw[:, 4 * ct:4 * ct + 1], None, ALU.mult)],
                       deps=[ev, cwl, prev.get("sil_c")], tag=f"mc0{cb}")
            cks = c0
            for k in (1, 2, 3):
                cks = S.add("vector", lambda e, ct=ct, k=k: [e.scalar_tensor_tensor(cacc[:, :],
                            pre[:, 125 + k:1149 + k], convw[:, 4 * ct + k:4 * ct + k + 1],
                            cacc[:, :], ALU.mult, ALU.add)], deps=[cks], tag=f"mck{cb}_{k}")
            cb_add = S.add("vector", lambda e, ct=ct: [e.tensor_scalar(cacc[:, :], cacc[:, :],
                           convb[:, ct:ct + 1], None, ALU.add)], deps=[cks, cbl], tag=f"mcb{cb}")
            sil = S.add("scalar", lambda e: [e.activation(obuf[:, :], cacc[:, :], AF.Silu)],
                        deps=[cb_add, prev.get("obuf_use")], tag=f"msilc{cb}")
            prev["sil_c"] = sil
            st = S.dma("gpsimd", lambda e, ct=ct: [e.dma_start(out=io["xbc_conv"][ct * 128:(ct + 1) * 128, :], in_=obuf[:, :])],
                       deps=[sil], tag=f"mxst{cb}")
            prev["obuf_use"] = st
            prev["pre_done"] = cb_add
        else:
            d0 = S.add("vector", lambda e: [e.tensor_scalar(dtt1[:, :], pre[:, 128:1152], dtb[:, :], None, ALU.add)],
                       deps=[ev, dtbl], tag="mdt0")
            d1 = S.add("scalar", lambda e: [e.activation(dtv[:, :], dtt1[:, :], AF.Abs)], deps=[d0], tag="mdt1")
            d2 = S.add("scalar", lambda e: [e.activation(dtv[:, :], dtv[:, :], AF.Exp, scale=-1.0)], deps=[d1], tag="mdt2")
            d3 = S.add("vector", lambda e: [e.tensor_scalar(dtv[:, :], dtv[:, :], 1.0, None, ALU.add)], deps=[d2], tag="mdt3")
            d4 = S.add("scalar", lambda e: [e.activation(dtv[:, :], dtv[:, :], AF.Ln)], deps=[d3], tag="mdt4")
            d5 = S.add("vector", lambda e: [e.tensor_scalar(dtt1[:, :], dtt1[:, :], 0.0, None, ALU.max)], deps=[d4, d0], tag="mdt5")
            d6 = S.add("vector", lambda e: [e.tensor_tensor(dtv[:, :], dtv[:, :], dtt1[:, :], ALU.add)], deps=[d5], tag="mdt6")
            d7 = S.add("vector", lambda e: [e.tensor_scalar(dtv[:, :], dtv[:, :], 0.001, 100.0, ALU.max, ALU.min)],
                       deps=[d6], tag="mdt7")
            d8 = S.add("scalar", lambda e: [e.activation(lndt[:, :], dtv[:, :], AF.Ln)], deps=[d7], tag="mdt8")
            dt_done[0] = d8
            prev["pre_done"] = d0

    A_col = S.sbuf("A_col_t", [128, 1], F32)
    D_rep = S.sbuf("D_rep_t", [128, 128], F32)
    triT = S.sbuf("triT_t", [128, 128], BF16)
    identb = S.sbuf("identb", [128, 128], BF16)
    identf = S.sbuf("identf_t", [128, 128], F32)
    al = S.dma("sync", lambda e: [e.dma_start(out=A_col[:, :], in_=io["A_col"])], tag="al")
    dl = S.dma("sync", lambda e: [e.dma_start(out=D_rep[:, :], in_=io["D_rep"])], tag="dl")
    tl = S.dma("sync", lambda e: [e.dma_start(out=triT[:, :], in_=io["triT"])], tag="tl")
    il = S.dma("sync", lambda e: [e.dma_start(out=identb[:, :], in_=io["ident"])], tag="il")
    ifl = S.dma("sync", lambda e: [e.dma_start(out=identf[:, :], in_=io["identf"])], tag="ifl")

    R = xT_bf[:, 0:16384].bitcast(F32)
    R_bf = xT_bf[:, 16384:24576]
    P_run = S.sbuf("P_run", [128, 1], F32)
    cdprefs = S.sbuf("cdprefs", [128, NCH], F32)
    r0 = S.add("vector", lambda e: [e.memset(R[:, :], 0.0)], deps=[dt_done[0]], tag="r0")
    p0 = S.add("vector", lambda e: [e.memset(P_run[:, :], 1.0)], tag="p0")
    rb0 = S.add("vector", lambda e: [e.memset(R_bf[:, :], 0.0)], deps=[dt_done[0]], tag="rb0")

    cs_hl = S.sbuf("cs_hl", [128, 128], F32)
    lncs_hl = S.sbuf("lncs_hl", [128, 128], F32)
    csrow = [S.sbuf("csrow", [1, 512], F32), xT_bf[0:1, 34816:35840].bitcast(F32)]
    lncsrow = [S.sbuf("lncsrow", [1, 512], F32), xT_bf[0:1, 35840:36864].bitcast(F32)]
    cslast = S.sbuf("cslast", [128, 1], F32)
    cd_col = S.sbuf("cd_col", [128, 1], F32)
    cd_row = S.sbuf("cd_row", [1, 128], F32)
    cd_bc = S.sbuf("cd_bc", [128, 128], F32)
    ds_hl = S.sbuf("ds_hl", [128, 128], F32)
    dtds_hl = S.sbuf("dtds_hl", [128, 128], F32)
    csT = S.sbuf("csT", [128, 128], F32)
    dtdsT = S.sbuf("dtdsT", [128, 128], F32)
    excsT = S.sbuf("excsT", [128, 128], F32)
    BT = S.sbuf("BT", [128, 8 * 128], BF16)
    CT = S.sbuf("CTm", [128, 8 * 128], BF16)
    Btok = S.sbuf("Btok", [128, 8 * 128], BF16)
    sgm = S.sbuf("sgm", [128, 8 * 128], BF16)
    sgm4 = S.sbuf("sgm4", [128, 512], BF16)
    xchan = [S.sbuf(f"xchan{i}", [128, 512], BF16) for i in range(2)]
    xtok = xT_bf[:, 24576:32768]
    seg_sb = [S.sbuf("seg_sb", [128, 512], BF16), xT_bf[:, 32768:33280]]
    expseg = [S.sbuf("expseg", [128, 512], BF16), xT_bf[:, 33280:33792]]
    scoresM = [S.sbuf("scoresM", [128, 512], BF16), xT_bf[:, 33792:34304]]
    B4 = [S.sbuf("B4", [128, 512], BF16), xT_bf[:, 34304:34816]]
    tmp_y = S.sbuf("tmp_y", [128, 256], F32)
    tmp2_y = S.sbuf("tmp2_y", [128, 256], F32)
    ypre = [S.sbuf(f"ypre{i}", [128, 256], BF16) for i in range(2)]

    ps_seg = [pool[0], S.psum("ps_segB", [128, 512], F32)]
    ps_yyo = [pool[1], S.psum("ps_yyoB", [128, 512], F32)]
    ps_st = [pool[2][:, 0:256], pool[2][:, 256:512]]
    ps_trf = trfb[:, 0:128]
    ps_small = aux[:, 256:512]
    ps_trb = trbb[:, 0:512]

    m1outs = []
    xbc_done = prev.get("obuf_use")
    gate_done = prev.get("obuf_use")
    for ch in range(NCH):
        t0 = ch * 128
        sc0 = S.add("vector", lambda e, t0=t0: [e.tensor_tensor_scan(cs_hl[:, :], dtv[:, t0:t0 + 128],
                    dtv[:, t0:t0 + 128], 0.0, ALU.add, ALU.bypass)],
                    deps=[dt_done[0], prev.get("cs_use")], tag=f"sc0_{ch}")
        sc1 = S.add("vector", lambda e: [e.tensor_scalar(cs_hl[:, :], cs_hl[:, :], A_col[:, :], None, ALU.mult)],
                    deps=[sc0, al], tag=f"sc1_{ch}")
        cl0 = S.add("vector", lambda e: [e.tensor_copy(cslast[:, :], cs_hl[:, 127:128])],
                    deps=[sc1, prev.get("cslast_use")], tag=f"cl0_{ch}")
        cd0 = S.add("scalar", lambda e: [e.activation(cd_col[:, :], cslast[:, :], AF.Exp)],
                    deps=[cl0, prev.get("cd_use")], tag=f"cd0_{ch}")
        cp0 = S.add("vector", lambda e, ch=ch: [e.tensor_copy(cdprefs[:, ch:ch + 1], P_run[:, :])],
                    deps=[p0, prev.get("pr_upd")], tag=f"cp0_{ch}")
        pr1 = S.add("vector", lambda e: [e.tensor_tensor(P_run[:, :], P_run[:, :], cd_col[:, :], ALU.mult)],
                    deps=[cp0, cd0], tag=f"pr1_{ch}")
        prev["pr_upd"] = pr1
        lc0 = S.add("vector", lambda e, t0=t0: [e.tensor_tensor(lncs_hl[:, :], lndt[:, t0:t0 + 128], cs_hl[:, :], ALU.subtract)],
                    deps=[sc1, prev.get("lncs_use")], tag=f"lc0_{ch}")

        fr3 = S.dma("gpsimd", lambda e: [e.dma_start(out=cd_row[:, :], in_=cd_col[:, :])],
                    deps=[cd0, prev.get("cdrow_use")], tag=f"fr3_{ch}")
        cdb = S.add("tensor", lambda e: [e.matmul(ps_small[:, 128:256], ones_f[:, :], cd_row[:, :],
                    start=True, stop=True, skip_group_check=True)],
                    deps=[fr3, of, prev.get("cdbc_use")], tag=f"cdb_{ch}")
        cdbc = S.add("vector", lambda e: [e.tensor_copy(cd_bc[:, :], ps_small[:, 128:256])], deps=[cdb], tag=f"cdbc_{ch}")
        ds0 = S.add("vector", lambda e: [e.tensor_scalar(ds_hl[:, :], cs_hl[:, :], cslast[:, :], None, ALU.subtract)],
                    deps=[sc1, cl0, prev.get("ds_use")], tag=f"ds0_{ch}")
        ds1 = S.add("scalar", lambda e: [e.activation(ds_hl[:, :], ds_hl[:, :], AF.Exp, scale=-1.0)],
                    deps=[ds0], tag=f"ds1_{ch}")
        dd0 = S.add("vector", lambda e, t0=t0: [e.tensor_tensor(dtds_hl[:, :], dtv[:, t0:t0 + 128], ds_hl[:, :], ALU.mult)],
                    deps=[ds1, prev.get("dtds_use")], tag=f"dd0_{ch}")
        tr1 = S.add("tensor", lambda e: [e.transpose(ps_trf[:, :], cs_hl[:, :], identf[:, :])],
                    deps=[sc1, ifl, prev.get("trf_ev")], tag=f"tr1_{ch}")
        te1 = S.add("vector", lambda e: [e.tensor_copy(csT[:, :], ps_trf[:, :])],
                    deps=[tr1, prev.get("csT_use")], tag=f"te1_{ch}")
        tr2 = S.add("tensor", lambda e: [e.transpose(ps_trf[:, :], dtds_hl[:, :], identf[:, :])],
                    deps=[dd0, te1], tag=f"tr2_{ch}")
        te2 = S.add("vector", lambda e: [e.tensor_copy(dtdsT[:, :], ps_trf[:, :])],
                    deps=[tr2, prev.get("dtdsT_use")], tag=f"te2_{ch}")
        prev["trf_ev"] = te2
        ex0 = S.add("scalar", lambda e: [e.activation(excsT[:, :], csT[:, :], AF.Exp)],
                    deps=[te1, prev.get("excs_use")], tag=f"ex0_{ch}")
        exst = S.dma("gpsimd", lambda e, ch=ch: [e.dma_start(out=io["excs"][:, ch * 128:(ch + 1) * 128], in_=excsT[:, :])],
                     deps=[ex0], tag=f"exst_{ch}")
        gprep = {}
        for g in range(8):
            bld = S.dma("sync", lambda e, g=g, t0=t0: [e.dma_start(out=BT[:, g * 128:(g + 1) * 128],
                        in_=io["xbc_conv"][8192 + g * 128:8192 + (g + 1) * 128, t0:t0 + 128])],
                        deps=[xbc_done, prev.get(("BTu", g))], tag=f"bld{ch}_{g}")
            cld = S.dma("sync", lambda e, g=g, t0=t0: [e.dma_start(out=CT[:, g * 128:(g + 1) * 128],
                        in_=io["xbc_conv"][9216 + g * 128:9216 + (g + 1) * 128, t0:t0 + 128])],
                        deps=[xbc_done, prev.get(("CTu", g))], tag=f"cld{ch}_{g}")
            scm = S.add("tensor", lambda e, g=g: [e.matmul(ps_small[:, 0:128], BT[:, g * 128:(g + 1) * 128],
                        CT[:, g * 128:(g + 1) * 128], start=True, stop=True, skip_group_check=True)],
                        deps=[bld, cld, prev.get("scm_ev")], tag=f"scm{ch}_{g}")
            sce = S.add("vector", lambda e, g=g: [e.tensor_tensor(sgm[:, g * 128:(g + 1) * 128],
                        ps_small[:, 0:128], triT[:, :], ALU.mult)],
                        deps=[scm, tl, prev.get(("sgm_u", g))], tag=f"sce{ch}_{g}")
            prev["scm_ev"] = sce
            btr = S.add("tensor", lambda e, g=g: [e.transpose(ps_trb[:, 0:128], BT[:, g * 128:(g + 1) * 128], identb[:, :])],
                        deps=[bld, il, prev.get("trb_use")], tag=f"btr{ch}_{g}")
            bte = S.add("vector", lambda e, g=g: [e.tensor_copy(Btok[:, g * 128:(g + 1) * 128], ps_trb[:, 0:128])],
                        deps=[btr, prev.get(("Btok_u", g))], tag=f"bte{ch}_{g}")
            prev["trb_use"] = bte
            gprep[g] = (sce, bte, bld, cld)

        xt_evs = []
        for xq in range(16):
            p = xq % 2
            def xldf(e, xq=xq, p=p, t0=t0):
                return [e.dma_start(out=xchan[p][:, i * 128:(i + 1) * 128],
                        in_=io["xbc_conv"][(xq * 4 + i) * 128:(xq * 4 + i + 1) * 128, t0:t0 + 128])
                        for i in range(4)]
            xl = S.dma("sync", xldf, n=4, deps=[xbc_done, prev.get(("xch", p))], tag=f"xl{ch}_{xq}")
            def xtrf(e, p=p):
                return [e.transpose(ps_trb[:, 128 + i * 128:128 + (i + 1) * 128],
                        xchan[p][:, i * 128:(i + 1) * 128], identb[:, :]) for i in range(3)]
            xt1 = S.add("tensor", xtrf, n_instr=3, deps=[xl, il, prev.get("trb_use")], tag=f"xt1_{ch}_{xq}")
            xe1 = S.add("vector", lambda e, xq=xq: [e.tensor_copy(xtok[:, xq * 512:xq * 512 + 384], ps_trb[:, 128:512])],
                        deps=[xt1, prev.get("xtok_use")], tag=f"xe1_{ch}_{xq}")
            xt2 = S.add("tensor", lambda e, p=p: [e.transpose(ps_trb[:, 0:128], xchan[p][:, 384:512], identb[:, :])],
                        deps=[xl, il, prev.get("trb_use")], tag=f"xt2_{ch}_{xq}")
            xe2 = S.add("vector", lambda e, xq=xq: [e.tensor_copy(xtok[:, xq * 512 + 384:(xq + 1) * 512], ps_trb[:, 0:128])],
                        deps=[xt2, prev.get("xtok_use")], tag=f"xe2_{ch}_{xq}")
            prev["trb_use"] = xe2
            prev[("xch", p)] = xe2
            xt_evs += [xe1, xe2]

        rbf = S.add("vector", lambda e: [e.tensor_copy(R_bf[:, :], R[:, :])],
                    deps=[r0, rb0, prev.get("R_upd"), prev.get("Rbf_use")], tag=f"rbf_{ch}")

        ylast = []
        rlast = []
        for hb in range(32):
            g = hb // 4
            sce, bte, bld, cld = gprep[g]
            h0 = hb * 4
            if hb % 4 == 0:
                s4 = S.add("vector", lambda e, g=g: [e.tensor_copy(sgm4[:, :],
                           sgm[:, g * 128:(g + 1) * 128].unsqueeze(1).broadcast_to([128, 4, 128]))],
                           deps=[sce, prev.get("sgm4_use")], tag=f"s4_{ch}_{hb}")
                prev["sgm4_cur"] = s4
            s4 = prev["sgm4_cur"]
            pb = hb % 2
            fr1 = S.dma(("gpsimd", "scalar")[pb], lambda e, h0=h0, pb=pb: [e.dma_start(out=csrow[pb][:, :],
                        in_=cs_hl[h0:h0 + 4, :])],
                        deps=[sc1, prev.get(("csrow_use", pb))], tag=f"fr1_{ch}_{hb}")
            fr2 = S.dma(("gpsimd", "scalar")[pb], lambda e, h0=h0, pb=pb: [e.dma_start(out=lncsrow[pb][:, :],
                        in_=lncs_hl[h0:h0 + 4, :])],
                        deps=[lc0, prev.get(("lncsrow_use", pb))], tag=f"fr2_{ch}_{hb}")
            def segmm(e, pb=pb):
                last = None
                for hh in range(4):
                    e.matmul(ps_seg[pb][:, hh * 128:(hh + 1) * 128], ones_f[:, :],
                             csrow[pb][:, hh * 128:(hh + 1) * 128], start=True, stop=False, skip_group_check=True)
                    last = e.matmul(ps_seg[pb][:, hh * 128:(hh + 1) * 128], lncsrow[pb][:, hh * 128:(hh + 1) * 128],
                                    ones_f[:, :], start=False, stop=True, skip_group_check=True)
                return [last]
            sg1 = S.add("tensor", segmm, deps=[fr1, fr2, of, prev.get(("seg_ev", pb))], tag=f"sg1_{ch}_{hb}")
            prev[("csrow_use", pb)] = sg1
            prev[("lncsrow_use", pb)] = sg1
            sg2 = S.add("vector", lambda e, pb=pb: [e.tensor_scalar(seg_sb[pb][:, :], ps_seg[pb][:, :], 4.7, None, ALU.min)],
                        deps=[sg1, prev.get(("seg_sb_use", pb))], tag=f"sg2_{ch}_{hb}")
            prev[("seg_ev", pb)] = sg2
            sg3 = S.add("scalar", lambda e, pb=pb: [e.activation(expseg[pb][:, :], seg_sb[pb][:, :], AF.Exp)],
                        deps=[sg2, prev.get(("expseg_use", pb))], tag=f"sg3_{ch}_{hb}")
            prev[("seg_sb_use", pb)] = sg3
            sg4 = S.add("vector", lambda e, pb=pb: [e.tensor_tensor(scoresM[pb][:, :], sgm4[:, :], expseg[pb][:, :], ALU.mult)],
                        deps=[sg3, s4, prev.get(("scoresM_use", pb))], tag=f"sg4_{ch}_{hb}")
            prev[("expseg_use", pb)] = sg4
            prev["sgm4_use"] = sg4 if hb % 4 == 3 else prev.get("sgm4_use")
            b4s = S.add("vector", lambda e, h0=h0, g=g, pb=pb: [[e.tensor_scalar(B4[pb][:, hh * 128:(hh + 1) * 128],
                        Btok[:, g * 128:(g + 1) * 128], dtdsT[:, h0 + hh:h0 + hh + 1], None, ALU.mult)
                        for hh in range(4)][-1]],
                        deps=[bte, te2, prev.get(("B4_use", pb))], tag=f"b4_{ch}_{hb}")
            def ymm(e, h0=h0, g=g, pb=pb):
                last = None
                for hh in range(4):
                    h = h0 + hh
                    e.matmul(ps_yyo[pb][:, hh * 64:(hh + 1) * 64], scoresM[pb][:, hh * 128:(hh + 1) * 128],
                             xtok[:, h * 64:(h + 1) * 64], start=True, stop=True, skip_group_check=True)
                    last = e.matmul(ps_yyo[pb][:, 256 + hh * 64:256 + (hh + 1) * 64], CT[:, g * 128:(g + 1) * 128],
                                    R_bf[:, h * 64:(h + 1) * 64], start=True, stop=True, skip_group_check=True)
                return [last]
            ym = S.add("tensor", ymm, deps=[sg4, cld, rbf] + xt_evs + [prev.get(("yev", pb))], tag=f"ymm_{ch}_{hb}")
            prev[("scoresM_use", pb)] = ym
            def stmm(e, h0=h0, pb=pb):
                last = None
                for hh in range(4):
                    h = h0 + hh
                    last = e.matmul(ps_st[pb][:, hh * 64:(hh + 1) * 64], B4[pb][:, hh * 128:(hh + 1) * 128],
                                    xtok[:, h * 64:(h + 1) * 64], start=True, stop=True, skip_group_check=True)
                return [last]
            sm = S.add("tensor", stmm, deps=[b4s] + xt_evs + [prev.get(("rup", pb))], tag=f"stmm_{ch}_{hb}")
            prev[("B4_use", pb)] = sm
            ev1 = S.add("vector", lambda e, h0=h0, pb=pb: [[e.tensor_scalar(tmp_y[:, hh * 64:(hh + 1) * 64],
                        ps_yyo[pb][:, 256 + hh * 64:256 + (hh + 1) * 64], excsT[:, h0 + hh:h0 + hh + 1], None, ALU.mult)
                        for hh in range(4)][-1]],
                        deps=[ym, ex0], tag=f"ev1_{ch}_{hb}")
            ev2 = S.add("vector", lambda e, h0=h0, pb=pb: [[e.scalar_tensor_tensor(tmp2_y[:, hh * 64:(hh + 1) * 64],
                        xtok[:, (h0 + hh) * 64:(h0 + hh + 1) * 64], D_rep[:, h0 + hh:h0 + hh + 1],
                        ps_yyo[pb][:, hh * 64:(hh + 1) * 64], ALU.mult, ALU.add)
                        for hh in range(4)][-1]],
                        deps=[ym, dl] + xt_evs, tag=f"ev2_{ch}_{hb}")
            prev[("yev", pb)] = ev2
            yb = hb % 2
            ev3 = S.add("vector", lambda e, yb=yb: [e.tensor_tensor(ypre[yb][:, :],
                        tmp_y[:, :], tmp2_y[:, :], ALU.add)],
                        deps=[ev1, ev2, prev.get(("ypre_use", yb))], tag=f"ev3_{ch}_{hb}")
            yst1 = S.dma("gpsimd", lambda e, t0=t0, hb=hb, yb=yb: [e.dma_start(
                        out=io["y_pre"][t0:t0 + 128, hb * 256:(hb + 1) * 256], in_=ypre[yb][:, :])],
                        deps=[ev3], tag=f"yst_{ch}_{hb}")
            prev[("ypre_use", yb)] = yst1
            ylast.append(yst1)
            rup = S.add("vector", lambda e, h0=h0, pb=pb: [[e.scalar_tensor_tensor(R[:, (h0 + hh) * 64:(h0 + hh + 1) * 64],
                        R[:, (h0 + hh) * 64:(h0 + hh + 1) * 64], cd_bc[:, h0 + hh:h0 + hh + 1],
                        ps_st[pb][:, hh * 64:(hh + 1) * 64], ALU.mult, ALU.add) for hh in range(4)][-1]],
                        deps=[sm, cdbc, rbf, r0], tag=f"rup_{ch}_{hb}")
            prev[("rup", pb)] = rup
            rlast.append(rup)
        m1outs.append(ylast[-1])
        prev["R_upd"] = rlast[-1]
        prev["Rbf_use"] = ylast[-1]
        prev["cs_use"] = te2
        prev["cslast_use"] = ds0
        prev["cd_use"] = pr1
        prev["lncs_use"] = fr2
        prev["cdrow_use"] = cdb
        for pbx in range(2):
            prev[("csrow_use", pbx)] = ylast[-1]
            prev[("lncsrow_use", pbx)] = ylast[-1]
        prev["cdbc_use"] = rlast[-1]
        prev["ds_use"] = dd0
        prev["dtds_use"] = te2
        prev["csT_use"] = ex0
        prev["dtdsT_use"] = rlast[-1]
        prev["excs_use"] = ylast[-1]
        for g in range(8):
            prev[("BTu", g)] = ylast[-1]
            prev[("CTu", g)] = ylast[-1]
            prev[("sgm_u", g)] = ylast[-1]
            prev[("Btok_u", g)] = ylast[-1]
        prev["xtok_use"] = rlast[-1]
    o1 = S.dma("sync", lambda e: [e.dma_start(out=io["R_agg"], in_=R[:, :])], deps=rlast, tag="o_R")
    o2 = S.dma("sync", lambda e: [e.dma_start(out=io["P_core"], in_=P_run[:, :])], deps=[prev["pr_upd"]], tag="o_P")
    o3 = S.dma("sync", lambda e: [e.dma_start(out=io["cdpref"], in_=cdprefs[:, :])], deps=[prev["pr_upd"]], tag="o_cd")
    return m1outs + [o1, o2, o3, exst]


def build_mamba_m1_nc():
    nc = bass.Bass("TRN2", target_bir_lowering=False)
    io = {
        "xT": nc.dram_tensor("xT", [D, TE + 1], F32, kind="ExternalInput").ap(),
        "x_tok": nc.dram_tensor("x_tok", [TE, D], F32, kind="ExternalInput").ap(),
        "wip_s": nc.dram_tensor("wip_s", [145, 128, D], F32, kind="ExternalInput").ap(),
        "conv_w": nc.dram_tensor("conv_w", [128, 320], F32, kind="ExternalInput").ap(),
        "conv_b": nc.dram_tensor("conv_b", [128, 80], F32, kind="ExternalInput").ap(),
        "dt_bias": nc.dram_tensor("dt_bias", [128, 1], F32, kind="ExternalInput").ap(),
        "A_col": nc.dram_tensor("A_col", [128, 1], F32, kind="ExternalInput").ap(),
        "D_rep": nc.dram_tensor("D_rep", [128, 128], F32, kind="ExternalInput").ap(),
        "triT": nc.dram_tensor("triT", [128, 128], BF16, kind="ExternalInput").ap(),
        "ident": nc.dram_tensor("ident", [128, 128], BF16, kind="ExternalInput").ap(),
        "identf": nc.dram_tensor("identf", [128, 128], F32, kind="ExternalInput").ap(),
        "gateT": nc.dram_tensor("gateT", [GATE, T], BF16, kind="ExternalOutput").ap(),
        "xbc_conv": nc.dram_tensor("xbc_conv", [CONV_DIM, T], BF16, kind="ExternalOutput").ap(),
        "y_pre": nc.dram_tensor("y_pre", [T, GATE], BF16, kind="ExternalOutput").ap(),
        "R_agg": nc.dram_tensor("R_agg", [128, 8192], F32, kind="ExternalOutput").ap(),
        "P_core": nc.dram_tensor("P_core", [128, 1], F32, kind="ExternalOutput").ap(),
        "cdpref": nc.dram_tensor("cdpref", [128, NCH], F32, kind="ExternalOutput").ap(),
        "excs": nc.dram_tensor("excs", [128, NCH * 128], F32, kind="ExternalOutput").ap(),
    }
    S = Sched(nc)
    outs = emit_mamba_m1(S, io)
    S.final_deps = outs
    S.final_engine = "gpsimd"
    S.emit()
    return nc


def emit_mamba_m2(S, io):
    nc = S.nc
    S0 = S.sbuf("S0", [128, 8192], F32)
    buf_a = S.sbuf("buf_a", [128, 8192], F32)
    buf_r = S.sbuf("buf_r", [128, 8192], F32)
    S0_bf = S.sbuf("S0_bf", [128, 8192], BF16)
    ypre_t = S.sbuf("ypre_t", [128, 8192], BF16)
    gate_t = S.sbuf("gate_t", [128, 8192], BF16)
    CTm2 = S.sbuf("CTm2", [128, 8 * 128], BF16)
    excs2 = S.sbuf("excs2", [128, NCH * 128], F32)
    nw_bc = S.sbuf("nw_bc", [128, 8192], F32)
    identb2 = S.sbuf("identb2", [128, 128], BF16)
    tmp_yc = S.sbuf("tmp_yc", [128, 256], F32)
    qg = S.sbuf("qg", [128, 1], F32)
    sqg = S.sbuf("sqg", [128, 1024], BF16)
    ytb = S.sbuf("ytb", [128, 512], BF16)
    gtb = [S.sbuf(f"gtb{i}", [128, 512], BF16) for i in range(2)]
    rstdg = S.sbuf("rstdg", [128, 1], F32)
    ps_pool = [S.psum(f"m2p{i}", [128, 512], F32) for i in range(7)]
    ps_tr2 = S.psum("ps_tr2", [128, 1024], BF16)
    ynt_scr = nc.dram_tensor("ynt_scr", [GATE, T], BF16, kind="Internal").ap()

    il2 = S.dma("sync", lambda e: [e.dma_start(out=identb2[:, :], in_=io["ident"])], tag="il2")
    exl = S.dma("sync", lambda e: [e.dma_start(out=excs2[:, :], in_=io["excs"])], tag="exl")
    nl3 = S.dma("sync", lambda e: [e.dma_start(out=nw_bc[:, :], in_=io["norm_w"].partition_broadcast(128))],
                tag="nl3")

    fused = "R_ag" in io
    if fused:
        cmask = S.sbuf("cmask_sb", [128, 7], F32)
        cml = S.dma("sync", lambda e: [e.dma_start(out=cmask[:, :], in_=io["cmask"])], tag="cml")
    z0 = S.add("vector", lambda e: [e.memset(S0[:, :], 0.0)], tag="z0")
    prev = {"s0": z0}
    for cp in range(7):
        if fused:
            pld0 = S.dma("sync", lambda e, cp=cp: [e.dma_start(out=buf_a[:, :],
                        in_=io["P_ag"][cp:cp + 1, :].partition_broadcast(128))],
                        deps=[prev.get("pa_use")], tag=f"pld{cp}")
            pm1 = S.add("vector", lambda e: [e.tensor_scalar(buf_a[:, :], buf_a[:, :], 1.0, None, ALU.subtract)],
                        deps=[pld0], tag=f"pm1_{cp}")
            pld = S.add("vector", lambda e, cp=cp: [e.tensor_scalar(buf_a[:, :], buf_a[:, :], cmask[:, cp:cp + 1], 1.0, ALU.mult, ALU.add)],
                        deps=[pm1, cml], tag=f"pm2_{cp}")
            rld0 = S.dma("sync", lambda e, cp=cp: [e.dma_start(out=buf_r[:, :],
                        in_=io["R_ag"][cp * 128:(cp + 1) * 128, :])],
                        deps=[prev.get("rb_use")], tag=f"rld{cp}")
            rld = S.add("vector", lambda e, cp=cp: [e.tensor_scalar(buf_r[:, :], buf_r[:, :], cmask[:, cp:cp + 1], None, ALU.mult)],
                        deps=[rld0, cml], tag=f"rm_{cp}")
        else:
            pld = S.dma("sync", lambda e, cp=cp: [e.dma_start(out=buf_a[:, :],
                        in_=io["P_exp"][cp:cp + 1, :].partition_broadcast(128))],
                        deps=[prev.get("pa_use")], tag=f"pld{cp}")
            rld = S.dma("sync", lambda e, cp=cp: [e.dma_start(out=buf_r[:, :], in_=io["R_pre"][cp])],
                        deps=[prev.get("rb_use")], tag=f"rld{cp}")
        f1 = S.add("vector", lambda e: [e.tensor_tensor(S0[:, :], S0[:, :], buf_a[:, :], ALU.mult)],
                   deps=[pld, prev["s0"]], tag=f"f1_{cp}")
        f2 = S.add("vector", lambda e: [e.tensor_tensor(S0[:, :], S0[:, :], buf_r[:, :], ALU.add)],
                   deps=[rld, f1], tag=f"f2_{cp}")
        prev["s0"] = f2
        prev["pa_use"] = f1
        prev["rb_use"] = f2

    outs = []
    for ch in range(NCH):
        t0 = ch * 128
        cb = S.dma("sync", lambda e, ch=ch: [e.dma_start(out=buf_a[:, :],
                   in_=io["cdpref_exp"][ch:ch + 1, :].partition_broadcast(128))],
                   deps=[prev.get("pa_use")], tag=f"cpb{ch}")
        sj = S.add("vector", lambda e: [e.tensor_tensor(buf_r[:, :], S0[:, :], buf_a[:, :], ALU.mult)],
                   deps=[cb, prev["s0"], prev.get("rb_use")], tag=f"sj{ch}")
        prev["pa_use"] = sj
        prev["rb_use"] = sj
        sjb = S.add("vector", lambda e: [e.tensor_copy(S0_bf[:, :], buf_r[:, :])],
                    deps=[sj, prev.get("s0b_use")], tag=f"sjb{ch}")
        yld = S.dma("scalar", lambda e, t0=t0: [e.dma_start(out=ypre_t[:, :], in_=io["y_pre"][t0:t0 + 128, :])],
                    deps=[prev.get("yp_use")], tag=f"yld{ch}")
        if fused:
            gevs = []
            for q8 in range(16):
                gp = q8 % 2
                gdst = (ps_tr2, ps_pool[1][:, 0:256].bitcast(BF16))[gp]
                gl4 = S.dma(("scalar", "sync")[gp], lambda e, q8=q8, t0=t0, gp=gp: [e.dma_start(
                            out=gtb[gp][:, i * 128:(i + 1) * 128],
                            in_=io["gateT"][(q8 * 4 + i) * 128:(q8 * 4 + i + 1) * 128, t0:t0 + 128])
                            for i in range(4)],
                            n=4, deps=[prev.get(("gtr", gp))], tag=f"gl{ch}_{q8}")
                gtr = S.add("tensor", lambda e, gp=gp, gdst=gdst: [e.transpose(gdst[:, i * 128:(i + 1) * 128],
                            gtb[gp][:, i * 128:(i + 1) * 128], identb2[:, :]) for i in range(4)],
                            n_instr=4, deps=[gl4, il2, prev.get(("gev", gp)),
                                             prev.get("tr2_ev") if q8 == 0 else None,
                                             prev.get("pp1_use") if q8 == 1 else None],
                            tag=f"gtr{ch}_{q8}")
                prev[("gtr", gp)] = gtr
                gev = S.add("vector", lambda e, q8=q8, gdst=gdst: [e.tensor_copy(gate_t[:, q8 * 512:(q8 + 1) * 512],
                            gdst[:, 0:512])], deps=[gtr, prev.get("g_use")], tag=f"gev{ch}_{q8}")
                prev[("gev", gp)] = gev
                gevs.append(gev)
            gld = gevs[-1]
            prev["tr2_ev"] = gevs[-1]
        else:
            gld = S.dma("sync", lambda e, t0=t0: [e.dma_start(out=gate_t[:, :], in_=io["gate_tok"][t0:t0 + 128, :])],
                        deps=[prev.get("g_use")], tag=f"gld{ch}")
        clds = []
        for g in range(8):
            cl = S.dma("scalar", lambda e, g=g, t0=t0: [e.dma_start(out=CTm2[:, g * 128:(g + 1) * 128],
                       in_=io["CT_in"][g * 128:(g + 1) * 128, t0:t0 + 128])],
                       deps=[prev.get("ct_use")] if g == 0 else [], tag=f"cl2_{ch}_{g}")
            clds.append(cl)
        yadds = []
        for hb in range(32):
            g = hb // 4
            h0 = hb * 4
            def ycmm(e, h0=h0, g=g):
                last = None
                for hh in range(4):
                    h = h0 + hh
                    last = e.matmul(ps_pool[0][:, hh * 64:(hh + 1) * 64], CTm2[:, g * 128:(g + 1) * 128],
                                    S0_bf[:, h * 64:(h + 1) * 64], start=True, stop=True, skip_group_check=True)
                return [last]
            ym = S.add("tensor", ycmm, deps=[sjb] + clds + [prev.get("yc_ev")], tag=f"ycm{ch}_{hb}")
            e1 = S.add("vector", lambda e, h0=h0, ch=ch: [[e.tensor_scalar(tmp_yc[:, hh * 64:(hh + 1) * 64],
                        ps_pool[0][:, hh * 64:(hh + 1) * 64], excs2[:, ch * 128 + h0 + hh:ch * 128 + h0 + hh + 1],
                        None, ALU.mult) for hh in range(4)][-1]],
                       deps=[ym, exl, prev.get("tyc_use")], tag=f"e1_{ch}_{hb}")
            prev["yc_ev"] = e1
            e2 = S.add("vector", lambda e, hb=hb: [e.tensor_tensor(ypre_t[:, hb * 256:(hb + 1) * 256],
                        ypre_t[:, hb * 256:(hb + 1) * 256], tmp_yc[:, :], ALU.add)],
                       deps=[e1, yld], tag=f"e2_{ch}_{hb}")
            prev["tyc_use"] = e2
            yadds.append(e2)
        yg = S.add("vector", lambda e: [e.tensor_tensor(ypre_t[:, :], ypre_t[:, :], gate_t[:, :], ALU.mult)],
                   deps=yadds + [gld], tag=f"yg{ch}")
        prev["g_use"] = yg
        gn_done = []
        for g in range(8):
            g0 = g * 1024
            q1 = S.add("scalar", lambda e, g0=g0: [e.activation(sqg[:, :], ypre_t[:, g0:g0 + 1024], AF.Square)],
                       deps=[yg, prev.get("sqg_use")], tag=f"q1_{ch}_{g}")
            q2 = S.add("vector", lambda e: [e.reduce_sum(qg[:, :], sqg[:, :], axis=AX)],
                       deps=[q1, prev.get("qg_use")], tag=f"q2_{ch}_{g}")
            prev["sqg_use"] = q2
            q3 = S.add("vector", lambda e: [e.tensor_scalar(qg[:, :], qg[:, :], 1.0 / 1024, EPS, ALU.mult, ALU.add)],
                       deps=[q2], tag=f"q3_{ch}_{g}")
            q4 = S.add("scalar", lambda e: [e.activation(rstdg[:, :], qg[:, :], AF.Sqrt)],
                       deps=[q3, prev.get("rstdg_use")], tag=f"q4_{ch}_{g}")
            prev["qg_use"] = q4
            q5 = S.add("vector", lambda e: [e.reciprocal(rstdg[:, :], rstdg[:, :])], deps=[q4], tag=f"q5_{ch}_{g}")
            q6 = S.add("vector", lambda e, g0=g0: [e.tensor_scalar(ypre_t[:, g0:g0 + 1024], ypre_t[:, g0:g0 + 1024],
                        rstdg[:, :], None, ALU.mult)], deps=[q5], tag=f"q6_{ch}_{g}")
            prev["rstdg_use"] = q6
            gn_done.append(q6)
        nw1 = S.add("vector", lambda e: [e.tensor_tensor(ypre_t[:, :], ypre_t[:, :], nw_bc[:, :], ALU.mult)],
                    deps=gn_done + [nl3], tag=f"nw1_{ch}")
        prev["yp_use"] = None
        tr_done = []
        last_t1 = None
        for yq in range(16):
            def ytr(e, yq=yq):
                return [e.transpose(ps_tr2[:, i * 128:(i + 1) * 128],
                        ypre_t[:, (yq * 4 + i) * 128:(yq * 4 + i + 1) * 128], identb2[:, :]) for i in range(4)]
            t1 = S.add("tensor", ytr, n_instr=4, deps=[nw1, il2, prev.get("tr2_ev")], tag=f"ytr{ch}_{yq}")
            last_t1 = t1
            t2 = S.add("vector", lambda e: [e.tensor_copy(ytb[:, :], ps_tr2[:, 0:512])],
                       deps=[t1, prev.get("ytb_use")], tag=f"yte{ch}_{yq}")
            prev["tr2_ev"] = t2
            t3 = S.dma("gpsimd", lambda e, yq=yq, t0=t0: [e.dma_start(
                        out=ynt_scr[yq * 512:(yq + 1) * 512, t0:t0 + 128].rearrange("(q p) t -> p q t", p=128),
                        in_=ytb[:, :].rearrange("p (q t) -> p q t", q=4))],
                       deps=[t2], tag=f"yts{ch}_{yq}")
            prev["ytb_use"] = t3
            tr_done.append(t3)
        prev["ct_use"] = yadds[-1]
        prev["yp_use"] = last_t1
        outs += tr_done

    ynt_b = [S.sbuf(f"yntb{i}", [128, T], BF16) for i in range(2)]
    wo_stg = [buf_a[:, i * 512:(i + 1) * 512] for i in range(4)]
    wo_bf = [buf_r[:, i * 256:(i + 1) * 256].bitcast(BF16) for i in range(4)]
    oev2 = S.sbuf("oev2", [128, 512], F32)
    final = []
    for dtg in range(8):
        last_mm = None
        mm_prev = None
        for ct in range(64):
            p = ct % 2
            p4 = ct % 4
            yl = S.dma("scalar", lambda e, ct=ct, p=p: [e.dma_start(out=ynt_b[p][:, :],
                        in_=ynt_scr[ct * 128:(ct + 1) * 128, :])],
                        deps=outs + [prev.get(("ynb", p))], tag=f"oyl{dtg}_{ct}")
            wl = S.dma(("sync", "gpsimd")[ct % 2], lambda e, dtg=dtg, ct=ct, p4=p4: [e.dma_start(out=wo_stg[p4][:, :],
                        in_=io["wout_s"][dtg, ct])],
                        deps=[prev.get(("mwo", p4)) or prev.get("pa_use"), None if prev.get(("mwo", p4)) else prev.get("rb_use")],
                        tag=f"owl{dtg}_{ct}")
            wc = S.add("vector", lambda e, p4=p4: [e.tensor_copy(wo_bf[p4][:, :], wo_stg[p4][:, :])],
                       deps=[wl, prev.get(("mwobf", p4))], tag=f"owc{dtg}_{ct}")
            prev[("mwo", p4)] = wc
            def omm(e, ct=ct, p=p, p4=p4):
                last = None
                for dt4 in range(4):
                    for th in range(2):
                        ps = ps_pool[dt4] if th == 0 else (ps_pool[4 + dt4] if dt4 < 3 else ps_tr2.bitcast(F32))
                        last = e.matmul(ps[:, :], wo_bf[p4][:, dt4 * 128:(dt4 + 1) * 128],
                                        ynt_b[p][:, th * 512:(th + 1) * 512],
                                        start=(ct == 0), stop=(ct == 63))
                return [last]
            dd = [yl, wc] + ([mm_prev] if mm_prev else [])
            if ct == 0:
                dd += [prev.get("oev2_done"), prev.get("yc_ev")]
            mm = S.add("tensor", omm, deps=dd, tag=f"omm{dtg}_{ct}")
            mm_prev = mm
            prev[("ynb", p)] = mm
            prev[("mwobf", p4)] = mm
            last_mm = mm
        for dt4 in range(4):
            for th in range(2):
                psx = ps_pool[dt4] if th == 0 else (ps_pool[4 + dt4] if dt4 < 3 else ps_tr2.bitcast(F32))
                ev = S.add("vector", lambda e, psx=psx: [e.tensor_copy(oev2[:, :], psx[:, :])],
                           deps=[last_mm, prev.get("ost_m2")], tag=f"oev{dtg}_{dt4}_{th}")
                st = S.dma("gpsimd", lambda e, dtg=dtg, dt4=dt4, th=th: [e.dma_start(
                        out=io["soT"][dtg * 512 + dt4 * 128:dtg * 512 + (dt4 + 1) * 128, th * 512:(th + 1) * 512],
                        in_=oev2[:, :])], deps=[ev], tag=f"osm{dtg}_{dt4}_{th}")
                prev["ost_m2"] = st
                prev["oev2_done"] = ev
                final.append(st)
    return final


def build_mamba_m2_nc():
    nc = bass.Bass("TRN2", target_bir_lowering=False)
    io = {
        "y_pre": nc.dram_tensor("y_pre", [T, GATE], BF16, kind="ExternalInput").ap(),
        "gate_tok": nc.dram_tensor("gate_tok", [T, GATE], BF16, kind="ExternalInput").ap(),
        "R_pre": nc.dram_tensor("R_pre", [7, 128, 8192], F32, kind="ExternalInput").ap(),
        "P_exp": nc.dram_tensor("P_exp", [7, 8192], F32, kind="ExternalInput").ap(),
        "cdpref_exp": nc.dram_tensor("cdpref_exp", [NCH, 8192], F32, kind="ExternalInput").ap(),
        "CT_in": nc.dram_tensor("CT_in", [1024, T], BF16, kind="ExternalInput").ap(),
        "excs": nc.dram_tensor("excs", [128, NCH * 128], F32, kind="ExternalInput").ap(),
        "norm_w": nc.dram_tensor("norm_w", [1, GATE], F32, kind="ExternalInput").ap(),
        "wout_s": nc.dram_tensor("wout_s", [8, 64, 128, 512], F32, kind="ExternalInput").ap(),
        "ident": nc.dram_tensor("ident", [128, 128], BF16, kind="ExternalInput").ap(),
        "soT": nc.dram_tensor("soT", [D, T], F32, kind="ExternalOutput").ap(),
    }
    S = Sched(nc)
    outs = emit_mamba_m2(S, io)
    S.final_deps = outs
    S.final_engine = "gpsimd"
    S.emit()
    return nc



def emit_xT_build(S, x_dram, identf, ifl, name, perm=False, ones_col=False, deps=()):
    TP = T + (1 if ones_col else 0)
    xT_bf = S.sbuf(name, [128, 32 * TP], BF16)
    S.scope_push()
    xtile = S.sbuf(name + "_xt", [128, D], F32)
    ps_t = S.psum(name + "_pst", [128, 512], F32)
    ready = []
    prev = {}
    one_done = None
    if ones_col:
        one_done = S.add("vector", lambda e: [e.memset(xT_bf[:, 1024::TP], 1.0)],
                         deps=list(deps), tag=name + "_ones")
        ready.append(one_done)
    for tt in range(T // 128):
        ld = S.dma("scalar", lambda e, tt=tt: [e.dma_start(out=xtile[:, :], in_=x_dram[tt * 128:(tt + 1) * 128, :])],
                   deps=list(deps) + [prev.get("used")], tag=f"{name}_ld{tt}")
        if not perm:
            for q in range(8):
                tr = S.add("tensor", lambda e, q=q: [e.transpose(ps_t[:, i * 128:(i + 1) * 128],
                            xtile[:, (q * 4 + i) * 128:(q * 4 + i + 1) * 128], identf[:, :]) for i in range(4)],
                           n_instr=4, deps=[ld, ifl, prev.get("ev")], tag=f"{name}_tr{tt}_{q}")
                ev = S.add("vector", lambda e, q=q, tt=tt: [[e.tensor_copy(
                            xT_bf[:, (q * 4 + i) * TP + tt * 128:(q * 4 + i) * TP + (tt + 1) * 128],
                            ps_t[:, i * 128:(i + 1) * 128]) for i in range(4)][-1]],
                           deps=[tr], tag=f"{name}_ev{tt}_{q}")
                prev["ev"] = ev
                ready.append(ev)
            prev["used"] = ready[-1]
        else:
            for q in range(8):
                def trf(e, q=q):
                    ii = []
                    for i in range(4):
                        g = q * 4 + i
                        ii.append(e.transpose(ps_t[0:64, i * 128:(i + 1) * 128],
                                  xtile[:, 128 * g:128 * (g + 1):2], identf[:, :]))
                    return ii
                tr1 = S.add("tensor", trf, n_instr=4, deps=[ld, ifl, prev.get("ev")], tag=f"{name}_tre{tt}_{q}")
                ev1 = S.add("vector", lambda e, q=q, tt=tt: [[e.tensor_copy(
                            xT_bf[0:64, (q * 4 + i) * TP + tt * 128:(q * 4 + i) * TP + (tt + 1) * 128],
                            ps_t[0:64, i * 128:(i + 1) * 128]) for i in range(4)][-1]],
                            deps=[tr1], tag=f"{name}_eve{tt}_{q}")
                def trg(e, q=q):
                    ii = []
                    for i in range(4):
                        g = q * 4 + i
                        ii.append(e.transpose(ps_t[0:64, i * 128:(i + 1) * 128],
                                  xtile[:, 128 * g + 1:128 * (g + 1):2], identf[:, :]))
                    return ii
                tr2 = S.add("tensor", trg, n_instr=4, deps=[ld, ifl, ev1], tag=f"{name}_tro{tt}_{q}")
                ev2 = S.add("vector", lambda e, q=q, tt=tt: [[e.tensor_copy(
                            xT_bf[64:128, (q * 4 + i) * TP + tt * 128:(q * 4 + i) * TP + (tt + 1) * 128],
                            ps_t[0:64, i * 128:(i + 1) * 128]) for i in range(4)][-1]],
                            deps=[tr2], tag=f"{name}_evo{tt}_{q}")
                prev["ev"] = ev2
                ready += [ev1, ev2]
            prev["used"] = ready[-1]
    S.scope_pop()
    S.barrier(name + "_bar")
    return xT_bf, ready


def emit_ln_residual_T(S, io, identf, ifl, tag="lrT"):
    nt = T // 128
    so = S.sbuf(tag + "_so", [128, D], F32)
    sld4 = [S.sbuf(tag + f"_l{i}", [128, 512], F32) for i in range(2)]
    ps_t = [S.psum(tag + f"_pst{i}", [128, 512], F32) for i in range(2)]
    sq = S.sbuf(tag + "_sq", [128, D], BF16)
    xr = S.sbuf(tag + "_xr", [128, D], F32)
    m = S.sbuf(tag + "_m", [128, 1], F32); q = S.sbuf(tag + "_q", [128, 1], F32)
    v = S.sbuf(tag + "_v", [128, 1], F32); sd = S.sbuf(tag + "_sd", [128, 1], F32)
    r = S.sbuf(tag + "_r", [128, 1], F32)
    outs = []
    prev = {}
    for tt in range(nt):
        so_evs = []
        for q8 in range(8):
            lp = q8 % 2
            ld = S.dma(("scalar", "sync")[lp], lambda e, q8=q8, tt=tt, lp=lp: [e.dma_start(
                        out=sld4[lp][:, i * 128:(i + 1) * 128],
                        in_=io["soT"][(q8 * 4 + i) * 128:(q8 * 4 + i + 1) * 128, tt * 128:(tt + 1) * 128])
                        for i in range(4)],
                       n=4, deps=[prev.get(("tr", lp))], tag=f"{tag}sld{tt}_{q8}")
            tr = S.add("tensor", lambda e, lp=lp: [e.transpose(ps_t[lp][:, i * 128:(i + 1) * 128],
                        sld4[lp][:, i * 128:(i + 1) * 128], identf[:, :]) for i in range(4)],
                       n_instr=4, deps=[ld, ifl, prev.get(("ev", lp))], tag=f"{tag}tr{tt}_{q8}")
            prev[("tr", lp)] = tr
            ev = S.add("vector", lambda e, q8=q8, lp=lp: [e.tensor_copy(so[:, q8 * 512:(q8 + 1) * 512], ps_t[lp][:, :])],
                       deps=[tr, prev.get("oln"), prev.get("oadd")], tag=f"{tag}ev{tt}_{q8}")
            prev[("ev", lp)] = ev
            so_evs.append(ev)
        xld = S.dma("gpsimd", lambda e, tt=tt: [e.dma_start(out=xr[:, :], in_=io["x_tok"][tt * 128:(tt + 1) * 128, :])],
                    deps=[prev.get("ost")], tag=f"{tag}xld{tt}")
        rs = S.add("vector", lambda e: [e.reduce_sum(m[:, :], so[:, :], axis=AX)],
                   deps=so_evs + [prev.get("n1")], tag=f"{tag}rs{tt}")
        sqs = S.add("scalar", lambda e: [e.activation(sq[:, :], so[:, :], AF.Square)],
                    deps=so_evs + [prev.get("rq")], tag=f"{tag}sq{tt}")
        rq = S.add("vector", lambda e: [e.reduce_sum(q[:, :], sq[:, :], axis=AX)], deps=[sqs], tag=f"{tag}rq{tt}")
        prev["rq"] = rq
        n1 = S.add("vector", lambda e: [e.tensor_scalar(m[:, :], m[:, :], 1.0 / D, None, ALU.mult)],
                   deps=[rs], tag=f"{tag}n1{tt}")
        prev["n1"] = n1
        va = S.add("vector", lambda e: [e.tensor_scalar(v[:, :], m[:, :], m[:, :], None, ALU.mult)],
                   deps=[n1, prev.get("vc")], tag=f"{tag}va{tt}")
        vb = S.add("vector", lambda e: [e.scalar_tensor_tensor(v[:, :], q[:, :], 1.0 / D, v[:, :], ALU.mult, ALU.subtract)],
                   deps=[va, rq], tag=f"{tag}vb{tt}")
        vc = S.add("vector", lambda e: [e.tensor_scalar(v[:, :], v[:, :], EPS, None, ALU.add)],
                   deps=[vb, prev.get("sds")], tag=f"{tag}vc{tt}")
        prev["vc"] = vc
        sds = S.add("scalar", lambda e: [e.activation(sd[:, :], v[:, :], AF.Sqrt)], deps=[vc], tag=f"{tag}sd{tt}")
        prev["sds"] = sds
        rr = S.add("vector", lambda e: [e.reciprocal(r[:, :], sd[:, :])], deps=[sds, prev.get("oln")], tag=f"{tag}r{tt}")
        oln = S.add("vector", lambda e: [e.tensor_scalar(so[:, :], so[:, :], m[:, :], r[:, :], ALU.subtract, ALU.mult)],
                    deps=[rr, n1, rs, rq], tag=f"{tag}oln{tt}")
        prev["oln"] = oln
        oadd = S.add("vector", lambda e: [e.tensor_tensor(xr[:, :], xr[:, :], so[:, :], ALU.add)],
                     deps=[oln, xld, prev.get("ost")], tag=f"{tag}oadd{tt}")
        prev["oadd"] = oadd
        ost = S.dma("gpsimd", lambda e, tt=tt: [e.dma_start(out=io["xout"][tt * 128:(tt + 1) * 128, :], in_=xr[:, :])],
                    deps=[oadd], tag=f"{tag}ost{tt}")
        prev["ost"] = ost
        outs.append(ost)
    return outs


def build_fused_nc(upto="all", sim1=False):
    nc = bass.Bass("TRN2", target_bir_lowering=False)
    S = Sched(nc)
    S.setup_arena(204800)
    A = lambda name, shape, dt, kind="ExternalInput": nc.dram_tensor(name, shape, dt, kind=kind).ap()
    RG = [list(range(8))]

    def AG(in_ap, out_ap, rows, deps, tag):
        if sim1:
            def repl(e):
                ii = []
                for r in range(8):
                    o = out_ap[r * rows:(r + 1) * rows] if out_ap.ndim == 3 else out_ap[r * rows:(r + 1) * rows, :]
                    ii.append(e.dma_start(out=o, in_=in_ap))
                return ii
            return S.dma("gpsimd", repl, n=8, deps=deps, tag=tag)
        return S.add("gpsimd", lambda e: [e.collective_compute("AllGather", ALU.bypass, RG,
                     ins=[in_ap.opt()], outs=[out_ap.opt()])], deps=deps, tag=tag)

    xT0 = A("xT", [D, TE + 1], F32)
    x0_tok = A("x_tok", [TE, D], F32)
    wip_s = A("wip_s", [145, 128, D], F32)
    conv_w = A("conv_w", [128, 320], F32)
    conv_b = A("conv_b", [128, 80], F32)
    dt_bias = A("dt_bias", [128, 1], F32)
    A_col = A("A_col", [128, 1], F32)
    D_rep = A("D_rep", [128, 128], F32)
    triT = A("triT", [128, 128], BF16)
    ident = A("ident", [128, 128], BF16)
    identf = A("identf", [128, 128], F32)
    qb_gu = A("qb_gu", [256, 22016], U8)
    qz_gu = A("qz_gu", [4, 11008], U8)
    sc_gu = A("sc_gu", [4, 22016], F32)
    qb_d = A("qb_d", [704, 4096], U8)
    qz_d = A("qz_d", [11, 2048], U8)
    sc_d = A("sc_d", [11, 4096], F32)
    norm_w = A("norm_w", [1, GATE], F32)
    wout_s = A("wout_s", [8, 64, 128, 512], F32)
    Wqkv = A("Wqkv", [D, 3072], F32)
    bqkv = A("bqkv", [1, 3072], F32)
    Wo = A("Wo", [1024, D], F32)
    bo = A("bo", [1, D], F32)
    WrT = A("WrT", [D, 8], F32)
    br = A("br", [1, 8], F32)
    AT = A("AT", [D, 128], F32)
    Ball = A("Ball", [128, D], F32)
    E8 = A("E8", [8, 128], F32)
    cmask = A("cmask", [128, 7], F32)

    I = lambda name, shape, dt: nc.dram_tensor(name, shape, dt, kind="Internal").ap()
    wgu_part = I("wgu_part", [172, 128, 512], BF16)
    wgu_ag = I("wgu_ag", [1376, 128, 512], BF16)
    wd_part = I("wd_part", [1408, 4096], BF16)
    wd_ag = I("wd_ag", [11264, 4096], BF16)
    gateT_d = I("gateT", [GATE, T], BF16)
    xbc_d = I("xbc_conv", [CONV_DIM, T], BF16)
    ypre_d = I("y_pre", [T, GATE], BF16)
    excs_d = I("excs", [128, NCH * 128], F32)
    R_d = I("R_d", [128, 8192], F32)
    R_ag = I("R_ag", [1024, 8192], F32)
    cdpref_d = I("cdpref_d", [128, NCH], F32)
    cdpref_exp = I("cdpref_exp", [NCH, 8192], F32)
    P_d = I("P_d", [128, 1], F32)
    P_exp_d = I("P_exp_d", [1, 8192], F32)
    P_ag = I("P_ag", [8, 8192], F32)
    soT1 = I("soT1", [D, T], F32)
    x1_d = A("x1_d", [T, D], F32, kind="ExternalOutput") if upto in ("mamba", "l0ffn") else I("x1_d", [T, D], F32)
    soT2 = A("soT2", [D, T], F32, kind="ExternalOutput") if upto == "l0ffn" else I("soT2", [D, T], F32)
    x2_d = A("x2_d", [T, D], F32, kind="ExternalOutput") if upto == "l0ffn" else I("x2_d", [T, D], F32)
    qT_d = I("qT_d", [1024, T], BF16)
    kv_part = I("kv_part", [2048, 1024], BF16)
    kv_ag = I("kv_ag", [16384, 1024], BF16)
    x3_d = A("x3_d", [T, D], F32, kind="ExternalOutput") if upto == "l1attn" else I("x3_d", [T, D], F32)
    xout = A("xout", [T, D], F32, kind="ExternalOutput")

    identf_sb = S.sbuf("identf_root", [128, 128], F32)
    ifl = S.dma("sync", lambda e: [e.dma_start(out=identf_sb[:, :], in_=identf)], tag="ifl_root")

    io_dq = {"qb_gu": qb_gu, "qz_gu": qz_gu, "sc_gu": sc_gu, "wgu_part": wgu_part,
             "qb_d": qb_d, "qz_d": qz_d, "sc_d": sc_d, "wd_part": wd_part}
    S.scope_push()
    o_gu = emit_gptq_dequant(S, io_dq, None, 22016, 4, "gu", "qb_gu", "qz_gu", "sc_gu", "wgu_part", col_chunk=5504, blocked=True)
    S.scope_pop()
    S.barrier("b_dq")
    S.scope_push()
    o_d = emit_gptq_dequant(S, io_dq, None, 4096, 11, "dw", "qb_d", "qz_d", "sc_d", "wd_part", col_chunk=4096)
    S.scope_pop()
    ag_wgu = AG(wgu_part, wgu_ag, 172, o_gu, "ag_wgu")
    ag_wd = AG(wd_part, wd_ag, 1408, o_d, "ag_wd")

    S.barrier("b_m1")
    S.scope_push()
    io1 = {"xT": xT0, "x_tok": x0_tok, "wip_s": wip_s, "conv_w": conv_w, "conv_b": conv_b,
           "dt_bias": dt_bias, "A_col": A_col, "D_rep": D_rep, "triT": triT, "ident": ident,
           "identf": identf, "gateT": gateT_d, "xbc_conv": xbc_d, "y_pre": ypre_d,
           "R_agg": R_d, "P_core": P_d, "cdpref": cdpref_d, "excs": excs_d}
    m1o = emit_mamba_m1(S, io1)
    S.scope_pop()
    S.barrier("b_pexp")
    S.scope_push()
    pcol = S.sbuf("pcol", [128, 1], F32)
    prow = S.sbuf("prow", [1, 128], F32)
    perow = S.sbuf("perow", [1, 8192], F32)
    l1p = S.dma("sync", lambda e: [e.dma_start(out=pcol[:, :], in_=P_d)], deps=m1o, tag="pl1")
    l2p = S.dma("gpsimd", lambda e: [e.dma_start(out=prow[:, :], in_=pcol[:, :])], deps=[l1p], tag="pl2")
    l3p = S.add("vector", lambda e: [e.tensor_copy(perow[:, :].rearrange("a (h r) -> a h r", r=64),
                prow[:, :].unsqueeze(2).broadcast_to([1, 128, 64]))], deps=[l2p], tag="pl3")
    pexp = S.dma("sync", lambda e: [e.dma_start(out=P_exp_d, in_=perow[:, :])], deps=[l3p], tag="pexp")
    cdexps = []
    prev_cd = None
    for ch in range(NCH):
        c1 = S.dma("sync", lambda e, ch=ch: [e.dma_start(out=pcol[:, :], in_=cdpref_d[:, ch:ch + 1], allow_slow_non_contiguous=True)],
                   deps=[m1o[-1], l3p if ch == 0 else prev_cd], tag=f"cd1_{ch}")
        c2 = S.dma("gpsimd", lambda e: [e.dma_start(out=prow[:, :], in_=pcol[:, :])], deps=[c1], tag=f"cd2_{ch}")
        c3 = S.add("vector", lambda e: [e.tensor_copy(perow[:, :].rearrange("a (h r) -> a h r", r=64),
                    prow[:, :].unsqueeze(2).broadcast_to([1, 128, 64]))], deps=[c2], tag=f"cd3_{ch}")
        c4 = S.dma("sync", lambda e, ch=ch: [e.dma_start(out=cdpref_exp[ch:ch + 1, :], in_=perow[:, :])],
                   deps=[c3], tag=f"cd4_{ch}")
        prev_cd = c4
        cdexps.append(c4)
    ag_R = AG(R_d, R_ag, 128, m1o, "ag_R")
    ag_P = AG(P_exp_d, P_ag, 1, [pexp], "ag_P")
    S.scope_pop()

    S.barrier("b_m2")
    S.scope_push()
    io2 = {"y_pre": ypre_d, "gateT": gateT_d, "R_ag": R_ag, "P_ag": P_ag, "cmask": cmask,
           "cdpref_exp": cdpref_exp, "CT_in": xbc_d[9216:10240, :], "excs": excs_d,
           "norm_w": norm_w, "wout_s": wout_s, "ident": ident, "soT": soT1}
    m2o = emit_mamba_m2(S, io2)
    for st in [ag_R, ag_P]:
        pass
    S.scope_pop()

    S.barrier("b_lr1")
    S.scope_push()
    lr1 = emit_ln_residual_T(S, {"soT": soT1, "x_tok": x0_tok[128:TE, :], "xout": x1_d},
                             identf_sb, ifl, tag="lr1")
    S.scope_pop()
    if upto == "mamba":
        S.final_deps = lr1
        S.final_engine = "gpsimd"
        S.emit()
        return nc

    S.barrier("b_f2")
    S.scope_push()
    xTp_bf, xTp_rdy = emit_xT_build(S, x1_d, identf_sb, ifl, "xTp_f", perm=True, ones_col=True, deps=lr1)
    io_f2 = {"xT_pre": (xTp_bf, xTp_rdy), "x_tok": x1_d, "wgu_ag": wgu_ag, "wd_ag": wd_ag, "soT": soT2}
    f2o = emit_l0ffn_main(S, io_f2)
    S.scope_pop()
    S.barrier("b_lr2")
    S.scope_push()
    lr2 = emit_ln_residual_T(S, {"soT": soT2, "x_tok": x1_d, "xout": x2_d}, identf_sb, ifl, tag="lr2")
    S.scope_pop()
    if upto == "l0ffn":
        S.final_deps = lr2
        S.final_engine = "gpsimd"
        S.emit()
        return nc

    S.barrier("b_qkv")
    S.scope_push()
    xT2_bf, xT2_rdy = emit_xT_build(S, x2_d, identf_sb, ifl, "xT2_f", deps=lr2)
    io_qkv = {"xT_pre": (xT2_bf, xT2_rdy), "x_tok": x2_d, "Wqkv": Wqkv, "bqkv": bqkv,
              "qT": qT_d, "kT": kv_part[0:1024, :], "v": kv_part[1024:2048, :]}
    qo = emit_l1attn_qkv(S, io_qkv)
    S.scope_pop()
    ag_kv = AG(kv_part, kv_ag, 2048, qo, "ag_kv")
    S.barrier("b_flash")
    S.scope_push()
    io_fl = {"qT": qT_d, "kv_ag": kv_ag, "Wo": Wo, "bo": bo, "x_tok": x2_d, "xout": x3_d}
    flo = emit_l1attn_flash(S, io_fl)
    S.scope_pop()
    if upto == "l1attn":
        S.final_deps = flo
        S.final_engine = "gpsimd"
        S.emit()
        return nc

    S.barrier("b_l1f")
    S.scope_push()
    xT3_bf, xT3_rdy = emit_xT_build(S, x3_d, identf_sb, ifl, "xT3_f", deps=flo)
    io_lf = {"xT_pre": (xT3_bf, xT3_rdy), "x_tok": x3_d, "WrT": WrT, "br": br, "AT": AT,
             "Ball": Ball, "E8": E8, "ident": ident, "xout": xout}
    lfo = emit_l1ffn_core(S, io_lf)
    S.scope_pop()
    S.final_deps = lfo
    S.final_engine = "gpsimd"
    S.emit()
    return nc


from concourse.bass_utils import run_bass_kernel_spmd
import ml_dtypes

_NC_CACHE = {}


def _stage_fused(ins):
    x0 = np.ascontiguousarray(np.asarray(ins["L0_attn_hidden_states"], dtype=np.float32).reshape(8192, D))
    Wt = np.ascontiguousarray(ins["L0_attn_in_proj_w"].T)
    blocks = []
    for cb in range(145):
        blk = Wt[:, cb * 128:(cb + 1) * 128]
        blocks.append(blk.reshape(32, 128, 128).transpose(1, 0, 2).reshape(128, 4096))
    wip_s = np.ascontiguousarray(np.stack(blocks, 0))
    conv_w = np.ascontiguousarray(ins["L0_attn_conv_w"][:, 0, :].reshape(80, 128, 4).transpose(1, 0, 2).reshape(128, 320)).astype(np.float32)
    conv_b = np.ascontiguousarray(ins["L0_attn_conv_b"].reshape(80, 128).T).astype(np.float32)
    dtb = ins["L0_attn_dt_bias"].reshape(128, 1).astype(np.float32)
    A_col = (-np.arange(1, 129, dtype=np.float32)).reshape(128, 1)
    D_rep = np.ascontiguousarray(np.broadcast_to(ins["L0_attn_D"][None, :], (128, 128))).astype(np.float32)
    triT = (np.arange(128)[None, :] >= np.arange(128)[:, None]).astype(ml_dtypes.bfloat16)
    ident = np.eye(128).astype(ml_dtypes.bfloat16)
    identf = np.eye(128).astype(np.float32)
    WtO = np.ascontiguousarray(ins["L0_attn_out_proj_w"].T)
    wout_s = np.ascontiguousarray(WtO.reshape(64, 128, 8, 512).transpose(2, 0, 1, 3))
    nw = ins["L0_attn_norm_w"].reshape(1, GATE).astype(np.float32)
    qb_gu = pack_bytes_rows(ins["L0_ffn_qweight_gu"])
    qz_gu_u8 = np.ascontiguousarray(ins["L0_ffn_qzeros_gu"].view(np.uint8))
    sc_gu = ins["L0_ffn_scales_gu"].astype(np.float32)
    qb_d = np.concatenate([pack_bytes_rows(ins["L0_ffn_qweight_down"]), np.zeros((128, 4096), np.uint8)], axis=0)
    qz_d = np.concatenate([np.ascontiguousarray(ins["L0_ffn_qzeros_down"].view(np.uint8)), np.zeros((2, 2048), np.uint8)], axis=0)
    sc_d = np.concatenate([ins["L0_ffn_scales_down"].astype(np.float32), np.zeros((2, 4096), np.float32)], axis=0)
    Wqkv = ins["L1_attn_Wqkv"].astype(np.float32)
    bqkv = ins["L1_attn_bqkv"].reshape(1, -1).astype(np.float32)
    Wo = ins["L1_attn_Wo"].astype(np.float32)
    bo = ins["L1_attn_bo"].reshape(1, -1).astype(np.float32)
    E8m = np.zeros((8, 128), np.float32)
    for h in range(8):
        E8m[h, 16 * h:16 * (h + 1)] = 16.0
    WrT = np.ascontiguousarray(ins["L1_ffn_Wr"].astype(np.float32).T)
    AT = np.ascontiguousarray(ins["L1_ffn_A"].astype(np.float32).reshape(128, D).T)
    BallM = np.ascontiguousarray(ins["L1_ffn_B"].astype(np.float32).transpose(0, 2, 1).reshape(128, D))
    brr = ins["L1_ffn_br"].reshape(1, 8).astype(np.float32)
    in_maps = []
    for c in range(NCORES):
        lo = c * T - 128
        if lo < 0:
            xe = np.concatenate([np.zeros((128, D), np.float32), x0[:c * T + T]], axis=0)
        else:
            xe = x0[lo:c * T + T]
        xT = np.concatenate([np.ascontiguousarray(xe.T), np.ones((D, 1), np.float32)], axis=1)
        cm = np.zeros((128, 7), np.float32)
        cm[:, :c] = 1.0
        in_maps.append({
            "xT": np.ascontiguousarray(xT), "x_tok": np.ascontiguousarray(xe),
            "wip_s": wip_s, "conv_w": conv_w, "conv_b": conv_b, "dt_bias": dtb,
            "A_col": A_col, "D_rep": D_rep, "triT": triT, "ident": ident, "identf": identf,
            "qb_gu": np.ascontiguousarray(qb_gu[c * 256:(c + 1) * 256, :]),
            "qz_gu": np.ascontiguousarray(qz_gu_u8[c * 4:(c + 1) * 4, :]),
            "sc_gu": np.ascontiguousarray(sc_gu[c * 4:(c + 1) * 4, :]),
            "qb_d": np.ascontiguousarray(qb_d[c * 704:(c + 1) * 704, :]),
            "qz_d": np.ascontiguousarray(qz_d[c * 11:(c + 1) * 11, :]),
            "sc_d": np.ascontiguousarray(sc_d[c * 11:(c + 1) * 11, :]),
            "norm_w": nw, "wout_s": wout_s,
            "Wqkv": Wqkv, "bqkv": bqkv, "Wo": Wo, "bo": bo,
            "WrT": WrT, "br": brr, "AT": AT, "Ball": BallM, "E8": E8m,
            "cmask": cm,
        })
    return in_maps


def kernel(**inputs):
    ins = inputs
    assert np.array_equal(np.asarray(ins["L0_ffn_g_idx_gu"]).ravel(), np.arange(4096) // 128), "nonstandard g_idx_gu"
    assert np.array_equal(np.asarray(ins["L0_ffn_g_idx_down"]).ravel(), np.arange(INTER) // 128), "nonstandard g_idx_down"
    in_maps = _stage_fused(ins)
    if "fused" not in _NC_CACHE:
        _NC_CACHE["fused"] = build_fused_nc(upto="all")
    res = run_bass_kernel_spmd(_NC_CACHE["fused"], in_maps, core_ids=list(range(NCORES)))
    x4 = np.concatenate([np.asarray(res.results[c]["xout"]) for c in range(NCORES)], axis=0)
    return x4.reshape(1, 8192, D)
